# version 03
# baseline (speedup 1.0000x reference)
import contextlib

import numpy as np

import concourse.bass as bass
import concourse.mybir as mybir
from concourse.bass_utils import run_bass_kernel_spmd

F32 = mybir.dt.float32
BF16 = mybir.dt.bfloat16
I32 = mybir.dt.int32
AF = mybir.ActivationFunctionType
ALU = mybir.AluOpType
AX = mybir.AxisListType

B, T, D = 2, 8192, 1024
TLOC = 2048
DG = D // 128
DIN = 2048
DSTT = 16
DTR = 64
CH = DIN // 4
CGN = CH // 128
HF = 4096
HG = HF // 128
LN_EPS = 1e-5
ABS_EPS = 1e-6
TBS = 1024
NTB = T // TBS


class Prog:
    def __init__(self):
        self.nc = bass.Bass("TRN2", target_bir_lowering=False, debug=False)
        self.stack = contextlib.ExitStack()
        self._sems = {}

    def sem(self, name):
        if name not in self._sems:
            h = self.stack.enter_context(self.nc.semaphore(name))
            self._sems[name] = [h, 0]
        return self._sems[name][0]

    def count(self, name):
        self.sem(name)
        return self._sems[name][1]

    def inc(self, instr, name, n=1):
        self.sem(name)
        rec = self._sems[name]
        instr.then_inc(rec[0], n)
        rec[1] += n
        return rec[1]

    def dma(self, eng, out, in_, name):
        return self.inc(eng.dma_start(out=out, in_=in_), name, 16)

    def wait(self, eng, name, at=None):
        self.sem(name)
        rec = self._sems[name]
        val = rec[1] if at is None else at
        if val > 0:
            eng.wait_ge(rec[0], val)


def build_program(upto="full"):
    p = Prog()
    nc = p.nc
    v, s, t, g, sy = nc.vector, nc.scalar, nc.tensor, nc.gpsimd, nc.sync
    sb = p.stack.enter_context

    dr = {}

    def din(name, shape, dtype=F32):
        dr[name] = nc.dram_tensor(name, list(shape), dtype, kind="ExternalInput").ap()
        return dr[name]

    din("x0T", [128, DG, TLOC])
    out_ap = nc.dram_tensor("outT", [128, DG, TLOC], F32, kind="ExternalOutput").ap()

    din("l0_gamma", [128, DG]); din("l0_beta", [128, DG])
    din("w_in_u", [DIN // 128, 128, DG, 128])
    din("w_in_z", [DIN // 128, 128, DG, 128])
    din("conv_w", [128, CGN, 4]); din("conv_b", [128, CGN])
    din("w_xproj", [128, CGN, 96])
    din("w_dt", [DTR, CH])
    din("b_dt", [128, CGN])
    din("a_log", [128, CGN, DSTT])
    din("d_vec", [128, CGN])
    din("w_out", [DG, 128, DIN // 128, 128])
    din("w1T", [HG, 128, DG, 128]); din("w2T", [HG, 128, DG, 128])
    din("w1s", [128, HG]); din("w2s", [128, HG])
    din("h1T", [HG, 128, HG, 128]); din("h2T", [HG, 128, HG, 128])
    din("h1s", [128, HG]); din("h2s", [128, HG])
    din("w3T", [DG, 128, HG, 128]); din("w3s", [128, DG])
    din("tvec", [128, DG, 5])
    din("wkT", [DG, 128, DG, 128]); din("wvT", [DG, 128, DG, 128])
    din("wrT", [DG, 128, DG, 128]); din("woT", [DG, 128, DG, 128])
    din("wfcT", [DG, 128, DG, 128, 4] if False else [DG, 128, HF])
    din("lnw", [128, HG]); din("lnb", [128, HG])
    din("wprT", [DG, 128, HG * 128])
    din("selprev", [1, 8]); din("selchain", [1, 8]); din("selpen", [1, 8])
    din("onehot16", [DSTT, DSTT, 128], BF16)
    din("dynoff", [1, 4], I32)

    ag1_in = nc.dram_tensor("ag1_in", [DIN, TLOC], BF16)
    ag1_out = nc.dram_tensor("ag1_out", [4, DIN, TLOC], BF16)
    ar_in = nc.dram_tensor("ar_in", [96, T], F32)
    ar_out = nc.dram_tensor("ar_out", [96, T], F32)
    ag2_in = nc.dram_tensor("ag2_in", [CH, T], BF16)
    ag2_out = nc.dram_tensor("ag2_out", [DIN, T], BF16)
    u_spill = nc.dram_tensor("u_spill", [CH, T], BF16)
    x1_spill = nc.dram_tensor("x1_spill", [128, DG, TLOC], F32)
    xn2_spill = nc.dram_tensor("xn2_spill", [128, DG, TLOC], BF16)
    ffn_spill = nc.dram_tensor("ffn_spill", [128, DG, TLOC], BF16)
    rwkv_spill = nc.dram_tensor("rwkv_spill", [3, DG, 128, TLOC], BF16)
    wq_spill = nc.dram_tensor("wq_spill", [HG, 128, DG, 128], BF16)
    wqp_spill = nc.dram_tensor("wqp_spill", [DG, 128, HG, 128], BF16)
    e2_out = nc.dram_tensor("e2_out", [8, 128, DG], BF16)
    e2_in = nc.dram_tensor("e2_in", [128, DG], BF16)
    e3_out = nc.dram_tensor("e3_out", [8, 128, DG], F32)
    e3_in = nc.dram_tensor("e3_in", [128, DG], F32)
    e4_out = nc.dram_tensor("e4_out", [8, 128, 2, DG], F32)
    e4_in = nc.dram_tensor("e4_in", [128, 2, DG], F32)

    gamma_t = sb(nc.sbuf_tensor("gamma_t", [128, DG], F32))
    beta_t = sb(nc.sbuf_tensor("beta_t", [128, DG], F32))
    convb_t = sb(nc.sbuf_tensor("convb_t", [128, CGN], F32))
    negbdt_t = sb(nc.sbuf_tensor("negbdt_t", [128, CGN], F32))
    dvec_t = sb(nc.sbuf_tensor("dvec_t", [128, CGN], F32))
    convw_t = sb(nc.sbuf_tensor("convw_t", [128, CGN, 4], F32))
    e_a = sb(nc.sbuf_tensor("e_a", [128, CGN, DSTT], F32))
    wxp_t = sb(nc.sbuf_tensor("wxp_t", [128, CGN, 96], BF16))
    wdt_t = sb(nc.sbuf_tensor("wdt_t", [DTR, CH], BF16))
    tvec_t = sb(nc.sbuf_tensor("tvec_t", [128, DG, 5], F32))
    wlane = sb(nc.sbuf_tensor("wlane", [128, DG, 5], F32))
    omix = sb(nc.sbuf_tensor("omix", [128, DG, 3], F32))
    w1s_t = sb(nc.sbuf_tensor("w1s_t", [128, HG], F32))
    w2s_t = sb(nc.sbuf_tensor("w2s_t", [128, HG], F32))
    h1s_t = sb(nc.sbuf_tensor("h1s_t", [128, HG], F32))
    h2s_t = sb(nc.sbuf_tensor("h2s_t", [128, HG], F32))
    w3s_t = sb(nc.sbuf_tensor("w3s_t", [128, DG], F32))
    lnw_t = sb(nc.sbuf_tensor("lnw_t", [128, HG], F32))
    lnb_t = sb(nc.sbuf_tensor("lnb_t", [128, HG], F32))
    selp = sb(nc.sbuf_tensor("selp", [128, 8], F32))
    selc_pen = sb(nc.sbuf_tensor("selc_pen", [128, 8], F32))
    selrow = sb(nc.sbuf_tensor("selrow", [1, 24], F32))
    ones_col = sb(nc.sbuf_tensor("ones_col", [128, 1], BF16))
    ones_row = sb(nc.sbuf_tensor("ones_row", [1, 128], BF16))
    ones_row_f = sb(nc.sbuf_tensor("ones_row_f", [1, 128], F32))
    offt = sb(nc.sbuf_tensor("offt", [1, 4], I32))
    stat = sb(nc.sbuf_tensor("stat", [128, 32], F32))
    srow = sb(nc.sbuf_tensor("srow", [1, 4, 512], F32))
    stg_x = sb(nc.sbuf_tensor("stg_x", [128, 512], BF16))
    stg_q = sb(nc.sbuf_tensor("stg_q", [128, 512], BF16))
    stg_y = sb(nc.sbuf_tensor("stg_y", [128, 512], BF16))
    epsc = sb(nc.sbuf_tensor("epsc", [1, 2], F32))
    tmp_a = sb(nc.sbuf_tensor("tmp_a", [128, TLOC], F32))
    tmp_b = sb(nc.sbuf_tensor("tmp_b", [128, TLOC], F32))
    tmp_c = sb(nc.sbuf_tensor("tmp_c", [128, TLOC], BF16))
    tmp_e = sb(nc.sbuf_tensor("tmp_e", [128, TLOC], BF16))

    xn_scope = contextlib.ExitStack()
    xn = xn_scope.enter_context(nc.sbuf_tensor("xn", [128, DG, TLOC], BF16))
    pb = [sb(nc.psum_tensor(f"pb{i}", [128, 512], F32)) for i in range(8)]
    roff = sb(g.register("roff"))
    rtmp = sb(g.register("rtmp"))

    uid = [0]

    def gemm(wdram, n_oc, n_kg, wtile, rhs_fn, evac_fn, n_tq, bank_of,
             pre_wait=()):
        uid[0] += 1
        ldsem = [f"gw{uid[0]}a", f"gw{uid[0]}b"]
        for nm in pre_wait:
            p.wait(t, nm)
        m0 = p.count("mm")
        last_use = [m0, m0]
        bank_free = {}
        for oc in range(n_oc):
            par = oc % 2
            p.wait(g, "mm", at=last_use[par])
            ldc = p.dma(g, wtile[:, par, 0:n_kg * 128], wdram(oc), ldsem[par])
            for tqi in range(n_tq):
                bk = bank_of(oc, tqi)
                if bk in bank_free:
                    nm, cnt = bank_free[bk]
                    p.wait(t, nm, at=cnt)
                p.wait(t, ldsem[par], at=ldc)
                for kg in range(n_kg):
                    i1 = t.matmul(pb[bk][:, :], wtile[:, par, kg * 128:(kg + 1) * 128],
                                  rhs_fn(kg, tqi), start=(kg == 0), stop=(kg == n_kg - 1))
                mmc = p.inc(i1, "mm")
                last_use[par] = mmc
                bank_free[bk] = evac_fn(oc, tqi, pb[bk])

    def phase_barrier():
        for nm in ("dsy", "dgp", "vx", "sc", "mm", "gp", "cc"):
            for eng in (v, s, t, g, sy):
                p.wait(eng, nm)

    def ln_emit(src, n_g, dst_bf, eps, plain_cb=None, gamma_ap=None, beta_ap=None,
                tcols=TLOC, toff=0):
        inv_n = 1.0 / (n_g * 128)
        p.wait(s, "vx")
        ps_x, ps_q, ps_r, ps_m = pb[4], pb[5], pb[6], pb[7]
        for blk in range(tcols // 512):
            t0 = toff + blk * 512
            for gi in range(n_g):
                p.wait(v, "mm"); p.wait(s, "mm")
                p.inc(v.tensor_copy(stg_x[:], src[:, gi, t0:t0 + 512]), "vx")
                p.inc(s.activation(stg_q[:], src[:, gi, t0:t0 + 512], AF.Square), "sc")
                p.wait(t, "vx"); p.wait(t, "sc")
                last = gi == n_g - 1
                t.matmul(ps_x[:1, :], ones_col[:], stg_x[:], start=(gi == 0), stop=last)
                i2 = t.matmul(ps_q[:1, :], ones_col[:], stg_q[:], start=(gi == 0), stop=last)
                p.inc(i2, "mm")
            p.wait(v, "mm")
            p.inc(v.tensor_scalar_mul(srow[:1, 0, :], ps_x[:1, :], inv_n), "vx")
            v.tensor_mul(srow[:1, 1, :], srow[:1, 0, :], srow[:1, 0, :])
            p.inc(v.scalar_tensor_tensor(srow[:1, 1, :], ps_q[:1, :], inv_n,
                                         srow[:1, 1, :], op0=ALU.mult,
                                         op1=ALU.subtract), "vx")
            p.wait(s, "vx")
            s.activation(srow[:1, 2, :], srow[:1, 1, :], AF.Ln, bias=epsc[:1, 0:1])
            p.inc(s.activation(srow[:1, 2, :], srow[:1, 2, :], AF.Exp, scale=-0.5), "sc")
            p.wait(v, "sc")
            v.tensor_mul(srow[:1, 3, :], srow[:1, 0, :], srow[:1, 2, :])
            p.inc(v.tensor_scalar_mul(srow[:1, 3, :], srow[:1, 3, :], -1.0), "vx")
            p.wait(t, "vx")
            t.matmul(ps_r[:, :], ones_row_f[:], srow[:1, 2, :], start=True, stop=True)
            p.inc(t.matmul(ps_m[:, :], ones_row_f[:], srow[:1, 3, :], start=True,
                           stop=True), "mm")
            p.wait(v, "mm")
            for gi in range(n_g):
                v.tensor_mul(tmp_a[:, 0:512], src[:, gi, t0:t0 + 512], ps_r[:, :])
                ia = v.tensor_add(tmp_a[:, 0:512], tmp_a[:, 0:512], ps_m[:, :])
                if gamma_ap is not None:
                    ia = v.tensor_scalar(dst_bf[:, gi, t0:t0 + 512], tmp_a[:, 0:512],
                                         gamma_ap[:, gi:gi + 1], beta_ap[:, gi:gi + 1],
                                         op0=ALU.mult, op1=ALU.add)
                else:
                    ia = v.tensor_copy(dst_bf[:, gi, t0:t0 + 512], tmp_a[:, 0:512])
                p.inc(ia, "vx")
                if plain_cb is not None:
                    plain_cb(gi, t0)

    def chain_select(out_col_fn, gath_fn, n_items):
        for it in range(n_items):
            v.memset(tmp_a[:, 0:DG], 0.0)
            for j in range(8):
                ia = v.scalar_tensor_tensor(tmp_a[:, 0:DG], gath_fn(it, j),
                                            selp[:, j:j + 1], tmp_a[:, 0:DG],
                                            op0=ALU.mult, op1=ALU.add)
            p.inc(v.tensor_copy(out_col_fn(it), tmp_a[:, 0:DG]), "vx")

    with contextlib.ExitStack() as ph01:
        x1 = ph01.enter_context(nc.sbuf_tensor("x1", [128, DG, TLOC], F32))
        xn2 = ph01.enter_context(nc.sbuf_tensor("xn2", [128, DG, TLOC], BF16))
        wz = ph01.enter_context(nc.sbuf_tensor("wz", [128, 2, DG * 128], BF16))

        p.dma(sy, x1[:], dr["x0T"], "dsy")
        for nm, tl in [("l0_gamma", gamma_t), ("l0_beta", beta_t), ("conv_b", convb_t),
                       ("d_vec", dvec_t), ("conv_w", convw_t), ("a_log", e_a),
                       ("tvec", tvec_t), ("w1s", w1s_t), ("w2s", w2s_t),
                       ("h1s", h1s_t), ("h2s", h2s_t), ("w3s", w3s_t),
                       ("lnw", lnw_t), ("lnb", lnb_t)]:
            p.dma(sy, tl[:], dr[nm], "dsy")
        p.dma(sy, stat[:, 0:CGN], dr["b_dt"], "dsy")
        p.dma(sy, selrow[:, 0:8], dr["selprev"], "dsy")
        p.dma(sy, selrow[:, 8:16], dr["selchain"], "dsy")
        p.dma(sy, selrow[:, 16:24], dr["selpen"], "dsy")
        p.dma(g, wxp_t[:], dr["w_xproj"], "dgp")
        p.dma(g, wdt_t[:], dr["w_dt"], "dgp")
        p.dma(g, offt[:], dr["dynoff"], "dgp")

        p.wait(v, "dsy")
        v.memset(epsc[:], LN_EPS)
        v.memset(ones_col[:], 1.0)
        v.memset(ones_row[:], 1.0)
        v.memset(ones_row_f[:], 1.0)
        p.inc(v.tensor_scalar_mul(negbdt_t[:], stat[:, 0:CGN], -1.0), "vx")
        for j in range(3):
            p.inc(v.tensor_scalar(omix[:, :, j], tvec_t[:, :, 2 + j], -1.0, 1.0,
                                  op0=ALU.mult, op1=ALU.add), "vx")
        p.wait(s, "dsy")
        p.inc(s.activation(e_a[:], e_a[:], AF.Exp), "sc")
        p.inc(s.activation(wlane[:, :, 0], tvec_t[:, :, 0], AF.Exp), "sc")
        p.wait(v, "sc")
        p.inc(v.tensor_scalar_mul(wlane[:, :, 0], wlane[:, :, 0], -1.0), "vx")
        p.wait(s, "vx")
        p.inc(s.activation(wlane[:, :, 1], wlane[:, :, 0], AF.Exp), "sc")
        p.inc(s.activation(wlane[:, :, 2], wlane[:, :, 0], AF.Exp, scale=float(TLOC)), "sc")
        p.inc(s.activation(wlane[:, :, 3], tvec_t[:, :, 1], AF.Exp), "sc")
        p.wait(t, "dsy"); p.wait(t, "vx")
        p.inc(t.matmul(pb[0][:, 0:24], ones_row_f[:], selrow[:1, :], start=True,
                       stop=True), "mm")
        p.wait(v, "mm")
        p.inc(v.tensor_copy(selp[:], pb[0][:, 0:8]), "vx")
        p.inc(v.tensor_copy(selc_pen[:], pb[0][:, 16:24]), "vx")

        def add_xn(gi, t0):
            p.inc(v.tensor_add(x1[:, gi, t0:t0 + 512], x1[:, gi, t0:t0 + 512],
                               tmp_a[:, 0:512]), "vx")

        ln_emit(x1, DG, xn, LN_EPS, plain_cb=add_xn)
        ln_emit(xn, DG, xn2, LN_EPS, gamma_ap=gamma_t, beta_ap=beta_t)

        def evac_u(oc, tqi, bank):
            p.wait(s, "mm")
            p.wait(s, "dsy")
            cnt = p.inc(s.copy(stg_y[:], bank[:, :]), "sc")
            p.wait(sy, "sc")
            p.dma(sy, ag1_in.ap()[oc * 128:(oc + 1) * 128, tqi * 512:(tqi + 1) * 512],
                  stg_y[:], "dsy")
            return ("sc", cnt)

        gemm(lambda oc: dr["w_in_u"][oc], DIN // 128, DG, wz,
             lambda kg, tqi: xn2[:, kg, tqi * 512:(tqi + 1) * 512],
             evac_u, 4, lambda oc, tqi: (oc * 4 + tqi) % 4, pre_wait=("vx",))

        p.wait(g, "dsy")
        p.inc(g.collective_compute("AllGather", ALU.bypass,
                                   replica_groups=[[0, 1, 2, 3], [4, 5, 6, 7]],
                                   ins=[ag1_in.ap()], outs=[ag1_out.ap()]), "cc")
        p.wait(sy, "vx")
        p.dma(sy, x1_spill.ap(), x1[:], "dsy")
        p.dma(sy, xn2_spill.ap(), xn2[:], "dsy")
        p.wait(g, "dsy")

    with contextlib.ExitStack() as ph2:
        phase_barrier()
        xdbl = ph2.enter_context(nc.sbuf_tensor("xdbl", [96, T], F32))
        up_h = ph2.enter_context(nc.sbuf_tensor("up_h", [128, CGN, 3 + TBS], BF16))
        u_ch = ph2.enter_context(nc.sbuf_tensor("u_ch", [128, CGN, TBS], BF16))
        dl_ch = ph2.enter_context(nc.sbuf_tensor("dl_ch", [128, CGN, TBS], BF16))
        du_ch = ph2.enter_context(nc.sbuf_tensor("du_ch", [128, CGN, TBS], BF16))
        bb_s = ph2.enter_context(nc.sbuf_tensor("bb_s", [128, TBS], BF16))
        cb_s = ph2.enter_context(nc.sbuf_tensor("cb_s", [128, TBS], BF16))
        da_t = ph2.enter_context(nc.sbuf_tensor("da_t", [128, TBS], BF16))
        dbu_t = ph2.enter_context(nc.sbuf_tensor("dbu_t", [128, TBS], BF16))
        h_t = ph2.enter_context(nc.sbuf_tensor("h_t", [128, 2, TBS], BF16))
        y_acc = ph2.enter_context(nc.sbuf_tensor("y_acc", [128, CGN, TBS], F32))
        carry = ph2.enter_context(nc.sbuf_tensor("carry", [128, CGN * DSTT], F32))
        bneg_rows = ph2.enter_context(nc.sbuf_tensor("bneg_rows", [DSTT, T], BF16))
        c_rows = ph2.enter_context(nc.sbuf_tensor("c_rows", [DSTT, T], BF16))
        onehot = ph2.enter_context(nc.sbuf_tensor("onehot", [DSTT, DSTT, 128], BF16))
        p.dma(sy, onehot[:], dr["onehot16"], "dsy")
        dt_bf = ph2.enter_context(nc.sbuf_tensor("dt_bf", [DTR, TBS], BF16))

        v.memset(up_h[:, :, 0:3], 0.0)
        p.inc(v.memset(carry[:], 0.0), "vx")
        p.wait(g, "cc")
        for tb in range(NTB):
            g.reg_load(roff, offt[:1, 0:1])
            g.reg_add(rtmp, roff, (((tb * TBS) // TLOC) * DIN) * TLOC
                      + (tb * TBS) % TLOC)
            srcu = bass.AP(ag1_out, rtmp,
                           [[TLOC, 128], [128 * TLOC, CGN], [1, TBS]])
            p.wait(g, "vx")
            nc.allow_non_contiguous_dma("strided gather from AG output")
            with nc.allow_non_contiguous_dma("strided gather from AG output"):
                p.inc(g.dma_start(out=up_h[:, :, 3:3 + TBS], in_=srcu,
                                  bounds_check=None), "dgp", 16)
            p.wait(v, "dgp")
            p.wait(v, "dsy")
            p.wait(s, "dsy"); p.wait(s, "mm")
            for cg in range(CGN):
                p.wait(v, "sc")
                v.tensor_scalar_mul(tmp_a[:, 0:TBS], up_h[:, cg, 0:TBS],
                                    convw_t[:, cg, 0:1])
                for j in range(1, 4):
                    ic = v.scalar_tensor_tensor(tmp_a[:, 0:TBS], up_h[:, cg, j:j + TBS],
                                                convw_t[:, cg, j:j + 1], tmp_a[:, 0:TBS],
                                                op0=ALU.mult, op1=ALU.add)
                p.inc(ic, "vx")
                p.inc(v.tensor_copy(up_h[:, cg, 0:3], up_h[:, cg, TBS:TBS + 3]), "vx")
                p.wait(s, "vx")
                p.inc(s.activation(u_ch[:, cg, :], tmp_a[:, 0:TBS], AF.Silu,
                                   bias=convb_t[:, cg:cg + 1]), "sc")
            p.wait(sy, "sc")
            p.dma(sy, u_spill.ap()[:, tb * TBS:(tb + 1) * TBS]
                  .rearrange("(cg q) t -> q cg t", q=128), u_ch[:], "dsy")
            p.wait(t, "sc"); p.wait(t, "vx")
            for nb in range(TBS // 512):
                for cg in range(CGN):
                    i1 = t.matmul(pb[4 + nb][:96, :], wxp_t[:, cg, :],
                                  u_ch[:, cg, nb * 512:(nb + 1) * 512],
                                  start=(cg == 0), stop=(cg == CGN - 1))
                p.inc(i1, "mm")
            p.wait(v, "mm")
            for nb in range(TBS // 512):
                p.inc(v.tensor_copy(xdbl[:, tb * TBS + nb * 512:tb * TBS + (nb + 1) * 512],
                                    pb[4 + nb][:96, :]), "vx")
        p.wait(sy, "vx")
        p.dma(sy, ar_in.ap(), xdbl[:], "dsy")
        p.wait(g, "dsy")
        p.inc(g.collective_compute("AllReduce", ALU.add,
                                   replica_groups=[[0, 1, 2, 3], [4, 5, 6, 7]],
                                   ins=[ar_in.ap()], outs=[ar_out.ap()]), "cc")
        p.wait(sy, "cc")
        p.dma(sy, xdbl[:], ar_out.ap(), "dsy")
        p.wait(v, "dsy"); p.wait(s, "dsy"); p.wait(t, "dsy")
        p.dma(g, bneg_rows[:], xdbl[64:80, :], "dgp")
        p.dma(g, c_rows[:], xdbl[80:96, :], "dgp")
        p.wait(v, "dgp")
        p.inc(v.tensor_scalar_mul(bneg_rows[:], bneg_rows[:], -1.0), "vx")


        for tb in range(NTB):
            t0 = tb * TBS
            p.wait(g, "vx")
            p.dma(g, u_ch[:], u_spill.ap()[:, t0:t0 + TBS]
                  .rearrange("(cg q) t -> q cg t", q=128), "dgp")
            p.wait(v, "mm")
            p.inc(v.tensor_copy(dt_bf[:], xdbl[0:DTR, t0:t0 + TBS]), "vx")
            p.wait(t, "sc")
            p.wait(t, "vx")
            p.wait(s, "vx")
            for cg in range(CGN):
                for nb in range(TBS // 512):
                    i1 = t.matmul(pb[4 + nb][:, :], wdt_t[:, cg * 128:(cg + 1) * 128],
                                  dt_bf[:, nb * 512:(nb + 1) * 512],
                                  start=True, stop=True)
                    p.inc(i1, "mm")
                    p.wait(s, "mm")
                    p.inc(s.activation(dl_ch[:, cg, nb * 512:(nb + 1) * 512],
                                       pb[4 + nb][:, :], AF.Sigmoid, scale=-1.0,
                                       bias=negbdt_t[:, cg:cg + 1]), "sc")
                    p.wait(t, "sc")
            for cg in range(CGN):
                p.inc(s.activation(dl_ch[:, cg, :], dl_ch[:, cg, :], AF.Ln), "sc")
            p.wait(v, "sc"); p.wait(v, "dgp")
            for cg in range(CGN):
                p.inc(v.tensor_mul(du_ch[:, cg, :], dl_ch[:, cg, :], u_ch[:, cg, :]), "vx")
            p.inc(v.memset(y_acc[:], 0.0), "vx")
            for si in range(DSTT):
                p.wait(t, "sc"); p.wait(t, "vx")
                for nb in range(TBS // 512):
                    t.matmul(pb[4 + nb][:, :], onehot[:, si, :],
                             bneg_rows[:, t0 + nb * 512:t0 + (nb + 1) * 512],
                             start=True, stop=True)
                    i1 = t.matmul(pb[6 + nb][:, :], onehot[:, si, :],
                                  c_rows[:, t0 + nb * 512:t0 + (nb + 1) * 512],
                                  start=True, stop=True)
                p.inc(i1, "mm")
                p.wait(s, "mm")
                p.wait(s, "vx")
                for nb in range(TBS // 512):
                    s.copy(bb_s[:, nb * 512:(nb + 1) * 512], pb[4 + nb][:, :])
                    p.inc(s.copy(cb_s[:, nb * 512:(nb + 1) * 512], pb[6 + nb][:, :]), "sc")
                for cg in range(CGN):
                    par = (si * CGN + cg) % 2
                    p.wait(s, "vx")
                    p.inc(s.activation(da_t[:], dl_ch[:, cg, :], AF.Exp,
                                       scale=e_a[:, cg, si:si + 1]), "sc")
                    p.wait(v, "sc")
                    p.inc(v.tensor_mul(dbu_t[:], du_ch[:, cg, :], bb_s[:]), "vx")
                    p.wait(v, "gp")
                    p.inc(v.tensor_tensor_scan(h_t[:, par, :], da_t[:], dbu_t[:],
                                               carry[:, cg * DSTT + si:cg * DSTT + si + 1],
                                               ALU.mult, ALU.add), "vx")
                    p.wait(g, "vx")
                    p.inc(g.tensor_copy(carry[:, cg * DSTT + si:cg * DSTT + si + 1],
                                        h_t[:, par, TBS - 1:TBS]), "gp")
                    v.tensor_mul(tmp_c[:, 0:TBS], h_t[:, par, :], cb_s[:])
                    p.inc(v.tensor_add(y_acc[:, cg, :], y_acc[:, cg, :],
                                       tmp_c[:, 0:TBS]), "vx")
            for cg in range(CGN):
                p.inc(v.scalar_tensor_tensor(y_acc[:, cg, :], u_ch[:, cg, :],
                                             dvec_t[:, cg:cg + 1], y_acc[:, cg, :],
                                             op0=ALU.mult, op1=ALU.add), "vx")
                p.wait(v, "dsy")
                p.inc(v.tensor_copy(tmp_e[:, 0:TBS], y_acc[:, cg, :]), "vx")
                p.wait(sy, "vx")
                p.dma(sy, ag2_in.ap()[cg * 128:(cg + 1) * 128, t0:t0 + TBS],
                      tmp_e[:, 0:TBS], "dsy")
        p.wait(g, "dsy")
        p.inc(g.collective_compute("AllGather", ALU.bypass,
                                   replica_groups=[[0, 1, 2, 3], [4, 5, 6, 7]],
                                   ins=[ag2_in.ap()], outs=[ag2_out.ap()]), "cc")

    with contextlib.ExitStack() as ph3:
        phase_barrier()
        hbuf = ph3.enter_context(nc.sbuf_tensor("hbuf", [128, HG, 512], BF16))
        h2buf = ph3.enter_context(nc.sbuf_tensor("h2buf", [128, HG, 512], BF16))
        wba = ph3.enter_context(nc.sbuf_tensor("wba", [128, 2, HG * 128], BF16))
        wbb = ph3.enter_context(nc.sbuf_tensor("wbb", [128, 2, HG * 128], BF16))

        for q in range(TLOC // 512):
            q0 = q * 512

            def evac_g1(oc, tqi, bank):
                p.wait(s, "mm")
                cnt = p.inc(s.activation(hbuf[:, oc, :], bank[:, :], AF.Silu,
                                         scale=w1s_t[:, oc:oc + 1]), "sc")
                return ("sc", cnt)

            def evac_g2(oc, tqi, bank):
                p.wait(v, "mm"); p.wait(v, "sc")
                cnt = p.inc(v.scalar_tensor_tensor(hbuf[:, oc, :], bank[:, :],
                                                   w2s_t[:, oc:oc + 1], hbuf[:, oc, :],
                                                   op0=ALU.mult, op1=ALU.mult), "vx")
                return ("vx", cnt)

            gemm(lambda oc: dr["w1T"][oc], HG, DG, wba,
                 lambda kg, tqi: xn[:, kg, q0:q0 + 512], evac_g1, 1,
                 lambda oc, tqi: oc % 2, pre_wait=("vx",))
            gemm(lambda oc: dr["w2T"][oc], HG, DG, wbb,
                 lambda kg, tqi: xn[:, kg, q0:q0 + 512], evac_g2, 1,
                 lambda oc, tqi: 2 + oc % 2, pre_wait=("sc",))

            def evac_h1(oc, tqi, bank):
                p.wait(s, "mm")
                cnt = p.inc(s.activation(h2buf[:, oc, :], bank[:, :], AF.Gelu,
                                         scale=h1s_t[:, oc:oc + 1]), "sc")
                return ("sc", cnt)

            gemm(lambda oc: dr["h1T"][oc], HG, HG, wba,
                 lambda kg, tqi: hbuf[:, kg, :], evac_h1, 1,
                 lambda oc, tqi: 4 + oc % 2, pre_wait=("vx",))

            def evac_h2(oc, tqi, bank):
                p.wait(s, "mm")
                cnt = p.inc(s.activation(hbuf[:, oc, :], bank[:, :], AF.Gelu,
                                         scale=h2s_t[:, oc:oc + 1]), "sc")
                return ("sc", cnt)

            gemm(lambda oc: dr["h2T"][oc], HG, HG, wbb,
                 lambda kg, tqi: h2buf[:, kg, :], evac_h2, 1,
                 lambda oc, tqi: 6 + oc % 2, pre_wait=("sc",))

            def evac_w3(oc, tqi, bank):
                p.wait(s, "mm"); p.wait(s, "dsy")
                cnt = p.inc(s.activation(stg_y[:], bank[:, :], AF.Copy,
                                         scale=w3s_t[:, oc:oc + 1]), "sc")
                p.wait(sy, "sc")
                p.dma(sy, ffn_spill.ap()[:, oc, q0:q0 + 512], stg_y[:], "dsy")
                return ("sc", cnt)

            gemm(lambda oc: dr["w3T"][oc], DG, HG, wba,
                 lambda kg, tqi: hbuf[:, kg, :], evac_w3, 1,
                 lambda oc, tqi: oc % 2, pre_wait=("sc",))

    xn_scope.close()
    ph4 = contextlib.ExitStack()
    with ph4:
        phase_barrier()
        ph4b = ph4.enter_context(contextlib.ExitStack())
        x1b = ph4b.enter_context(nc.sbuf_tensor("x1b", [128, DG, TLOC], F32, side="right"))
        ph4a = ph4.enter_context(contextlib.ExitStack())
        xn2b = ph4a.enter_context(nc.sbuf_tensor("xn2b", [128, DG, TLOC], BF16))
        szq = ph4a.enter_context(nc.sbuf_tensor("szq", [128, DIN // 128, 512], BF16))
        yzq = ph4a.enter_context(nc.sbuf_tensor("yzq", [128, DIN // 128, 512], BF16))
        wbo = ph4a.enter_context(nc.sbuf_tensor("wbo", [128, 2, DIN], BF16))

        p.dma(sy, x1b[:], x1_spill.ap(), "dsy")
        p.dma(sy, xn2b[:], xn2_spill.ap(), "dsy")
        p.wait(g, "cc")
        p.wait(t, "dsy"); p.wait(v, "dsy")
        for q in range(TLOC // 512):
            q0 = q * 512

            def evac_z(oc, tqi, bank):
                p.wait(s, "mm"); p.wait(s, "vx")
                cnt = p.inc(s.activation(szq[:, oc, :], bank[:, :], AF.Silu), "sc")
                return ("sc", cnt)

            gemm(lambda oc: dr["w_in_z"][oc], DIN // 128, DG, wbo,
                 lambda kg, tqi: xn2b[:, kg, q0:q0 + 512], evac_z, 1,
                 lambda oc, tqi: 4 + oc % 2)
            g.reg_load(roff, offt[:1, 1:2])
            g.reg_add(rtmp, roff, q0)
            srcy = bass.AP(ag2_out, rtmp, [[T, 128], [128 * T, DIN // 128], [1, 512]])
            p.wait(g, "vx"); p.wait(g, "mm")
            with nc.allow_non_contiguous_dma("strided gather from AG output"):
                p.inc(g.dma_start(out=yzq[:, :, :], in_=srcy, bounds_check=None),
                      "dgp", 16)
            p.wait(v, "dgp"); p.wait(v, "sc")
            for oc in range(DIN // 128):
                p.inc(v.tensor_mul(yzq[:, oc, :], yzq[:, oc, :], szq[:, oc, :]), "vx")

            def evac_wo(oc, tqi, bank):
                p.wait(v, "mm")
                cnt = p.inc(v.tensor_add(x1b[:, oc, q0:q0 + 512],
                                         x1b[:, oc, q0:q0 + 512], bank[:, :]), "vx")
                return ("vx", cnt)

            gemm(lambda oc: dr["w_out"][oc], DG, DIN // 128, wbo,
                 lambda kg, tqi: yzq[:, kg, :], evac_wo, 1,
                 lambda oc, tqi: oc % 2, pre_wait=("vx",))
        for gi in range(DG):
            buf = tmp_c if gi % 2 == 0 else tmp_e
            p.wait(sy, "vx")
            p.dma(sy, buf[:, 0:TLOC], ffn_spill.ap()[:, gi, :], "dsy")
            p.wait(v, "dsy")
            p.inc(v.tensor_add(x1b[:, gi, :], x1b[:, gi, :], buf[:, 0:TLOC]), "vx")

        if upto == "l0dbg":
            dbg_ffn = nc.dram_tensor("dbg_ffn", [128, DG, TLOC], BF16,
                                     kind="ExternalOutput").ap()
            dbg_y = nc.dram_tensor("dbg_y", [DIN, T], BF16, kind="ExternalOutput").ap()
            dbg_xdbl = nc.dram_tensor("dbg_xdbl", [96, T], F32, kind="ExternalOutput").ap()
            dbg_xn = nc.dram_tensor("dbg_xn", [128, DG, TLOC], BF16,
                                    kind="ExternalOutput").ap()
            dbg_u = nc.dram_tensor("dbg_u", [CH, T], BF16, kind="ExternalOutput").ap()
            dbg_up = nc.dram_tensor("dbg_up", [DIN, TLOC], BF16, kind="ExternalOutput").ap()
            p.dma(g, dbg_up, ag1_in.ap(), "dgp")
            dbg_xn2 = nc.dram_tensor("dbg_xn2", [128, DG, TLOC], BF16, kind="ExternalOutput").ap()
            p.dma(g, dbg_xn2, xn2_spill.ap(), "dgp")
            p.dma(g, dbg_ffn, ffn_spill.ap(), "dgp")
            p.dma(g, dbg_y, ag2_out.ap(), "dgp")
            p.dma(g, dbg_u, u_spill.ap(), "dgp")
            p.dma(g, dbg_xdbl, ar_out.ap(), "dgp")
            p.wait(sy, "vx")
            p.dma(sy, dbg_xn, xn[:], "dsy")
            p.dma(sy, out_ap, x1b[:], "dst")
            p.wait(sy, "dst"); p.wait(sy, "dsy"); p.wait(g, "dgp")
            return p, dr, out_ap
        if upto == "l0":
            p.wait(sy, "vx")
            p.dma(sy, out_ap, x1b[:], "dst")
            p.wait(sy, "dst")
            return p, dr, out_ap

        ph4a.close()
        phase_barrier()
        l1 = ph4.enter_context(contextlib.ExitStack())
        xn1 = l1.enter_context(nc.sbuf_tensor("xn1", [128, DG, TLOC], BF16))
        ffn1_out = l1.enter_context(nc.sbuf_tensor("ffn1_out", [128, DG, TLOC], BF16))
        e2g = l1.enter_context(nc.sbuf_tensor("e2g", [128, 8, DG], BF16))
        e3g = l1.enter_context(nc.sbuf_tensor("e3g", [128, 8, DG], F32))
        e4g = l1.enter_context(nc.sbuf_tensor("e4g", [128, 8, 2, DG], F32))
        bndc = l1.enter_context(nc.sbuf_tensor("bndc", [128, DG], F32))
        kmx = l1.enter_context(nc.sbuf_tensor("kmx", [128, DG], F32))
        aend = l1.enter_context(nc.sbuf_tensor("aend", [128, 2, DG], F32))
        ainb = l1.enter_context(nc.sbuf_tensor("ainb", [128, 2, DG], F32))
        accf = l1.enter_context(nc.sbuf_tensor("accf", [128, 2, DG], F32))
        gq_t = l1.enter_context(nc.sbuf_tensor("gq_t", [128, 24], F32))
        thrc = l1.enter_context(nc.sbuf_tensor("thrc", [128, 2], F32))
        ones_col_f = l1.enter_context(nc.sbuf_tensor("ones_col_f", [128, 1], F32))

        ln_emit(x1b, DG, xn1, LN_EPS)
        p.wait(sy, "vx")
        with nc.allow_non_contiguous_dma("tiny boundary column"):
            p.dma(sy, e2_in.ap(), xn1[:, :, TLOC - 1:TLOC], "dsy")
        p.wait(g, "dsy")
        p.inc(g.collective_compute("AllGather", ALU.bypass,
                                   replica_groups=[list(range(8))],
                                   ins=[e2_in.ap()], outs=[e2_out.ap()]), "cc")
        p.dma(sy, x1_spill.ap(), x1b[:], "dsy")
        p.wait(v, "dsy"); p.wait(s, "dsy"); p.wait(t, "dsy"); p.wait(g, "dsy")
        ph4b.close()

        p.wait(v, "dsy")
        v.memset(gq_t[:], 0.0)
        p.inc(v.memset(ones_col_f[:], 1.0), "vx")
        with nc.sbuf_tensor("qbuf", [128, HF], F32) as qbuf:
            for mi, mat in enumerate((dr["wfcT"], dr["wprT"])):
                for kg in range(DG):
                    p.wait(g, "vx")
                    p.dma(g, qbuf[:], mat[kg], "dgp")
                    p.wait(v, "dgp")
                    p.inc(v.tensor_reduce(gq_t[:, mi * 8 + kg:mi * 8 + kg + 1],
                                          qbuf[:], axis=AX.X, op=ALU.add,
                                          apply_absolute_value=True), "vx")
            p.wait(t, "vx")
            p.inc(t.matmul(pb[4][:1, 0:16], ones_col_f[:], gq_t[:, 0:16], start=True,
                           stop=True), "mm")
            p.wait(v, "mm")
            v.tensor_reduce(gq_t[:1, 16:17], pb[4][:1, 0:8], axis=AX.X, op=ALU.add)
            p.inc(v.tensor_reduce(gq_t[:1, 17:18], pb[4][:1, 8:16], axis=AX.X,
                                  op=ALU.add), "vx")
            p.wait(v, "vx")
            p.inc(v.tensor_scalar(gq_t[:1, 20:22], gq_t[:1, 16:18], 0.5 / (HF * D),
                                  0.5 * ABS_EPS, op0=ALU.mult, op1=ALU.add), "vx")
            p.wait(t, "vx")
            p.inc(t.matmul(pb[5][:, 0:2], ones_row_f[:], gq_t[:1, 20:22], start=True,
                           stop=True), "mm")
            p.wait(v, "mm")
            p.inc(v.tensor_copy(thrc[:], pb[5][:, 0:2]), "vx")
            for mat, col in ((dr["wfcT"], 0), (dr["wprT"], 1)):
                for kg in range(DG):
                    p.wait(g, "vx"); p.wait(g, "sc"); p.wait(g, "dsy")
                    p.dma(g, qbuf[:], mat[kg], "dgp")
                    p.wait(s, "dgp"); p.wait(s, "vx")
                    p.inc(s.sign(tmp_c[:, 0:HF] if False else tmp_c[:, 0:TLOC], qbuf[:, 0:TLOC]), "sc")
                    p.inc(s.sign(tmp_e[:, 0:TLOC], qbuf[:, TLOC:HF]), "sc")
                    p.inc(s.activation(tmp_a[:, 0:TLOC], qbuf[:, 0:TLOC], AF.Abs), "sc")
                    p.inc(s.activation(tmp_b[:, 0:TLOC], qbuf[:, TLOC:HF], AF.Abs), "sc")
                    p.wait(v, "sc"); p.wait(v, "dsy")
                    p.inc(v.tensor_single_scalar(tmp_a[:, 0:TLOC], tmp_a[:, 0:TLOC],
                                                 thrc[:, col:col + 1], op=ALU.is_ge), "vx")
                    p.inc(v.tensor_single_scalar(tmp_b[:, 0:TLOC], tmp_b[:, 0:TLOC],
                                                 thrc[:, col:col + 1], op=ALU.is_ge), "vx")
                    p.inc(v.tensor_mul(tmp_c[:, 0:TLOC], tmp_c[:, 0:TLOC],
                                       tmp_a[:, 0:TLOC]), "vx")
                    p.inc(v.tensor_mul(tmp_e[:, 0:TLOC], tmp_e[:, 0:TLOC],
                                       tmp_b[:, 0:TLOC]), "vx")
                    p.wait(sy, "vx")
                    if col == 0:
                        dst0 = wq_spill.ap()[4 * kg:4 * kg + 2].rearrange("b p k m -> p b k m")
                        dst1 = wq_spill.ap()[4 * kg + 2:4 * kg + 4].rearrange("b p k m -> p b k m")
                    else:
                        dst0 = wqp_spill.ap()[kg][:, 0:HG // 2, :]
                        dst1 = wqp_spill.ap()[kg][:, HG // 2:, :]
                    with nc.allow_non_contiguous_dma("band-order spill"):
                        p.dma(sy, dst0, tmp_c[:, 0:TLOC], "dsy")
                        p.dma(sy, dst1, tmp_e[:, 0:TLOC], "dsy")
        with contextlib.ExitStack() as lf:
            hq = lf.enter_context(nc.sbuf_tensor("hq", [128, HG, 512], BF16))
            hn = lf.enter_context(nc.sbuf_tensor("hn", [128, HG, 512], BF16))
            wqa = lf.enter_context(nc.sbuf_tensor("wqa", [128, 2, HG * 128], BF16))
            for q in range(TLOC // 512):
                q0 = q * 512

                def evac_fc(oc, tqi, bank):
                    p.wait(s, "mm")
                    cnt = p.inc(s.activation(hq[:, oc, :], bank[:, :], AF.Gelu), "sc")
                    return ("sc", cnt)

                gemm(lambda oc: wq_spill.ap()[oc], HG, DG, wqa,
                     lambda kg, tqi: xn1[:, kg, q0:q0 + 512], evac_fc, 1,
                     lambda oc, tqi: oc % 2, pre_wait=("vx", "dsy"))
                ln_emit(hq, HG, hn, LN_EPS, gamma_ap=lnw_t, beta_ap=lnb_t, tcols=512)

                def evac_pr(oc, tqi, bank):
                    p.wait(v, "mm")
                    cnt = p.inc(v.tensor_copy(ffn1_out[:, oc, q0:q0 + 512],
                                              bank[:, :]), "vx")
                    return ("vx", cnt)

                gemm(lambda oc: wqp_spill.ap()[oc], DG, HG, wqa,
                     lambda kg, tqi: hn[:, kg, :], evac_pr, 1,
                     lambda oc, tqi: 2 + oc % 2, pre_wait=("vx", "dsy"))

        with contextlib.ExitStack() as lr:
            ywo = lr.enter_context(nc.sbuf_tensor("ywo", [128, DG, TLOC], BF16))
            wrk = lr.enter_context(nc.sbuf_tensor("wrk", [128, 2, D], BF16))
            lr_m = lr.enter_context(contextlib.ExitStack())
            xmix = lr_m.enter_context(nc.sbuf_tensor("xmix", [128, DG, TLOC], BF16))

            p.wait(g, "cc")
            p.dma(g, e2g[:], e2_out.ap().rearrange("j p g -> p j g"), "dgp")
            p.wait(v, "dgp")
            v.memset(bndc[:], 0.0)
            for j in range(8):
                ia = v.scalar_tensor_tensor(bndc[:], e2g[:, j, :], selp[:, j:j + 1],
                                            bndc[:], op0=ALU.mult, op1=ALU.add)
            p.inc(ia, "vx")
            for mx, wname, mixcol in ((0, "wkT", 2), (1, "wvT", 3), (2, "wrT", 4)):
                p.wait(v, "mm")
                for gi in range(DG):
                    v.tensor_copy(tmp_c[:, 1:TLOC], xn1[:, gi, 0:TLOC - 1])
                    v.tensor_copy(tmp_c[:, 0:1], bndc[:, gi:gi + 1])
                    v.tensor_scalar_mul(tmp_a[:, 0:TLOC], tmp_c[:, 0:TLOC],
                                        omix[:, gi:gi + 1, mx].rearrange("p g -> p g"))
                    ia = v.scalar_tensor_tensor(xmix[:, gi, :], xn1[:, gi, :],
                                                tvec_t[:, gi:gi + 1, mixcol]
                                                .rearrange("p g -> p g"),
                                                tmp_a[:, 0:TLOC],
                                                op0=ALU.mult, op1=ALU.add)
                p.inc(ia, "vx")

                def evac_kv(oc, tqi, bank, mx=mx):
                    p.wait(s, "mm"); p.wait(s, "dsy")
                    if mx == 2:
                        cnt = p.inc(s.activation(stg_y[:], bank[:, :], AF.Sigmoid), "sc")
                    else:
                        cnt = p.inc(s.copy(stg_y[:], bank[:, :]), "sc")
                    p.wait(sy, "sc")
                    p.dma(sy, rwkv_spill.ap()[mx, oc, :, tqi * 512:(tqi + 1) * 512],
                          stg_y[:], "dsy")
                    return ("sc", cnt)

                gemm(lambda oc: dr[wname][oc], DG, DG, wrk,
                     lambda kg, tqi: xmix[:, kg, tqi * 512:(tqi + 1) * 512], evac_kv,
                     4, lambda oc, tqi: (oc * 4 + tqi) % 4, pre_wait=("vx",))
            p.wait(v, "mm"); p.wait(s, "mm"); p.wait(v, "dsy"); p.wait(s, "dsy")
            lr_m.close()
            kgt = lr.enter_context(nc.sbuf_tensor("kgt", [128, TLOC], BF16))
            vgt = lr.enter_context(nc.sbuf_tensor("vgt", [128, TLOC], BF16))
            sgt = lr.enter_context(nc.sbuf_tensor("sgt", [128, TLOC], BF16))
            ekv_t = lr.enter_context(nc.sbuf_tensor("ekv_t", [128, TLOC], F32))
            nmt = lr.enter_context(nc.sbuf_tensor("nmt", [128, 1 + TLOC], F32))
            dnt = lr.enter_context(nc.sbuf_tensor("dnt", [128, 1 + TLOC], F32))
            ones2k = lr.enter_context(nc.sbuf_tensor("ones2k", [128, TLOC], BF16))
            tmp_d = lr.enter_context(nc.sbuf_tensor("tmp_d", [128, TLOC], F32))
            iota_t = lr.enter_context(nc.sbuf_tensor("iota_t", [128, TLOC], F32))
            p.inc(g.iota(iota_t[:], pattern=[[1, TLOC]], base=1, channel_multiplier=0,
                         allow_small_or_imprecise_dtypes=True), "gp")
            p.wait(s, "gp")
            p.inc(v.memset(ones2k[:], 1.0), "vx")
            for gi in range(DG):
                p.wait(g, "vx")
                p.dma(g, kgt[:], rwkv_spill.ap()[0, gi], "dgp")
                p.wait(v, "dgp")
                p.inc(v.reduce_max(kmx[:, gi:gi + 1], kgt[:], axis=AX.X), "vx")
            p.wait(sy, "vx")
            p.dma(sy, e3_in.ap(), kmx[:], "dsy")
            p.wait(g, "dsy")
            p.inc(g.collective_compute("AllGather", ALU.bypass,
                                       replica_groups=[list(range(8))],
                                       ins=[e3_in.ap()], outs=[e3_out.ap()]), "cc")
            p.wait(g, "cc")
            p.dma(g, e3g[:], e3_out.ap().rearrange("j p g -> p j g"), "dgp")
            p.wait(v, "dgp")
            v.memset(kmx[:], -1e38)
            for j in range(8):
                ia = v.scalar_tensor_tensor(kmx[:], e3g[:, j, :], selc_pen[:, j:j + 1],
                                            kmx[:], op0=ALU.add, op1=ALU.max)
            p.inc(ia, "vx")
            p.inc(v.tensor_scalar_mul(kmx[:], kmx[:], -1.0), "vx")
            for gi in range(DG):
                p.wait(g, "vx"); p.wait(g, "sc")
                p.dma(g, kgt[:], rwkv_spill.ap()[0, gi], "dgp")
                p.dma(g, vgt[:], rwkv_spill.ap()[1, gi], "dgp")
                p.wait(s, "dgp"); p.wait(s, "vx")
                p.inc(s.activation(tmp_b[:, 0:TLOC], kgt[:], AF.Exp,
                                   bias=kmx[:, gi:gi + 1]), "sc")
                p.wait(v, "sc"); p.wait(v, "dgp")
                p.inc(v.tensor_mul(ekv_t[:], tmp_b[:, 0:TLOC], vgt[:]), "vx")
                v.tensor_scalar_mul(tmp_a[:, 0:TLOC], ones2k[:],
                                    wlane[:, gi:gi + 1, 1].rearrange("p g -> p g"))
                p.inc(v.tensor_tensor_scan(nmt[:, 1:], tmp_a[:, 0:TLOC], ekv_t[:],
                                           0.0, ALU.mult, ALU.add), "vx")
                p.inc(v.tensor_tensor_scan(dnt[:, 1:], tmp_a[:, 0:TLOC],
                                           tmp_b[:, 0:TLOC], 0.0, ALU.mult, ALU.add), "vx")
                v.tensor_copy(aend[:, 0, gi:gi + 1], nmt[:, TLOC:TLOC + 1])
                p.inc(v.tensor_copy(aend[:, 1, gi:gi + 1], dnt[:, TLOC:TLOC + 1]), "vx")
            p.wait(sy, "vx")
            p.dma(sy, e4_in.ap(), aend[:], "dsy")
            p.wait(g, "dsy")
            p.inc(g.collective_compute("AllGather", ALU.bypass,
                                       replica_groups=[list(range(8))],
                                       ins=[e4_in.ap()], outs=[e4_out.ap()]), "cc")
            p.wait(g, "cc")
            p.dma(g, e4g[:], e4_out.ap().rearrange("j p c g -> p j c g"), "dgp")
            p.wait(v, "dgp")
            v.memset(ainb[:], 0.0)
            for j in range(8):
                if j in (0, 4):
                    v.tensor_copy(accf[:], e4g[:, j, :, :])
                else:
                    for comp in range(2):
                        v.tensor_mul(accf[:, comp, :], accf[:, comp, :],
                                     wlane[:, :, 2])
                        v.tensor_add(accf[:, comp, :], accf[:, comp, :],
                                     e4g[:, j, comp, :])
                ia = None
                for comp in range(2):
                    ia = v.scalar_tensor_tensor(ainb[:, comp, :], accf[:, comp, :],
                                                selp[:, j:j + 1], ainb[:, comp, :],
                                                op0=ALU.mult, op1=ALU.add)
            p.inc(ia, "vx")
            for gi in range(DG):
                p.wait(g, "vx"); p.wait(g, "sc")
                p.dma(g, kgt[:], rwkv_spill.ap()[0, gi], "dgp")
                p.dma(g, vgt[:], rwkv_spill.ap()[1, gi], "dgp")
                p.dma(g, sgt[:], rwkv_spill.ap()[2, gi], "dgp")
                p.wait(s, "dgp"); p.wait(s, "vx")
                p.inc(s.activation(tmp_b[:, 0:TLOC], kgt[:], AF.Exp,
                                   bias=kmx[:, gi:gi + 1]), "sc")
                p.inc(s.activation(tmp_d[:, 0:TLOC], iota_t[:], AF.Exp,
                                   scale=wlane[:, gi:gi + 1, 0]
                                   .rearrange("p g -> p g")), "sc")
                p.wait(v, "sc"); p.wait(v, "dgp")
                p.inc(v.tensor_mul(ekv_t[:], tmp_b[:, 0:TLOC], vgt[:]), "vx")
                v.tensor_scalar_mul(tmp_a[:, 0:TLOC], ones2k[:],
                                    wlane[:, gi:gi + 1, 1].rearrange("p g -> p g"))
                p.inc(v.tensor_tensor_scan(nmt[:, 1:], tmp_a[:, 0:TLOC], ekv_t[:],
                                           0.0, ALU.mult, ALU.add), "vx")
                p.inc(v.tensor_tensor_scan(dnt[:, 1:], tmp_a[:, 0:TLOC],
                                           tmp_b[:, 0:TLOC], 0.0, ALU.mult, ALU.add), "vx")
                p.wait(v, "vx")
                p.inc(v.scalar_tensor_tensor(nmt[:, 1:], tmp_d[:, 0:TLOC],
                                             ainb[:, 0, gi:gi + 1], nmt[:, 1:],
                                             op0=ALU.mult, op1=ALU.add), "vx")
                p.inc(v.scalar_tensor_tensor(dnt[:, 1:], tmp_d[:, 0:TLOC],
                                             ainb[:, 1, gi:gi + 1], dnt[:, 1:],
                                             op0=ALU.mult, op1=ALU.add), "vx")
                v.tensor_copy(nmt[:, 0:1], ainb[:, 0, gi:gi + 1])
                p.inc(v.tensor_copy(dnt[:, 0:1], ainb[:, 1, gi:gi + 1]), "vx")
                p.inc(v.tensor_scalar_mul(ekv_t[:], ekv_t[:],
                                          wlane[:, gi:gi + 1, 3]
                                          .rearrange("p g -> p g")), "vx")
                p.inc(v.tensor_scalar_mul(tmp_b[:, 0:TLOC], tmp_b[:, 0:TLOC],
                                          wlane[:, gi:gi + 1, 3]
                                          .rearrange("p g -> p g")), "vx")
                p.inc(v.tensor_add(ekv_t[:], ekv_t[:], nmt[:, 0:TLOC]), "vx")
                p.inc(v.tensor_add(tmp_b[:, 0:TLOC], tmp_b[:, 0:TLOC],
                                   dnt[:, 0:TLOC]), "vx")
                p.wait(s, "vx")
                s.activation(tmp_d[:, 0:TLOC], tmp_b[:, 0:TLOC], AF.Ln)
                p.inc(s.activation(tmp_d[:, 0:TLOC], tmp_d[:, 0:TLOC], AF.Exp,
                                   scale=-1.0), "sc")
                p.wait(v, "sc")
                p.inc(v.tensor_mul(ekv_t[:], ekv_t[:], tmp_d[:, 0:TLOC]), "vx")
                p.inc(v.tensor_mul(ywo[:, gi, :], ekv_t[:], sgt[:]), "vx")
            def evac_wo1(oc, tqi, bank):
                q0 = tqi * 512
                p.wait(g, "vx")
                p.dma(g, tmp_d[:, 0:512], x1_spill.ap()[:, oc, q0:q0 + 512], "dgp")
                p.wait(v, "mm"); p.wait(v, "dgp"); p.wait(v, "dst")
                v.tensor_add(tmp_a[:, 0:512], bank[:, :], ffn1_out[:, oc, q0:q0 + 512])
                cnt = p.inc(v.tensor_add(tmp_a[:, 0:512], tmp_a[:, 0:512],
                                         tmp_d[:, 0:512]), "vx")
                p.wait(sy, "vx")
                p.dma(sy, out_ap[:, oc, q0:q0 + 512], tmp_a[:, 0:512], "dst")
                return ("vx", cnt)

            gemm(lambda oc: dr["woT"][oc], DG, DG, wrk,
                 lambda kg, tqi: ywo[:, kg, tqi * 512:(tqi + 1) * 512], evac_wo1,
                 4, lambda oc, tqi: 4 + (oc * 4 + tqi) % 4, pre_wait=("vx",))
            if upto == "fulldbg":
                d_x1 = nc.dram_tensor("d_x1", [128, DG, TLOC], F32, kind="ExternalOutput").ap()
                d_ffn1 = nc.dram_tensor("d_ffn1", [128, DG, TLOC], BF16, kind="ExternalOutput").ap()
                d_ywo = nc.dram_tensor("d_ywo", [128, DG, TLOC], BF16, kind="ExternalOutput").ap()
                d_kvs = nc.dram_tensor("d_kvs", [3, DG, 128, TLOC], BF16, kind="ExternalOutput").ap()
                d_small = nc.dram_tensor("d_small", [128, 80], F32, kind="ExternalOutput").ap()
                d_xn1 = nc.dram_tensor("d_xn1", [128, DG, TLOC], BF16, kind="ExternalOutput").ap()
                p.dma(g, d_x1, x1_spill.ap(), "dgp")
                p.dma(g, d_kvs, rwkv_spill.ap(), "dgp")
                p.dma(g, d_xn1, xn1_spill.ap() if False else xn1[:], "dgp")
                p.wait(sy, "vx")
                p.dma(sy, d_ffn1, ffn1_out[:], "dsy")
                p.dma(sy, d_ywo, ywo[:], "dsy")
                p.dma(sy, d_small[:, 0:8], kmx[:], "dsy")
                p.dma(sy, d_small[:, 8:24], ainb[:].rearrange("p c g -> p (c g)"), "dsy")
                p.dma(sy, d_small[:, 24:40], aend[:].rearrange("p c g -> p (c g)"), "dsy")
                p.dma(sy, d_small[:, 40:42], thrc[:], "dsy")
                p.dma(sy, d_small[:, 42:50], selp[:], "dsy")
                p.dma(sy, d_small[:, 50:58], selc_pen[:], "dsy")
                p.dma(sy, d_small[:1, 64:72], selrow[:, 0:8], "dsy")
                p.dma(sy, d_small[:1, 72:80], selrow[:, 8:16], "dsy")
                p.dma(sy, d_small[:, 58:64], gq_t[:, 14:20], "dsy")
                d_e3g = nc.dram_tensor("d_e3g", [128, 8, DG], F32, kind="ExternalOutput").ap()
                p.dma(sy, d_e3g, e3g[:], "dsy")
                p.wait(sy, "dsy"); p.wait(g, "dgp")
            p.wait(sy, "dst")
    return p, dr, out_ap


def _band(w_T, n_oc, n_kg):
    K, M = w_T.shape
    assert K == n_kg * 128 and M == n_oc * 128
    return np.ascontiguousarray(
        w_T.reshape(n_kg, 128, n_oc, 128).transpose(2, 1, 0, 3))


def _prep_core_inputs(inputs, core):
    t0 = (core % 4) * TLOC
    b = core // 4
    r = {}

    def fm(x2d):
        f = x2d.shape[1]
        return np.ascontiguousarray(
            x2d.T.reshape(f // 128, 128, x2d.shape[0]).transpose(1, 0, 2))

    def pcol(vec, ng):
        return np.ascontiguousarray(np.asarray(vec).reshape(ng, 128).T)

    A = lambda k: np.asarray(inputs[k])
    r["x0T"] = fm(A("L0_attn_x")[b, t0:t0 + TLOC, :])
    r["l0_gamma"] = pcol(A("L0_attn_gamma"), DG)
    r["l0_beta"] = pcol(A("L0_attn_beta"), DG)
    w_in = A("L0_attn_W_in")
    r["w_in_u"] = _band(w_in[:, :DIN], DIN // 128, DG)
    r["w_in_z"] = _band(w_in[:, DIN:], DIN // 128, DG)
    chb = (core % 4) * CH
    cw = A("L0_attn_conv_w")[:, 0, :]
    r["conv_w"] = np.ascontiguousarray(
        cw[chb:chb + CH].reshape(CGN, 128, 4).transpose(1, 0, 2))
    r["conv_b"] = pcol(A("L0_attn_conv_b")[chb:chb + CH], CGN)
    r["w_xproj"] = np.ascontiguousarray(
        A("L0_attn_W_xproj")[chb:chb + CH].reshape(CGN, 128, 96).transpose(1, 0, 2))
    r["w_dt"] = np.ascontiguousarray(A("L0_attn_W_dt")[:, chb:chb + CH])
    r["b_dt"] = pcol(A("L0_attn_b_dt")[chb:chb + CH], CGN)
    r["a_log"] = np.ascontiguousarray(
        A("L0_attn_A_log")[chb:chb + CH].reshape(CGN, 128, DSTT).transpose(1, 0, 2))
    r["d_vec"] = pcol(A("L0_attn_D")[chb:chb + CH], CGN)
    r["w_out"] = _band(A("L0_attn_W_out"), DG, DIN // 128)
    r["w1T"] = _band(A("L0_ffn_w1_w").T, HG, DG)
    r["w2T"] = _band(A("L0_ffn_w2_w").T, HG, DG)
    r["w1s"] = pcol(A("L0_ffn_w1_s"), HG)
    r["w2s"] = pcol(A("L0_ffn_w2_s"), HG)
    r["h1T"] = _band(A("L0_ffn_h1_w").T, HG, HG)
    r["h2T"] = _band(A("L0_ffn_h2_w").T, HG, HG)
    r["h1s"] = pcol(A("L0_ffn_h1_s"), HG)
    r["h2s"] = pcol(A("L0_ffn_h2_s"), HG)
    r["w3T"] = _band(A("L0_ffn_w3_w").T, DG, HG)
    r["w3s"] = pcol(A("L0_ffn_w3_s"), DG)
    tv = np.stack([A("L1_attn_time_decay").reshape(D),
                   A("L1_attn_time_first").reshape(D),
                   A("L1_attn_time_mix_k").reshape(D),
                   A("L1_attn_time_mix_v").reshape(D),
                   A("L1_attn_time_mix_r").reshape(D)], axis=1)
    r["tvec"] = np.ascontiguousarray(tv.reshape(DG, 128, 5).transpose(1, 0, 2))
    r["wkT"] = _band(A("L1_attn_Wk").T, DG, DG)
    r["wvT"] = _band(A("L1_attn_Wv").T, DG, DG)
    r["wrT"] = _band(A("L1_attn_Wr").T, DG, DG)
    r["woT"] = _band(A("L1_attn_Wo").T, DG, DG)
    bfc = _band(A("L1_ffn_W_fc").T, HG, DG)
    r["wfcT"] = np.ascontiguousarray(
        bfc.reshape(DG, 4, 128, DG, 128).transpose(0, 2, 1, 3, 4)
        .reshape(DG, 128, HF))
    r["lnw"] = pcol(A("L1_ffn_ln_w"), HG)
    r["lnb"] = pcol(A("L1_ffn_ln_b"), HG)
    bpr = _band(A("L1_ffn_W_proj").T, DG, HG)
    r["wprT"] = np.ascontiguousarray(bpr.reshape(DG, 128, HG * 128))
    sp = np.zeros((1, 8), np.float32)
    if core % 4 != 0:
        sp[0, core - 1] = 1.0
    r["selprev"] = sp
    scv = np.zeros((1, 8), np.float32)
    for j in (range(4) if core < 4 else range(4, 8)):
        scv[0, j] = 1.0
    r["selchain"] = scv
    r["selpen"] = np.where(scv > 0, 0.0, -1e38).astype(np.float32)
    r["dynoff"] = np.array([[chb * TLOC, t0, 0, 0]], np.int32)
    import ml_dtypes
    oh = np.zeros((DSTT, DSTT, 128), np.float32)
    for si_ in range(DSTT):
        oh[si_, si_, :] = 1.0
    r["onehot16"] = oh.astype(ml_dtypes.bfloat16)
    for k in list(r):
        if r[k].dtype == np.float64:
            r[k] = r[k].astype(np.float32)
    return r


def kernel(**inputs):
    prog, dr, out_ap = build_program("full")
    maps = [_prep_core_inputs(inputs, c) for c in range(8)]
    res = run_bass_kernel_spmd(prog.nc, maps, core_ids=list(range(8)))
    out = np.empty((B, T, D), np.float32)
    for c in range(8):
        b, t0 = c // 4, (c % 4) * TLOC
        o = res.results[c]["outT"]
        out[b, t0:t0 + TLOC, :] = o.transpose(2, 1, 0).reshape(TLOC, D)
    return out


# version 04
# speedup vs baseline: 1.0931x; 1.0931x over previous
import contextlib

import numpy as np

import concourse.bass as bass
import concourse.mybir as mybir
from concourse.bass_utils import run_bass_kernel_spmd

F32 = mybir.dt.float32
BF16 = mybir.dt.bfloat16
I32 = mybir.dt.int32
AF = mybir.ActivationFunctionType
ALU = mybir.AluOpType
AX = mybir.AxisListType

B, T, D = 2, 8192, 1024
TLOC = 2048
DG = D // 128
DIN = 2048
DSTT = 16
DTR = 64
CH = DIN // 4
CGN = CH // 128
HF = 4096
HG = HF // 128
LN_EPS = 1e-5
ABS_EPS = 1e-6
TBS = 1024
NTB = T // TBS


class Prog:
    def __init__(self):
        self.nc = bass.Bass("TRN2", target_bir_lowering=False, debug=False)
        self.stack = contextlib.ExitStack()
        self._sems = {}

    def sem(self, name):
        if name not in self._sems:
            h = self.stack.enter_context(self.nc.semaphore(name))
            self._sems[name] = [h, 0]
        return self._sems[name][0]

    def count(self, name):
        self.sem(name)
        return self._sems[name][1]

    def inc(self, instr, name, n=1):
        self.sem(name)
        rec = self._sems[name]
        instr.then_inc(rec[0], n)
        rec[1] += n
        return rec[1]

    def dma(self, eng, out, in_, name):
        return self.inc(eng.dma_start(out=out, in_=in_), name, 16)

    def wait(self, eng, name, at=None):
        self.sem(name)
        rec = self._sems[name]
        val = rec[1] if at is None else at
        if val > 0:
            eng.wait_ge(rec[0], val)


def build_program(upto="full"):
    p = Prog()
    nc = p.nc
    v, s, t, g, sy = nc.vector, nc.scalar, nc.tensor, nc.gpsimd, nc.sync
    sb = p.stack.enter_context

    dr = {}

    def din(name, shape, dtype=F32):
        dr[name] = nc.dram_tensor(name, list(shape), dtype, kind="ExternalInput").ap()
        return dr[name]

    din("x0T", [128, DG, TLOC])
    out_ap = nc.dram_tensor("outT", [128, DG, TLOC], F32, kind="ExternalOutput").ap()

    din("l0_gamma", [128, DG]); din("l0_beta", [128, DG])
    din("w_in_u", [DIN // 128, 128, DG, 128])
    din("w_in_z", [DIN // 128, 128, DG, 128])
    din("conv_w", [128, CGN, 4]); din("conv_b", [128, CGN])
    din("w_xproj", [128, CGN, 96])
    din("w_dt", [DTR, CH])
    din("b_dt", [128, CGN])
    din("a_log", [128, CGN, DSTT])
    din("d_vec", [128, CGN])
    din("w_out", [DG, 128, DIN // 128, 128])
    din("w1T", [HG, 128, DG, 128]); din("w2T", [HG, 128, DG, 128])
    din("w1s", [128, HG]); din("w2s", [128, HG])
    din("h1T", [HG, 128, HG, 128]); din("h2T", [HG, 128, HG, 128])
    din("h1s", [128, HG]); din("h2s", [128, HG])
    din("w3T", [DG, 128, HG, 128]); din("w3s", [128, DG])
    din("tvec", [128, DG, 5])
    din("wkT", [DG, 128, DG, 128]); din("wvT", [DG, 128, DG, 128])
    din("wrT", [DG, 128, DG, 128]); din("woT", [DG, 128, DG, 128])
    din("wfcT", [DG, 128, DG, 128, 4] if False else [DG, 128, HF])
    din("lnw", [128, HG]); din("lnb", [128, HG])
    din("wprT", [DG, 128, HG * 128])
    din("selprev", [1, 8]); din("selchain", [1, 8]); din("selpen", [1, 8])
    din("onehot16", [DSTT, DSTT, 128], BF16)
    din("dynoff", [1, 4], I32)

    ag1_in = nc.dram_tensor("ag1_in", [DIN, TLOC], BF16)
    ag1_out = nc.dram_tensor("ag1_out", [4, DIN, TLOC], BF16)
    ar_in = nc.dram_tensor("ar_in", [96, T], F32)
    ar_out = nc.dram_tensor("ar_out", [96, T], F32)
    ag2_in = nc.dram_tensor("ag2_in", [CH, T], BF16)
    ag2_out = nc.dram_tensor("ag2_out", [DIN, T], BF16)
    u_spill = nc.dram_tensor("u_spill", [CH, T], BF16)
    x1_spill = nc.dram_tensor("x1_spill", [128, DG, TLOC], F32)
    xn2_spill = nc.dram_tensor("xn2_spill", [128, DG, TLOC], BF16)
    xn_sp = nc.dram_tensor("xn_sp", [128, DG, TLOC], BF16)
    ffn_spill = nc.dram_tensor("ffn_spill", [128, DG, TLOC], BF16)
    rwkv_spill = nc.dram_tensor("rwkv_spill", [3, DG, 128, TLOC], BF16)
    wq_spill = nc.dram_tensor("wq_spill", [HG, 128, DG, 128], BF16)
    wqp_spill = nc.dram_tensor("wqp_spill", [DG, 128, HG, 128], BF16)
    e2_out = nc.dram_tensor("e2_out", [8, 128, DG], BF16)
    e2_in = nc.dram_tensor("e2_in", [128, DG], BF16)
    e3_out = nc.dram_tensor("e3_out", [8, 128, DG], F32)
    e3_in = nc.dram_tensor("e3_in", [128, DG], F32)
    e4_out = nc.dram_tensor("e4_out", [8, 128, 2, DG], F32)
    e4_in = nc.dram_tensor("e4_in", [128, 2, DG], F32)

    gamma_t = sb(nc.sbuf_tensor("gamma_t", [128, DG], F32))
    beta_t = sb(nc.sbuf_tensor("beta_t", [128, DG], F32))
    convb_t = sb(nc.sbuf_tensor("convb_t", [128, CGN], F32))
    negbdt_t = sb(nc.sbuf_tensor("negbdt_t", [128, CGN], F32))
    dvec_t = sb(nc.sbuf_tensor("dvec_t", [128, CGN], F32))
    convw_t = sb(nc.sbuf_tensor("convw_t", [128, CGN, 4], F32))
    e_a = sb(nc.sbuf_tensor("e_a", [128, CGN, DSTT], F32))
    wxp_t = sb(nc.sbuf_tensor("wxp_t", [128, CGN, 96], BF16))
    wdt_t = sb(nc.sbuf_tensor("wdt_t", [DTR, CH], BF16))
    tvec_t = sb(nc.sbuf_tensor("tvec_t", [128, DG, 5], F32))
    wlane = sb(nc.sbuf_tensor("wlane", [128, DG, 5], F32))
    omix = sb(nc.sbuf_tensor("omix", [128, DG, 3], F32))
    w1s_t = sb(nc.sbuf_tensor("w1s_t", [128, HG], F32))
    w2s_t = sb(nc.sbuf_tensor("w2s_t", [128, HG], F32))
    h1s_t = sb(nc.sbuf_tensor("h1s_t", [128, HG], F32))
    h2s_t = sb(nc.sbuf_tensor("h2s_t", [128, HG], F32))
    w3s_t = sb(nc.sbuf_tensor("w3s_t", [128, DG], F32))
    lnw_t = sb(nc.sbuf_tensor("lnw_t", [128, HG], F32))
    lnb_t = sb(nc.sbuf_tensor("lnb_t", [128, HG], F32))
    selp = sb(nc.sbuf_tensor("selp", [128, 8], F32))
    selc_pen = sb(nc.sbuf_tensor("selc_pen", [128, 8], F32))
    selrow = sb(nc.sbuf_tensor("selrow", [1, 24], F32))
    ones_col = sb(nc.sbuf_tensor("ones_col", [128, 1], BF16))
    ones_row = sb(nc.sbuf_tensor("ones_row", [1, 128], BF16))
    ones_row_f = sb(nc.sbuf_tensor("ones_row_f", [1, 128], F32))
    offt = sb(nc.sbuf_tensor("offt", [1, 4], I32))
    stat = sb(nc.sbuf_tensor("stat", [128, 32], F32))
    srow = sb(nc.sbuf_tensor("srow", [1, 4, 512], F32))
    stg_x = sb(nc.sbuf_tensor("stg_x", [128, 512], BF16))
    stg_q = sb(nc.sbuf_tensor("stg_q", [128, 512], BF16))
    stg_y = sb(nc.sbuf_tensor("stg_y", [128, 512], BF16))
    epsc = sb(nc.sbuf_tensor("epsc", [1, 2], F32))
    tmp_a = sb(nc.sbuf_tensor("tmp_a", [128, TLOC], F32))
    tmp_c = sb(nc.sbuf_tensor("tmp_c", [128, TLOC], BF16))

    xn_scope = contextlib.ExitStack()
    xn = xn_scope.enter_context(nc.sbuf_tensor("xn", [128, DG, TLOC], BF16))
    pb = [sb(nc.psum_tensor(f"pb{i}", [128, 512], F32)) for i in range(8)]
    roff = sb(g.register("roff"))
    rtmp = sb(g.register("rtmp"))
    sroff = sb(sy.register("sroff"))
    srtmp = sb(sy.register("srtmp"))

    uid = [0]

    def gemm_gen(wdram, n_oc, n_kg, wtile, rhs_fn, evac_fn, n_tq, bank_of,
                 pre_wait=(), mmtag="mm"):
        uid[0] += 1
        ldsem = [f"gw{uid[0]}a", f"gw{uid[0]}b"]
        for nm in pre_wait:
            p.wait(t, nm)
        m0 = p.count(mmtag)
        last_use = [m0, m0]
        bank_free = {}
        for oc in range(n_oc):
            par = oc % 2
            p.wait(g, mmtag, at=last_use[par])
            ldc = p.dma(g, wtile[:, par, 0:n_kg * 128], wdram(oc), ldsem[par])
            for tqi in range(n_tq):
                bk = bank_of(oc, tqi)
                if bk in bank_free:
                    nm, cnt = bank_free[bk]
                    p.wait(t, nm, at=cnt)
                p.wait(t, ldsem[par], at=ldc)
                for kg in range(n_kg):
                    i1 = t.matmul(pb[bk][:, :], wtile[:, par, kg * 128:(kg + 1) * 128],
                                  rhs_fn(kg, tqi), start=(kg == 0), stop=(kg == n_kg - 1))
                mmc = p.inc(i1, mmtag)
                last_use[par] = mmc
                bank_free[bk] = evac_fn(oc, tqi, pb[bk])
            yield

    def gemm(*args, **kw):
        for _ in gemm_gen(*args, **kw):
            pass

    def run_interleaved(gens):
        active = list(gens)
        while active:
            keep = []
            for gen, w in active:
                alive = True
                for _ in range(w):
                    try:
                        next(gen)
                    except StopIteration:
                        alive = False
                        break
                if alive:
                    keep.append((gen, w))
            active = keep

    def phase_barrier():
        for nm in ("dsy", "dgp", "vx", "sc", "mm", "gp", "cc"):
            for eng in (v, s, t, g, sy):
                p.wait(eng, nm)

    def ln_emit(src, n_g, dst_bf, eps, plain_cb=None, gamma_ap=None, beta_ap=None,
                tcols=TLOC, toff=0):
        inv_n = 1.0 / (n_g * 128)
        p.wait(s, "vx")
        ps_x, ps_q, ps_r, ps_m = pb[4], pb[5], pb[6], pb[7]
        for blk in range(tcols // 512):
            t0 = toff + blk * 512
            for gi in range(n_g):
                p.wait(v, "mm"); p.wait(s, "mm")
                p.inc(v.tensor_copy(stg_x[:], src[:, gi, t0:t0 + 512]), "vx")
                p.inc(s.activation(stg_q[:], src[:, gi, t0:t0 + 512], AF.Square), "sc")
                p.wait(t, "vx"); p.wait(t, "sc")
                last = gi == n_g - 1
                t.matmul(ps_x[:1, :], ones_col[:], stg_x[:], start=(gi == 0), stop=last)
                i2 = t.matmul(ps_q[:1, :], ones_col[:], stg_q[:], start=(gi == 0), stop=last)
                p.inc(i2, "mm")
            p.wait(v, "mm")
            p.inc(v.tensor_scalar_mul(srow[:1, 0, :], ps_x[:1, :], inv_n), "vx")
            v.tensor_mul(srow[:1, 1, :], srow[:1, 0, :], srow[:1, 0, :])
            p.inc(v.scalar_tensor_tensor(srow[:1, 1, :], ps_q[:1, :], inv_n,
                                         srow[:1, 1, :], op0=ALU.mult,
                                         op1=ALU.subtract), "vx")
            p.wait(s, "vx")
            s.activation(srow[:1, 2, :], srow[:1, 1, :], AF.Ln, bias=epsc[:1, 0:1])
            p.inc(s.activation(srow[:1, 2, :], srow[:1, 2, :], AF.Exp, scale=-0.5), "sc")
            p.wait(v, "sc")
            v.tensor_mul(srow[:1, 3, :], srow[:1, 0, :], srow[:1, 2, :])
            p.inc(v.tensor_scalar_mul(srow[:1, 3, :], srow[:1, 3, :], -1.0), "vx")
            p.wait(t, "vx")
            t.matmul(ps_r[:, :], ones_row_f[:], srow[:1, 2, :], start=True, stop=True)
            p.inc(t.matmul(ps_m[:, :], ones_row_f[:], srow[:1, 3, :], start=True,
                           stop=True), "mm")
            p.wait(v, "mm")
            for gi in range(n_g):
                v.tensor_mul(tmp_a[:, 0:512], src[:, gi, t0:t0 + 512], ps_r[:, :])
                ia = v.tensor_add(tmp_a[:, 0:512], tmp_a[:, 0:512], ps_m[:, :])
                if gamma_ap is not None:
                    ia = v.tensor_scalar(dst_bf[:, gi, t0:t0 + 512], tmp_a[:, 0:512],
                                         gamma_ap[:, gi:gi + 1], beta_ap[:, gi:gi + 1],
                                         op0=ALU.mult, op1=ALU.add)
                else:
                    ia = v.tensor_copy(dst_bf[:, gi, t0:t0 + 512], tmp_a[:, 0:512])
                p.inc(ia, "vx")
                if plain_cb is not None:
                    plain_cb(gi, t0)

    def chain_select(out_col_fn, gath_fn, n_items):
        for it in range(n_items):
            v.memset(tmp_a[:, 0:DG], 0.0)
            for j in range(8):
                ia = v.scalar_tensor_tensor(tmp_a[:, 0:DG], gath_fn(it, j),
                                            selp[:, j:j + 1], tmp_a[:, 0:DG],
                                            op0=ALU.mult, op1=ALU.add)
            p.inc(v.tensor_copy(out_col_fn(it), tmp_a[:, 0:DG]), "vx")

    with contextlib.ExitStack() as ph01:
        x1 = ph01.enter_context(nc.sbuf_tensor("x1", [128, DG, TLOC], F32))
        xn2 = ph01.enter_context(nc.sbuf_tensor("xn2", [128, DG, TLOC], BF16))
        wz = ph01.enter_context(nc.sbuf_tensor("wz", [128, 2, DG * 128], BF16))

        p.dma(sy, x1[:], dr["x0T"], "dsy")
        for nm, tl in [("l0_gamma", gamma_t), ("l0_beta", beta_t), ("conv_b", convb_t),
                       ("d_vec", dvec_t), ("conv_w", convw_t), ("a_log", e_a),
                       ("tvec", tvec_t), ("w1s", w1s_t), ("w2s", w2s_t),
                       ("h1s", h1s_t), ("h2s", h2s_t), ("w3s", w3s_t),
                       ("lnw", lnw_t), ("lnb", lnb_t)]:
            p.dma(sy, tl[:], dr[nm], "dsy")
        p.dma(sy, stat[:, 0:CGN], dr["b_dt"], "dsy")
        p.dma(sy, selrow[:, 0:8], dr["selprev"], "dsy")
        p.dma(sy, selrow[:, 8:16], dr["selchain"], "dsy")
        p.dma(sy, selrow[:, 16:24], dr["selpen"], "dsy")
        p.dma(g, wxp_t[:], dr["w_xproj"], "dgp")
        p.dma(g, wdt_t[:], dr["w_dt"], "dgp")
        p.dma(g, offt[:], dr["dynoff"], "dgp")

        p.wait(v, "dsy")
        v.memset(epsc[:], LN_EPS)
        v.memset(ones_col[:], 1.0)
        v.memset(ones_row[:], 1.0)
        v.memset(ones_row_f[:], 1.0)
        p.inc(v.tensor_scalar_mul(negbdt_t[:], stat[:, 0:CGN], -1.0), "vx")
        for j in range(3):
            p.inc(v.tensor_scalar(omix[:, :, j], tvec_t[:, :, 2 + j], -1.0, 1.0,
                                  op0=ALU.mult, op1=ALU.add), "vx")
        p.wait(s, "dsy")
        p.inc(s.activation(e_a[:], e_a[:], AF.Exp), "sc")
        p.inc(s.activation(wlane[:, :, 0], tvec_t[:, :, 0], AF.Exp), "sc")
        p.wait(v, "sc")
        p.inc(v.tensor_scalar_mul(wlane[:, :, 0], wlane[:, :, 0], -1.0), "vx")
        p.wait(s, "vx")
        p.inc(s.activation(wlane[:, :, 1], wlane[:, :, 0], AF.Exp), "sc")
        p.inc(s.activation(wlane[:, :, 2], wlane[:, :, 0], AF.Exp, scale=float(TLOC)), "sc")
        p.inc(s.activation(wlane[:, :, 3], tvec_t[:, :, 1], AF.Exp), "sc")
        p.wait(t, "dsy"); p.wait(t, "vx")
        p.inc(t.matmul(pb[0][:, 0:24], ones_row_f[:], selrow[:1, :], start=True,
                       stop=True), "mm")
        p.wait(v, "mm")
        p.inc(v.tensor_copy(selp[:], pb[0][:, 0:8]), "vx")
        p.inc(v.tensor_copy(selc_pen[:], pb[0][:, 16:24]), "vx")

        def add_xn(gi, t0):
            p.inc(v.tensor_add(x1[:, gi, t0:t0 + 512], x1[:, gi, t0:t0 + 512],
                               tmp_a[:, 0:512]), "vx")

        ln_emit(x1, DG, xn, LN_EPS, plain_cb=add_xn)
        ln_emit(xn, DG, xn2, LN_EPS, gamma_ap=gamma_t, beta_ap=beta_t)

        def evac_u(oc, tqi, bank):
            p.wait(s, "mm")
            p.wait(s, "dsy")
            cnt = p.inc(s.copy(stg_y[:], bank[:, :]), "sc")
            p.wait(sy, "sc")
            p.dma(sy, ag1_in.ap()[oc * 128:(oc + 1) * 128, tqi * 512:(tqi + 1) * 512],
                  stg_y[:], "dsy")
            return ("sc", cnt)

        gemm(lambda oc: dr["w_in_u"][oc], DIN // 128, DG, wz,
             lambda kg, tqi: xn2[:, kg, tqi * 512:(tqi + 1) * 512],
             evac_u, 4, lambda oc, tqi: (oc * 4 + tqi) % 4, pre_wait=("vx",))

        p.wait(g, "dsy")
        p.inc(g.collective_compute("AllGather", ALU.bypass,
                                   replica_groups=[[0, 1, 2, 3], [4, 5, 6, 7]],
                                   ins=[ag1_in.ap()], outs=[ag1_out.ap()]), "cc")
        p.wait(sy, "vx")
        p.dma(sy, x1_spill.ap(), x1[:], "dsy")
        p.dma(sy, xn2_spill.ap(), xn2[:], "dsy")
        p.dma(sy, xn_sp.ap(), xn[:], "dsy")
        p.wait(g, "dsy")

    xn_scope.close()
    with contextlib.ExitStack() as ph2:
        phase_barrier()
        xdbl_tb = ph2.enter_context(nc.sbuf_tensor("xdbl_tb", [96, TBS], F32))
        up_h = ph2.enter_context(nc.sbuf_tensor("up_h", [128, CGN, 3 + TBS], BF16))
        u_ch = ph2.enter_context(nc.sbuf_tensor("u_ch", [128, CGN, TBS], BF16))
        dl_ch = ph2.enter_context(nc.sbuf_tensor("dl_ch", [128, CGN, TBS], BF16))
        du_ch = ph2.enter_context(nc.sbuf_tensor("du_ch", [128, CGN, TBS], BF16))
        bb_s = ph2.enter_context(nc.sbuf_tensor("bb_s", [128, TBS], BF16))
        cb_s = ph2.enter_context(nc.sbuf_tensor("cb_s", [128, TBS], BF16))
        da_t = ph2.enter_context(nc.sbuf_tensor("da_t", [128, 2, TBS], BF16))
        dbu_t = ph2.enter_context(nc.sbuf_tensor("dbu_t", [128, TBS], BF16))
        h_t = ph2.enter_context(nc.sbuf_tensor("h_t", [128, 2, TBS], BF16))
        prod_t = ph2.enter_context(nc.sbuf_tensor("prod_t", [128, TBS], BF16))
        y_acc = ph2.enter_context(nc.sbuf_tensor("y_acc", [128, CGN, TBS], F32))
        ycp = ph2.enter_context(nc.sbuf_tensor("ycp", [128, 2, TBS], BF16))
        carry = ph2.enter_context(nc.sbuf_tensor("carry", [128, CGN * DSTT], F32))
        bneg_tb = ph2.enter_context(nc.sbuf_tensor("bneg_tb", [DSTT, TBS], BF16))
        c_tb = ph2.enter_context(nc.sbuf_tensor("c_tb", [DSTT, TBS], BF16))
        onehot = ph2.enter_context(nc.sbuf_tensor("onehot", [DSTT, DSTT, 128], BF16))
        dt_bf = ph2.enter_context(nc.sbuf_tensor("dt_bf", [DTR, TBS], BF16))
        xn_q = ph2.enter_context(nc.sbuf_tensor("xn_q", [128, DG, 512], BF16))
        hbuf = ph2.enter_context(nc.sbuf_tensor("hbuf", [128, HG, 512], BF16))
        h2buf = ph2.enter_context(nc.sbuf_tensor("h2buf", [128, HG, 512], BF16))
        wba = ph2.enter_context(nc.sbuf_tensor("wba", [128, 2, HG * 128], BF16))

        p.dma(sy, onehot[:], dr["onehot16"], "dsyM")

        def chan_gen():
            v.memset(up_h[:, :, 0:3], 0.0)
            p.inc(v.memset(carry[:], 0.0), "vxM")
            p.wait(sy, "cc")
            for tb in range(NTB):
                sy.reg_load(sroff, offt[:1, 0:1])
                sy.reg_add(srtmp, sroff, (((tb * TBS) // TLOC) * DIN) * TLOC
                           + (tb * TBS) % TLOC)
                srcu = bass.AP(ag1_out, srtmp,
                               [[TLOC, 128], [128 * TLOC, CGN], [1, TBS]])
                p.wait(sy, "vxM")
                with nc.allow_non_contiguous_dma("AG gather"):
                    p.inc(sy.dma_start(out=up_h[:, :, 3:3 + TBS], in_=srcu,
                                       bounds_check=None), "dsyM", 16)
                p.wait(v, "dsyM")
                p.wait(s, "dsyM"); p.wait(s, "mmM")
                for cg in range(CGN):
                    p.wait(v, "scM")
                    v.tensor_scalar_mul(tmp_a[:, 0:TBS], up_h[:, cg, 0:TBS],
                                        convw_t[:, cg, 0:1])
                    for j in range(1, 4):
                        ic = v.scalar_tensor_tensor(tmp_a[:, 0:TBS],
                                                    up_h[:, cg, j:j + TBS],
                                                    convw_t[:, cg, j:j + 1],
                                                    tmp_a[:, 0:TBS],
                                                    op0=ALU.mult, op1=ALU.add)
                    p.inc(ic, "vxM")
                    p.inc(v.tensor_copy(up_h[:, cg, 0:3], up_h[:, cg, TBS:TBS + 3]),
                          "vxM")
                    p.wait(s, "vxM")
                    p.inc(s.activation(u_ch[:, cg, :], tmp_a[:, 0:TBS], AF.Silu,
                                       bias=convb_t[:, cg:cg + 1]), "scM")
                p.wait(sy, "scM")
                p.dma(sy, u_spill.ap()[:, tb * TBS:(tb + 1) * TBS]
                      .rearrange("(cg q) t -> q cg t", q=128), u_ch[:], "dsyM")
                p.wait(t, "scM"); p.wait(t, "vxM")
                for nb in range(TBS // 512):
                    for cg in range(CGN):
                        i1 = t.matmul(pb[4 + nb][:96, :], wxp_t[:, cg, :],
                                      u_ch[:, cg, nb * 512:(nb + 1) * 512],
                                      start=(cg == 0), stop=(cg == CGN - 1))
                    p.inc(i1, "mmM")
                p.wait(sy, "mmM")
                for nb in range(TBS // 512):
                    pass
                p.wait(v, "mmM"); p.wait(v, "scM")
                for nb in range(TBS // 512):
                    p.wait(v, "dsyM", at=None if nb == 0 else None)
                    p.inc(v.tensor_copy(tmp_a[:96, nb * 512:(nb + 1) * 512],
                                        pb[4 + nb][:96, :]), "vxM")
                p.wait(sy, "vxM")
                p.dma(sy, ar_in.ap()[:, tb * TBS:(tb + 1) * TBS],
                      tmp_a[:96, 0:TBS], "dsyM")
                yield
            p.wait(g, "dsyM")
            p.inc(g.collective_compute("AllReduce", ALU.add,
                                       replica_groups=[[0, 1, 2, 3], [4, 5, 6, 7]],
                                       ins=[ar_in.ap()], outs=[ar_out.ap()]), "cc")
            yield
            p.wait(sy, "cc")
            crm_hist = {}
            scan_hist = {}
            hpar_hist = {}
            for tb in range(NTB):
                t0 = tb * TBS
                p.wait(sy, "vxM"); p.wait(sy, "scM"); p.wait(sy, "mmM")
                p.dma(sy, u_ch[:], u_spill.ap()[:, t0:t0 + TBS]
                      .rearrange("(cg q) t -> q cg t", q=128), "dsyM")
                p.dma(sy, xdbl_tb[:], ar_out.ap()[:, t0:t0 + TBS], "dsyM")
                p.wait(g, "vxM")
                p.dma(g, bneg_tb[:], ar_out.ap()[64:80, t0:t0 + TBS], "dgpM")
                p.dma(g, c_tb[:], ar_out.ap()[80:96, t0:t0 + TBS], "dgpM")
                p.wait(v, "dsyM"); p.wait(v, "dgpM")
                p.inc(v.tensor_scalar_mul(bneg_tb[:], bneg_tb[:], -1.0), "vxM")
                p.inc(v.tensor_copy(dt_bf[:], xdbl_tb[0:DTR, :]), "vxM")
                p.wait(t, "scM"); p.wait(t, "vxM")
                p.wait(s, "vxM")
                for cg in range(CGN):
                    for nb in range(TBS // 512):
                        i1 = t.matmul(pb[4 + nb][:, :],
                                      wdt_t[:, cg * 128:(cg + 1) * 128],
                                      dt_bf[:, nb * 512:(nb + 1) * 512],
                                      start=True, stop=True)
                        p.inc(i1, "mmM")
                        p.wait(s, "mmM")
                        p.inc(s.activation(dl_ch[:, cg, nb * 512:(nb + 1) * 512],
                                           pb[4 + nb][:, :], AF.Sigmoid, scale=-1.0,
                                           bias=negbdt_t[:, cg:cg + 1]), "scM")
                        p.wait(t, "scM")
                for cg in range(CGN):
                    p.inc(s.activation(dl_ch[:, cg, :], dl_ch[:, cg, :], AF.Ln), "scM")
                p.wait(v, "scM")
                for cg in range(CGN):
                    p.inc(v.tensor_mul(du_ch[:, cg, :], dl_ch[:, cg, :],
                                       u_ch[:, cg, :]), "vxM")
                p.inc(v.memset(y_acc[:], 0.0), "vxM")
                yield
                for si in range(DSTT):
                    p.wait(t, "scM"); p.wait(t, "vxM")
                    for nb in range(TBS // 512):
                        t.matmul(pb[4 + nb][:, :], onehot[:, si, :],
                                 bneg_tb[:, nb * 512:(nb + 1) * 512],
                                 start=True, stop=True)
                        i1 = t.matmul(pb[6 + nb][:, :], onehot[:, si, :],
                                      c_tb[:, nb * 512:(nb + 1) * 512],
                                      start=True, stop=True)
                    p.inc(i1, "mmM")
                    p.wait(s, "mmM")
                    p.wait(s, "vxM")
                    for nb in range(TBS // 512):
                        s.copy(bb_s[:, nb * 512:(nb + 1) * 512], pb[4 + nb][:, :])
                        p.inc(s.copy(cb_s[:, nb * 512:(nb + 1) * 512],
                                     pb[6 + nb][:, :]), "scM")
                    for cg in range(CGN):
                        k = si * CGN + cg
                        pp = k % 2
                        if pp in scan_hist:
                            p.wait(s, "vxM", at=scan_hist[pp])
                        p.inc(s.activation(da_t[:, pp, :], dl_ch[:, cg, :], AF.Exp,
                                           scale=e_a[:, cg, si:si + 1]), "scM")
                        p.wait(v, "scM")
                        p.inc(v.tensor_mul(dbu_t[:], du_ch[:, cg, :], bb_s[:]), "vxM")
                        cr = crm_hist.get((cg, si))
                        if cr is not None:
                            p.wait(v, "crM", at=cr)
                        hp = hpar_hist.get(pp)
                        if hp is not None:
                            p.wait(v, "crM", at=hp)
                        scan_hist[pp] = p.inc(
                            v.tensor_tensor_scan(h_t[:, pp, :], da_t[:, pp, :],
                                                 dbu_t[:],
                                                 carry[:, k:k + 1],
                                                 ALU.mult, ALU.add), "vxM")
                        p.wait(s, "vxM")
                        cnt = p.inc(s.copy(carry[:, k:k + 1],
                                           h_t[:, pp, TBS - 1:TBS]), "crM")
                        crm_hist[(cg, si)] = cnt
                        hpar_hist[pp] = cnt
                        v.tensor_mul(prod_t[:], h_t[:, pp, :], cb_s[:])
                        p.inc(v.tensor_add(y_acc[:, cg, :], y_acc[:, cg, :],
                                           prod_t[:]), "vxM")
                    if si % 4 == 3:
                        yield
                for cg in range(CGN):
                    p.inc(v.scalar_tensor_tensor(y_acc[:, cg, :], u_ch[:, cg, :],
                                                 dvec_t[:, cg:cg + 1], y_acc[:, cg, :],
                                                 op0=ALU.mult, op1=ALU.add), "vxM")
                    p.wait(v, "dsyM")
                    p.inc(v.tensor_copy(ycp[:, cg % 2, :], y_acc[:, cg, :]), "vxM")
                    p.wait(sy, "vxM")
                    p.dma(sy, ag2_in.ap()[cg * 128:(cg + 1) * 128, t0:t0 + TBS],
                          ycp[:, cg % 2, :], "dsyM")
                yield
            p.wait(g, "dsyM")
            p.inc(g.collective_compute("AllGather", ALU.bypass,
                                       replica_groups=[[0, 1, 2, 3], [4, 5, 6, 7]],
                                       ins=[ag2_in.ap()], outs=[ag2_out.ap()]), "cc")

        def ffn_gen():
            for q in range(TLOC // 512):
                q0 = q * 512
                p.wait(sy, "mmF")
                p.dma(sy, xn_q[:], xn_sp.ap()[:, :, q0:q0 + 512], "dsyF")

                def evac_g1(oc, tqi, bank):
                    p.wait(s, "mmF")
                    cnt = p.inc(s.activation(hbuf[:, oc, :], bank[:, :], AF.Silu,
                                             scale=w1s_t[:, oc:oc + 1]), "scF")
                    return ("scF", cnt)

                def evac_g2(oc, tqi, bank):
                    p.wait(v, "mmF"); p.wait(v, "scF")
                    cnt = p.inc(v.scalar_tensor_tensor(hbuf[:, oc, :], bank[:, :],
                                                       w2s_t[:, oc:oc + 1],
                                                       hbuf[:, oc, :],
                                                       op0=ALU.mult, op1=ALU.mult),
                                "vxF")
                    return ("vxF", cnt)

                yield from gemm_gen(lambda oc: dr["w1T"][oc], HG, DG, wba,
                                    lambda kg, tqi: xn_q[:, kg, :], evac_g1,
                                    1, lambda oc, tqi: oc % 2,
                                    pre_wait=("dsyF",), mmtag="mmF")
                yield from gemm_gen(lambda oc: dr["w2T"][oc], HG, DG, wba,
                                    lambda kg, tqi: xn_q[:, kg, :], evac_g2,
                                    1, lambda oc, tqi: 2 + oc % 2,
                                    pre_wait=("scF",), mmtag="mmF")

                def evac_h1(oc, tqi, bank):
                    p.wait(s, "mmF")
                    cnt = p.inc(s.activation(h2buf[:, oc, :], bank[:, :], AF.Gelu,
                                             scale=h1s_t[:, oc:oc + 1]), "scF")
                    return ("scF", cnt)

                yield from gemm_gen(lambda oc: dr["h1T"][oc], HG, HG, wba,
                                    lambda kg, tqi: hbuf[:, kg, :], evac_h1, 1,
                                    lambda oc, tqi: oc % 2, pre_wait=("vxF",),
                                    mmtag="mmF")

                def evac_h2(oc, tqi, bank):
                    p.wait(s, "mmF")
                    cnt = p.inc(s.activation(hbuf[:, oc, :], bank[:, :], AF.Gelu,
                                             scale=h2s_t[:, oc:oc + 1]), "scF")
                    return ("scF", cnt)

                yield from gemm_gen(lambda oc: dr["h2T"][oc], HG, HG, wba,
                                    lambda kg, tqi: h2buf[:, kg, :], evac_h2, 1,
                                    lambda oc, tqi: 2 + oc % 2, pre_wait=("scF",),
                                    mmtag="mmF")

                def evac_w3(oc, tqi, bank):
                    p.wait(s, "mmF"); p.wait(s, "dsyF")
                    cnt = p.inc(s.activation(stg_y[:], bank[:, :], AF.Copy,
                                             scale=w3s_t[:, oc:oc + 1]), "scF")
                    p.wait(sy, "scF")
                    p.dma(sy, ffn_spill.ap()[:, oc, q0:q0 + 512], stg_y[:], "dsyF")
                    return ("scF", cnt)

                yield from gemm_gen(lambda oc: dr["w3T"][oc], DG, HG, wba,
                                    lambda kg, tqi: hbuf[:, kg, :], evac_w3, 1,
                                    lambda oc, tqi: oc % 2, pre_wait=("scF",),
                                    mmtag="mmF")

        chang, ffng = chan_gen(), ffn_gen()
        for _ in range(30):
            next(ffng)
        run_interleaved([(ffng, 10), (chang, 1)])

    ph4 = contextlib.ExitStack()
    with ph4:
        phase_barrier()
        ph4b = ph4.enter_context(contextlib.ExitStack())
        x1b = ph4b.enter_context(nc.sbuf_tensor("x1b", [128, DG, TLOC], F32, side="right"))
        ph4a = ph4.enter_context(contextlib.ExitStack())
        xn2b = ph4a.enter_context(nc.sbuf_tensor("xn2b", [128, DG, TLOC], BF16))
        szq = ph4a.enter_context(nc.sbuf_tensor("szq", [128, DIN // 128, 512], BF16))
        yzq = ph4a.enter_context(nc.sbuf_tensor("yzq", [128, DIN // 128, 512], BF16))
        wbo = ph4a.enter_context(nc.sbuf_tensor("wbo", [128, 2, DIN], BF16))

        p.dma(sy, x1b[:], x1_spill.ap(), "dsy")
        p.dma(sy, xn2b[:], xn2_spill.ap(), "dsy")
        p.wait(g, "cc")
        p.wait(t, "dsy"); p.wait(v, "dsy")
        for q in range(TLOC // 512):
            q0 = q * 512

            def evac_z(oc, tqi, bank):
                p.wait(s, "mm"); p.wait(s, "vx")
                cnt = p.inc(s.activation(szq[:, oc, :], bank[:, :], AF.Silu), "sc")
                return ("sc", cnt)

            gemm(lambda oc: dr["w_in_z"][oc], DIN // 128, DG, wbo,
                 lambda kg, tqi: xn2b[:, kg, q0:q0 + 512], evac_z, 1,
                 lambda oc, tqi: 4 + oc % 2)
            g.reg_load(roff, offt[:1, 1:2])
            g.reg_add(rtmp, roff, q0)
            srcy = bass.AP(ag2_out, rtmp, [[T, 128], [128 * T, DIN // 128], [1, 512]])
            p.wait(g, "vx"); p.wait(g, "mm")
            with nc.allow_non_contiguous_dma("strided gather from AG output"):
                p.inc(g.dma_start(out=yzq[:, :, :], in_=srcy, bounds_check=None),
                      "dgp", 16)
            p.wait(v, "dgp"); p.wait(v, "sc")
            for oc in range(DIN // 128):
                p.inc(v.tensor_mul(yzq[:, oc, :], yzq[:, oc, :], szq[:, oc, :]), "vx")

            def evac_wo(oc, tqi, bank):
                p.wait(v, "mm")
                cnt = p.inc(v.tensor_add(x1b[:, oc, q0:q0 + 512],
                                         x1b[:, oc, q0:q0 + 512], bank[:, :]), "vx")
                return ("vx", cnt)

            gemm(lambda oc: dr["w_out"][oc], DG, DIN // 128, wbo,
                 lambda kg, tqi: yzq[:, kg, :], evac_wo, 1,
                 lambda oc, tqi: oc % 2, pre_wait=("vx",))
        for gi in range(DG):
            p.wait(sy, "vx")
            p.dma(sy, tmp_c[:, 0:TLOC], ffn_spill.ap()[:, gi, :], "dsy")
            p.wait(v, "dsy")
            p.inc(v.tensor_add(x1b[:, gi, :], x1b[:, gi, :], tmp_c[:, 0:TLOC]), "vx")

        if upto == "l0dbg":
            dbg_ffn = nc.dram_tensor("dbg_ffn", [128, DG, TLOC], BF16,
                                     kind="ExternalOutput").ap()
            dbg_y = nc.dram_tensor("dbg_y", [DIN, T], BF16, kind="ExternalOutput").ap()
            dbg_xdbl = nc.dram_tensor("dbg_xdbl", [96, T], F32, kind="ExternalOutput").ap()
            dbg_xn = nc.dram_tensor("dbg_xn", [128, DG, TLOC], BF16,
                                    kind="ExternalOutput").ap()
            dbg_u = nc.dram_tensor("dbg_u", [CH, T], BF16, kind="ExternalOutput").ap()
            dbg_up = nc.dram_tensor("dbg_up", [DIN, TLOC], BF16, kind="ExternalOutput").ap()
            p.dma(g, dbg_up, ag1_in.ap(), "dgp")
            dbg_xn2 = nc.dram_tensor("dbg_xn2", [128, DG, TLOC], BF16, kind="ExternalOutput").ap()
            p.dma(g, dbg_xn2, xn2_spill.ap(), "dgp")
            p.dma(g, dbg_ffn, ffn_spill.ap(), "dgp")
            p.dma(g, dbg_y, ag2_out.ap(), "dgp")
            p.dma(g, dbg_u, u_spill.ap(), "dgp")
            p.dma(g, dbg_xdbl, ar_out.ap(), "dgp")
            p.wait(sy, "vx")
            p.dma(sy, dbg_xn, xn[:], "dsy")
            p.dma(sy, out_ap, x1b[:], "dst")
            p.wait(sy, "dst"); p.wait(sy, "dsy"); p.wait(g, "dgp")
            return p, dr, out_ap
        if upto == "l0":
            p.wait(sy, "vx")
            p.dma(sy, out_ap, x1b[:], "dst")
            p.wait(sy, "dst")
            return p, dr, out_ap

        ph4a.close()
        phase_barrier()
        l1 = ph4.enter_context(contextlib.ExitStack())
        xn1 = l1.enter_context(nc.sbuf_tensor("xn1", [128, DG, TLOC], BF16))
        ffn1_out = l1.enter_context(nc.sbuf_tensor("ffn1_out", [128, DG, TLOC], BF16))
        e2g = l1.enter_context(nc.sbuf_tensor("e2g", [128, 8, DG], BF16))
        e3g = l1.enter_context(nc.sbuf_tensor("e3g", [128, 8, DG], F32))
        e4g = l1.enter_context(nc.sbuf_tensor("e4g", [128, 8, 2, DG], F32))
        bndc = l1.enter_context(nc.sbuf_tensor("bndc", [128, DG], F32))
        kmx = l1.enter_context(nc.sbuf_tensor("kmx", [128, DG], F32))
        aend = l1.enter_context(nc.sbuf_tensor("aend", [128, 2, DG], F32))
        ainb = l1.enter_context(nc.sbuf_tensor("ainb", [128, 2, DG], F32))
        accf = l1.enter_context(nc.sbuf_tensor("accf", [128, 2, DG], F32))
        gq_t = l1.enter_context(nc.sbuf_tensor("gq_t", [128, 24], F32))
        thrc = l1.enter_context(nc.sbuf_tensor("thrc", [128, 2], F32))
        ones_col_f = l1.enter_context(nc.sbuf_tensor("ones_col_f", [128, 1], F32))
        tmp_b = l1.enter_context(nc.sbuf_tensor("tmp_b", [128, TLOC], F32))
        tmp_e = l1.enter_context(nc.sbuf_tensor("tmp_e", [128, TLOC], BF16))

        ln_emit(x1b, DG, xn1, LN_EPS)
        p.wait(sy, "vx")
        with nc.allow_non_contiguous_dma("tiny boundary column"):
            p.dma(sy, e2_in.ap(), xn1[:, :, TLOC - 1:TLOC], "dsy")
        p.wait(g, "dsy")
        p.inc(g.collective_compute("AllGather", ALU.bypass,
                                   replica_groups=[list(range(8))],
                                   ins=[e2_in.ap()], outs=[e2_out.ap()]), "cc")
        p.dma(sy, x1_spill.ap(), x1b[:], "dsy")
        p.wait(v, "dsy"); p.wait(s, "dsy"); p.wait(t, "dsy"); p.wait(g, "dsy")
        ph4b.close()

        p.wait(v, "dsy")
        v.memset(gq_t[:], 0.0)
        p.inc(v.memset(ones_col_f[:], 1.0), "vx")
        with nc.sbuf_tensor("qbuf", [128, HF], F32) as qbuf:
            for mi, mat in enumerate((dr["wfcT"], dr["wprT"])):
                for kg in range(DG):
                    p.wait(g, "vx")
                    p.dma(g, qbuf[:], mat[kg], "dgp")
                    p.wait(v, "dgp")
                    p.inc(v.tensor_reduce(gq_t[:, mi * 8 + kg:mi * 8 + kg + 1],
                                          qbuf[:], axis=AX.X, op=ALU.add,
                                          apply_absolute_value=True), "vx")
            p.wait(t, "vx")
            p.inc(t.matmul(pb[4][:1, 0:16], ones_col_f[:], gq_t[:, 0:16], start=True,
                           stop=True), "mm")
            p.wait(v, "mm")
            v.tensor_reduce(gq_t[:1, 16:17], pb[4][:1, 0:8], axis=AX.X, op=ALU.add)
            p.inc(v.tensor_reduce(gq_t[:1, 17:18], pb[4][:1, 8:16], axis=AX.X,
                                  op=ALU.add), "vx")
            p.wait(v, "vx")
            p.inc(v.tensor_scalar(gq_t[:1, 20:22], gq_t[:1, 16:18], 0.5 / (HF * D),
                                  0.5 * ABS_EPS, op0=ALU.mult, op1=ALU.add), "vx")
            p.wait(t, "vx")
            p.inc(t.matmul(pb[5][:, 0:2], ones_row_f[:], gq_t[:1, 20:22], start=True,
                           stop=True), "mm")
            p.wait(v, "mm")
            p.inc(v.tensor_copy(thrc[:], pb[5][:, 0:2]), "vx")
            for mat, col in ((dr["wfcT"], 0), (dr["wprT"], 1)):
                for kg in range(DG):
                    p.wait(g, "vx"); p.wait(g, "sc"); p.wait(g, "dsy")
                    p.dma(g, qbuf[:], mat[kg], "dgp")
                    p.wait(s, "dgp"); p.wait(s, "vx")
                    p.inc(s.sign(tmp_c[:, 0:HF] if False else tmp_c[:, 0:TLOC], qbuf[:, 0:TLOC]), "sc")
                    p.inc(s.sign(tmp_e[:, 0:TLOC], qbuf[:, TLOC:HF]), "sc")
                    p.inc(s.activation(tmp_a[:, 0:TLOC], qbuf[:, 0:TLOC], AF.Abs), "sc")
                    p.inc(s.activation(tmp_b[:, 0:TLOC], qbuf[:, TLOC:HF], AF.Abs), "sc")
                    p.wait(v, "sc"); p.wait(v, "dsy")
                    p.inc(v.tensor_single_scalar(tmp_a[:, 0:TLOC], tmp_a[:, 0:TLOC],
                                                 thrc[:, col:col + 1], op=ALU.is_ge), "vx")
                    p.inc(v.tensor_single_scalar(tmp_b[:, 0:TLOC], tmp_b[:, 0:TLOC],
                                                 thrc[:, col:col + 1], op=ALU.is_ge), "vx")
                    p.inc(v.tensor_mul(tmp_c[:, 0:TLOC], tmp_c[:, 0:TLOC],
                                       tmp_a[:, 0:TLOC]), "vx")
                    p.inc(v.tensor_mul(tmp_e[:, 0:TLOC], tmp_e[:, 0:TLOC],
                                       tmp_b[:, 0:TLOC]), "vx")
                    p.wait(sy, "vx")
                    if col == 0:
                        dst0 = wq_spill.ap()[4 * kg:4 * kg + 2].rearrange("b p k m -> p b k m")
                        dst1 = wq_spill.ap()[4 * kg + 2:4 * kg + 4].rearrange("b p k m -> p b k m")
                    else:
                        dst0 = wqp_spill.ap()[kg][:, 0:HG // 2, :]
                        dst1 = wqp_spill.ap()[kg][:, HG // 2:, :]
                    with nc.allow_non_contiguous_dma("band-order spill"):
                        p.dma(sy, dst0, tmp_c[:, 0:TLOC], "dsy")
                        p.dma(sy, dst1, tmp_e[:, 0:TLOC], "dsy")
        with contextlib.ExitStack() as lf:
            hq = lf.enter_context(nc.sbuf_tensor("hq", [128, HG, 512], BF16))
            hn = lf.enter_context(nc.sbuf_tensor("hn", [128, HG, 512], BF16))
            wqa = lf.enter_context(nc.sbuf_tensor("wqa", [128, 2, HG * 128], BF16))
            for q in range(TLOC // 512):
                q0 = q * 512

                def evac_fc(oc, tqi, bank):
                    p.wait(s, "mm")
                    cnt = p.inc(s.activation(hq[:, oc, :], bank[:, :], AF.Gelu), "sc")
                    return ("sc", cnt)

                gemm(lambda oc: wq_spill.ap()[oc], HG, DG, wqa,
                     lambda kg, tqi: xn1[:, kg, q0:q0 + 512], evac_fc, 1,
                     lambda oc, tqi: oc % 2, pre_wait=("vx", "dsy"))
                ln_emit(hq, HG, hn, LN_EPS, gamma_ap=lnw_t, beta_ap=lnb_t, tcols=512)

                def evac_pr(oc, tqi, bank):
                    p.wait(v, "mm")
                    cnt = p.inc(v.tensor_copy(ffn1_out[:, oc, q0:q0 + 512],
                                              bank[:, :]), "vx")
                    return ("vx", cnt)

                gemm(lambda oc: wqp_spill.ap()[oc], DG, HG, wqa,
                     lambda kg, tqi: hn[:, kg, :], evac_pr, 1,
                     lambda oc, tqi: 2 + oc % 2, pre_wait=("vx", "dsy"))

        with contextlib.ExitStack() as lr:
            ywo = lr.enter_context(nc.sbuf_tensor("ywo", [128, DG, TLOC], BF16))
            wrk = lr.enter_context(nc.sbuf_tensor("wrk", [128, 2, D], BF16))
            lr_m = lr.enter_context(contextlib.ExitStack())
            xmix = lr_m.enter_context(nc.sbuf_tensor("xmix", [128, DG, TLOC], BF16))

            p.wait(g, "cc")
            p.dma(g, e2g[:], e2_out.ap().rearrange("j p g -> p j g"), "dgp")
            p.wait(v, "dgp")
            v.memset(bndc[:], 0.0)
            for j in range(8):
                ia = v.scalar_tensor_tensor(bndc[:], e2g[:, j, :], selp[:, j:j + 1],
                                            bndc[:], op0=ALU.mult, op1=ALU.add)
            p.inc(ia, "vx")
            for mx, wname, mixcol in ((0, "wkT", 2), (1, "wvT", 3), (2, "wrT", 4)):
                p.wait(v, "mm")
                for gi in range(DG):
                    v.tensor_copy(tmp_c[:, 1:TLOC], xn1[:, gi, 0:TLOC - 1])
                    v.tensor_copy(tmp_c[:, 0:1], bndc[:, gi:gi + 1])
                    v.tensor_scalar_mul(tmp_a[:, 0:TLOC], tmp_c[:, 0:TLOC],
                                        omix[:, gi:gi + 1, mx].rearrange("p g -> p g"))
                    ia = v.scalar_tensor_tensor(xmix[:, gi, :], xn1[:, gi, :],
                                                tvec_t[:, gi:gi + 1, mixcol]
                                                .rearrange("p g -> p g"),
                                                tmp_a[:, 0:TLOC],
                                                op0=ALU.mult, op1=ALU.add)
                p.inc(ia, "vx")

                def evac_kv(oc, tqi, bank, mx=mx):
                    p.wait(s, "mm"); p.wait(s, "dsy")
                    if mx == 2:
                        cnt = p.inc(s.activation(stg_y[:], bank[:, :], AF.Sigmoid), "sc")
                    else:
                        cnt = p.inc(s.copy(stg_y[:], bank[:, :]), "sc")
                    p.wait(sy, "sc")
                    p.dma(sy, rwkv_spill.ap()[mx, oc, :, tqi * 512:(tqi + 1) * 512],
                          stg_y[:], "dsy")
                    return ("sc", cnt)

                gemm(lambda oc: dr[wname][oc], DG, DG, wrk,
                     lambda kg, tqi: xmix[:, kg, tqi * 512:(tqi + 1) * 512], evac_kv,
                     4, lambda oc, tqi: (oc * 4 + tqi) % 4, pre_wait=("vx",))
            p.wait(v, "mm"); p.wait(s, "mm"); p.wait(v, "dsy"); p.wait(s, "dsy")
            lr_m.close()
            kgt = lr.enter_context(nc.sbuf_tensor("kgt", [128, TLOC], BF16))
            vgt = lr.enter_context(nc.sbuf_tensor("vgt", [128, TLOC], BF16))
            sgt = lr.enter_context(nc.sbuf_tensor("sgt", [128, TLOC], BF16))
            ekv_t = lr.enter_context(nc.sbuf_tensor("ekv_t", [128, TLOC], F32))
            nmt = lr.enter_context(nc.sbuf_tensor("nmt", [128, 1 + TLOC], F32))
            dnt = lr.enter_context(nc.sbuf_tensor("dnt", [128, 1 + TLOC], F32))
            ones2k = lr.enter_context(nc.sbuf_tensor("ones2k", [128, TLOC], BF16))
            tmp_d = lr.enter_context(nc.sbuf_tensor("tmp_d", [128, TLOC], F32))
            iota_t = lr.enter_context(nc.sbuf_tensor("iota_t", [128, TLOC], F32))
            p.inc(g.iota(iota_t[:], pattern=[[1, TLOC]], base=1, channel_multiplier=0,
                         allow_small_or_imprecise_dtypes=True), "gp")
            p.wait(s, "gp")
            p.inc(v.memset(ones2k[:], 1.0), "vx")
            for gi in range(DG):
                p.wait(g, "vx")
                p.dma(g, kgt[:], rwkv_spill.ap()[0, gi], "dgp")
                p.wait(v, "dgp")
                p.inc(v.reduce_max(kmx[:, gi:gi + 1], kgt[:], axis=AX.X), "vx")
            p.wait(sy, "vx")
            p.dma(sy, e3_in.ap(), kmx[:], "dsy")
            p.wait(g, "dsy")
            p.inc(g.collective_compute("AllGather", ALU.bypass,
                                       replica_groups=[list(range(8))],
                                       ins=[e3_in.ap()], outs=[e3_out.ap()]), "cc")
            p.wait(g, "cc")
            p.dma(g, e3g[:], e3_out.ap().rearrange("j p g -> p j g"), "dgp")
            p.wait(v, "dgp")
            v.memset(kmx[:], -1e38)
            for j in range(8):
                ia = v.scalar_tensor_tensor(kmx[:], e3g[:, j, :], selc_pen[:, j:j + 1],
                                            kmx[:], op0=ALU.add, op1=ALU.max)
            p.inc(ia, "vx")
            p.inc(v.tensor_scalar_mul(kmx[:], kmx[:], -1.0), "vx")
            for gi in range(DG):
                p.wait(g, "vx"); p.wait(g, "sc")
                p.dma(g, kgt[:], rwkv_spill.ap()[0, gi], "dgp")
                p.dma(g, vgt[:], rwkv_spill.ap()[1, gi], "dgp")
                p.wait(s, "dgp"); p.wait(s, "vx")
                p.inc(s.activation(tmp_b[:, 0:TLOC], kgt[:], AF.Exp,
                                   bias=kmx[:, gi:gi + 1]), "sc")
                p.wait(v, "sc"); p.wait(v, "dgp")
                p.inc(v.tensor_mul(ekv_t[:], tmp_b[:, 0:TLOC], vgt[:]), "vx")
                v.tensor_scalar_mul(tmp_a[:, 0:TLOC], ones2k[:],
                                    wlane[:, gi:gi + 1, 1].rearrange("p g -> p g"))
                p.inc(v.tensor_tensor_scan(nmt[:, 1:], tmp_a[:, 0:TLOC], ekv_t[:],
                                           0.0, ALU.mult, ALU.add), "vx")
                p.inc(v.tensor_tensor_scan(dnt[:, 1:], tmp_a[:, 0:TLOC],
                                           tmp_b[:, 0:TLOC], 0.0, ALU.mult, ALU.add), "vx")
                v.tensor_copy(aend[:, 0, gi:gi + 1], nmt[:, TLOC:TLOC + 1])
                p.inc(v.tensor_copy(aend[:, 1, gi:gi + 1], dnt[:, TLOC:TLOC + 1]), "vx")
            p.wait(sy, "vx")
            p.dma(sy, e4_in.ap(), aend[:], "dsy")
            p.wait(g, "dsy")
            p.inc(g.collective_compute("AllGather", ALU.bypass,
                                       replica_groups=[list(range(8))],
                                       ins=[e4_in.ap()], outs=[e4_out.ap()]), "cc")
            p.wait(g, "cc")
            p.dma(g, e4g[:], e4_out.ap().rearrange("j p c g -> p j c g"), "dgp")
            p.wait(v, "dgp")
            v.memset(ainb[:], 0.0)
            for j in range(8):
                if j in (0, 4):
                    v.tensor_copy(accf[:], e4g[:, j, :, :])
                else:
                    for comp in range(2):
                        v.tensor_mul(accf[:, comp, :], accf[:, comp, :],
                                     wlane[:, :, 2])
                        v.tensor_add(accf[:, comp, :], accf[:, comp, :],
                                     e4g[:, j, comp, :])
                ia = None
                for comp in range(2):
                    ia = v.scalar_tensor_tensor(ainb[:, comp, :], accf[:, comp, :],
                                                selp[:, j:j + 1], ainb[:, comp, :],
                                                op0=ALU.mult, op1=ALU.add)
            p.inc(ia, "vx")
            for gi in range(DG):
                p.wait(g, "vx"); p.wait(g, "sc")
                p.dma(g, kgt[:], rwkv_spill.ap()[0, gi], "dgp")
                p.dma(g, vgt[:], rwkv_spill.ap()[1, gi], "dgp")
                p.dma(g, sgt[:], rwkv_spill.ap()[2, gi], "dgp")
                p.wait(s, "dgp"); p.wait(s, "vx")
                p.inc(s.activation(tmp_b[:, 0:TLOC], kgt[:], AF.Exp,
                                   bias=kmx[:, gi:gi + 1]), "sc")
                p.inc(s.activation(tmp_d[:, 0:TLOC], iota_t[:], AF.Exp,
                                   scale=wlane[:, gi:gi + 1, 0]
                                   .rearrange("p g -> p g")), "sc")
                p.wait(v, "sc"); p.wait(v, "dgp")
                p.inc(v.tensor_mul(ekv_t[:], tmp_b[:, 0:TLOC], vgt[:]), "vx")
                v.tensor_scalar_mul(tmp_a[:, 0:TLOC], ones2k[:],
                                    wlane[:, gi:gi + 1, 1].rearrange("p g -> p g"))
                p.inc(v.tensor_tensor_scan(nmt[:, 1:], tmp_a[:, 0:TLOC], ekv_t[:],
                                           0.0, ALU.mult, ALU.add), "vx")
                p.inc(v.tensor_tensor_scan(dnt[:, 1:], tmp_a[:, 0:TLOC],
                                           tmp_b[:, 0:TLOC], 0.0, ALU.mult, ALU.add), "vx")
                p.wait(v, "vx")
                p.inc(v.scalar_tensor_tensor(nmt[:, 1:], tmp_d[:, 0:TLOC],
                                             ainb[:, 0, gi:gi + 1], nmt[:, 1:],
                                             op0=ALU.mult, op1=ALU.add), "vx")
                p.inc(v.scalar_tensor_tensor(dnt[:, 1:], tmp_d[:, 0:TLOC],
                                             ainb[:, 1, gi:gi + 1], dnt[:, 1:],
                                             op0=ALU.mult, op1=ALU.add), "vx")
                v.tensor_copy(nmt[:, 0:1], ainb[:, 0, gi:gi + 1])
                p.inc(v.tensor_copy(dnt[:, 0:1], ainb[:, 1, gi:gi + 1]), "vx")
                p.inc(v.tensor_scalar_mul(ekv_t[:], ekv_t[:],
                                          wlane[:, gi:gi + 1, 3]
                                          .rearrange("p g -> p g")), "vx")
                p.inc(v.tensor_scalar_mul(tmp_b[:, 0:TLOC], tmp_b[:, 0:TLOC],
                                          wlane[:, gi:gi + 1, 3]
                                          .rearrange("p g -> p g")), "vx")
                p.inc(v.tensor_add(ekv_t[:], ekv_t[:], nmt[:, 0:TLOC]), "vx")
                p.inc(v.tensor_add(tmp_b[:, 0:TLOC], tmp_b[:, 0:TLOC],
                                   dnt[:, 0:TLOC]), "vx")
                p.wait(s, "vx")
                s.activation(tmp_d[:, 0:TLOC], tmp_b[:, 0:TLOC], AF.Ln)
                p.inc(s.activation(tmp_d[:, 0:TLOC], tmp_d[:, 0:TLOC], AF.Exp,
                                   scale=-1.0), "sc")
                p.wait(v, "sc")
                p.inc(v.tensor_mul(ekv_t[:], ekv_t[:], tmp_d[:, 0:TLOC]), "vx")
                p.inc(v.tensor_mul(ywo[:, gi, :], ekv_t[:], sgt[:]), "vx")
            def evac_wo1(oc, tqi, bank):
                q0 = tqi * 512
                p.wait(g, "vx")
                p.dma(g, tmp_d[:, 0:512], x1_spill.ap()[:, oc, q0:q0 + 512], "dgp")
                p.wait(v, "mm"); p.wait(v, "dgp"); p.wait(v, "dst")
                v.tensor_add(tmp_a[:, 0:512], bank[:, :], ffn1_out[:, oc, q0:q0 + 512])
                cnt = p.inc(v.tensor_add(tmp_a[:, 0:512], tmp_a[:, 0:512],
                                         tmp_d[:, 0:512]), "vx")
                p.wait(sy, "vx")
                p.dma(sy, out_ap[:, oc, q0:q0 + 512], tmp_a[:, 0:512], "dst")
                return ("vx", cnt)

            gemm(lambda oc: dr["woT"][oc], DG, DG, wrk,
                 lambda kg, tqi: ywo[:, kg, tqi * 512:(tqi + 1) * 512], evac_wo1,
                 4, lambda oc, tqi: 4 + (oc * 4 + tqi) % 4, pre_wait=("vx",))
            if upto == "fulldbg":
                d_x1 = nc.dram_tensor("d_x1", [128, DG, TLOC], F32, kind="ExternalOutput").ap()
                d_ffn1 = nc.dram_tensor("d_ffn1", [128, DG, TLOC], BF16, kind="ExternalOutput").ap()
                d_ywo = nc.dram_tensor("d_ywo", [128, DG, TLOC], BF16, kind="ExternalOutput").ap()
                d_kvs = nc.dram_tensor("d_kvs", [3, DG, 128, TLOC], BF16, kind="ExternalOutput").ap()
                d_small = nc.dram_tensor("d_small", [128, 80], F32, kind="ExternalOutput").ap()
                d_xn1 = nc.dram_tensor("d_xn1", [128, DG, TLOC], BF16, kind="ExternalOutput").ap()
                p.dma(g, d_x1, x1_spill.ap(), "dgp")
                p.dma(g, d_kvs, rwkv_spill.ap(), "dgp")
                p.dma(g, d_xn1, xn1_spill.ap() if False else xn1[:], "dgp")
                p.wait(sy, "vx")
                p.dma(sy, d_ffn1, ffn1_out[:], "dsy")
                p.dma(sy, d_ywo, ywo[:], "dsy")
                p.dma(sy, d_small[:, 0:8], kmx[:], "dsy")
                p.dma(sy, d_small[:, 8:24], ainb[:].rearrange("p c g -> p (c g)"), "dsy")
                p.dma(sy, d_small[:, 24:40], aend[:].rearrange("p c g -> p (c g)"), "dsy")
                p.dma(sy, d_small[:, 40:42], thrc[:], "dsy")
                p.dma(sy, d_small[:, 42:50], selp[:], "dsy")
                p.dma(sy, d_small[:, 50:58], selc_pen[:], "dsy")
                p.dma(sy, d_small[:1, 64:72], selrow[:, 0:8], "dsy")
                p.dma(sy, d_small[:1, 72:80], selrow[:, 8:16], "dsy")
                p.dma(sy, d_small[:, 58:64], gq_t[:, 14:20], "dsy")
                d_e3g = nc.dram_tensor("d_e3g", [128, 8, DG], F32, kind="ExternalOutput").ap()
                p.dma(sy, d_e3g, e3g[:], "dsy")
                p.wait(sy, "dsy"); p.wait(g, "dgp")
            p.wait(sy, "dst")
    return p, dr, out_ap


def _band(w_T, n_oc, n_kg):
    K, M = w_T.shape
    assert K == n_kg * 128 and M == n_oc * 128
    return np.ascontiguousarray(
        w_T.reshape(n_kg, 128, n_oc, 128).transpose(2, 1, 0, 3))


def _prep_core_inputs(inputs, core):
    t0 = (core % 4) * TLOC
    b = core // 4
    r = {}

    def fm(x2d):
        f = x2d.shape[1]
        return np.ascontiguousarray(
            x2d.T.reshape(f // 128, 128, x2d.shape[0]).transpose(1, 0, 2))

    def pcol(vec, ng):
        return np.ascontiguousarray(np.asarray(vec).reshape(ng, 128).T)

    A = lambda k: np.asarray(inputs[k])
    r["x0T"] = fm(A("L0_attn_x")[b, t0:t0 + TLOC, :])
    r["l0_gamma"] = pcol(A("L0_attn_gamma"), DG)
    r["l0_beta"] = pcol(A("L0_attn_beta"), DG)
    w_in = A("L0_attn_W_in")
    r["w_in_u"] = _band(w_in[:, :DIN], DIN // 128, DG)
    r["w_in_z"] = _band(w_in[:, DIN:], DIN // 128, DG)
    chb = (core % 4) * CH
    cw = A("L0_attn_conv_w")[:, 0, :]
    r["conv_w"] = np.ascontiguousarray(
        cw[chb:chb + CH].reshape(CGN, 128, 4).transpose(1, 0, 2))
    r["conv_b"] = pcol(A("L0_attn_conv_b")[chb:chb + CH], CGN)
    r["w_xproj"] = np.ascontiguousarray(
        A("L0_attn_W_xproj")[chb:chb + CH].reshape(CGN, 128, 96).transpose(1, 0, 2))
    r["w_dt"] = np.ascontiguousarray(A("L0_attn_W_dt")[:, chb:chb + CH])
    r["b_dt"] = pcol(A("L0_attn_b_dt")[chb:chb + CH], CGN)
    r["a_log"] = np.ascontiguousarray(
        A("L0_attn_A_log")[chb:chb + CH].reshape(CGN, 128, DSTT).transpose(1, 0, 2))
    r["d_vec"] = pcol(A("L0_attn_D")[chb:chb + CH], CGN)
    r["w_out"] = _band(A("L0_attn_W_out"), DG, DIN // 128)
    r["w1T"] = _band(A("L0_ffn_w1_w").T, HG, DG)
    r["w2T"] = _band(A("L0_ffn_w2_w").T, HG, DG)
    r["w1s"] = pcol(A("L0_ffn_w1_s"), HG)
    r["w2s"] = pcol(A("L0_ffn_w2_s"), HG)
    r["h1T"] = _band(A("L0_ffn_h1_w").T, HG, HG)
    r["h2T"] = _band(A("L0_ffn_h2_w").T, HG, HG)
    r["h1s"] = pcol(A("L0_ffn_h1_s"), HG)
    r["h2s"] = pcol(A("L0_ffn_h2_s"), HG)
    r["w3T"] = _band(A("L0_ffn_w3_w").T, DG, HG)
    r["w3s"] = pcol(A("L0_ffn_w3_s"), DG)
    tv = np.stack([A("L1_attn_time_decay").reshape(D),
                   A("L1_attn_time_first").reshape(D),
                   A("L1_attn_time_mix_k").reshape(D),
                   A("L1_attn_time_mix_v").reshape(D),
                   A("L1_attn_time_mix_r").reshape(D)], axis=1)
    r["tvec"] = np.ascontiguousarray(tv.reshape(DG, 128, 5).transpose(1, 0, 2))
    r["wkT"] = _band(A("L1_attn_Wk").T, DG, DG)
    r["wvT"] = _band(A("L1_attn_Wv").T, DG, DG)
    r["wrT"] = _band(A("L1_attn_Wr").T, DG, DG)
    r["woT"] = _band(A("L1_attn_Wo").T, DG, DG)
    bfc = _band(A("L1_ffn_W_fc").T, HG, DG)
    r["wfcT"] = np.ascontiguousarray(
        bfc.reshape(DG, 4, 128, DG, 128).transpose(0, 2, 1, 3, 4)
        .reshape(DG, 128, HF))
    r["lnw"] = pcol(A("L1_ffn_ln_w"), HG)
    r["lnb"] = pcol(A("L1_ffn_ln_b"), HG)
    bpr = _band(A("L1_ffn_W_proj").T, DG, HG)
    r["wprT"] = np.ascontiguousarray(bpr.reshape(DG, 128, HG * 128))
    sp = np.zeros((1, 8), np.float32)
    if core % 4 != 0:
        sp[0, core - 1] = 1.0
    r["selprev"] = sp
    scv = np.zeros((1, 8), np.float32)
    for j in (range(4) if core < 4 else range(4, 8)):
        scv[0, j] = 1.0
    r["selchain"] = scv
    r["selpen"] = np.where(scv > 0, 0.0, -1e38).astype(np.float32)
    r["dynoff"] = np.array([[chb * TLOC, t0, 0, 0]], np.int32)
    import ml_dtypes
    oh = np.zeros((DSTT, DSTT, 128), np.float32)
    for si_ in range(DSTT):
        oh[si_, si_, :] = 1.0
    r["onehot16"] = oh.astype(ml_dtypes.bfloat16)
    for k in list(r):
        if r[k].dtype == np.float64:
            r[k] = r[k].astype(np.float32)
    return r


def kernel(**inputs):
    prog, dr, out_ap = build_program("full")
    maps = [_prep_core_inputs(inputs, c) for c in range(8)]
    res = run_bass_kernel_spmd(prog.nc, maps, core_ids=list(range(8)))
    out = np.empty((B, T, D), np.float32)
    for c in range(8):
        b, t0 = c // 4, (c % 4) * TLOC
        o = res.results[c]["outT"]
        out[b, t0:t0 + TLOC, :] = o.transpose(2, 1, 0).reshape(TLOC, D)
    return out


# version 05
# speedup vs baseline: 1.1315x; 1.0351x over previous
import contextlib

import numpy as np

import concourse.bass as bass
import concourse.mybir as mybir
from concourse.bass_utils import run_bass_kernel_spmd

F32 = mybir.dt.float32
BF16 = mybir.dt.bfloat16
I32 = mybir.dt.int32
AF = mybir.ActivationFunctionType
ALU = mybir.AluOpType
AX = mybir.AxisListType

B, T, D = 2, 8192, 1024
TLOC = 2048
DG = D // 128
DIN = 2048
DSTT = 16
DTR = 64
CH = DIN // 4
CGN = CH // 128
HF = 4096
HG = HF // 128
LN_EPS = 1e-5
ABS_EPS = 1e-6
TBS = 1024
NTB = T // TBS


class Prog:
    def __init__(self):
        self.nc = bass.Bass("TRN2", target_bir_lowering=False, debug=False)
        self.stack = contextlib.ExitStack()
        self._sems = {}

    def sem(self, name):
        if name not in self._sems:
            h = self.stack.enter_context(self.nc.semaphore(name))
            self._sems[name] = [h, 0]
        return self._sems[name][0]

    def count(self, name):
        self.sem(name)
        return self._sems[name][1]

    def inc(self, instr, name, n=1):
        self.sem(name)
        rec = self._sems[name]
        instr.then_inc(rec[0], n)
        rec[1] += n
        return rec[1]

    def dma(self, eng, out, in_, name):
        return self.inc(eng.dma_start(out=out, in_=in_), name, 16)

    def wait(self, eng, name, at=None):
        self.sem(name)
        rec = self._sems[name]
        val = rec[1] if at is None else at
        if val > 0:
            eng.wait_ge(rec[0], val)


def build_program(upto="full"):
    p = Prog()
    nc = p.nc
    v, s, t, g, sy = nc.vector, nc.scalar, nc.tensor, nc.gpsimd, nc.sync
    sb = p.stack.enter_context

    dr = {}

    def din(name, shape, dtype=F32):
        dr[name] = nc.dram_tensor(name, list(shape), dtype, kind="ExternalInput").ap()
        return dr[name]

    din("x0T", [128, DG, TLOC])
    out_ap = nc.dram_tensor("outT", [128, DG, TLOC], F32, kind="ExternalOutput").ap()

    din("l0_gamma", [128, DG]); din("l0_beta", [128, DG])
    din("w_in_u", [DIN // 128, 128, DG, 128])
    din("w_in_z", [DIN // 128, 128, DG, 128])
    din("conv_w", [128, CGN, 4]); din("conv_b", [128, CGN])
    din("w_xproj", [128, CGN, 96])
    din("w_dt", [DTR, CH])
    din("b_dt", [128, CGN])
    din("a_log", [128, CGN, DSTT])
    din("d_vec", [128, CGN])
    din("w_out", [DG, 128, DIN // 128, 128])
    din("w1T", [HG, 128, DG, 128]); din("w2T", [HG, 128, DG, 128])
    din("w1s", [128, HG]); din("w2s", [128, HG])
    din("h1T", [HG, 128, HG, 128]); din("h2T", [HG, 128, HG, 128])
    din("h1s", [128, HG]); din("h2s", [128, HG])
    din("w3T", [DG, 128, HG, 128]); din("w3s", [128, DG])
    din("tvec", [128, DG, 5])
    din("wkT", [DG, 128, DG, 128]); din("wvT", [DG, 128, DG, 128])
    din("wrT", [DG, 128, DG, 128]); din("woT", [DG, 128, DG, 128])
    din("wfcT", [DG, 128, DG, 128, 4] if False else [DG, 128, HF])
    din("lnw", [128, HG]); din("lnb", [128, HG])
    din("wprT", [DG, 128, HG * 128])
    din("selprev", [1, 8]); din("selchain", [1, 8]); din("selpen", [1, 8])
    din("onehot16", [DSTT, DSTT, 128], BF16)
    din("dynoff", [1, 4], I32)

    ag1_in = nc.dram_tensor("ag1_in", [DIN, TLOC], BF16)
    ag1_out = nc.dram_tensor("ag1_out", [4, DIN, TLOC], BF16)
    ar_in = nc.dram_tensor("ar_in", [96, T], F32)
    ar_out = nc.dram_tensor("ar_out", [96, T], F32)
    ag2_in = nc.dram_tensor("ag2_in", [CH, T], BF16)
    ag2_out = nc.dram_tensor("ag2_out", [DIN, T], BF16)
    u_spill = nc.dram_tensor("u_spill", [CH, T], BF16)
    x1_spill = nc.dram_tensor("x1_spill", [128, DG, TLOC], F32)
    xn2_spill = nc.dram_tensor("xn2_spill", [128, DG, TLOC], BF16)
    xn_sp = nc.dram_tensor("xn_sp", [128, DG, TLOC], BF16)
    ffn_spill = nc.dram_tensor("ffn_spill", [128, DG, TLOC], BF16)
    rwkv_spill = nc.dram_tensor("rwkv_spill", [3, DG, 128, TLOC], BF16)
    wq_spill = nc.dram_tensor("wq_spill", [HG, 128, DG, 128], BF16)
    wqp_spill = nc.dram_tensor("wqp_spill", [DG, 128, HG, 128], BF16)
    e2_out = nc.dram_tensor("e2_out", [8, 128, DG], BF16)
    e2_in = nc.dram_tensor("e2_in", [128, DG], BF16)
    e3_out = nc.dram_tensor("e3_out", [8, 128, DG], F32)
    e3_in = nc.dram_tensor("e3_in", [128, DG], F32)
    e4_out = nc.dram_tensor("e4_out", [8, 128, 2, DG], F32)
    e4_in = nc.dram_tensor("e4_in", [128, 2, DG], F32)

    gamma_t = sb(nc.sbuf_tensor("gamma_t", [128, DG], F32))
    beta_t = sb(nc.sbuf_tensor("beta_t", [128, DG], F32))
    convb_t = sb(nc.sbuf_tensor("convb_t", [128, CGN], F32))
    negbdt_t = sb(nc.sbuf_tensor("negbdt_t", [128, CGN], F32))
    dvec_t = sb(nc.sbuf_tensor("dvec_t", [128, CGN], F32))
    convw_t = sb(nc.sbuf_tensor("convw_t", [128, CGN, 4], F32))
    e_a = sb(nc.sbuf_tensor("e_a", [128, CGN, DSTT], F32))
    wxp_t = sb(nc.sbuf_tensor("wxp_t", [128, CGN, 96], BF16))
    wdt_t = sb(nc.sbuf_tensor("wdt_t", [DTR, CH], BF16))
    tvec_t = sb(nc.sbuf_tensor("tvec_t", [128, DG, 5], F32))
    wlane = sb(nc.sbuf_tensor("wlane", [128, DG, 5], F32))
    omix = sb(nc.sbuf_tensor("omix", [128, DG, 3], F32))
    w1s_t = sb(nc.sbuf_tensor("w1s_t", [128, HG], F32))
    w2s_t = sb(nc.sbuf_tensor("w2s_t", [128, HG], F32))
    h1s_t = sb(nc.sbuf_tensor("h1s_t", [128, HG], F32))
    h2s_t = sb(nc.sbuf_tensor("h2s_t", [128, HG], F32))
    w3s_t = sb(nc.sbuf_tensor("w3s_t", [128, DG], F32))
    lnw_t = sb(nc.sbuf_tensor("lnw_t", [128, HG], F32))
    lnb_t = sb(nc.sbuf_tensor("lnb_t", [128, HG], F32))
    selp = sb(nc.sbuf_tensor("selp", [128, 8], F32))
    selc_pen = sb(nc.sbuf_tensor("selc_pen", [128, 8], F32))
    selrow = sb(nc.sbuf_tensor("selrow", [1, 24], F32))
    ones_col = sb(nc.sbuf_tensor("ones_col", [128, 1], BF16))
    ones_row = sb(nc.sbuf_tensor("ones_row", [1, 128], BF16))
    ones_row_f = sb(nc.sbuf_tensor("ones_row_f", [1, 128], F32))
    offt = sb(nc.sbuf_tensor("offt", [1, 4], I32))
    stat = sb(nc.sbuf_tensor("stat", [128, 32], F32))
    srow = sb(nc.sbuf_tensor("srow", [1, 4, 512], F32))
    stg_x = sb(nc.sbuf_tensor("stg_x", [128, 512], BF16))
    stg_q = sb(nc.sbuf_tensor("stg_q", [128, 512], BF16))
    stg_y = sb(nc.sbuf_tensor("stg_y", [128, 512], BF16))
    epsc = sb(nc.sbuf_tensor("epsc", [1, 2], F32))
    tmp_a = sb(nc.sbuf_tensor("tmp_a", [128, TLOC], F32))
    tmp_c = sb(nc.sbuf_tensor("tmp_c", [128, TLOC], BF16))

    xn_scope = contextlib.ExitStack()
    xn = xn_scope.enter_context(nc.sbuf_tensor("xn", [128, DG, TLOC], BF16))
    pb = [sb(nc.psum_tensor(f"pb{i}", [128, 512], F32)) for i in range(8)]
    roff = sb(g.register("roff"))
    rtmp = sb(g.register("rtmp"))
    sroff = sb(sy.register("sroff"))
    srtmp = sb(sy.register("srtmp"))

    uid = [0]

    def gemm_gen(wdram, n_oc, n_kg, wtile, rhs_fn, evac_fn, n_tq, bank_of,
                 pre_wait=(), mmtag="mm"):
        uid[0] += 1
        ldsem = [f"gw{uid[0]}a", f"gw{uid[0]}b"]
        for nm in pre_wait:
            p.wait(t, nm)
        m0 = p.count(mmtag)
        last_use = [m0, m0]
        bank_free = {}
        for oc in range(n_oc):
            par = oc % 2
            p.wait(g, mmtag, at=last_use[par])
            ldc = p.dma(g, wtile[:, par, 0:n_kg * 128], wdram(oc), ldsem[par])
            for tqi in range(n_tq):
                bk = bank_of(oc, tqi)
                if bk in bank_free:
                    nm, cnt = bank_free[bk]
                    p.wait(t, nm, at=cnt)
                p.wait(t, ldsem[par], at=ldc)
                for kg in range(n_kg):
                    i1 = t.matmul(pb[bk][:, :], wtile[:, par, kg * 128:(kg + 1) * 128],
                                  rhs_fn(kg, tqi), start=(kg == 0), stop=(kg == n_kg - 1))
                mmc = p.inc(i1, mmtag)
                last_use[par] = mmc
                bank_free[bk] = evac_fn(oc, tqi, pb[bk])
            yield

    def gemm(*args, **kw):
        for _ in gemm_gen(*args, **kw):
            pass

    def run_interleaved(gens):
        active = list(gens)
        while active:
            keep = []
            for gen, w in active:
                alive = True
                for _ in range(w):
                    try:
                        next(gen)
                    except StopIteration:
                        alive = False
                        break
                if alive:
                    keep.append((gen, w))
            active = keep

    def phase_barrier(sems=("dsy", "dgp", "vx", "sc", "mm", "gp", "cc",
                            "dsyM", "dgpM", "vxM", "scM", "mmM", "crM",
                            "dsyF", "vxF", "scF", "mmF")):
        for nm in sems:
            for eng in (v, s, t, g, sy):
                p.wait(eng, nm)

    def ln_emit(src, n_g, dst_bf, eps, plain_cb=None, gamma_ap=None, beta_ap=None,
                tcols=TLOC, toff=0):
        inv_n = 1.0 / (n_g * 128)
        p.wait(s, "vx")
        ps_x, ps_q, ps_r, ps_m = pb[4], pb[5], pb[6], pb[7]
        for blk in range(tcols // 512):
            t0 = toff + blk * 512
            for gi in range(n_g):
                p.wait(v, "mm"); p.wait(s, "mm")
                p.inc(v.tensor_copy(stg_x[:], src[:, gi, t0:t0 + 512]), "vx")
                p.inc(s.activation(stg_q[:], src[:, gi, t0:t0 + 512], AF.Square), "sc")
                p.wait(t, "vx"); p.wait(t, "sc")
                last = gi == n_g - 1
                t.matmul(ps_x[:1, :], ones_col[:], stg_x[:], start=(gi == 0), stop=last)
                i2 = t.matmul(ps_q[:1, :], ones_col[:], stg_q[:], start=(gi == 0), stop=last)
                p.inc(i2, "mm")
            p.wait(v, "mm")
            p.inc(v.tensor_scalar_mul(srow[:1, 0, :], ps_x[:1, :], inv_n), "vx")
            v.tensor_mul(srow[:1, 1, :], srow[:1, 0, :], srow[:1, 0, :])
            p.inc(v.scalar_tensor_tensor(srow[:1, 1, :], ps_q[:1, :], inv_n,
                                         srow[:1, 1, :], op0=ALU.mult,
                                         op1=ALU.subtract), "vx")
            p.wait(s, "vx")
            s.activation(srow[:1, 2, :], srow[:1, 1, :], AF.Ln, bias=epsc[:1, 0:1])
            p.inc(s.activation(srow[:1, 2, :], srow[:1, 2, :], AF.Exp, scale=-0.5), "sc")
            p.wait(v, "sc")
            v.tensor_mul(srow[:1, 3, :], srow[:1, 0, :], srow[:1, 2, :])
            p.inc(v.tensor_scalar_mul(srow[:1, 3, :], srow[:1, 3, :], -1.0), "vx")
            p.wait(t, "vx")
            t.matmul(ps_r[:, :], ones_row_f[:], srow[:1, 2, :], start=True, stop=True)
            p.inc(t.matmul(ps_m[:, :], ones_row_f[:], srow[:1, 3, :], start=True,
                           stop=True), "mm")
            p.wait(v, "mm")
            for gi in range(n_g):
                v.tensor_mul(tmp_a[:, 0:512], src[:, gi, t0:t0 + 512], ps_r[:, :])
                ia = v.tensor_add(tmp_a[:, 0:512], tmp_a[:, 0:512], ps_m[:, :])
                if gamma_ap is not None:
                    ia = v.tensor_scalar(dst_bf[:, gi, t0:t0 + 512], tmp_a[:, 0:512],
                                         gamma_ap[:, gi:gi + 1], beta_ap[:, gi:gi + 1],
                                         op0=ALU.mult, op1=ALU.add)
                else:
                    ia = v.tensor_copy(dst_bf[:, gi, t0:t0 + 512], tmp_a[:, 0:512])
                p.inc(ia, "vx")
                if plain_cb is not None:
                    plain_cb(gi, t0)

    def chain_select(out_col_fn, gath_fn, n_items):
        for it in range(n_items):
            v.memset(tmp_a[:, 0:DG], 0.0)
            for j in range(8):
                ia = v.scalar_tensor_tensor(tmp_a[:, 0:DG], gath_fn(it, j),
                                            selp[:, j:j + 1], tmp_a[:, 0:DG],
                                            op0=ALU.mult, op1=ALU.add)
            p.inc(v.tensor_copy(out_col_fn(it), tmp_a[:, 0:DG]), "vx")

    with contextlib.ExitStack() as ph01:
        x1 = ph01.enter_context(nc.sbuf_tensor("x1", [128, DG, TLOC], F32))
        xn2 = ph01.enter_context(nc.sbuf_tensor("xn2", [128, DG, TLOC], BF16))
        wz = ph01.enter_context(nc.sbuf_tensor("wz", [128, 2, DG * 128], BF16))

        p.dma(sy, x1[:], dr["x0T"], "dsy")
        for nm, tl in [("l0_gamma", gamma_t), ("l0_beta", beta_t), ("conv_b", convb_t),
                       ("d_vec", dvec_t), ("conv_w", convw_t), ("a_log", e_a),
                       ("tvec", tvec_t), ("w1s", w1s_t), ("w2s", w2s_t),
                       ("h1s", h1s_t), ("h2s", h2s_t), ("w3s", w3s_t),
                       ("lnw", lnw_t), ("lnb", lnb_t)]:
            p.dma(sy, tl[:], dr[nm], "dsy")
        p.dma(sy, stat[:, 0:CGN], dr["b_dt"], "dsy")
        p.dma(sy, selrow[:, 0:8], dr["selprev"], "dsy")
        p.dma(sy, selrow[:, 8:16], dr["selchain"], "dsy")
        p.dma(sy, selrow[:, 16:24], dr["selpen"], "dsy")
        p.dma(g, wxp_t[:], dr["w_xproj"], "dgp")
        p.dma(g, wdt_t[:], dr["w_dt"], "dgp")
        p.dma(g, offt[:], dr["dynoff"], "dgp")

        p.wait(v, "dsy")
        v.memset(epsc[:], LN_EPS)
        v.memset(ones_col[:], 1.0)
        v.memset(ones_row[:], 1.0)
        v.memset(ones_row_f[:], 1.0)
        p.inc(v.tensor_scalar_mul(negbdt_t[:], stat[:, 0:CGN], -1.0), "vx")
        for j in range(3):
            p.inc(v.tensor_scalar(omix[:, :, j], tvec_t[:, :, 2 + j], -1.0, 1.0,
                                  op0=ALU.mult, op1=ALU.add), "vx")
        p.wait(s, "dsy")
        p.inc(s.activation(e_a[:], e_a[:], AF.Exp), "sc")
        p.inc(s.activation(wlane[:, :, 0], tvec_t[:, :, 0], AF.Exp), "sc")
        p.wait(v, "sc")
        p.inc(v.tensor_scalar_mul(wlane[:, :, 0], wlane[:, :, 0], -1.0), "vx")
        p.wait(s, "vx")
        p.inc(s.activation(wlane[:, :, 1], wlane[:, :, 0], AF.Exp), "sc")
        p.inc(s.activation(wlane[:, :, 2], wlane[:, :, 0], AF.Exp, scale=float(TLOC)), "sc")
        p.inc(s.activation(wlane[:, :, 3], tvec_t[:, :, 1], AF.Exp), "sc")
        p.wait(t, "dsy"); p.wait(t, "vx")
        p.inc(t.matmul(pb[0][:, 0:24], ones_row_f[:], selrow[:1, :], start=True,
                       stop=True), "mm")
        p.wait(v, "mm")
        p.inc(v.tensor_copy(selp[:], pb[0][:, 0:8]), "vx")
        p.inc(v.tensor_copy(selc_pen[:], pb[0][:, 16:24]), "vx")

        def add_xn(gi, t0):
            p.inc(v.tensor_add(x1[:, gi, t0:t0 + 512], x1[:, gi, t0:t0 + 512],
                               tmp_a[:, 0:512]), "vx")

        ln_emit(x1, DG, xn, LN_EPS, plain_cb=add_xn)
        ln_emit(xn, DG, xn2, LN_EPS, gamma_ap=gamma_t, beta_ap=beta_t)

        def evac_u(oc, tqi, bank):
            p.wait(s, "mm")
            p.wait(s, "dsy")
            cnt = p.inc(s.copy(stg_y[:], bank[:, :]), "sc")
            p.wait(sy, "sc")
            p.dma(sy, ag1_in.ap()[oc * 128:(oc + 1) * 128, tqi * 512:(tqi + 1) * 512],
                  stg_y[:], "dsy")
            return ("sc", cnt)

        gemm(lambda oc: dr["w_in_u"][oc], DIN // 128, DG, wz,
             lambda kg, tqi: xn2[:, kg, tqi * 512:(tqi + 1) * 512],
             evac_u, 4, lambda oc, tqi: (oc * 4 + tqi) % 4, pre_wait=("vx",))

        p.wait(g, "dsy")
        p.inc(g.collective_compute("AllGather", ALU.bypass,
                                   replica_groups=[[0, 1, 2, 3], [4, 5, 6, 7]],
                                   ins=[ag1_in.ap()], outs=[ag1_out.ap()]), "cc")
        p.wait(sy, "vx")
        p.dma(sy, x1_spill.ap(), x1[:], "dsy")
        p.dma(sy, xn2_spill.ap(), xn2[:], "dsy")
        p.dma(sy, xn_sp.ap(), xn[:], "dsy")
        p.wait(g, "dsy")

    xn_scope.close()
    with contextlib.ExitStack() as ph2:
        phase_barrier(("dsy", "dgp", "vx", "sc", "mm", "gp"))
        xdbl_tb = ph2.enter_context(nc.sbuf_tensor("xdbl_tb", [96, TBS], F32))
        up_h = ph2.enter_context(nc.sbuf_tensor("up_h", [128, CGN, 3 + TBS], BF16))
        u_ch = ph2.enter_context(nc.sbuf_tensor("u_ch", [128, CGN, TBS], BF16))
        dl_ch = ph2.enter_context(nc.sbuf_tensor("dl_ch", [128, CGN, TBS], BF16))
        du_ch = ph2.enter_context(nc.sbuf_tensor("du_ch", [128, CGN, TBS], BF16))
        bb_s = ph2.enter_context(nc.sbuf_tensor("bb_s", [128, 2, TBS], BF16))
        cb_s = ph2.enter_context(nc.sbuf_tensor("cb_s", [128, 2, TBS], BF16))
        da_t = ph2.enter_context(nc.sbuf_tensor("da_t", [128, 2, TBS], BF16))
        dbu_t = ph2.enter_context(nc.sbuf_tensor("dbu_t", [128, TBS], BF16))
        h_t = ph2.enter_context(nc.sbuf_tensor("h_t", [128, 2, TBS], BF16))
        prod_t = ph2.enter_context(nc.sbuf_tensor("prod_t", [128, TBS], BF16))
        y_acc = ph2.enter_context(nc.sbuf_tensor("y_acc", [128, CGN, TBS], F32))
        ycp = ph2.enter_context(nc.sbuf_tensor("ycp", [128, 2, TBS], BF16))
        carry = ph2.enter_context(nc.sbuf_tensor("carry", [128, CGN * DSTT], F32))
        bneg_tb = ph2.enter_context(nc.sbuf_tensor("bneg_tb", [DSTT, TBS], BF16))
        c_tb = ph2.enter_context(nc.sbuf_tensor("c_tb", [DSTT, TBS], BF16))
        onehot = ph2.enter_context(nc.sbuf_tensor("onehot", [DSTT, DSTT, 128], BF16))
        dt_bf = ph2.enter_context(nc.sbuf_tensor("dt_bf", [DTR, TBS], BF16))
        xn_q = ph2.enter_context(nc.sbuf_tensor("xn_q", [128, DG, 512], BF16))
        hbuf = ph2.enter_context(nc.sbuf_tensor("hbuf", [128, HG, 512], BF16))
        h2buf = ph2.enter_context(nc.sbuf_tensor("h2buf", [128, HG, 512], BF16))
        wba = ph2.enter_context(nc.sbuf_tensor("wba", [128, 2, HG * 128], BF16))

        p.dma(sy, onehot[:], dr["onehot16"], "dsyM")

        def chan_gen():
            v.memset(up_h[:, :, 0:3], 0.0)
            p.inc(v.memset(carry[:], 0.0), "vxM")
            p.wait(sy, "cc")
            for tb in range(NTB):
                sy.reg_load(sroff, offt[:1, 0:1])
                sy.reg_add(srtmp, sroff, (((tb * TBS) // TLOC) * DIN) * TLOC
                           + (tb * TBS) % TLOC)
                srcu = bass.AP(ag1_out, srtmp,
                               [[TLOC, 128], [128 * TLOC, CGN], [1, TBS]])
                p.wait(sy, "vxM")
                with nc.allow_non_contiguous_dma("AG gather"):
                    p.inc(sy.dma_start(out=up_h[:, :, 3:3 + TBS], in_=srcu,
                                       bounds_check=None), "dsyM", 16)
                p.wait(v, "dsyM")
                p.wait(s, "dsyM"); p.wait(s, "mmM")
                for cg in range(CGN):
                    p.wait(v, "scM")
                    v.tensor_scalar_mul(tmp_a[:, 0:TBS], up_h[:, cg, 0:TBS],
                                        convw_t[:, cg, 0:1])
                    for j in range(1, 4):
                        ic = v.scalar_tensor_tensor(tmp_a[:, 0:TBS],
                                                    up_h[:, cg, j:j + TBS],
                                                    convw_t[:, cg, j:j + 1],
                                                    tmp_a[:, 0:TBS],
                                                    op0=ALU.mult, op1=ALU.add)
                    p.inc(ic, "vxM")
                    p.inc(v.tensor_copy(up_h[:, cg, 0:3], up_h[:, cg, TBS:TBS + 3]),
                          "vxM")
                    p.wait(s, "vxM")
                    p.inc(s.activation(u_ch[:, cg, :], tmp_a[:, 0:TBS], AF.Silu,
                                       bias=convb_t[:, cg:cg + 1]), "scM")
                p.wait(sy, "scM")
                p.dma(sy, u_spill.ap()[:, tb * TBS:(tb + 1) * TBS]
                      .rearrange("(cg q) t -> q cg t", q=128), u_ch[:], "dsyM")
                p.wait(t, "scM"); p.wait(t, "vxM")
                for nb in range(TBS // 512):
                    for cg in range(CGN):
                        i1 = t.matmul(pb[4 + nb][:96, :], wxp_t[:, cg, :],
                                      u_ch[:, cg, nb * 512:(nb + 1) * 512],
                                      start=(cg == 0), stop=(cg == CGN - 1))
                    p.inc(i1, "mmM")
                p.wait(sy, "mmM")
                for nb in range(TBS // 512):
                    pass
                p.wait(v, "mmM"); p.wait(v, "scM")
                for nb in range(TBS // 512):
                    p.wait(v, "dsyM", at=None if nb == 0 else None)
                    p.inc(v.tensor_copy(tmp_a[:96, nb * 512:(nb + 1) * 512],
                                        pb[4 + nb][:96, :]), "vxM")
                p.wait(sy, "vxM")
                p.dma(sy, ar_in.ap()[:, tb * TBS:(tb + 1) * TBS],
                      tmp_a[:96, 0:TBS], "dsyM")
                yield
            p.wait(g, "dsyM")
            p.inc(g.collective_compute("AllReduce", ALU.add,
                                       replica_groups=[[0, 1, 2, 3], [4, 5, 6, 7]],
                                       ins=[ar_in.ap()], outs=[ar_out.ap()]), "cc")
            yield
            p.wait(sy, "cc")
            crm_hist = {}
            scan_hist = {}
            hpar_hist = {}
            for tb in range(NTB):
                t0 = tb * TBS
                p.wait(sy, "vxM"); p.wait(sy, "scM"); p.wait(sy, "mmM")
                p.dma(sy, u_ch[:], u_spill.ap()[:, t0:t0 + TBS]
                      .rearrange("(cg q) t -> q cg t", q=128), "dsyM")
                p.dma(sy, xdbl_tb[:], ar_out.ap()[:, t0:t0 + TBS], "dsyM")
                p.wait(g, "vxM")
                p.dma(g, bneg_tb[:], ar_out.ap()[64:80, t0:t0 + TBS], "dgpM")
                p.dma(g, c_tb[:], ar_out.ap()[80:96, t0:t0 + TBS], "dgpM")
                p.wait(v, "dsyM"); p.wait(v, "dgpM")
                p.inc(v.tensor_scalar_mul(bneg_tb[:], bneg_tb[:], -1.0), "vxM")
                p.inc(v.tensor_copy(dt_bf[:], xdbl_tb[0:DTR, :]), "vxM")
                p.wait(t, "scM"); p.wait(t, "vxM")
                p.wait(s, "vxM")
                for cg in range(CGN):
                    for nb in range(TBS // 512):
                        i1 = t.matmul(pb[4 + nb][:, :],
                                      wdt_t[:, cg * 128:(cg + 1) * 128],
                                      dt_bf[:, nb * 512:(nb + 1) * 512],
                                      start=True, stop=True)
                        p.inc(i1, "mmM")
                        p.wait(s, "mmM")
                        p.inc(s.activation(dl_ch[:, cg, nb * 512:(nb + 1) * 512],
                                           pb[4 + nb][:, :], AF.Sigmoid, scale=-1.0,
                                           bias=negbdt_t[:, cg:cg + 1]), "scM")
                        p.wait(t, "scM")
                for cg in range(CGN):
                    p.inc(s.activation(dl_ch[:, cg, :], dl_ch[:, cg, :], AF.Ln), "scM")
                p.wait(v, "scM")
                for cg in range(CGN):
                    p.inc(v.tensor_mul(du_ch[:, cg, :], dl_ch[:, cg, :],
                                       u_ch[:, cg, :]), "vxM")
                p.inc(v.memset(y_acc[:], 0.0), "vxM")
                yield
                for si in range(DSTT):
                    sp = si % 2
                    prev = scan_hist.get(("bc_evac", 1 - sp))
                    if prev is not None:
                        p.wait(t, "scM", at=prev)
                    i1 = None
                    for nb in range(TBS // 512):
                        t.matmul(pb[4 + nb][:, :], onehot[:, si, :],
                                 bneg_tb[:, nb * 512:(nb + 1) * 512],
                                 start=True, stop=True)
                        i1 = t.matmul(pb[6 + nb][:, :], onehot[:, si, :],
                                      c_tb[:, nb * 512:(nb + 1) * 512],
                                      start=True, stop=True)
                    p.inc(i1, "mmM")
                    p.wait(s, "mmM")
                    rd = scan_hist.get(("bc_read", sp))
                    if rd is not None:
                        p.wait(s, "vxM", at=rd)
                    for nb in range(TBS // 512):
                        s.copy(bb_s[:, sp, nb * 512:(nb + 1) * 512], pb[4 + nb][:, :])
                        cnt = p.inc(s.copy(cb_s[:, sp, nb * 512:(nb + 1) * 512],
                                           pb[6 + nb][:, :]), "scM")
                    scan_hist[("bc_evac", sp)] = cnt
                    for cg in range(CGN):
                        k = si * CGN + cg
                        pp = k % 2
                        if pp in scan_hist:
                            p.wait(s, "vxM", at=scan_hist[pp])
                        p.inc(s.activation(da_t[:, pp, :], dl_ch[:, cg, :], AF.Exp,
                                           scale=e_a[:, cg, si:si + 1]), "scM")
                        p.wait(v, "scM")
                        p.inc(v.tensor_mul(dbu_t[:], du_ch[:, cg, :],
                                           bb_s[:, sp, :]), "vxM")
                        cr = crm_hist.get((cg, si))
                        if cr is not None:
                            p.wait(v, "crM", at=cr)
                        hp = hpar_hist.get(pp)
                        if hp is not None:
                            p.wait(v, "crM", at=hp)
                        scan_hist[pp] = p.inc(
                            v.tensor_tensor_scan(h_t[:, pp, :], da_t[:, pp, :],
                                                 dbu_t[:], carry[:, k:k + 1],
                                                 ALU.mult, ALU.add), "vxM")
                        p.wait(s, "vxM")
                        cnt2 = p.inc(s.copy(carry[:, k:k + 1],
                                            h_t[:, pp, TBS - 1:TBS]), "crM")
                        crm_hist[(cg, si)] = cnt2
                        hpar_hist[pp] = cnt2
                        v.tensor_mul(prod_t[:], h_t[:, pp, :], cb_s[:, sp, :])
                        scan_hist[("bc_read", sp)] = p.inc(
                            v.tensor_add(y_acc[:, cg, :], y_acc[:, cg, :],
                                         prod_t[:]), "vxM")
                    if si % 4 == 3:
                        yield
                for cg in range(CGN):
                    p.inc(v.scalar_tensor_tensor(y_acc[:, cg, :], u_ch[:, cg, :],
                                                 dvec_t[:, cg:cg + 1], y_acc[:, cg, :],
                                                 op0=ALU.mult, op1=ALU.add), "vxM")
                    p.wait(v, "dsyM")
                    p.inc(v.tensor_copy(ycp[:, cg % 2, :], y_acc[:, cg, :]), "vxM")
                    p.wait(sy, "vxM")
                    p.dma(sy, ag2_in.ap()[cg * 128:(cg + 1) * 128, t0:t0 + TBS],
                          ycp[:, cg % 2, :], "dsyM")
                yield
            p.wait(g, "dsyM")
            p.inc(g.collective_compute("AllGather", ALU.bypass,
                                       replica_groups=[[0, 1, 2, 3], [4, 5, 6, 7]],
                                       ins=[ag2_in.ap()], outs=[ag2_out.ap()]), "cc")

        def ffn_gen():
            for q in range(TLOC // 512):
                q0 = q * 512
                p.wait(sy, "mmF")
                p.dma(sy, xn_q[:], xn_sp.ap()[:, :, q0:q0 + 512], "dsyF")

                def evac_g1(oc, tqi, bank):
                    p.wait(s, "mmF")
                    cnt = p.inc(s.activation(hbuf[:, oc, :], bank[:, :], AF.Silu,
                                             scale=w1s_t[:, oc:oc + 1]), "scF")
                    return ("scF", cnt)

                def evac_g2(oc, tqi, bank):
                    p.wait(v, "mmF"); p.wait(v, "scF")
                    cnt = p.inc(v.scalar_tensor_tensor(hbuf[:, oc, :], bank[:, :],
                                                       w2s_t[:, oc:oc + 1],
                                                       hbuf[:, oc, :],
                                                       op0=ALU.mult, op1=ALU.mult),
                                "vxF")
                    return ("vxF", cnt)

                yield from gemm_gen(lambda oc: dr["w1T"][oc], HG, DG, wba,
                                    lambda kg, tqi: xn_q[:, kg, :], evac_g1,
                                    1, lambda oc, tqi: oc % 2,
                                    pre_wait=("dsyF",), mmtag="mmF")
                yield from gemm_gen(lambda oc: dr["w2T"][oc], HG, DG, wba,
                                    lambda kg, tqi: xn_q[:, kg, :], evac_g2,
                                    1, lambda oc, tqi: 2 + oc % 2,
                                    pre_wait=("scF",), mmtag="mmF")

                def evac_h1(oc, tqi, bank):
                    p.wait(s, "mmF")
                    cnt = p.inc(s.activation(h2buf[:, oc, :], bank[:, :], AF.Gelu,
                                             scale=h1s_t[:, oc:oc + 1]), "scF")
                    return ("scF", cnt)

                yield from gemm_gen(lambda oc: dr["h1T"][oc], HG, HG, wba,
                                    lambda kg, tqi: hbuf[:, kg, :], evac_h1, 1,
                                    lambda oc, tqi: oc % 2, pre_wait=("vxF",),
                                    mmtag="mmF")

                def evac_h2(oc, tqi, bank):
                    p.wait(s, "mmF")
                    cnt = p.inc(s.activation(hbuf[:, oc, :], bank[:, :], AF.Gelu,
                                             scale=h2s_t[:, oc:oc + 1]), "scF")
                    return ("scF", cnt)

                yield from gemm_gen(lambda oc: dr["h2T"][oc], HG, HG, wba,
                                    lambda kg, tqi: h2buf[:, kg, :], evac_h2, 1,
                                    lambda oc, tqi: 2 + oc % 2, pre_wait=("scF",),
                                    mmtag="mmF")

                def evac_w3(oc, tqi, bank):
                    p.wait(s, "mmF"); p.wait(s, "dsyF")
                    cnt = p.inc(s.activation(stg_y[:], bank[:, :], AF.Copy,
                                             scale=w3s_t[:, oc:oc + 1]), "scF")
                    p.wait(sy, "scF")
                    p.dma(sy, ffn_spill.ap()[:, oc, q0:q0 + 512], stg_y[:], "dsyF")
                    return ("scF", cnt)

                yield from gemm_gen(lambda oc: dr["w3T"][oc], DG, HG, wba,
                                    lambda kg, tqi: hbuf[:, kg, :], evac_w3, 1,
                                    lambda oc, tqi: oc % 2, pre_wait=("scF",),
                                    mmtag="mmF")

        chang, ffng = chan_gen(), ffn_gen()
        for _ in range(30):
            next(ffng)
        run_interleaved([(ffng, 10), (chang, 1)])

    ph4 = contextlib.ExitStack()
    with ph4:
        phase_barrier()
        ph4b = ph4.enter_context(contextlib.ExitStack())
        x1b = ph4b.enter_context(nc.sbuf_tensor("x1b", [128, DG, TLOC], F32, side="right"))
        ph4a = ph4.enter_context(contextlib.ExitStack())
        xn2b = ph4a.enter_context(nc.sbuf_tensor("xn2b", [128, DG, TLOC], BF16))
        szq = ph4a.enter_context(nc.sbuf_tensor("szq", [128, DIN // 128, 512], BF16))
        yzq = ph4a.enter_context(nc.sbuf_tensor("yzq", [128, DIN // 128, 512], BF16))
        wbo = ph4a.enter_context(nc.sbuf_tensor("wbo", [128, 2, DIN], BF16))

        p.dma(sy, x1b[:], x1_spill.ap(), "dsy")
        p.dma(sy, xn2b[:], xn2_spill.ap(), "dsy")
        p.wait(t, "dsy"); p.wait(v, "dsy")
        for q in range(TLOC // 512):
            q0 = q * 512

            def evac_z(oc, tqi, bank):
                p.wait(s, "mm"); p.wait(s, "vx")
                cnt = p.inc(s.activation(szq[:, oc, :], bank[:, :], AF.Silu), "sc")
                return ("sc", cnt)

            gemm(lambda oc: dr["w_in_z"][oc], DIN // 128, DG, wbo,
                 lambda kg, tqi: xn2b[:, kg, q0:q0 + 512], evac_z, 1,
                 lambda oc, tqi: 4 + oc % 2)
            g.reg_load(roff, offt[:1, 1:2])
            g.reg_add(rtmp, roff, q0)
            srcy = bass.AP(ag2_out, rtmp, [[T, 128], [128 * T, DIN // 128], [1, 512]])
            p.wait(g, "cc")
            p.wait(g, "vx"); p.wait(g, "mm")
            with nc.allow_non_contiguous_dma("strided gather from AG output"):
                p.inc(g.dma_start(out=yzq[:, :, :], in_=srcy, bounds_check=None),
                      "dgp", 16)
            p.wait(v, "dgp"); p.wait(v, "sc")
            for oc in range(DIN // 128):
                p.inc(v.tensor_mul(yzq[:, oc, :], yzq[:, oc, :], szq[:, oc, :]), "vx")

            def evac_wo(oc, tqi, bank):
                p.wait(v, "mm")
                cnt = p.inc(v.tensor_add(x1b[:, oc, q0:q0 + 512],
                                         x1b[:, oc, q0:q0 + 512], bank[:, :]), "vx")
                return ("vx", cnt)

            gemm(lambda oc: dr["w_out"][oc], DG, DIN // 128, wbo,
                 lambda kg, tqi: yzq[:, kg, :], evac_wo, 1,
                 lambda oc, tqi: oc % 2, pre_wait=("vx",))
        for gi in range(DG):
            p.wait(sy, "vx")
            p.dma(sy, tmp_c[:, 0:TLOC], ffn_spill.ap()[:, gi, :], "dsy")
            p.wait(v, "dsy")
            p.inc(v.tensor_add(x1b[:, gi, :], x1b[:, gi, :], tmp_c[:, 0:TLOC]), "vx")

        if upto == "l0dbg":
            dbg_ffn = nc.dram_tensor("dbg_ffn", [128, DG, TLOC], BF16,
                                     kind="ExternalOutput").ap()
            dbg_y = nc.dram_tensor("dbg_y", [DIN, T], BF16, kind="ExternalOutput").ap()
            dbg_xdbl = nc.dram_tensor("dbg_xdbl", [96, T], F32, kind="ExternalOutput").ap()
            dbg_xn = nc.dram_tensor("dbg_xn", [128, DG, TLOC], BF16,
                                    kind="ExternalOutput").ap()
            dbg_u = nc.dram_tensor("dbg_u", [CH, T], BF16, kind="ExternalOutput").ap()
            dbg_up = nc.dram_tensor("dbg_up", [DIN, TLOC], BF16, kind="ExternalOutput").ap()
            p.dma(g, dbg_up, ag1_in.ap(), "dgp")
            dbg_xn2 = nc.dram_tensor("dbg_xn2", [128, DG, TLOC], BF16, kind="ExternalOutput").ap()
            p.dma(g, dbg_xn2, xn2_spill.ap(), "dgp")
            p.dma(g, dbg_ffn, ffn_spill.ap(), "dgp")
            p.dma(g, dbg_y, ag2_out.ap(), "dgp")
            p.dma(g, dbg_u, u_spill.ap(), "dgp")
            p.dma(g, dbg_xdbl, ar_out.ap(), "dgp")
            p.wait(sy, "vx")
            p.dma(sy, dbg_xn, xn[:], "dsy")
            p.dma(sy, out_ap, x1b[:], "dst")
            p.wait(sy, "dst"); p.wait(sy, "dsy"); p.wait(g, "dgp")
            return p, dr, out_ap
        if upto == "l0":
            p.wait(sy, "vx")
            p.dma(sy, out_ap, x1b[:], "dst")
            p.wait(sy, "dst")
            return p, dr, out_ap

        ph4a.close()
        phase_barrier()
        l1 = ph4.enter_context(contextlib.ExitStack())
        xn1 = l1.enter_context(nc.sbuf_tensor("xn1", [128, DG, TLOC], BF16))
        ffn1_out = l1.enter_context(nc.sbuf_tensor("ffn1_out", [128, DG, TLOC], BF16))
        e2g = l1.enter_context(nc.sbuf_tensor("e2g", [128, 8, DG], BF16))
        e3g = l1.enter_context(nc.sbuf_tensor("e3g", [128, 8, DG], F32))
        e4g = l1.enter_context(nc.sbuf_tensor("e4g", [128, 8, 2, DG], F32))
        bndc = l1.enter_context(nc.sbuf_tensor("bndc", [128, DG], F32))
        kmx = l1.enter_context(nc.sbuf_tensor("kmx", [128, DG], F32))
        aend = l1.enter_context(nc.sbuf_tensor("aend", [128, 2, DG], F32))
        ainb = l1.enter_context(nc.sbuf_tensor("ainb", [128, 2, DG], F32))
        accf = l1.enter_context(nc.sbuf_tensor("accf", [128, 2, DG], F32))
        gq_t = l1.enter_context(nc.sbuf_tensor("gq_t", [128, 24], F32))
        thrc = l1.enter_context(nc.sbuf_tensor("thrc", [128, 2], F32))
        ones_col_f = l1.enter_context(nc.sbuf_tensor("ones_col_f", [128, 1], F32))
        tmp_b = l1.enter_context(nc.sbuf_tensor("tmp_b", [128, TLOC], F32))
        tmp_e = l1.enter_context(nc.sbuf_tensor("tmp_e", [128, TLOC], BF16))

        ln_emit(x1b, DG, xn1, LN_EPS)
        p.wait(sy, "vx")
        with nc.allow_non_contiguous_dma("tiny boundary column"):
            p.dma(sy, e2_in.ap(), xn1[:, :, TLOC - 1:TLOC], "dsy")
        p.wait(g, "dsy")
        p.inc(g.collective_compute("AllGather", ALU.bypass,
                                   replica_groups=[list(range(8))],
                                   ins=[e2_in.ap()], outs=[e2_out.ap()]), "cc")
        p.dma(sy, x1_spill.ap(), x1b[:], "dsy")
        p.wait(v, "dsy"); p.wait(s, "dsy"); p.wait(t, "dsy"); p.wait(g, "dsy")
        ph4b.close()

        p.wait(v, "dsy")
        v.memset(gq_t[:], 0.0)
        p.inc(v.memset(ones_col_f[:], 1.0), "vx")
        with nc.sbuf_tensor("qbuf", [128, HF], F32) as qbuf:
            for mi, mat in enumerate((dr["wfcT"], dr["wprT"])):
                for kg in range(DG):
                    p.wait(g, "vx")
                    p.dma(g, qbuf[:], mat[kg], "dgp")
                    p.wait(v, "dgp")
                    p.inc(v.tensor_reduce(gq_t[:, mi * 8 + kg:mi * 8 + kg + 1],
                                          qbuf[:], axis=AX.X, op=ALU.add,
                                          apply_absolute_value=True), "vx")
            p.wait(t, "vx")
            p.inc(t.matmul(pb[4][:1, 0:16], ones_col_f[:], gq_t[:, 0:16], start=True,
                           stop=True), "mm")
            p.wait(v, "mm")
            v.tensor_reduce(gq_t[:1, 16:17], pb[4][:1, 0:8], axis=AX.X, op=ALU.add)
            p.inc(v.tensor_reduce(gq_t[:1, 17:18], pb[4][:1, 8:16], axis=AX.X,
                                  op=ALU.add), "vx")
            p.wait(v, "vx")
            p.inc(v.tensor_scalar(gq_t[:1, 20:22], gq_t[:1, 16:18], 0.5 / (HF * D),
                                  0.5 * ABS_EPS, op0=ALU.mult, op1=ALU.add), "vx")
            p.wait(t, "vx")
            p.inc(t.matmul(pb[5][:, 0:2], ones_row_f[:], gq_t[:1, 20:22], start=True,
                           stop=True), "mm")
            p.wait(v, "mm")
            p.inc(v.tensor_copy(thrc[:], pb[5][:, 0:2]), "vx")
            for mat, col in ((dr["wfcT"], 0), (dr["wprT"], 1)):
                for kg in range(DG):
                    p.wait(g, "vx"); p.wait(g, "sc"); p.wait(g, "dsy")
                    p.dma(g, qbuf[:], mat[kg], "dgp")
                    p.wait(s, "dgp"); p.wait(s, "vx")
                    p.inc(s.sign(tmp_c[:, 0:HF] if False else tmp_c[:, 0:TLOC], qbuf[:, 0:TLOC]), "sc")
                    p.inc(s.sign(tmp_e[:, 0:TLOC], qbuf[:, TLOC:HF]), "sc")
                    p.inc(s.activation(tmp_a[:, 0:TLOC], qbuf[:, 0:TLOC], AF.Abs), "sc")
                    p.inc(s.activation(tmp_b[:, 0:TLOC], qbuf[:, TLOC:HF], AF.Abs), "sc")
                    p.wait(v, "sc"); p.wait(v, "dsy")
                    p.inc(v.tensor_single_scalar(tmp_a[:, 0:TLOC], tmp_a[:, 0:TLOC],
                                                 thrc[:, col:col + 1], op=ALU.is_ge), "vx")
                    p.inc(v.tensor_single_scalar(tmp_b[:, 0:TLOC], tmp_b[:, 0:TLOC],
                                                 thrc[:, col:col + 1], op=ALU.is_ge), "vx")
                    p.inc(v.tensor_mul(tmp_c[:, 0:TLOC], tmp_c[:, 0:TLOC],
                                       tmp_a[:, 0:TLOC]), "vx")
                    p.inc(v.tensor_mul(tmp_e[:, 0:TLOC], tmp_e[:, 0:TLOC],
                                       tmp_b[:, 0:TLOC]), "vx")
                    p.wait(sy, "vx")
                    if col == 0:
                        dst0 = wq_spill.ap()[4 * kg:4 * kg + 2].rearrange("b p k m -> p b k m")
                        dst1 = wq_spill.ap()[4 * kg + 2:4 * kg + 4].rearrange("b p k m -> p b k m")
                    else:
                        dst0 = wqp_spill.ap()[kg][:, 0:HG // 2, :]
                        dst1 = wqp_spill.ap()[kg][:, HG // 2:, :]
                    with nc.allow_non_contiguous_dma("band-order spill"):
                        p.dma(sy, dst0, tmp_c[:, 0:TLOC], "dsy")
                        p.dma(sy, dst1, tmp_e[:, 0:TLOC], "dsy")
        with contextlib.ExitStack() as lf:
            hq = lf.enter_context(nc.sbuf_tensor("hq", [128, HG, 512], BF16))
            hn = lf.enter_context(nc.sbuf_tensor("hn", [128, HG, 512], BF16))
            wqa = lf.enter_context(nc.sbuf_tensor("wqa", [128, 2, HG * 128], BF16))
            for q in range(TLOC // 512):
                q0 = q * 512

                def evac_fc(oc, tqi, bank):
                    p.wait(s, "mm")
                    cnt = p.inc(s.activation(hq[:, oc, :], bank[:, :], AF.Gelu), "sc")
                    return ("sc", cnt)

                gemm(lambda oc: wq_spill.ap()[oc], HG, DG, wqa,
                     lambda kg, tqi: xn1[:, kg, q0:q0 + 512], evac_fc, 1,
                     lambda oc, tqi: oc % 2, pre_wait=("vx", "dsy"))
                ln_emit(hq, HG, hn, LN_EPS, gamma_ap=lnw_t, beta_ap=lnb_t, tcols=512)

                def evac_pr(oc, tqi, bank):
                    p.wait(v, "mm")
                    cnt = p.inc(v.tensor_copy(ffn1_out[:, oc, q0:q0 + 512],
                                              bank[:, :]), "vx")
                    return ("vx", cnt)

                gemm(lambda oc: wqp_spill.ap()[oc], DG, HG, wqa,
                     lambda kg, tqi: hn[:, kg, :], evac_pr, 1,
                     lambda oc, tqi: 2 + oc % 2, pre_wait=("vx", "dsy"))

        with contextlib.ExitStack() as lr:
            ywo = lr.enter_context(nc.sbuf_tensor("ywo", [128, DG, TLOC], BF16))
            wrk = lr.enter_context(nc.sbuf_tensor("wrk", [128, 2, D], BF16))
            lr_m = lr.enter_context(contextlib.ExitStack())
            xmix = lr_m.enter_context(nc.sbuf_tensor("xmix", [128, DG, TLOC], BF16))

            p.wait(g, "cc")
            p.dma(g, e2g[:], e2_out.ap().rearrange("j p g -> p j g"), "dgp")
            p.wait(v, "dgp")
            v.memset(bndc[:], 0.0)
            for j in range(8):
                ia = v.scalar_tensor_tensor(bndc[:], e2g[:, j, :], selp[:, j:j + 1],
                                            bndc[:], op0=ALU.mult, op1=ALU.add)
            p.inc(ia, "vx")
            for mx, wname, mixcol in ((0, "wkT", 2), (1, "wvT", 3), (2, "wrT", 4)):
                p.wait(v, "mm")
                for gi in range(DG):
                    v.tensor_copy(tmp_c[:, 1:TLOC], xn1[:, gi, 0:TLOC - 1])
                    v.tensor_copy(tmp_c[:, 0:1], bndc[:, gi:gi + 1])
                    v.tensor_scalar_mul(tmp_a[:, 0:TLOC], tmp_c[:, 0:TLOC],
                                        omix[:, gi:gi + 1, mx].rearrange("p g -> p g"))
                    ia = v.scalar_tensor_tensor(xmix[:, gi, :], xn1[:, gi, :],
                                                tvec_t[:, gi:gi + 1, mixcol]
                                                .rearrange("p g -> p g"),
                                                tmp_a[:, 0:TLOC],
                                                op0=ALU.mult, op1=ALU.add)
                p.inc(ia, "vx")

                def evac_kv(oc, tqi, bank, mx=mx):
                    p.wait(s, "mm"); p.wait(s, "dsy")
                    if mx == 2:
                        cnt = p.inc(s.activation(stg_y[:], bank[:, :], AF.Sigmoid), "sc")
                    else:
                        cnt = p.inc(s.copy(stg_y[:], bank[:, :]), "sc")
                    p.wait(sy, "sc")
                    p.dma(sy, rwkv_spill.ap()[mx, oc, :, tqi * 512:(tqi + 1) * 512],
                          stg_y[:], "dsy")
                    return ("sc", cnt)

                gemm(lambda oc: dr[wname][oc], DG, DG, wrk,
                     lambda kg, tqi: xmix[:, kg, tqi * 512:(tqi + 1) * 512], evac_kv,
                     4, lambda oc, tqi: (oc * 4 + tqi) % 4, pre_wait=("vx",))
            p.wait(v, "mm"); p.wait(s, "mm"); p.wait(v, "dsy"); p.wait(s, "dsy")
            lr_m.close()
            kgt = lr.enter_context(nc.sbuf_tensor("kgt", [128, TLOC], BF16))
            vgt = lr.enter_context(nc.sbuf_tensor("vgt", [128, TLOC], BF16))
            sgt = lr.enter_context(nc.sbuf_tensor("sgt", [128, TLOC], BF16))
            ekv_t = lr.enter_context(nc.sbuf_tensor("ekv_t", [128, TLOC], F32))
            nmt = lr.enter_context(nc.sbuf_tensor("nmt", [128, 1 + TLOC], F32))
            dnt = lr.enter_context(nc.sbuf_tensor("dnt", [128, 1 + TLOC], F32))
            ones2k = lr.enter_context(nc.sbuf_tensor("ones2k", [128, TLOC], BF16))
            tmp_d = lr.enter_context(nc.sbuf_tensor("tmp_d", [128, TLOC], F32))
            iota_t = lr.enter_context(nc.sbuf_tensor("iota_t", [128, TLOC], F32))
            p.inc(g.iota(iota_t[:], pattern=[[1, TLOC]], base=1, channel_multiplier=0,
                         allow_small_or_imprecise_dtypes=True), "gp")
            p.wait(s, "gp")
            p.inc(v.memset(ones2k[:], 1.0), "vx")
            for gi in range(DG):
                p.wait(g, "vx")
                p.dma(g, kgt[:], rwkv_spill.ap()[0, gi], "dgp")
                p.wait(v, "dgp")
                p.inc(v.reduce_max(kmx[:, gi:gi + 1], kgt[:], axis=AX.X), "vx")
            p.wait(sy, "vx")
            p.dma(sy, e3_in.ap(), kmx[:], "dsy")
            p.wait(g, "dsy")
            p.inc(g.collective_compute("AllGather", ALU.bypass,
                                       replica_groups=[list(range(8))],
                                       ins=[e3_in.ap()], outs=[e3_out.ap()]), "cc")
            p.wait(g, "cc")
            p.dma(g, e3g[:], e3_out.ap().rearrange("j p g -> p j g"), "dgp")
            p.wait(v, "dgp")
            v.memset(kmx[:], -1e38)
            for j in range(8):
                ia = v.scalar_tensor_tensor(kmx[:], e3g[:, j, :], selc_pen[:, j:j + 1],
                                            kmx[:], op0=ALU.add, op1=ALU.max)
            p.inc(ia, "vx")
            p.inc(v.tensor_scalar_mul(kmx[:], kmx[:], -1.0), "vx")
            for gi in range(DG):
                p.wait(g, "vx"); p.wait(g, "sc")
                p.dma(g, kgt[:], rwkv_spill.ap()[0, gi], "dgp")
                p.dma(g, vgt[:], rwkv_spill.ap()[1, gi], "dgp")
                p.wait(s, "dgp"); p.wait(s, "vx")
                p.inc(s.activation(tmp_b[:, 0:TLOC], kgt[:], AF.Exp,
                                   bias=kmx[:, gi:gi + 1]), "sc")
                p.wait(v, "sc"); p.wait(v, "dgp")
                p.inc(v.tensor_mul(ekv_t[:], tmp_b[:, 0:TLOC], vgt[:]), "vx")
                v.tensor_scalar_mul(tmp_a[:, 0:TLOC], ones2k[:],
                                    wlane[:, gi:gi + 1, 1].rearrange("p g -> p g"))
                p.inc(v.tensor_tensor_scan(nmt[:, 1:], tmp_a[:, 0:TLOC], ekv_t[:],
                                           0.0, ALU.mult, ALU.add), "vx")
                p.inc(v.tensor_tensor_scan(dnt[:, 1:], tmp_a[:, 0:TLOC],
                                           tmp_b[:, 0:TLOC], 0.0, ALU.mult, ALU.add), "vx")
                v.tensor_copy(aend[:, 0, gi:gi + 1], nmt[:, TLOC:TLOC + 1])
                p.inc(v.tensor_copy(aend[:, 1, gi:gi + 1], dnt[:, TLOC:TLOC + 1]), "vx")
            p.wait(sy, "vx")
            p.dma(sy, e4_in.ap(), aend[:], "dsy")
            p.wait(g, "dsy")
            p.inc(g.collective_compute("AllGather", ALU.bypass,
                                       replica_groups=[list(range(8))],
                                       ins=[e4_in.ap()], outs=[e4_out.ap()]), "cc")
            p.wait(g, "cc")
            p.dma(g, e4g[:], e4_out.ap().rearrange("j p c g -> p j c g"), "dgp")
            p.wait(v, "dgp")
            v.memset(ainb[:], 0.0)
            for j in range(8):
                if j in (0, 4):
                    v.tensor_copy(accf[:], e4g[:, j, :, :])
                else:
                    for comp in range(2):
                        v.tensor_mul(accf[:, comp, :], accf[:, comp, :],
                                     wlane[:, :, 2])
                        v.tensor_add(accf[:, comp, :], accf[:, comp, :],
                                     e4g[:, j, comp, :])
                ia = None
                for comp in range(2):
                    ia = v.scalar_tensor_tensor(ainb[:, comp, :], accf[:, comp, :],
                                                selp[:, j:j + 1], ainb[:, comp, :],
                                                op0=ALU.mult, op1=ALU.add)
            p.inc(ia, "vx")
            for gi in range(DG):
                p.wait(g, "vx"); p.wait(g, "sc")
                p.dma(g, kgt[:], rwkv_spill.ap()[0, gi], "dgp")
                p.dma(g, vgt[:], rwkv_spill.ap()[1, gi], "dgp")
                p.dma(g, sgt[:], rwkv_spill.ap()[2, gi], "dgp")
                p.wait(s, "dgp"); p.wait(s, "vx")
                p.inc(s.activation(tmp_b[:, 0:TLOC], kgt[:], AF.Exp,
                                   bias=kmx[:, gi:gi + 1]), "sc")
                p.inc(s.activation(tmp_d[:, 0:TLOC], iota_t[:], AF.Exp,
                                   scale=wlane[:, gi:gi + 1, 0]
                                   .rearrange("p g -> p g")), "sc")
                p.wait(v, "sc"); p.wait(v, "dgp")
                p.inc(v.tensor_mul(ekv_t[:], tmp_b[:, 0:TLOC], vgt[:]), "vx")
                v.tensor_scalar_mul(tmp_a[:, 0:TLOC], ones2k[:],
                                    wlane[:, gi:gi + 1, 1].rearrange("p g -> p g"))
                p.inc(v.tensor_tensor_scan(nmt[:, 1:], tmp_a[:, 0:TLOC], ekv_t[:],
                                           0.0, ALU.mult, ALU.add), "vx")
                p.inc(v.tensor_tensor_scan(dnt[:, 1:], tmp_a[:, 0:TLOC],
                                           tmp_b[:, 0:TLOC], 0.0, ALU.mult, ALU.add), "vx")
                p.wait(v, "vx")
                p.inc(v.scalar_tensor_tensor(nmt[:, 1:], tmp_d[:, 0:TLOC],
                                             ainb[:, 0, gi:gi + 1], nmt[:, 1:],
                                             op0=ALU.mult, op1=ALU.add), "vx")
                p.inc(v.scalar_tensor_tensor(dnt[:, 1:], tmp_d[:, 0:TLOC],
                                             ainb[:, 1, gi:gi + 1], dnt[:, 1:],
                                             op0=ALU.mult, op1=ALU.add), "vx")
                v.tensor_copy(nmt[:, 0:1], ainb[:, 0, gi:gi + 1])
                p.inc(v.tensor_copy(dnt[:, 0:1], ainb[:, 1, gi:gi + 1]), "vx")
                p.inc(v.tensor_scalar_mul(ekv_t[:], ekv_t[:],
                                          wlane[:, gi:gi + 1, 3]
                                          .rearrange("p g -> p g")), "vx")
                p.inc(v.tensor_scalar_mul(tmp_b[:, 0:TLOC], tmp_b[:, 0:TLOC],
                                          wlane[:, gi:gi + 1, 3]
                                          .rearrange("p g -> p g")), "vx")
                p.inc(v.tensor_add(ekv_t[:], ekv_t[:], nmt[:, 0:TLOC]), "vx")
                p.inc(v.tensor_add(tmp_b[:, 0:TLOC], tmp_b[:, 0:TLOC],
                                   dnt[:, 0:TLOC]), "vx")
                p.wait(s, "vx")
                s.activation(tmp_d[:, 0:TLOC], tmp_b[:, 0:TLOC], AF.Ln)
                p.inc(s.activation(tmp_d[:, 0:TLOC], tmp_d[:, 0:TLOC], AF.Exp,
                                   scale=-1.0), "sc")
                p.wait(v, "sc")
                p.inc(v.tensor_mul(ekv_t[:], ekv_t[:], tmp_d[:, 0:TLOC]), "vx")
                p.inc(v.tensor_mul(ywo[:, gi, :], ekv_t[:], sgt[:]), "vx")
            def evac_wo1(oc, tqi, bank):
                q0 = tqi * 512
                p.wait(g, "vx")
                p.dma(g, tmp_d[:, 0:512], x1_spill.ap()[:, oc, q0:q0 + 512], "dgp")
                p.wait(v, "mm"); p.wait(v, "dgp"); p.wait(v, "dst")
                v.tensor_add(tmp_a[:, 0:512], bank[:, :], ffn1_out[:, oc, q0:q0 + 512])
                cnt = p.inc(v.tensor_add(tmp_a[:, 0:512], tmp_a[:, 0:512],
                                         tmp_d[:, 0:512]), "vx")
                p.wait(sy, "vx")
                p.dma(sy, out_ap[:, oc, q0:q0 + 512], tmp_a[:, 0:512], "dst")
                return ("vx", cnt)

            gemm(lambda oc: dr["woT"][oc], DG, DG, wrk,
                 lambda kg, tqi: ywo[:, kg, tqi * 512:(tqi + 1) * 512], evac_wo1,
                 4, lambda oc, tqi: 4 + (oc * 4 + tqi) % 4, pre_wait=("vx",))
            if upto == "fulldbg":
                d_x1 = nc.dram_tensor("d_x1", [128, DG, TLOC], F32, kind="ExternalOutput").ap()
                d_ffn1 = nc.dram_tensor("d_ffn1", [128, DG, TLOC], BF16, kind="ExternalOutput").ap()
                d_ywo = nc.dram_tensor("d_ywo", [128, DG, TLOC], BF16, kind="ExternalOutput").ap()
                d_kvs = nc.dram_tensor("d_kvs", [3, DG, 128, TLOC], BF16, kind="ExternalOutput").ap()
                d_small = nc.dram_tensor("d_small", [128, 80], F32, kind="ExternalOutput").ap()
                d_xn1 = nc.dram_tensor("d_xn1", [128, DG, TLOC], BF16, kind="ExternalOutput").ap()
                p.dma(g, d_x1, x1_spill.ap(), "dgp")
                p.dma(g, d_kvs, rwkv_spill.ap(), "dgp")
                p.dma(g, d_xn1, xn1_spill.ap() if False else xn1[:], "dgp")
                p.wait(sy, "vx")
                p.dma(sy, d_ffn1, ffn1_out[:], "dsy")
                p.dma(sy, d_ywo, ywo[:], "dsy")
                p.dma(sy, d_small[:, 0:8], kmx[:], "dsy")
                p.dma(sy, d_small[:, 8:24], ainb[:].rearrange("p c g -> p (c g)"), "dsy")
                p.dma(sy, d_small[:, 24:40], aend[:].rearrange("p c g -> p (c g)"), "dsy")
                p.dma(sy, d_small[:, 40:42], thrc[:], "dsy")
                p.dma(sy, d_small[:, 42:50], selp[:], "dsy")
                p.dma(sy, d_small[:, 50:58], selc_pen[:], "dsy")
                p.dma(sy, d_small[:1, 64:72], selrow[:, 0:8], "dsy")
                p.dma(sy, d_small[:1, 72:80], selrow[:, 8:16], "dsy")
                p.dma(sy, d_small[:, 58:64], gq_t[:, 14:20], "dsy")
                d_e3g = nc.dram_tensor("d_e3g", [128, 8, DG], F32, kind="ExternalOutput").ap()
                p.dma(sy, d_e3g, e3g[:], "dsy")
                p.wait(sy, "dsy"); p.wait(g, "dgp")
            p.wait(sy, "dst")
    return p, dr, out_ap


def _band(w_T, n_oc, n_kg):
    K, M = w_T.shape
    assert K == n_kg * 128 and M == n_oc * 128
    return np.ascontiguousarray(
        w_T.reshape(n_kg, 128, n_oc, 128).transpose(2, 1, 0, 3))


def _prep_core_inputs(inputs, core):
    t0 = (core % 4) * TLOC
    b = core // 4
    r = {}

    def fm(x2d):
        f = x2d.shape[1]
        return np.ascontiguousarray(
            x2d.T.reshape(f // 128, 128, x2d.shape[0]).transpose(1, 0, 2))

    def pcol(vec, ng):
        return np.ascontiguousarray(np.asarray(vec).reshape(ng, 128).T)

    A = lambda k: np.asarray(inputs[k])
    r["x0T"] = fm(A("L0_attn_x")[b, t0:t0 + TLOC, :])
    r["l0_gamma"] = pcol(A("L0_attn_gamma"), DG)
    r["l0_beta"] = pcol(A("L0_attn_beta"), DG)
    w_in = A("L0_attn_W_in")
    r["w_in_u"] = _band(w_in[:, :DIN], DIN // 128, DG)
    r["w_in_z"] = _band(w_in[:, DIN:], DIN // 128, DG)
    chb = (core % 4) * CH
    cw = A("L0_attn_conv_w")[:, 0, :]
    r["conv_w"] = np.ascontiguousarray(
        cw[chb:chb + CH].reshape(CGN, 128, 4).transpose(1, 0, 2))
    r["conv_b"] = pcol(A("L0_attn_conv_b")[chb:chb + CH], CGN)
    r["w_xproj"] = np.ascontiguousarray(
        A("L0_attn_W_xproj")[chb:chb + CH].reshape(CGN, 128, 96).transpose(1, 0, 2))
    r["w_dt"] = np.ascontiguousarray(A("L0_attn_W_dt")[:, chb:chb + CH])
    r["b_dt"] = pcol(A("L0_attn_b_dt")[chb:chb + CH], CGN)
    r["a_log"] = np.ascontiguousarray(
        A("L0_attn_A_log")[chb:chb + CH].reshape(CGN, 128, DSTT).transpose(1, 0, 2))
    r["d_vec"] = pcol(A("L0_attn_D")[chb:chb + CH], CGN)
    r["w_out"] = _band(A("L0_attn_W_out"), DG, DIN // 128)
    r["w1T"] = _band(A("L0_ffn_w1_w").T, HG, DG)
    r["w2T"] = _band(A("L0_ffn_w2_w").T, HG, DG)
    r["w1s"] = pcol(A("L0_ffn_w1_s"), HG)
    r["w2s"] = pcol(A("L0_ffn_w2_s"), HG)
    r["h1T"] = _band(A("L0_ffn_h1_w").T, HG, HG)
    r["h2T"] = _band(A("L0_ffn_h2_w").T, HG, HG)
    r["h1s"] = pcol(A("L0_ffn_h1_s"), HG)
    r["h2s"] = pcol(A("L0_ffn_h2_s"), HG)
    r["w3T"] = _band(A("L0_ffn_w3_w").T, DG, HG)
    r["w3s"] = pcol(A("L0_ffn_w3_s"), DG)
    tv = np.stack([A("L1_attn_time_decay").reshape(D),
                   A("L1_attn_time_first").reshape(D),
                   A("L1_attn_time_mix_k").reshape(D),
                   A("L1_attn_time_mix_v").reshape(D),
                   A("L1_attn_time_mix_r").reshape(D)], axis=1)
    r["tvec"] = np.ascontiguousarray(tv.reshape(DG, 128, 5).transpose(1, 0, 2))
    r["wkT"] = _band(A("L1_attn_Wk").T, DG, DG)
    r["wvT"] = _band(A("L1_attn_Wv").T, DG, DG)
    r["wrT"] = _band(A("L1_attn_Wr").T, DG, DG)
    r["woT"] = _band(A("L1_attn_Wo").T, DG, DG)
    bfc = _band(A("L1_ffn_W_fc").T, HG, DG)
    r["wfcT"] = np.ascontiguousarray(
        bfc.reshape(DG, 4, 128, DG, 128).transpose(0, 2, 1, 3, 4)
        .reshape(DG, 128, HF))
    r["lnw"] = pcol(A("L1_ffn_ln_w"), HG)
    r["lnb"] = pcol(A("L1_ffn_ln_b"), HG)
    bpr = _band(A("L1_ffn_W_proj").T, DG, HG)
    r["wprT"] = np.ascontiguousarray(bpr.reshape(DG, 128, HG * 128))
    sp = np.zeros((1, 8), np.float32)
    if core % 4 != 0:
        sp[0, core - 1] = 1.0
    r["selprev"] = sp
    scv = np.zeros((1, 8), np.float32)
    for j in (range(4) if core < 4 else range(4, 8)):
        scv[0, j] = 1.0
    r["selchain"] = scv
    r["selpen"] = np.where(scv > 0, 0.0, -1e38).astype(np.float32)
    r["dynoff"] = np.array([[chb * TLOC, t0, 0, 0]], np.int32)
    import ml_dtypes
    oh = np.zeros((DSTT, DSTT, 128), np.float32)
    for si_ in range(DSTT):
        oh[si_, si_, :] = 1.0
    r["onehot16"] = oh.astype(ml_dtypes.bfloat16)
    for k in list(r):
        if r[k].dtype == np.float64:
            r[k] = r[k].astype(np.float32)
    return r


def kernel(**inputs):
    prog, dr, out_ap = build_program("full")
    maps = [_prep_core_inputs(inputs, c) for c in range(8)]
    res = run_bass_kernel_spmd(prog.nc, maps, core_ids=list(range(8)))
    out = np.empty((B, T, D), np.float32)
    for c in range(8):
        b, t0 = c // 4, (c % 4) * TLOC
        o = res.results[c]["outT"]
        out[b, t0:t0 + TLOC, :] = o.transpose(2, 1, 0).reshape(TLOC, D)
    return out


# version 07
# speedup vs baseline: 1.2328x; 1.0895x over previous
import contextlib

import numpy as np

import concourse.bass as bass
import concourse.mybir as mybir
from concourse.bass_utils import run_bass_kernel_spmd

F32 = mybir.dt.float32
BF16 = mybir.dt.bfloat16
I32 = mybir.dt.int32
AF = mybir.ActivationFunctionType
ALU = mybir.AluOpType
AX = mybir.AxisListType

B, T, D = 2, 8192, 1024
TLOC = 2048
DG = D // 128
DIN = 2048
DSTT = 16
DTR = 64
CH = DIN // 4
CGN = CH // 128
HF = 4096
HG = HF // 128
LN_EPS = 1e-5
ABS_EPS = 1e-6
TBS = 1024
NTB = T // TBS


class Prog:
    def __init__(self):
        self.nc = bass.Bass("TRN2", target_bir_lowering=False, debug=False)
        self.stack = contextlib.ExitStack()
        self._sems = {}

    def sem(self, name):
        if name not in self._sems:
            h = self.stack.enter_context(self.nc.semaphore(name))
            self._sems[name] = [h, 0]
        return self._sems[name][0]

    def count(self, name):
        self.sem(name)
        return self._sems[name][1]

    def inc(self, instr, name, n=1):
        self.sem(name)
        rec = self._sems[name]
        instr.then_inc(rec[0], n)
        rec[1] += n
        return rec[1]

    def dma(self, eng, out, in_, name):
        return self.inc(eng.dma_start(out=out, in_=in_), name, 16)

    def wait(self, eng, name, at=None):
        self.sem(name)
        rec = self._sems[name]
        val = rec[1] if at is None else at
        if val > 0:
            eng.wait_ge(rec[0], val)


def build_program(upto="full"):
    p = Prog()
    nc = p.nc
    v, s, t, g, sy = nc.vector, nc.scalar, nc.tensor, nc.gpsimd, nc.sync
    sb = p.stack.enter_context

    dr = {}

    def din(name, shape, dtype=F32):
        dr[name] = nc.dram_tensor(name, list(shape), dtype, kind="ExternalInput").ap()
        return dr[name]

    din("x0T", [128, DG, TLOC])
    out_ap = nc.dram_tensor("outT", [128, DG, TLOC], F32, kind="ExternalOutput").ap()

    din("l0_gamma", [128, DG]); din("l0_beta", [128, DG])
    din("w_in_u", [DIN // 128, 128, DG, 128])
    din("w_in_z", [DIN // 128, 128, DG, 128])
    din("conv_w", [128, CGN, 4]); din("conv_b", [128, CGN])
    din("w_xproj", [128, CGN, 96])
    din("w_dt", [DTR, CH])
    din("b_dt", [128, CGN])
    din("a_log", [128, CGN, DSTT])
    din("d_vec", [128, CGN])
    din("w_out", [DG, 128, DIN // 128, 128])
    din("w1T", [HG, 128, DG, 128]); din("w2T", [HG, 128, DG, 128])
    din("w1s", [128, HG]); din("w2s", [128, HG])
    din("h1T", [HG, 128, HG, 128]); din("h2T", [HG, 128, HG, 128])
    din("h1s", [128, HG]); din("h2s", [128, HG])
    din("w3T", [DG, 128, HG, 128]); din("w3s", [128, DG])
    din("tvec", [128, DG, 5])
    din("wkT", [DG, 128, DG, 128]); din("wvT", [DG, 128, DG, 128])
    din("wrT", [DG, 128, DG, 128]); din("woT", [DG, 128, DG, 128])
    din("wfcT", [DG, 128, DG, 128, 4] if False else [DG, 128, HF])
    din("lnw", [128, HG]); din("lnb", [128, HG])
    din("wprT", [DG, 128, HG * 128])
    din("selprev", [1, 8]); din("selchain", [1, 8]); din("selpen", [1, 8])
    din("onehot16", [DSTT, DSTT, 128], BF16)
    din("dynoff", [1, 4], I32)

    ag1_in = nc.dram_tensor("ag1_in", [DIN, TLOC], BF16)
    ag1_out = nc.dram_tensor("ag1_out", [4, DIN, TLOC], BF16)
    ar_in = nc.dram_tensor("ar_in", [96, T], F32)
    ar_out = nc.dram_tensor("ar_out", [96, T], F32)
    ag2_in = nc.dram_tensor("ag2_in", [CH, T], BF16)
    ag2_out = nc.dram_tensor("ag2_out", [DIN, T], BF16)
    u_spill = nc.dram_tensor("u_spill", [CH, T], BF16)
    x1_spill = nc.dram_tensor("x1_spill", [128, DG, TLOC], F32)
    xn2_spill = nc.dram_tensor("xn2_spill", [128, DG, TLOC], BF16)
    xn_sp = nc.dram_tensor("xn_sp", [128, DG, TLOC], BF16)
    ffn_spill = nc.dram_tensor("ffn_spill", [128, DG, TLOC], BF16)
    rwkv_spill = nc.dram_tensor("rwkv_spill", [3, DG, 128, TLOC], BF16)
    wq_spill = nc.dram_tensor("wq_spill", [HG, 128, DG, 128], BF16)
    wqp_spill = nc.dram_tensor("wqp_spill", [DG, 128, HG, 128], BF16)
    e2_out = nc.dram_tensor("e2_out", [8, 128, DG], BF16)
    e2_in = nc.dram_tensor("e2_in", [128, DG], BF16)
    e3_out = nc.dram_tensor("e3_out", [8, 128, DG], F32)
    e3_in = nc.dram_tensor("e3_in", [128, DG], F32)
    e4_out = nc.dram_tensor("e4_out", [8, 128, 2, DG], F32)
    e4_in = nc.dram_tensor("e4_in", [128, 2, DG], F32)

    gamma_t = sb(nc.sbuf_tensor("gamma_t", [128, DG], F32))
    beta_t = sb(nc.sbuf_tensor("beta_t", [128, DG], F32))
    convb_t = sb(nc.sbuf_tensor("convb_t", [128, CGN], F32))
    negbdt_t = sb(nc.sbuf_tensor("negbdt_t", [128, CGN], F32))
    dvec_t = sb(nc.sbuf_tensor("dvec_t", [128, CGN], F32))
    convw_t = sb(nc.sbuf_tensor("convw_t", [128, CGN, 4], F32))
    e_a = sb(nc.sbuf_tensor("e_a", [128, CGN, DSTT], F32))
    wxp_t = sb(nc.sbuf_tensor("wxp_t", [128, CGN, 96], BF16))
    wdt_t = sb(nc.sbuf_tensor("wdt_t", [DTR, CH], BF16))
    tvec_t = sb(nc.sbuf_tensor("tvec_t", [128, DG, 5], F32))
    wlane = sb(nc.sbuf_tensor("wlane", [128, DG, 5], F32))
    omix = sb(nc.sbuf_tensor("omix", [128, DG, 3], F32))
    w1s_t = sb(nc.sbuf_tensor("w1s_t", [128, HG], F32))
    w2s_t = sb(nc.sbuf_tensor("w2s_t", [128, HG], F32))
    h1s_t = sb(nc.sbuf_tensor("h1s_t", [128, HG], F32))
    h2s_t = sb(nc.sbuf_tensor("h2s_t", [128, HG], F32))
    w3s_t = sb(nc.sbuf_tensor("w3s_t", [128, DG], F32))
    lnw_t = sb(nc.sbuf_tensor("lnw_t", [128, HG], F32))
    lnb_t = sb(nc.sbuf_tensor("lnb_t", [128, HG], F32))
    selp = sb(nc.sbuf_tensor("selp", [128, 8], F32))
    selc_pen = sb(nc.sbuf_tensor("selc_pen", [128, 8], F32))
    selrow = sb(nc.sbuf_tensor("selrow", [1, 24], F32))
    ones_col = sb(nc.sbuf_tensor("ones_col", [128, 1], BF16))
    ones_row = sb(nc.sbuf_tensor("ones_row", [1, 128], BF16))
    ones_row_f = sb(nc.sbuf_tensor("ones_row_f", [1, 128], F32))
    offt = sb(nc.sbuf_tensor("offt", [1, 4], I32))
    stat = sb(nc.sbuf_tensor("stat", [128, 32], F32))
    srow = sb(nc.sbuf_tensor("srow", [1, 4, 512], F32))
    stg_x = sb(nc.sbuf_tensor("stg_x", [128, 512], BF16))
    stg_q = sb(nc.sbuf_tensor("stg_q", [128, 512], BF16))
    stg_y = sb(nc.sbuf_tensor("stg_y", [128, 512], BF16))
    epsc = sb(nc.sbuf_tensor("epsc", [1, 2], F32))
    tmp_a = sb(nc.sbuf_tensor("tmp_a", [128, TLOC], F32))
    tmp_c = sb(nc.sbuf_tensor("tmp_c", [128, TLOC], BF16))

    xn_scope = contextlib.ExitStack()
    xn = xn_scope.enter_context(nc.sbuf_tensor("xn", [128, DG, TLOC], BF16))
    pb = [sb(nc.psum_tensor(f"pb{i}", [128, 512], F32)) for i in range(8)]
    roff = sb(g.register("roff"))
    rtmp = sb(g.register("rtmp"))
    sroff = sb(sy.register("sroff"))
    srtmp = sb(sy.register("srtmp"))

    uid = [0]

    def gemm_gen(wdram, n_oc, n_kg, wtile, rhs_fn, evac_fn, n_tq, bank_of,
                 pre_wait=(), mmtag="mm"):
        uid[0] += 1
        ldsem = [f"gw{uid[0]}a", f"gw{uid[0]}b"]
        for nm in pre_wait:
            p.wait(t, nm)
        m0 = p.count(mmtag)
        last_use = [m0, m0]
        bank_free = {}
        for oc in range(n_oc):
            par = oc % 2
            p.wait(g, mmtag, at=last_use[par])
            ldc = p.dma(g, wtile[:, par, 0:n_kg * 128], wdram(oc), ldsem[par])
            for tqi in range(n_tq):
                bk = bank_of(oc, tqi)
                if bk in bank_free:
                    nm, cnt = bank_free[bk]
                    p.wait(t, nm, at=cnt)
                p.wait(t, ldsem[par], at=ldc)
                for kg in range(n_kg):
                    i1 = t.matmul(pb[bk][:, :], wtile[:, par, kg * 128:(kg + 1) * 128],
                                  rhs_fn(kg, tqi), start=(kg == 0), stop=(kg == n_kg - 1))
                mmc = p.inc(i1, mmtag)
                last_use[par] = mmc
                bank_free[bk] = evac_fn(oc, tqi, pb[bk])
            yield

    def gemm(*args, **kw):
        for _ in gemm_gen(*args, **kw):
            pass

    def run_interleaved(gens):
        active = list(gens)
        while active:
            keep = []
            for gen, w in active:
                alive = True
                for _ in range(w):
                    try:
                        next(gen)
                    except StopIteration:
                        alive = False
                        break
                if alive:
                    keep.append((gen, w))
            active = keep

    def phase_barrier(sems=("dsy", "dgp", "vx", "sc", "mm", "gp", "cc",
                            "dsyM", "dgpM", "vxM", "scM", "mmM", "crM",
                            "dsyF", "vxF", "scF", "mmF")):
        for nm in sems:
            for eng in (v, s, t, g, sy):
                p.wait(eng, nm)

    def ln_emit(src, n_g, dst_bf, eps, plain_cb=None, gamma_ap=None, beta_ap=None,
                tcols=TLOC, toff=0):
        inv_n = 1.0 / (n_g * 128)
        p.wait(s, "vx")
        ps_x, ps_q, ps_r, ps_m = pb[4], pb[5], pb[6], pb[7]
        for blk in range(tcols // 512):
            t0 = toff + blk * 512
            for gi in range(n_g):
                p.wait(v, "mm"); p.wait(s, "mm")
                p.inc(v.tensor_copy(stg_x[:], src[:, gi, t0:t0 + 512]), "vx")
                p.inc(s.activation(stg_q[:], src[:, gi, t0:t0 + 512], AF.Square), "sc")
                p.wait(t, "vx"); p.wait(t, "sc")
                last = gi == n_g - 1
                t.matmul(ps_x[:1, :], ones_col[:], stg_x[:], start=(gi == 0), stop=last)
                i2 = t.matmul(ps_q[:1, :], ones_col[:], stg_q[:], start=(gi == 0), stop=last)
                p.inc(i2, "mm")
            p.wait(v, "mm")
            p.inc(v.tensor_scalar_mul(srow[:1, 0, :], ps_x[:1, :], inv_n), "vx")
            v.tensor_mul(srow[:1, 1, :], srow[:1, 0, :], srow[:1, 0, :])
            p.inc(v.scalar_tensor_tensor(srow[:1, 1, :], ps_q[:1, :], inv_n,
                                         srow[:1, 1, :], op0=ALU.mult,
                                         op1=ALU.subtract), "vx")
            p.wait(s, "vx")
            s.activation(srow[:1, 2, :], srow[:1, 1, :], AF.Ln, bias=epsc[:1, 0:1])
            p.inc(s.activation(srow[:1, 2, :], srow[:1, 2, :], AF.Exp, scale=-0.5), "sc")
            p.wait(v, "sc")
            v.tensor_mul(srow[:1, 3, :], srow[:1, 0, :], srow[:1, 2, :])
            p.inc(v.tensor_scalar_mul(srow[:1, 3, :], srow[:1, 3, :], -1.0), "vx")
            p.wait(t, "vx")
            t.matmul(ps_r[:, :], ones_row_f[:], srow[:1, 2, :], start=True, stop=True)
            p.inc(t.matmul(ps_m[:, :], ones_row_f[:], srow[:1, 3, :], start=True,
                           stop=True), "mm")
            p.wait(v, "mm")
            for gi in range(n_g):
                v.tensor_mul(tmp_a[:, 0:512], src[:, gi, t0:t0 + 512], ps_r[:, :])
                ia = v.tensor_add(tmp_a[:, 0:512], tmp_a[:, 0:512], ps_m[:, :])
                if gamma_ap is not None:
                    ia = v.tensor_scalar(dst_bf[:, gi, t0:t0 + 512], tmp_a[:, 0:512],
                                         gamma_ap[:, gi:gi + 1], beta_ap[:, gi:gi + 1],
                                         op0=ALU.mult, op1=ALU.add)
                else:
                    ia = v.tensor_copy(dst_bf[:, gi, t0:t0 + 512], tmp_a[:, 0:512])
                p.inc(ia, "vx")
                if plain_cb is not None:
                    plain_cb(gi, t0)

    def chain_select(out_col_fn, gath_fn, n_items):
        for it in range(n_items):
            v.memset(tmp_a[:, 0:DG], 0.0)
            for j in range(8):
                ia = v.scalar_tensor_tensor(tmp_a[:, 0:DG], gath_fn(it, j),
                                            selp[:, j:j + 1], tmp_a[:, 0:DG],
                                            op0=ALU.mult, op1=ALU.add)
            p.inc(v.tensor_copy(out_col_fn(it), tmp_a[:, 0:DG]), "vx")

    with contextlib.ExitStack() as ph01:
        x1 = ph01.enter_context(nc.sbuf_tensor("x1", [128, DG, TLOC], F32))
        xn2 = ph01.enter_context(nc.sbuf_tensor("xn2", [128, DG, TLOC], BF16))
        wz = ph01.enter_context(nc.sbuf_tensor("wz", [128, 2, DG * 128], BF16))

        p.dma(sy, x1[:], dr["x0T"], "dsy")
        for nm, tl in [("l0_gamma", gamma_t), ("l0_beta", beta_t), ("conv_b", convb_t),
                       ("d_vec", dvec_t), ("conv_w", convw_t), ("a_log", e_a),
                       ("tvec", tvec_t), ("w1s", w1s_t), ("w2s", w2s_t),
                       ("h1s", h1s_t), ("h2s", h2s_t), ("w3s", w3s_t),
                       ("lnw", lnw_t), ("lnb", lnb_t)]:
            p.dma(sy, tl[:], dr[nm], "dsy")
        p.dma(sy, stat[:, 0:CGN], dr["b_dt"], "dsy")
        p.dma(sy, selrow[:, 0:8], dr["selprev"], "dsy")
        p.dma(sy, selrow[:, 8:16], dr["selchain"], "dsy")
        p.dma(sy, selrow[:, 16:24], dr["selpen"], "dsy")
        p.dma(g, wxp_t[:], dr["w_xproj"], "dgp")
        p.dma(g, wdt_t[:], dr["w_dt"], "dgp")
        p.dma(g, offt[:], dr["dynoff"], "dgp")

        p.wait(v, "dsy")
        v.memset(epsc[:], LN_EPS)
        v.memset(ones_col[:], 1.0)
        v.memset(ones_row[:], 1.0)
        v.memset(ones_row_f[:], 1.0)
        p.inc(v.tensor_scalar_mul(negbdt_t[:], stat[:, 0:CGN], -1.0), "vx")
        for j in range(3):
            p.inc(v.tensor_scalar(omix[:, :, j], tvec_t[:, :, 2 + j], -1.0, 1.0,
                                  op0=ALU.mult, op1=ALU.add), "vx")
        p.wait(s, "dsy")
        p.inc(s.activation(e_a[:], e_a[:], AF.Exp), "sc")
        p.inc(s.activation(wlane[:, :, 0], tvec_t[:, :, 0], AF.Exp), "sc")
        p.wait(v, "sc")
        p.inc(v.tensor_scalar_mul(wlane[:, :, 0], wlane[:, :, 0], -1.0), "vx")
        p.wait(s, "vx")
        p.inc(s.activation(wlane[:, :, 1], wlane[:, :, 0], AF.Exp), "sc")
        p.inc(s.activation(wlane[:, :, 2], wlane[:, :, 0], AF.Exp, scale=float(TLOC)), "sc")
        p.inc(s.activation(wlane[:, :, 3], tvec_t[:, :, 1], AF.Exp), "sc")
        p.wait(t, "dsy"); p.wait(t, "vx")
        p.inc(t.matmul(pb[0][:, 0:24], ones_row_f[:], selrow[:1, :], start=True,
                       stop=True), "mm")
        p.wait(v, "mm")
        p.inc(v.tensor_copy(selp[:], pb[0][:, 0:8]), "vx")
        p.inc(v.tensor_copy(selc_pen[:], pb[0][:, 16:24]), "vx")

        def add_xn(gi, t0):
            p.inc(v.tensor_add(x1[:, gi, t0:t0 + 512], x1[:, gi, t0:t0 + 512],
                               tmp_a[:, 0:512]), "vx")

        ln_emit(x1, DG, xn, LN_EPS, plain_cb=add_xn)
        ln_emit(xn, DG, xn2, LN_EPS, gamma_ap=gamma_t, beta_ap=beta_t)

        def evac_u(oc, tqi, bank):
            p.wait(s, "mm")
            p.wait(s, "dsy")
            cnt = p.inc(s.copy(stg_y[:], bank[:, :]), "sc")
            p.wait(sy, "sc")
            p.dma(sy, ag1_in.ap()[oc * 128:(oc + 1) * 128, tqi * 512:(tqi + 1) * 512],
                  stg_y[:], "dsy")
            return ("sc", cnt)

        gemm(lambda oc: dr["w_in_u"][oc], DIN // 128, DG, wz,
             lambda kg, tqi: xn2[:, kg, tqi * 512:(tqi + 1) * 512],
             evac_u, 4, lambda oc, tqi: (oc * 4 + tqi) % 4, pre_wait=("vx",))

        p.wait(g, "dsy")
        p.inc(g.collective_compute("AllGather", ALU.bypass,
                                   replica_groups=[[0, 1, 2, 3], [4, 5, 6, 7]],
                                   ins=[ag1_in.ap()], outs=[ag1_out.ap()]), "cc")
        p.wait(sy, "vx")
        p.dma(sy, x1_spill.ap(), x1[:], "dsy")
        p.dma(sy, xn2_spill.ap(), xn2[:], "dsy")
        p.dma(sy, xn_sp.ap(), xn[:], "dsy")
        p.wait(g, "dsy")

    xn_scope.close()
    with contextlib.ExitStack() as ph2:
        phase_barrier(("dsy", "dgp", "vx", "sc", "mm", "gp"))
        xdbl_tb = ph2.enter_context(nc.sbuf_tensor("xdbl_tb", [96, TBS], F32))
        up_h = ph2.enter_context(nc.sbuf_tensor("up_h", [128, CGN, 3 + TBS], BF16))
        u_ch = ph2.enter_context(nc.sbuf_tensor("u_ch", [128, CGN, TBS], BF16))
        dl_ch = ph2.enter_context(nc.sbuf_tensor("dl_ch", [128, CGN, TBS], BF16))
        du_ch = ph2.enter_context(nc.sbuf_tensor("du_ch", [128, CGN, TBS], BF16))
        bb_s = ph2.enter_context(nc.sbuf_tensor("bb_s", [128, 2, TBS], BF16))
        cb_s = ph2.enter_context(nc.sbuf_tensor("cb_s", [128, 2, TBS], BF16))
        da_t = ph2.enter_context(nc.sbuf_tensor("da_t", [128, 2, TBS], BF16))
        dbu_t = ph2.enter_context(nc.sbuf_tensor("dbu_t", [128, TBS], BF16))
        h_t = ph2.enter_context(nc.sbuf_tensor("h_t", [128, 2, TBS], BF16))
        prod_t = ph2.enter_context(nc.sbuf_tensor("prod_t", [128, TBS], BF16))
        y_acc = ph2.enter_context(nc.sbuf_tensor("y_acc", [128, CGN, TBS], F32))
        ycp = ph2.enter_context(nc.sbuf_tensor("ycp", [128, 2, TBS], BF16))
        carry = ph2.enter_context(nc.sbuf_tensor("carry", [128, CGN * DSTT], F32))
        bneg_tb = ph2.enter_context(nc.sbuf_tensor("bneg_tb", [DSTT, TBS], BF16))
        c_tb = ph2.enter_context(nc.sbuf_tensor("c_tb", [DSTT, TBS], BF16))
        onehot = ph2.enter_context(nc.sbuf_tensor("onehot", [DSTT, DSTT, 128], BF16))
        dt_bf = ph2.enter_context(nc.sbuf_tensor("dt_bf", [DTR, TBS], BF16))
        xn_q = ph2.enter_context(nc.sbuf_tensor("xn_q", [128, DG, 512], BF16))
        hbuf = ph2.enter_context(nc.sbuf_tensor("hbuf", [128, HG, 512], BF16))
        h2buf = ph2.enter_context(nc.sbuf_tensor("h2buf", [128, HG, 512], BF16))
        wba = ph2.enter_context(nc.sbuf_tensor("wba", [128, 2, HG * 128], BF16))

        p.dma(sy, onehot[:], dr["onehot16"], "dsyM")

        def chan_gen():
            v.memset(up_h[:, :, 0:3], 0.0)
            p.inc(v.memset(carry[:], 0.0), "vxM")
            p.wait(sy, "cc")
            for tb in range(NTB):
                sy.reg_load(sroff, offt[:1, 0:1])
                sy.reg_add(srtmp, sroff, (((tb * TBS) // TLOC) * DIN) * TLOC
                           + (tb * TBS) % TLOC)
                srcu = bass.AP(ag1_out, srtmp,
                               [[TLOC, 128], [128 * TLOC, CGN], [1, TBS]])
                p.wait(sy, "vxM")
                with nc.allow_non_contiguous_dma("AG gather"):
                    p.inc(sy.dma_start(out=up_h[:, :, 3:3 + TBS], in_=srcu,
                                       bounds_check=None), "dsyM", 16)
                p.wait(v, "dsyM")
                p.wait(s, "dsyM"); p.wait(s, "mmM")
                for cg in range(CGN):
                    p.wait(v, "scM")
                    v.tensor_scalar_mul(tmp_a[:, 0:TBS], up_h[:, cg, 0:TBS],
                                        convw_t[:, cg, 0:1])
                    for j in range(1, 4):
                        ic = v.scalar_tensor_tensor(tmp_a[:, 0:TBS],
                                                    up_h[:, cg, j:j + TBS],
                                                    convw_t[:, cg, j:j + 1],
                                                    tmp_a[:, 0:TBS],
                                                    op0=ALU.mult, op1=ALU.add)
                    p.inc(ic, "vxM")
                    p.inc(v.tensor_copy(up_h[:, cg, 0:3], up_h[:, cg, TBS:TBS + 3]),
                          "vxM")
                    p.wait(s, "vxM")
                    p.inc(s.activation(u_ch[:, cg, :], tmp_a[:, 0:TBS], AF.Silu,
                                       bias=convb_t[:, cg:cg + 1]), "scM")
                p.wait(sy, "scM")
                p.dma(sy, u_spill.ap()[:, tb * TBS:(tb + 1) * TBS]
                      .rearrange("(cg q) t -> q cg t", q=128), u_ch[:], "dsyM")
                p.wait(t, "scM"); p.wait(t, "vxM")
                for nb in range(TBS // 512):
                    for cg in range(CGN):
                        i1 = t.matmul(pb[4 + nb][:96, :], wxp_t[:, cg, :],
                                      u_ch[:, cg, nb * 512:(nb + 1) * 512],
                                      start=(cg == 0), stop=(cg == CGN - 1))
                    p.inc(i1, "mmM")
                p.wait(sy, "mmM")
                for nb in range(TBS // 512):
                    pass
                p.wait(v, "mmM"); p.wait(v, "scM")
                for nb in range(TBS // 512):
                    p.wait(v, "dsyM", at=None if nb == 0 else None)
                    p.inc(v.tensor_copy(tmp_a[:96, nb * 512:(nb + 1) * 512],
                                        pb[4 + nb][:96, :]), "vxM")
                p.wait(sy, "vxM")
                p.dma(sy, ar_in.ap()[:, tb * TBS:(tb + 1) * TBS],
                      tmp_a[:96, 0:TBS], "dsyM")
                yield
            p.wait(g, "dsyM")
            p.inc(g.collective_compute("AllReduce", ALU.add,
                                       replica_groups=[[0, 1, 2, 3], [4, 5, 6, 7]],
                                       ins=[ar_in.ap()], outs=[ar_out.ap()]), "cc")
            yield
            p.wait(sy, "cc")
            crm_hist = {}
            scan_hist = {}
            hpar_hist = {}
            for tb in range(NTB):
                t0 = tb * TBS
                p.wait(sy, "vxM"); p.wait(sy, "scM"); p.wait(sy, "mmM")
                p.dma(sy, u_ch[:], u_spill.ap()[:, t0:t0 + TBS]
                      .rearrange("(cg q) t -> q cg t", q=128), "dsyM")
                p.dma(sy, xdbl_tb[:], ar_out.ap()[:, t0:t0 + TBS], "dsyM")
                p.wait(g, "vxM")
                p.dma(g, bneg_tb[:], ar_out.ap()[64:80, t0:t0 + TBS], "dgpM")
                p.dma(g, c_tb[:], ar_out.ap()[80:96, t0:t0 + TBS], "dgpM")
                p.wait(v, "dsyM"); p.wait(v, "dgpM"); p.wait(v, "mmM")
                p.inc(v.tensor_scalar_mul(bneg_tb[:], bneg_tb[:], -1.0), "vxM")
                p.inc(v.tensor_copy(dt_bf[:], xdbl_tb[0:DTR, :]), "vxM")
                p.wait(t, "scM"); p.wait(t, "vxM")
                p.wait(s, "vxM")
                for cg in range(CGN):
                    for nb in range(TBS // 512):
                        i1 = t.matmul(pb[4 + nb][:, :],
                                      wdt_t[:, cg * 128:(cg + 1) * 128],
                                      dt_bf[:, nb * 512:(nb + 1) * 512],
                                      start=True, stop=True)
                        p.inc(i1, "mmM")
                        p.wait(s, "mmM")
                        p.inc(s.activation(dl_ch[:, cg, nb * 512:(nb + 1) * 512],
                                           pb[4 + nb][:, :], AF.Sigmoid, scale=-1.0,
                                           bias=negbdt_t[:, cg:cg + 1]), "scM")
                        p.wait(t, "scM")
                for cg in range(CGN):
                    p.inc(s.activation(dl_ch[:, cg, :], dl_ch[:, cg, :], AF.Ln), "scM")
                p.wait(v, "scM")
                for cg in range(CGN):
                    p.inc(v.tensor_mul(du_ch[:, cg, :], dl_ch[:, cg, :],
                                       u_ch[:, cg, :]), "vxM")
                p.inc(v.memset(y_acc[:], 0.0), "vxM")
                yield
                for si in range(DSTT):
                    sp = si % 2
                    prev = scan_hist.get(("bc_evac", 1 - sp))
                    if prev is not None:
                        p.wait(t, "scM", at=prev)
                    i1 = None
                    for nb in range(TBS // 512):
                        t.matmul(pb[4 + nb][:, :], onehot[:, si, :],
                                 bneg_tb[:, nb * 512:(nb + 1) * 512],
                                 start=True, stop=True)
                        i1 = t.matmul(pb[6 + nb][:, :], onehot[:, si, :],
                                      c_tb[:, nb * 512:(nb + 1) * 512],
                                      start=True, stop=True)
                    p.inc(i1, "mmM")
                    p.wait(s, "mmM")
                    rd = scan_hist.get(("bc_read", sp))
                    if rd is not None:
                        p.wait(s, "vxM", at=rd)
                    for nb in range(TBS // 512):
                        s.copy(bb_s[:, sp, nb * 512:(nb + 1) * 512], pb[4 + nb][:, :])
                        cnt = p.inc(s.copy(cb_s[:, sp, nb * 512:(nb + 1) * 512],
                                           pb[6 + nb][:, :]), "scM")
                    scan_hist[("bc_evac", sp)] = cnt
                    for cg in range(CGN):
                        k = si * CGN + cg
                        pp = k % 2
                        if pp in scan_hist:
                            p.wait(s, "vxM", at=scan_hist[pp])
                        p.inc(s.activation(da_t[:, pp, :], dl_ch[:, cg, :], AF.Exp,
                                           scale=e_a[:, cg, si:si + 1]), "scM")
                        p.wait(v, "scM")
                        p.inc(v.tensor_mul(dbu_t[:], du_ch[:, cg, :],
                                           bb_s[:, sp, :]), "vxM")
                        cr = crm_hist.get((cg, si))
                        if cr is not None:
                            p.wait(v, "crM", at=cr)
                        hp = hpar_hist.get(pp)
                        if hp is not None:
                            p.wait(v, "crM", at=hp)
                        scan_hist[pp] = p.inc(
                            v.tensor_tensor_scan(h_t[:, pp, :], da_t[:, pp, :],
                                                 dbu_t[:], carry[:, k:k + 1],
                                                 ALU.mult, ALU.add), "vxM")
                        p.wait(s, "vxM")
                        cnt2 = p.inc(s.copy(carry[:, k:k + 1],
                                            h_t[:, pp, TBS - 1:TBS]), "crM")
                        crm_hist[(cg, si)] = cnt2
                        hpar_hist[pp] = cnt2
                        v.tensor_mul(prod_t[:], h_t[:, pp, :], cb_s[:, sp, :])
                        scan_hist[("bc_read", sp)] = p.inc(
                            v.tensor_add(y_acc[:, cg, :], y_acc[:, cg, :],
                                         prod_t[:]), "vxM")
                    yield
                for cg in range(CGN):
                    p.inc(v.scalar_tensor_tensor(y_acc[:, cg, :], u_ch[:, cg, :],
                                                 dvec_t[:, cg:cg + 1], y_acc[:, cg, :],
                                                 op0=ALU.mult, op1=ALU.add), "vxM")
                    p.wait(v, "dsyM")
                    p.inc(v.tensor_copy(ycp[:, cg % 2, :], y_acc[:, cg, :]), "vxM")
                    p.wait(sy, "vxM")
                    p.dma(sy, ag2_in.ap()[cg * 128:(cg + 1) * 128, t0:t0 + TBS],
                          ycp[:, cg % 2, :], "dsyM")
                yield
            p.wait(g, "dsyM")
            p.inc(g.collective_compute("AllGather", ALU.bypass,
                                       replica_groups=[[0, 1, 2, 3], [4, 5, 6, 7]],
                                       ins=[ag2_in.ap()], outs=[ag2_out.ap()]), "cc")

        def ffn_gen():
            for q in range(TLOC // 512):
                q0 = q * 512
                p.wait(sy, "mmF")
                p.dma(sy, xn_q[:], xn_sp.ap()[:, :, q0:q0 + 512], "dsyF")

                def evac_g1(oc, tqi, bank):
                    p.wait(s, "mmF")
                    cnt = p.inc(s.activation(hbuf[:, oc, :], bank[:, :], AF.Silu,
                                             scale=w1s_t[:, oc:oc + 1]), "scF")
                    return ("scF", cnt)

                def evac_g2(oc, tqi, bank):
                    p.wait(v, "mmF"); p.wait(v, "scF")
                    cnt = p.inc(v.scalar_tensor_tensor(hbuf[:, oc, :], bank[:, :],
                                                       w2s_t[:, oc:oc + 1],
                                                       hbuf[:, oc, :],
                                                       op0=ALU.mult, op1=ALU.mult),
                                "vxF")
                    return ("vxF", cnt)

                yield from gemm_gen(lambda oc: dr["w1T"][oc], HG, DG, wba,
                                    lambda kg, tqi: xn_q[:, kg, :], evac_g1,
                                    1, lambda oc, tqi: oc % 2,
                                    pre_wait=("dsyF",), mmtag="mmF")
                yield from gemm_gen(lambda oc: dr["w2T"][oc], HG, DG, wba,
                                    lambda kg, tqi: xn_q[:, kg, :], evac_g2,
                                    1, lambda oc, tqi: 2 + oc % 2,
                                    pre_wait=("scF",), mmtag="mmF")

                def evac_h1(oc, tqi, bank):
                    p.wait(s, "mmF")
                    cnt = p.inc(s.activation(h2buf[:, oc, :], bank[:, :], AF.Gelu,
                                             scale=h1s_t[:, oc:oc + 1]), "scF")
                    return ("scF", cnt)

                yield from gemm_gen(lambda oc: dr["h1T"][oc], HG, HG, wba,
                                    lambda kg, tqi: hbuf[:, kg, :], evac_h1, 1,
                                    lambda oc, tqi: oc % 2, pre_wait=("vxF",),
                                    mmtag="mmF")

                def evac_h2(oc, tqi, bank):
                    p.wait(s, "mmF")
                    cnt = p.inc(s.activation(hbuf[:, oc, :], bank[:, :], AF.Gelu,
                                             scale=h2s_t[:, oc:oc + 1]), "scF")
                    return ("scF", cnt)

                yield from gemm_gen(lambda oc: dr["h2T"][oc], HG, HG, wba,
                                    lambda kg, tqi: h2buf[:, kg, :], evac_h2, 1,
                                    lambda oc, tqi: 2 + oc % 2, pre_wait=("scF",),
                                    mmtag="mmF")

                def evac_w3(oc, tqi, bank):
                    p.wait(s, "mmF"); p.wait(s, "dsyF")
                    cnt = p.inc(s.activation(stg_y[:], bank[:, :], AF.Copy,
                                             scale=w3s_t[:, oc:oc + 1]), "scF")
                    p.wait(sy, "scF")
                    p.dma(sy, ffn_spill.ap()[:, oc, q0:q0 + 512], stg_y[:], "dsyF")
                    return ("scF", cnt)

                yield from gemm_gen(lambda oc: dr["w3T"][oc], DG, HG, wba,
                                    lambda kg, tqi: hbuf[:, kg, :], evac_w3, 1,
                                    lambda oc, tqi: oc % 2, pre_wait=("scF",),
                                    mmtag="mmF")

        chang, ffng = chan_gen(), ffn_gen()
        for _ in range(30):
            next(ffng)
        run_interleaved([(chang, 1), (ffng, 3)])

    ph4 = contextlib.ExitStack()
    with ph4:
        phase_barrier()
        ph4b = ph4.enter_context(contextlib.ExitStack())
        x1b = ph4b.enter_context(nc.sbuf_tensor("x1b", [128, DG, TLOC], F32, side="right"))
        ph4a = ph4.enter_context(contextlib.ExitStack())
        xn2b = ph4a.enter_context(nc.sbuf_tensor("xn2b", [128, DG, TLOC], BF16))
        szq = ph4a.enter_context(nc.sbuf_tensor("szq", [128, DIN // 128, 512], BF16))
        yzq = ph4a.enter_context(nc.sbuf_tensor("yzq", [128, DIN // 128, 512], BF16))
        wbo = ph4a.enter_context(nc.sbuf_tensor("wbo", [128, 2, DIN], BF16))

        p.dma(sy, x1b[:], x1_spill.ap(), "dsy")
        p.dma(sy, xn2b[:], xn2_spill.ap(), "dsy")
        p.wait(t, "dsy"); p.wait(v, "dsy")
        for q in range(TLOC // 512):
            q0 = q * 512

            def evac_z(oc, tqi, bank):
                p.wait(s, "mm"); p.wait(s, "vx")
                cnt = p.inc(s.activation(szq[:, oc, :], bank[:, :], AF.Silu), "sc")
                return ("sc", cnt)

            gemm(lambda oc: dr["w_in_z"][oc], DIN // 128, DG, wbo,
                 lambda kg, tqi: xn2b[:, kg, q0:q0 + 512], evac_z, 1,
                 lambda oc, tqi: 4 + oc % 2)
            g.reg_load(roff, offt[:1, 1:2])
            g.reg_add(rtmp, roff, q0)
            srcy = bass.AP(ag2_out, rtmp, [[T, 128], [128 * T, DIN // 128], [1, 512]])
            p.wait(g, "cc")
            p.wait(g, "vx"); p.wait(g, "mm")
            with nc.allow_non_contiguous_dma("strided gather from AG output"):
                p.inc(g.dma_start(out=yzq[:, :, :], in_=srcy, bounds_check=None),
                      "dgp", 16)
            p.wait(v, "dgp"); p.wait(v, "sc")
            for oc in range(DIN // 128):
                p.inc(v.tensor_mul(yzq[:, oc, :], yzq[:, oc, :], szq[:, oc, :]), "vx")

            def evac_wo(oc, tqi, bank):
                p.wait(v, "mm")
                cnt = p.inc(v.tensor_add(x1b[:, oc, q0:q0 + 512],
                                         x1b[:, oc, q0:q0 + 512], bank[:, :]), "vx")
                return ("vx", cnt)

            gemm(lambda oc: dr["w_out"][oc], DG, DIN // 128, wbo,
                 lambda kg, tqi: yzq[:, kg, :], evac_wo, 1,
                 lambda oc, tqi: oc % 2, pre_wait=("vx",))
        for gi in range(DG):
            p.wait(sy, "vx")
            p.dma(sy, tmp_c[:, 0:TLOC], ffn_spill.ap()[:, gi, :], "dsy")
            p.wait(v, "dsy")
            p.inc(v.tensor_add(x1b[:, gi, :], x1b[:, gi, :], tmp_c[:, 0:TLOC]), "vx")

        if upto == "l0dbg":
            dbg_ffn = nc.dram_tensor("dbg_ffn", [128, DG, TLOC], BF16,
                                     kind="ExternalOutput").ap()
            dbg_y = nc.dram_tensor("dbg_y", [DIN, T], BF16, kind="ExternalOutput").ap()
            dbg_xdbl = nc.dram_tensor("dbg_xdbl", [96, T], F32, kind="ExternalOutput").ap()
            dbg_xn = nc.dram_tensor("dbg_xn", [128, DG, TLOC], BF16,
                                    kind="ExternalOutput").ap()
            dbg_u = nc.dram_tensor("dbg_u", [CH, T], BF16, kind="ExternalOutput").ap()
            dbg_up = nc.dram_tensor("dbg_up", [DIN, TLOC], BF16, kind="ExternalOutput").ap()
            p.dma(g, dbg_up, ag1_in.ap(), "dgp")
            dbg_xn2 = nc.dram_tensor("dbg_xn2", [128, DG, TLOC], BF16, kind="ExternalOutput").ap()
            p.dma(g, dbg_xn2, xn2_spill.ap(), "dgp")
            p.dma(g, dbg_ffn, ffn_spill.ap(), "dgp")
            p.dma(g, dbg_y, ag2_out.ap(), "dgp")
            p.dma(g, dbg_u, u_spill.ap(), "dgp")
            p.dma(g, dbg_xdbl, ar_out.ap(), "dgp")
            p.wait(sy, "vx")
            p.dma(sy, dbg_xn, xn[:], "dsy")
            p.dma(sy, out_ap, x1b[:], "dst")
            p.wait(sy, "dst"); p.wait(sy, "dsy"); p.wait(g, "dgp")
            return p, dr, out_ap
        if upto == "l0":
            p.wait(sy, "vx")
            p.dma(sy, out_ap, x1b[:], "dst")
            p.wait(sy, "dst")
            return p, dr, out_ap

        ph4a.close()
        phase_barrier()
        l1 = ph4.enter_context(contextlib.ExitStack())
        xn1 = l1.enter_context(nc.sbuf_tensor("xn1", [128, DG, TLOC], BF16))
        ffn1_out = l1.enter_context(nc.sbuf_tensor("ffn1_out", [128, DG, TLOC], BF16))
        e2g = l1.enter_context(nc.sbuf_tensor("e2g", [128, 8, DG], BF16))
        e3g = l1.enter_context(nc.sbuf_tensor("e3g", [128, 8, DG], F32))
        e4g = l1.enter_context(nc.sbuf_tensor("e4g", [128, 8, 2, DG], F32))
        bndc = l1.enter_context(nc.sbuf_tensor("bndc", [128, DG], F32))
        kmx = l1.enter_context(nc.sbuf_tensor("kmx", [128, DG], F32))
        aend = l1.enter_context(nc.sbuf_tensor("aend", [128, 2, DG], F32))
        ainb = l1.enter_context(nc.sbuf_tensor("ainb", [128, 2, DG], F32))
        accf = l1.enter_context(nc.sbuf_tensor("accf", [128, 2, DG], F32))
        gq_t = l1.enter_context(nc.sbuf_tensor("gq_t", [128, 24], F32))
        thrc = l1.enter_context(nc.sbuf_tensor("thrc", [128, 2], F32))
        ones_col_f = l1.enter_context(nc.sbuf_tensor("ones_col_f", [128, 1], F32))
        tmp_b = l1.enter_context(nc.sbuf_tensor("tmp_b", [128, TLOC], F32))
        tmp_e = l1.enter_context(nc.sbuf_tensor("tmp_e", [128, TLOC], BF16))

        ln_emit(x1b, DG, xn1, LN_EPS)
        p.wait(sy, "vx")
        with nc.allow_non_contiguous_dma("tiny boundary column"):
            p.dma(sy, e2_in.ap(), xn1[:, :, TLOC - 1:TLOC], "dsy")
        p.wait(g, "dsy")
        p.inc(g.collective_compute("AllGather", ALU.bypass,
                                   replica_groups=[list(range(8))],
                                   ins=[e2_in.ap()], outs=[e2_out.ap()]), "cc")
        p.dma(sy, x1_spill.ap(), x1b[:], "dsy")
        p.wait(v, "dsy"); p.wait(s, "dsy"); p.wait(t, "dsy"); p.wait(g, "dsy")
        ph4b.close()

        p.wait(v, "dsy")
        v.memset(gq_t[:], 0.0)
        p.inc(v.memset(ones_col_f[:], 1.0), "vx")
        with nc.sbuf_tensor("qbuf", [128, HF], F32) as qbuf:
            for mi, mat in enumerate((dr["wfcT"], dr["wprT"])):
                for kg in range(DG):
                    p.wait(g, "vx")
                    p.dma(g, qbuf[:], mat[kg], "dgp")
                    p.wait(v, "dgp")
                    p.inc(v.tensor_reduce(gq_t[:, mi * 8 + kg:mi * 8 + kg + 1],
                                          qbuf[:], axis=AX.X, op=ALU.add,
                                          apply_absolute_value=True), "vx")
            p.wait(t, "vx")
            p.inc(t.matmul(pb[4][:1, 0:16], ones_col_f[:], gq_t[:, 0:16], start=True,
                           stop=True), "mm")
            p.wait(v, "mm")
            v.tensor_reduce(gq_t[:1, 16:17], pb[4][:1, 0:8], axis=AX.X, op=ALU.add)
            p.inc(v.tensor_reduce(gq_t[:1, 17:18], pb[4][:1, 8:16], axis=AX.X,
                                  op=ALU.add), "vx")
            p.wait(v, "vx")
            p.inc(v.tensor_scalar(gq_t[:1, 20:22], gq_t[:1, 16:18], 0.5 / (HF * D),
                                  0.5 * ABS_EPS, op0=ALU.mult, op1=ALU.add), "vx")
            p.wait(t, "vx")
            p.inc(t.matmul(pb[5][:, 0:2], ones_row_f[:], gq_t[:1, 20:22], start=True,
                           stop=True), "mm")
            p.wait(v, "mm")
            p.inc(v.tensor_copy(thrc[:], pb[5][:, 0:2]), "vx")
            for mat, col in ((dr["wfcT"], 0), (dr["wprT"], 1)):
                for kg in range(DG):
                    p.wait(g, "vx"); p.wait(g, "sc"); p.wait(g, "dsy")
                    p.dma(g, qbuf[:], mat[kg], "dgp")
                    p.wait(s, "dgp"); p.wait(s, "vx")
                    p.inc(s.sign(tmp_c[:, 0:HF] if False else tmp_c[:, 0:TLOC], qbuf[:, 0:TLOC]), "sc")
                    p.inc(s.sign(tmp_e[:, 0:TLOC], qbuf[:, TLOC:HF]), "sc")
                    p.inc(s.activation(tmp_a[:, 0:TLOC], qbuf[:, 0:TLOC], AF.Abs), "sc")
                    p.inc(s.activation(tmp_b[:, 0:TLOC], qbuf[:, TLOC:HF], AF.Abs), "sc")
                    p.wait(v, "sc"); p.wait(v, "dsy")
                    p.inc(v.tensor_single_scalar(tmp_a[:, 0:TLOC], tmp_a[:, 0:TLOC],
                                                 thrc[:, col:col + 1], op=ALU.is_ge), "vx")
                    p.inc(v.tensor_single_scalar(tmp_b[:, 0:TLOC], tmp_b[:, 0:TLOC],
                                                 thrc[:, col:col + 1], op=ALU.is_ge), "vx")
                    p.inc(v.tensor_mul(tmp_c[:, 0:TLOC], tmp_c[:, 0:TLOC],
                                       tmp_a[:, 0:TLOC]), "vx")
                    p.inc(v.tensor_mul(tmp_e[:, 0:TLOC], tmp_e[:, 0:TLOC],
                                       tmp_b[:, 0:TLOC]), "vx")
                    p.wait(sy, "vx")
                    if col == 0:
                        dst0 = wq_spill.ap()[4 * kg:4 * kg + 2].rearrange("b p k m -> p b k m")
                        dst1 = wq_spill.ap()[4 * kg + 2:4 * kg + 4].rearrange("b p k m -> p b k m")
                    else:
                        dst0 = wqp_spill.ap()[kg][:, 0:HG // 2, :]
                        dst1 = wqp_spill.ap()[kg][:, HG // 2:, :]
                    with nc.allow_non_contiguous_dma("band-order spill"):
                        p.dma(sy, dst0, tmp_c[:, 0:TLOC], "dsy")
                        p.dma(sy, dst1, tmp_e[:, 0:TLOC], "dsy")
        with contextlib.ExitStack() as lf:
            hq = lf.enter_context(nc.sbuf_tensor("hq", [128, HG, 512], BF16))
            hn = lf.enter_context(nc.sbuf_tensor("hn", [128, HG, 512], BF16))
            wqa = lf.enter_context(nc.sbuf_tensor("wqa", [128, 2, HG * 128], BF16))
            for q in range(TLOC // 512):
                q0 = q * 512

                def evac_fc(oc, tqi, bank):
                    p.wait(s, "mm")
                    cnt = p.inc(s.activation(hq[:, oc, :], bank[:, :], AF.Gelu), "sc")
                    return ("sc", cnt)

                gemm(lambda oc: wq_spill.ap()[oc], HG, DG, wqa,
                     lambda kg, tqi: xn1[:, kg, q0:q0 + 512], evac_fc, 1,
                     lambda oc, tqi: oc % 2, pre_wait=("vx", "dsy"))
                ln_emit(hq, HG, hn, LN_EPS, gamma_ap=lnw_t, beta_ap=lnb_t, tcols=512)

                def evac_pr(oc, tqi, bank):
                    p.wait(v, "mm")
                    cnt = p.inc(v.tensor_copy(ffn1_out[:, oc, q0:q0 + 512],
                                              bank[:, :]), "vx")
                    return ("vx", cnt)

                gemm(lambda oc: wqp_spill.ap()[oc], DG, HG, wqa,
                     lambda kg, tqi: hn[:, kg, :], evac_pr, 1,
                     lambda oc, tqi: 2 + oc % 2, pre_wait=("vx", "dsy"))

        with contextlib.ExitStack() as lr:
            ywo = lr.enter_context(nc.sbuf_tensor("ywo", [128, DG, TLOC], BF16))
            wrk = lr.enter_context(nc.sbuf_tensor("wrk", [128, 2, D], BF16))
            lr_m = lr.enter_context(contextlib.ExitStack())
            xmix = lr_m.enter_context(nc.sbuf_tensor("xmix", [128, DG, TLOC], BF16))

            p.wait(g, "cc")
            p.dma(g, e2g[:], e2_out.ap().rearrange("j p g -> p j g"), "dgp")
            p.wait(v, "dgp")
            v.memset(bndc[:], 0.0)
            for j in range(8):
                ia = v.scalar_tensor_tensor(bndc[:], e2g[:, j, :], selp[:, j:j + 1],
                                            bndc[:], op0=ALU.mult, op1=ALU.add)
            p.inc(ia, "vx")
            for mx, wname, mixcol in ((0, "wkT", 2), (1, "wvT", 3), (2, "wrT", 4)):
                p.wait(v, "mm")
                for gi in range(DG):
                    v.tensor_copy(tmp_c[:, 1:TLOC], xn1[:, gi, 0:TLOC - 1])
                    v.tensor_copy(tmp_c[:, 0:1], bndc[:, gi:gi + 1])
                    v.tensor_scalar_mul(tmp_a[:, 0:TLOC], tmp_c[:, 0:TLOC],
                                        omix[:, gi:gi + 1, mx].rearrange("p g -> p g"))
                    ia = v.scalar_tensor_tensor(xmix[:, gi, :], xn1[:, gi, :],
                                                tvec_t[:, gi:gi + 1, mixcol]
                                                .rearrange("p g -> p g"),
                                                tmp_a[:, 0:TLOC],
                                                op0=ALU.mult, op1=ALU.add)
                p.inc(ia, "vx")

                def evac_kv(oc, tqi, bank, mx=mx):
                    p.wait(s, "mm"); p.wait(s, "dsy")
                    if mx == 2:
                        cnt = p.inc(s.activation(stg_y[:], bank[:, :], AF.Sigmoid), "sc")
                    else:
                        cnt = p.inc(s.copy(stg_y[:], bank[:, :]), "sc")
                    p.wait(sy, "sc")
                    p.dma(sy, rwkv_spill.ap()[mx, oc, :, tqi * 512:(tqi + 1) * 512],
                          stg_y[:], "dsy")
                    return ("sc", cnt)

                gemm(lambda oc: dr[wname][oc], DG, DG, wrk,
                     lambda kg, tqi: xmix[:, kg, tqi * 512:(tqi + 1) * 512], evac_kv,
                     4, lambda oc, tqi: (oc * 4 + tqi) % 4, pre_wait=("vx",))
            p.wait(v, "mm"); p.wait(s, "mm"); p.wait(v, "dsy"); p.wait(s, "dsy")
            lr_m.close()
            kgt = lr.enter_context(nc.sbuf_tensor("kgt", [128, TLOC], BF16))
            vgt = lr.enter_context(nc.sbuf_tensor("vgt", [128, TLOC], BF16))
            sgt = lr.enter_context(nc.sbuf_tensor("sgt", [128, TLOC], BF16))
            ekv_t = lr.enter_context(nc.sbuf_tensor("ekv_t", [128, TLOC], F32))
            nmt = lr.enter_context(nc.sbuf_tensor("nmt", [128, 1 + TLOC], F32))
            dnt = lr.enter_context(nc.sbuf_tensor("dnt", [128, 1 + TLOC], F32))
            ones2k = lr.enter_context(nc.sbuf_tensor("ones2k", [128, TLOC], BF16))
            tmp_d = lr.enter_context(nc.sbuf_tensor("tmp_d", [128, TLOC], F32))
            iota_t = lr.enter_context(nc.sbuf_tensor("iota_t", [128, TLOC], F32))
            p.inc(g.iota(iota_t[:], pattern=[[1, TLOC]], base=1, channel_multiplier=0,
                         allow_small_or_imprecise_dtypes=True), "gp")
            p.wait(s, "gp")
            p.inc(v.memset(ones2k[:], 1.0), "vx")
            for gi in range(DG):
                p.wait(g, "vx")
                p.dma(g, kgt[:], rwkv_spill.ap()[0, gi], "dgp")
                p.wait(v, "dgp")
                p.inc(v.reduce_max(kmx[:, gi:gi + 1], kgt[:], axis=AX.X), "vx")
            p.wait(sy, "vx")
            p.dma(sy, e3_in.ap(), kmx[:], "dsy")
            p.wait(g, "dsy")
            p.inc(g.collective_compute("AllGather", ALU.bypass,
                                       replica_groups=[list(range(8))],
                                       ins=[e3_in.ap()], outs=[e3_out.ap()]), "cc")
            p.wait(g, "cc")
            p.dma(g, e3g[:], e3_out.ap().rearrange("j p g -> p j g"), "dgp")
            p.wait(v, "dgp")
            v.memset(kmx[:], -1e38)
            for j in range(8):
                ia = v.scalar_tensor_tensor(kmx[:], e3g[:, j, :], selc_pen[:, j:j + 1],
                                            kmx[:], op0=ALU.add, op1=ALU.max)
            p.inc(ia, "vx")
            p.inc(v.tensor_scalar_mul(kmx[:], kmx[:], -1.0), "vx")
            for gi in range(DG):
                p.wait(g, "vx"); p.wait(g, "sc")
                p.dma(g, kgt[:], rwkv_spill.ap()[0, gi], "dgp")
                p.dma(g, vgt[:], rwkv_spill.ap()[1, gi], "dgp")
                p.wait(s, "dgp"); p.wait(s, "vx")
                p.inc(s.activation(tmp_b[:, 0:TLOC], kgt[:], AF.Exp,
                                   bias=kmx[:, gi:gi + 1]), "sc")
                p.wait(v, "sc"); p.wait(v, "dgp")
                p.inc(v.tensor_mul(ekv_t[:], tmp_b[:, 0:TLOC], vgt[:]), "vx")
                v.tensor_scalar_mul(tmp_a[:, 0:TLOC], ones2k[:],
                                    wlane[:, gi:gi + 1, 1].rearrange("p g -> p g"))
                p.inc(v.tensor_tensor_scan(nmt[:, 1:], tmp_a[:, 0:TLOC], ekv_t[:],
                                           0.0, ALU.mult, ALU.add), "vx")
                p.inc(v.tensor_tensor_scan(dnt[:, 1:], tmp_a[:, 0:TLOC],
                                           tmp_b[:, 0:TLOC], 0.0, ALU.mult, ALU.add), "vx")
                v.tensor_copy(aend[:, 0, gi:gi + 1], nmt[:, TLOC:TLOC + 1])
                p.inc(v.tensor_copy(aend[:, 1, gi:gi + 1], dnt[:, TLOC:TLOC + 1]), "vx")
            p.wait(sy, "vx")
            p.dma(sy, e4_in.ap(), aend[:], "dsy")
            p.wait(g, "dsy")
            p.inc(g.collective_compute("AllGather", ALU.bypass,
                                       replica_groups=[list(range(8))],
                                       ins=[e4_in.ap()], outs=[e4_out.ap()]), "cc")
            p.wait(g, "cc")
            p.dma(g, e4g[:], e4_out.ap().rearrange("j p c g -> p j c g"), "dgp")
            p.wait(v, "dgp")
            v.memset(ainb[:], 0.0)
            for j in range(8):
                if j in (0, 4):
                    v.tensor_copy(accf[:], e4g[:, j, :, :])
                else:
                    for comp in range(2):
                        v.tensor_mul(accf[:, comp, :], accf[:, comp, :],
                                     wlane[:, :, 2])
                        v.tensor_add(accf[:, comp, :], accf[:, comp, :],
                                     e4g[:, j, comp, :])
                ia = None
                for comp in range(2):
                    ia = v.scalar_tensor_tensor(ainb[:, comp, :], accf[:, comp, :],
                                                selp[:, j:j + 1], ainb[:, comp, :],
                                                op0=ALU.mult, op1=ALU.add)
            p.inc(ia, "vx")
            for gi in range(DG):
                p.wait(g, "vx"); p.wait(g, "sc")
                p.dma(g, kgt[:], rwkv_spill.ap()[0, gi], "dgp")
                p.dma(g, vgt[:], rwkv_spill.ap()[1, gi], "dgp")
                p.dma(g, sgt[:], rwkv_spill.ap()[2, gi], "dgp")
                p.wait(s, "dgp"); p.wait(s, "vx")
                p.inc(s.activation(tmp_b[:, 0:TLOC], kgt[:], AF.Exp,
                                   bias=kmx[:, gi:gi + 1]), "sc")
                p.inc(s.activation(tmp_d[:, 0:TLOC], iota_t[:], AF.Exp,
                                   scale=wlane[:, gi:gi + 1, 0]
                                   .rearrange("p g -> p g")), "sc")
                p.wait(v, "sc"); p.wait(v, "dgp")
                p.inc(v.tensor_mul(ekv_t[:], tmp_b[:, 0:TLOC], vgt[:]), "vx")
                v.tensor_scalar_mul(tmp_a[:, 0:TLOC], ones2k[:],
                                    wlane[:, gi:gi + 1, 1].rearrange("p g -> p g"))
                p.inc(v.tensor_tensor_scan(nmt[:, 1:], tmp_a[:, 0:TLOC], ekv_t[:],
                                           0.0, ALU.mult, ALU.add), "vx")
                p.inc(v.tensor_tensor_scan(dnt[:, 1:], tmp_a[:, 0:TLOC],
                                           tmp_b[:, 0:TLOC], 0.0, ALU.mult, ALU.add), "vx")
                p.wait(v, "vx")
                p.inc(v.scalar_tensor_tensor(nmt[:, 1:], tmp_d[:, 0:TLOC],
                                             ainb[:, 0, gi:gi + 1], nmt[:, 1:],
                                             op0=ALU.mult, op1=ALU.add), "vx")
                p.inc(v.scalar_tensor_tensor(dnt[:, 1:], tmp_d[:, 0:TLOC],
                                             ainb[:, 1, gi:gi + 1], dnt[:, 1:],
                                             op0=ALU.mult, op1=ALU.add), "vx")
                v.tensor_copy(nmt[:, 0:1], ainb[:, 0, gi:gi + 1])
                p.inc(v.tensor_copy(dnt[:, 0:1], ainb[:, 1, gi:gi + 1]), "vx")
                p.inc(v.tensor_scalar_mul(ekv_t[:], ekv_t[:],
                                          wlane[:, gi:gi + 1, 3]
                                          .rearrange("p g -> p g")), "vx")
                p.inc(v.tensor_scalar_mul(tmp_b[:, 0:TLOC], tmp_b[:, 0:TLOC],
                                          wlane[:, gi:gi + 1, 3]
                                          .rearrange("p g -> p g")), "vx")
                p.inc(v.tensor_add(ekv_t[:], ekv_t[:], nmt[:, 0:TLOC]), "vx")
                p.inc(v.tensor_add(tmp_b[:, 0:TLOC], tmp_b[:, 0:TLOC],
                                   dnt[:, 0:TLOC]), "vx")
                p.wait(s, "vx")
                s.activation(tmp_d[:, 0:TLOC], tmp_b[:, 0:TLOC], AF.Ln)
                p.inc(s.activation(tmp_d[:, 0:TLOC], tmp_d[:, 0:TLOC], AF.Exp,
                                   scale=-1.0), "sc")
                p.wait(v, "sc")
                p.inc(v.tensor_mul(ekv_t[:], ekv_t[:], tmp_d[:, 0:TLOC]), "vx")
                p.inc(v.tensor_mul(ywo[:, gi, :], ekv_t[:], sgt[:]), "vx")
            def evac_wo1(oc, tqi, bank):
                q0 = tqi * 512
                p.wait(g, "vx")
                p.dma(g, tmp_d[:, 0:512], x1_spill.ap()[:, oc, q0:q0 + 512], "dgp")
                p.wait(v, "mm"); p.wait(v, "dgp"); p.wait(v, "dst")
                v.tensor_add(tmp_a[:, 0:512], bank[:, :], ffn1_out[:, oc, q0:q0 + 512])
                cnt = p.inc(v.tensor_add(tmp_a[:, 0:512], tmp_a[:, 0:512],
                                         tmp_d[:, 0:512]), "vx")
                p.wait(sy, "vx")
                p.dma(sy, out_ap[:, oc, q0:q0 + 512], tmp_a[:, 0:512], "dst")
                return ("vx", cnt)

            gemm(lambda oc: dr["woT"][oc], DG, DG, wrk,
                 lambda kg, tqi: ywo[:, kg, tqi * 512:(tqi + 1) * 512], evac_wo1,
                 4, lambda oc, tqi: 4 + (oc * 4 + tqi) % 4, pre_wait=("vx",))
            if upto == "fulldbg":
                d_x1 = nc.dram_tensor("d_x1", [128, DG, TLOC], F32, kind="ExternalOutput").ap()
                d_ffn1 = nc.dram_tensor("d_ffn1", [128, DG, TLOC], BF16, kind="ExternalOutput").ap()
                d_ywo = nc.dram_tensor("d_ywo", [128, DG, TLOC], BF16, kind="ExternalOutput").ap()
                d_kvs = nc.dram_tensor("d_kvs", [3, DG, 128, TLOC], BF16, kind="ExternalOutput").ap()
                d_small = nc.dram_tensor("d_small", [128, 80], F32, kind="ExternalOutput").ap()
                d_xn1 = nc.dram_tensor("d_xn1", [128, DG, TLOC], BF16, kind="ExternalOutput").ap()
                p.dma(g, d_x1, x1_spill.ap(), "dgp")
                p.dma(g, d_kvs, rwkv_spill.ap(), "dgp")
                p.dma(g, d_xn1, xn1_spill.ap() if False else xn1[:], "dgp")
                p.wait(sy, "vx")
                p.dma(sy, d_ffn1, ffn1_out[:], "dsy")
                p.dma(sy, d_ywo, ywo[:], "dsy")
                p.dma(sy, d_small[:, 0:8], kmx[:], "dsy")
                p.dma(sy, d_small[:, 8:24], ainb[:].rearrange("p c g -> p (c g)"), "dsy")
                p.dma(sy, d_small[:, 24:40], aend[:].rearrange("p c g -> p (c g)"), "dsy")
                p.dma(sy, d_small[:, 40:42], thrc[:], "dsy")
                p.dma(sy, d_small[:, 42:50], selp[:], "dsy")
                p.dma(sy, d_small[:, 50:58], selc_pen[:], "dsy")
                p.dma(sy, d_small[:1, 64:72], selrow[:, 0:8], "dsy")
                p.dma(sy, d_small[:1, 72:80], selrow[:, 8:16], "dsy")
                p.dma(sy, d_small[:, 58:64], gq_t[:, 14:20], "dsy")
                d_e3g = nc.dram_tensor("d_e3g", [128, 8, DG], F32, kind="ExternalOutput").ap()
                p.dma(sy, d_e3g, e3g[:], "dsy")
                p.wait(sy, "dsy"); p.wait(g, "dgp")
            p.wait(sy, "dst")
    return p, dr, out_ap


def _band(w_T, n_oc, n_kg):
    K, M = w_T.shape
    assert K == n_kg * 128 and M == n_oc * 128
    return np.ascontiguousarray(
        w_T.reshape(n_kg, 128, n_oc, 128).transpose(2, 1, 0, 3))


def _prep_core_inputs(inputs, core):
    t0 = (core % 4) * TLOC
    b = core // 4
    r = {}

    def fm(x2d):
        f = x2d.shape[1]
        return np.ascontiguousarray(
            x2d.T.reshape(f // 128, 128, x2d.shape[0]).transpose(1, 0, 2))

    def pcol(vec, ng):
        return np.ascontiguousarray(np.asarray(vec).reshape(ng, 128).T)

    A = lambda k: np.asarray(inputs[k])
    r["x0T"] = fm(A("L0_attn_x")[b, t0:t0 + TLOC, :])
    r["l0_gamma"] = pcol(A("L0_attn_gamma"), DG)
    r["l0_beta"] = pcol(A("L0_attn_beta"), DG)
    w_in = A("L0_attn_W_in")
    r["w_in_u"] = _band(w_in[:, :DIN], DIN // 128, DG)
    r["w_in_z"] = _band(w_in[:, DIN:], DIN // 128, DG)
    chb = (core % 4) * CH
    cw = A("L0_attn_conv_w")[:, 0, :]
    r["conv_w"] = np.ascontiguousarray(
        cw[chb:chb + CH].reshape(CGN, 128, 4).transpose(1, 0, 2))
    r["conv_b"] = pcol(A("L0_attn_conv_b")[chb:chb + CH], CGN)
    r["w_xproj"] = np.ascontiguousarray(
        A("L0_attn_W_xproj")[chb:chb + CH].reshape(CGN, 128, 96).transpose(1, 0, 2))
    r["w_dt"] = np.ascontiguousarray(A("L0_attn_W_dt")[:, chb:chb + CH])
    r["b_dt"] = pcol(A("L0_attn_b_dt")[chb:chb + CH], CGN)
    r["a_log"] = np.ascontiguousarray(
        A("L0_attn_A_log")[chb:chb + CH].reshape(CGN, 128, DSTT).transpose(1, 0, 2))
    r["d_vec"] = pcol(A("L0_attn_D")[chb:chb + CH], CGN)
    r["w_out"] = _band(A("L0_attn_W_out"), DG, DIN // 128)
    r["w1T"] = _band(A("L0_ffn_w1_w").T, HG, DG)
    r["w2T"] = _band(A("L0_ffn_w2_w").T, HG, DG)
    r["w1s"] = pcol(A("L0_ffn_w1_s"), HG)
    r["w2s"] = pcol(A("L0_ffn_w2_s"), HG)
    r["h1T"] = _band(A("L0_ffn_h1_w").T, HG, HG)
    r["h2T"] = _band(A("L0_ffn_h2_w").T, HG, HG)
    r["h1s"] = pcol(A("L0_ffn_h1_s"), HG)
    r["h2s"] = pcol(A("L0_ffn_h2_s"), HG)
    r["w3T"] = _band(A("L0_ffn_w3_w").T, DG, HG)
    r["w3s"] = pcol(A("L0_ffn_w3_s"), DG)
    tv = np.stack([A("L1_attn_time_decay").reshape(D),
                   A("L1_attn_time_first").reshape(D),
                   A("L1_attn_time_mix_k").reshape(D),
                   A("L1_attn_time_mix_v").reshape(D),
                   A("L1_attn_time_mix_r").reshape(D)], axis=1)
    r["tvec"] = np.ascontiguousarray(tv.reshape(DG, 128, 5).transpose(1, 0, 2))
    r["wkT"] = _band(A("L1_attn_Wk").T, DG, DG)
    r["wvT"] = _band(A("L1_attn_Wv").T, DG, DG)
    r["wrT"] = _band(A("L1_attn_Wr").T, DG, DG)
    r["woT"] = _band(A("L1_attn_Wo").T, DG, DG)
    bfc = _band(A("L1_ffn_W_fc").T, HG, DG)
    r["wfcT"] = np.ascontiguousarray(
        bfc.reshape(DG, 4, 128, DG, 128).transpose(0, 2, 1, 3, 4)
        .reshape(DG, 128, HF))
    r["lnw"] = pcol(A("L1_ffn_ln_w"), HG)
    r["lnb"] = pcol(A("L1_ffn_ln_b"), HG)
    bpr = _band(A("L1_ffn_W_proj").T, DG, HG)
    r["wprT"] = np.ascontiguousarray(bpr.reshape(DG, 128, HG * 128))
    sp = np.zeros((1, 8), np.float32)
    if core % 4 != 0:
        sp[0, core - 1] = 1.0
    r["selprev"] = sp
    scv = np.zeros((1, 8), np.float32)
    for j in (range(4) if core < 4 else range(4, 8)):
        scv[0, j] = 1.0
    r["selchain"] = scv
    r["selpen"] = np.where(scv > 0, 0.0, -1e38).astype(np.float32)
    r["dynoff"] = np.array([[chb * TLOC, t0, 0, 0]], np.int32)
    import ml_dtypes
    oh = np.zeros((DSTT, DSTT, 128), np.float32)
    for si_ in range(DSTT):
        oh[si_, si_, :] = 1.0
    r["onehot16"] = oh.astype(ml_dtypes.bfloat16)
    for k in list(r):
        if r[k].dtype == np.float64:
            r[k] = r[k].astype(np.float32)
    return r


def kernel(**inputs):
    prog, dr, out_ap = build_program("full")
    maps = [_prep_core_inputs(inputs, c) for c in range(8)]
    res = run_bass_kernel_spmd(prog.nc, maps, core_ids=list(range(8)))
    out = np.empty((B, T, D), np.float32)
    for c in range(8):
        b, t0 = c // 4, (c % 4) * TLOC
        o = res.results[c]["outT"]
        out[b, t0:t0 + TLOC, :] = o.transpose(2, 1, 0).reshape(TLOC, D)
    return out


# version 13
# speedup vs baseline: 1.2493x; 1.0134x over previous
import contextlib

import numpy as np

import concourse.bass as bass
import concourse.mybir as mybir
from concourse.bass_utils import run_bass_kernel_spmd

F32 = mybir.dt.float32
BF16 = mybir.dt.bfloat16
I32 = mybir.dt.int32
AF = mybir.ActivationFunctionType
ALU = mybir.AluOpType
AX = mybir.AxisListType

B, T, D = 2, 8192, 1024
TLOC = 2048
DG = D // 128
DIN = 2048
DSTT = 16
DTR = 64
CH = DIN // 4
CGN = CH // 128
HF = 4096
HG = HF // 128
LN_EPS = 1e-5
ABS_EPS = 1e-6
TBS = 1024
NTB = T // TBS


class Prog:
    def __init__(self):
        self.nc = bass.Bass("TRN2", target_bir_lowering=False, debug=False)
        self.stack = contextlib.ExitStack()
        self._sems = {}

    def sem(self, name):
        if name not in self._sems:
            h = self.stack.enter_context(self.nc.semaphore(name))
            self._sems[name] = [h, 0]
        return self._sems[name][0]

    def count(self, name):
        self.sem(name)
        return self._sems[name][1]

    def inc(self, instr, name, n=1):
        self.sem(name)
        rec = self._sems[name]
        instr.then_inc(rec[0], n)
        rec[1] += n
        return rec[1]

    def dma(self, eng, out, in_, name):
        return self.inc(eng.dma_start(out=out, in_=in_), name, 16)

    def wait(self, eng, name, at=None):
        self.sem(name)
        rec = self._sems[name]
        val = rec[1] if at is None else at
        if val > 0:
            eng.wait_ge(rec[0], val)


def build_program(upto="full"):
    p = Prog()
    nc = p.nc
    v, s, t, g, sy = nc.vector, nc.scalar, nc.tensor, nc.gpsimd, nc.sync
    sb = p.stack.enter_context

    dr = {}

    def din(name, shape, dtype=F32):
        dr[name] = nc.dram_tensor(name, list(shape), dtype, kind="ExternalInput").ap()
        return dr[name]

    din("x0T", [128, DG, TLOC])
    out_ap = nc.dram_tensor("outT", [128, DG, TLOC], F32, kind="ExternalOutput").ap()

    din("l0_gamma", [128, DG]); din("l0_beta", [128, DG])
    din("w_in_u", [DIN // 128, 128, DG, 128])
    din("w_in_z", [DIN // 128, 128, DG, 128])
    din("conv_w", [128, CGN, 4]); din("conv_b", [128, CGN])
    din("w_xproj", [128, CGN, 96])
    din("w_dt", [DTR, CH])
    din("b_dt", [128, CGN])
    din("a_log", [128, CGN, DSTT])
    din("d_vec", [128, CGN])
    din("w_out", [DG, 128, DIN // 128, 128])
    din("w1T", [HG, 128, DG, 128]); din("w2T", [HG, 128, DG, 128])
    din("w1s", [128, HG]); din("w2s", [128, HG])
    din("h1T", [HG, 128, HG, 128]); din("h2T", [HG, 128, HG, 128])
    din("h1s", [128, HG]); din("h2s", [128, HG])
    din("w3T", [DG, 128, HG, 128]); din("w3s", [128, DG])
    din("tvec", [128, DG, 5])
    din("wkT", [DG, 128, DG, 128]); din("wvT", [DG, 128, DG, 128])
    din("wrT", [DG, 128, DG, 128]); din("woT", [DG, 128, DG, 128])
    din("wfcT", [DG, 128, DG, 128, 4] if False else [DG, 128, HF])
    din("lnw", [128, HG]); din("lnb", [128, HG])
    din("wprT", [DG, 128, HG * 128])
    din("selprev", [1, 8]); din("selchain", [1, 8]); din("selpen", [1, 8])
    din("onehot16", [DSTT, DSTT, 128], BF16)
    din("dynoff", [1, 4], I32)

    ag1_in = nc.dram_tensor("ag1_in", [DIN, TLOC], BF16)
    ag1_out = nc.dram_tensor("ag1_out", [4, DIN, TLOC], BF16)
    ar_in = nc.dram_tensor("ar_in", [96, T], F32)
    ar_out = nc.dram_tensor("ar_out", [96, T], F32)
    ag2_in = nc.dram_tensor("ag2_in", [CH, T], BF16)
    ag2_out = nc.dram_tensor("ag2_out", [DIN, T], BF16)
    u_spill = nc.dram_tensor("u_spill", [CH, T], BF16)
    x1_spill = nc.dram_tensor("x1_spill", [128, DG, TLOC], F32)
    xn2_spill = nc.dram_tensor("xn2_spill", [128, DG, TLOC], BF16)
    xn_sp = nc.dram_tensor("xn_sp", [128, DG, TLOC], BF16)
    ffn_spill = nc.dram_tensor("ffn_spill", [128, DG, TLOC], BF16)
    rwkv_spill = nc.dram_tensor("rwkv_spill", [4, DG, 128, TLOC], BF16)
    wq_spill = nc.dram_tensor("wq_spill", [HG, 128, DG, 128], BF16)
    wqp_spill = nc.dram_tensor("wqp_spill", [DG, 128, HG, 128], BF16)
    e2_out = nc.dram_tensor("e2_out", [8, 128, DG], BF16)
    e2_in = nc.dram_tensor("e2_in", [128, DG], BF16)
    e3_out = nc.dram_tensor("e3_out", [8, 128, DG], F32)
    e3_in = nc.dram_tensor("e3_in", [128, DG], F32)
    e4_out = nc.dram_tensor("e4_out", [8, 128, 2, DG], F32)
    e4_in = nc.dram_tensor("e4_in", [128, 2, DG], F32)

    gamma_t = sb(nc.sbuf_tensor("gamma_t", [128, DG], F32))
    beta_t = sb(nc.sbuf_tensor("beta_t", [128, DG], F32))
    convb_t = sb(nc.sbuf_tensor("convb_t", [128, CGN], F32))
    negbdt_t = sb(nc.sbuf_tensor("negbdt_t", [128, CGN], F32))
    dvec_t = sb(nc.sbuf_tensor("dvec_t", [128, CGN], F32))
    convw_t = sb(nc.sbuf_tensor("convw_t", [128, CGN, 4], F32))
    e_a = sb(nc.sbuf_tensor("e_a", [128, CGN, DSTT], F32))
    wxp_t = sb(nc.sbuf_tensor("wxp_t", [128, CGN, 96], BF16))
    wdt_t = sb(nc.sbuf_tensor("wdt_t", [DTR, CH], BF16))
    tvec_t = sb(nc.sbuf_tensor("tvec_t", [128, DG, 5], F32))
    wlane = sb(nc.sbuf_tensor("wlane", [128, DG, 5], F32))
    omix = sb(nc.sbuf_tensor("omix", [128, DG, 3], F32))
    w1s_t = sb(nc.sbuf_tensor("w1s_t", [128, HG], F32))
    w2s_t = sb(nc.sbuf_tensor("w2s_t", [128, HG], F32))
    h1s_t = sb(nc.sbuf_tensor("h1s_t", [128, HG], F32))
    h2s_t = sb(nc.sbuf_tensor("h2s_t", [128, HG], F32))
    w3s_t = sb(nc.sbuf_tensor("w3s_t", [128, DG], F32))
    lnw_t = sb(nc.sbuf_tensor("lnw_t", [128, HG], F32))
    lnb_t = sb(nc.sbuf_tensor("lnb_t", [128, HG], F32))
    selp = sb(nc.sbuf_tensor("selp", [128, 8], F32))
    selc_pen = sb(nc.sbuf_tensor("selc_pen", [128, 8], F32))
    selrow = sb(nc.sbuf_tensor("selrow", [1, 24], F32))
    ones_col = sb(nc.sbuf_tensor("ones_col", [128, 1], BF16))
    ones_row = sb(nc.sbuf_tensor("ones_row", [1, 128], BF16))
    ones_row_f = sb(nc.sbuf_tensor("ones_row_f", [1, 128], F32))
    offt = sb(nc.sbuf_tensor("offt", [1, 4], I32))
    stat = sb(nc.sbuf_tensor("stat", [128, 32], F32))
    srow = sb(nc.sbuf_tensor("srow", [1, 4, 512], F32))
    stg_x = sb(nc.sbuf_tensor("stg_x", [128, 2, 512], BF16))
    stg_q = sb(nc.sbuf_tensor("stg_q", [128, 2, 512], BF16))
    stg_y = sb(nc.sbuf_tensor("stg_y", [128, 512], BF16))
    epsc = sb(nc.sbuf_tensor("epsc", [1, 2], F32))
    tmp_a = sb(nc.sbuf_tensor("tmp_a", [128, TLOC], F32))
    tmp_c = sb(nc.sbuf_tensor("tmp_c", [128, TLOC], BF16))

    xn_scope = contextlib.ExitStack()
    xn = xn_scope.enter_context(nc.sbuf_tensor("xn", [128, DG, TLOC], BF16))
    pb = [sb(nc.psum_tensor(f"pb{i}", [128, 512], F32)) for i in range(8)]
    roff = sb(g.register("roff"))
    rtmp = sb(g.register("rtmp"))
    sroff = sb(sy.register("sroff"))
    srtmp = sb(sy.register("srtmp"))

    uid = [0]

    def gemm_gen(wdram, n_oc, n_kg, wtile, rhs_fn, evac_fn, n_tq, bank_of,
                 pre_wait=(), mmtag="mm"):
        uid[0] += 1
        ldsem = [f"gw{uid[0]}a", f"gw{uid[0]}b"]
        for nm in pre_wait:
            p.wait(t, nm)
        m0 = p.count(mmtag)
        last_use = [m0, m0]
        bank_free = {}
        for oc in range(n_oc):
            par = oc % 2
            p.wait(g, mmtag, at=last_use[par])
            ldc = p.dma(g, wtile[:, par, 0:n_kg * 128], wdram(oc), ldsem[par])
            for tqi in range(n_tq):
                bk = bank_of(oc, tqi)
                if bk in bank_free:
                    nm, cnt = bank_free[bk]
                    p.wait(t, nm, at=cnt)
                p.wait(t, ldsem[par], at=ldc)
                for kg in range(n_kg):
                    i1 = t.matmul(pb[bk][:, :], wtile[:, par, kg * 128:(kg + 1) * 128],
                                  rhs_fn(kg, tqi), start=(kg == 0), stop=(kg == n_kg - 1))
                mmc = p.inc(i1, mmtag)
                last_use[par] = mmc
                bank_free[bk] = evac_fn(oc, tqi, pb[bk])
            yield

    def gemm(*args, **kw):
        for _ in gemm_gen(*args, **kw):
            pass

    def run_interleaved(gens):
        active = list(gens)
        while active:
            keep = []
            for gen, w in active:
                alive = True
                for _ in range(w):
                    try:
                        next(gen)
                    except StopIteration:
                        alive = False
                        break
                if alive:
                    keep.append((gen, w))
            active = keep

    def phase_barrier(sems=("dsy", "dgp", "vx", "sc", "mm", "gp", "cc",
                            "dsyM", "dgpM", "vxM", "scM", "mmM", "crM",
                            "dsyF", "vxF", "scF", "mmF")):
        for nm in sems:
            for eng in (v, s, t, g, sy):
                p.wait(eng, nm)

    def ln_emit(src, n_g, dst_bf, eps, plain_cb=None, gamma_ap=None, beta_ap=None,
                tcols=TLOC, toff=0, vxt="vx", sct="sc", mmt="mm", ps0=4,
                stgx=None, stgq=None, lnstg=None):
        inv_n = 1.0 / (n_g * 128)
        stgx = stgx if stgx is not None else stg_x
        stgq = stgq if stgq is not None else stg_q
        aply = lnstg if lnstg is not None else tmp_a
        p.wait(s, vxt)
        ps_x, ps_q = pb[ps0], pb[ps0 + 1]
        hist = {}
        for blk in range(tcols // 512):
            t0 = toff + blk * 512
            for gi in range(n_g):
                gp_ = gi % 2
                h_ = hist.get(("stg", gp_))
                if h_ is not None:
                    p.wait(v, mmt, at=h_[0]); p.wait(s, mmt, at=h_[0])
                p.inc(v.tensor_copy(stgx[:, gp_, :], src[:, gi, t0:t0 + 512]), vxt)
                p.inc(s.activation(stgq[:, gp_, :], src[:, gi, t0:t0 + 512],
                                   AF.Square), sct)
                p.wait(t, vxt); p.wait(t, sct)
                last = gi == n_g - 1
                t.matmul(ps_x[:1, :], ones_col[:], stgx[:, gp_, :],
                         start=(gi == 0), stop=last)
                i2 = t.matmul(ps_q[:1, :], ones_col[:], stgq[:, gp_, :],
                              start=(gi == 0), stop=last)
                hist[("stg", gp_)] = (p.inc(i2, mmt), None)
            p.wait(v, mmt)
            p.inc(v.tensor_scalar_mul(srow[:1, 0, :], ps_x[:1, :], inv_n), vxt)
            v.tensor_mul(srow[:1, 1, :], srow[:1, 0, :], srow[:1, 0, :])
            p.inc(v.scalar_tensor_tensor(srow[:1, 1, :], ps_q[:1, :], inv_n,
                                         srow[:1, 1, :], op0=ALU.mult,
                                         op1=ALU.subtract), vxt)
            p.wait(s, vxt)
            s.activation(srow[:1, 2, :], srow[:1, 1, :], AF.Ln, bias=epsc[:1, 0:1])
            p.inc(s.activation(srow[:1, 2, :], srow[:1, 2, :], AF.Exp, scale=-0.5), sct)
            p.wait(v, sct)
            v.tensor_mul(srow[:1, 3, :], srow[:1, 0, :], srow[:1, 2, :])
            p.inc(v.tensor_scalar_mul(srow[:1, 3, :], srow[:1, 3, :], -1.0), vxt)
            p.wait(t, vxt)
            t.matmul(ps_x[:, :], ones_row_f[:], srow[:1, 2, :], start=True, stop=True)
            p.inc(t.matmul(ps_q[:, :], ones_row_f[:], srow[:1, 3, :], start=True,
                           stop=True), mmt)
            p.wait(v, mmt)
            for gi in range(n_g):
                v.tensor_mul(aply[:, 0:512], src[:, gi, t0:t0 + 512], ps_x[:, :])
                ia = v.tensor_add(aply[:, 0:512], aply[:, 0:512], ps_q[:, :])
                if gamma_ap is not None:
                    ia = v.tensor_scalar(dst_bf[:, gi, t0:t0 + 512], aply[:, 0:512],
                                         gamma_ap[:, gi:gi + 1], beta_ap[:, gi:gi + 1],
                                         op0=ALU.mult, op1=ALU.add)
                else:
                    ia = v.tensor_copy(dst_bf[:, gi, t0:t0 + 512], aply[:, 0:512])
                p.inc(ia, vxt)
                if plain_cb is not None:
                    plain_cb(gi, t0)

    def chain_select(out_col_fn, gath_fn, n_items):
        for it in range(n_items):
            v.memset(tmp_a[:, 0:DG], 0.0)
            for j in range(8):
                ia = v.scalar_tensor_tensor(tmp_a[:, 0:DG], gath_fn(it, j),
                                            selp[:, j:j + 1], tmp_a[:, 0:DG],
                                            op0=ALU.mult, op1=ALU.add)
            p.inc(v.tensor_copy(out_col_fn(it), tmp_a[:, 0:DG]), "vx")

    with contextlib.ExitStack() as ph01:
        x1 = ph01.enter_context(nc.sbuf_tensor("x1", [128, DG, TLOC], F32))
        xn2 = ph01.enter_context(nc.sbuf_tensor("xn2", [128, DG, TLOC], BF16))
        wz = ph01.enter_context(nc.sbuf_tensor("wz", [128, 2, DG * 128], BF16))

        p.dma(sy, x1[:], dr["x0T"], "dsy")
        for nm, tl in [("l0_gamma", gamma_t), ("l0_beta", beta_t), ("conv_b", convb_t),
                       ("d_vec", dvec_t), ("conv_w", convw_t), ("a_log", e_a),
                       ("tvec", tvec_t), ("w1s", w1s_t), ("w2s", w2s_t),
                       ("h1s", h1s_t), ("h2s", h2s_t), ("w3s", w3s_t),
                       ("lnw", lnw_t), ("lnb", lnb_t)]:
            p.dma(sy, tl[:], dr[nm], "dsy")
        p.dma(sy, stat[:, 0:CGN], dr["b_dt"], "dsy")
        p.dma(sy, selrow[:, 0:8], dr["selprev"], "dsy")
        p.dma(sy, selrow[:, 8:16], dr["selchain"], "dsy")
        p.dma(sy, selrow[:, 16:24], dr["selpen"], "dsy")
        p.dma(g, wxp_t[:], dr["w_xproj"], "dgp")
        p.dma(g, wdt_t[:], dr["w_dt"], "dgp")
        p.dma(g, offt[:], dr["dynoff"], "dgp")

        p.wait(v, "dsy")
        v.memset(epsc[:], LN_EPS)
        v.memset(ones_col[:], 1.0)
        v.memset(ones_row[:], 1.0)
        v.memset(ones_row_f[:], 1.0)
        p.inc(v.tensor_scalar_mul(negbdt_t[:], stat[:, 0:CGN], -1.0), "vx")
        for j in range(3):
            p.inc(v.tensor_scalar(omix[:, :, j], tvec_t[:, :, 2 + j], -1.0, 1.0,
                                  op0=ALU.mult, op1=ALU.add), "vx")
        p.wait(s, "dsy")
        p.inc(s.activation(e_a[:], e_a[:], AF.Exp), "sc")
        p.inc(s.activation(wlane[:, :, 0], tvec_t[:, :, 0], AF.Exp), "sc")
        p.wait(v, "sc")
        p.inc(v.tensor_scalar_mul(wlane[:, :, 0], wlane[:, :, 0], -1.0), "vx")
        p.wait(s, "vx")
        p.inc(s.activation(wlane[:, :, 1], wlane[:, :, 0], AF.Exp), "sc")
        p.inc(s.activation(wlane[:, :, 2], wlane[:, :, 0], AF.Exp, scale=float(TLOC)), "sc")
        p.inc(s.activation(wlane[:, :, 3], tvec_t[:, :, 1], AF.Exp), "sc")
        p.wait(t, "dsy"); p.wait(t, "vx")
        p.inc(t.matmul(pb[0][:, 0:24], ones_row_f[:], selrow[:1, :], start=True,
                       stop=True), "mm")
        p.wait(v, "mm")
        p.inc(v.tensor_copy(selp[:], pb[0][:, 0:8]), "vx")
        p.inc(v.tensor_copy(selc_pen[:], pb[0][:, 16:24]), "vx")

        def add_xn(gi, t0):
            p.inc(v.tensor_add(x1[:, gi, t0:t0 + 512], x1[:, gi, t0:t0 + 512],
                               tmp_a[:, 0:512]), "vx")

        ln_emit(x1, DG, xn, LN_EPS, plain_cb=add_xn)
        ln_emit(xn, DG, xn2, LN_EPS, gamma_ap=gamma_t, beta_ap=beta_t)

        def evac_u(oc, tqi, bank):
            p.wait(s, "mm")
            p.wait(s, "dsy")
            cnt = p.inc(s.copy(stg_y[:], bank[:, :]), "sc")
            p.wait(sy, "sc")
            p.dma(sy, ag1_in.ap()[oc * 128:(oc + 1) * 128, tqi * 512:(tqi + 1) * 512],
                  stg_y[:], "dsy")
            return ("sc", cnt)

        gemm(lambda oc: dr["w_in_u"][oc], DIN // 128, DG, wz,
             lambda kg, tqi: xn2[:, kg, tqi * 512:(tqi + 1) * 512],
             evac_u, 4, lambda oc, tqi: (oc * 4 + tqi) % 4, pre_wait=("vx",))

        p.wait(g, "dsy")
        p.inc(g.collective_compute("AllGather", ALU.bypass,
                                   replica_groups=[[0, 1, 2, 3], [4, 5, 6, 7]],
                                   ins=[ag1_in.ap()], outs=[ag1_out.ap()]), "cc")
        p.wait(sy, "vx")
        p.dma(sy, x1_spill.ap(), x1[:], "dsy")
        p.dma(sy, xn2_spill.ap(), xn2[:], "dsy")
        p.dma(sy, xn_sp.ap(), xn[:], "dsy")
        p.wait(g, "dsy")

    xn_scope.close()
    with contextlib.ExitStack() as ph2:
        phase_barrier(("dsy", "dgp", "vx", "sc", "mm", "gp"))
        xdbl_tb = ph2.enter_context(nc.sbuf_tensor("xdbl_tb", [96, TBS], F32))
        up_h = ph2.enter_context(nc.sbuf_tensor("up_h", [128, CGN, 3 + TBS], BF16))
        u_ch = ph2.enter_context(nc.sbuf_tensor("u_ch", [128, CGN, TBS], BF16))
        dl_ch = ph2.enter_context(nc.sbuf_tensor("dl_ch", [128, CGN, TBS], BF16))
        du_ch = ph2.enter_context(nc.sbuf_tensor("du_ch", [128, CGN, TBS], BF16))
        bb_s = ph2.enter_context(nc.sbuf_tensor("bb_s", [128, 2, TBS], BF16))
        cb_s = ph2.enter_context(nc.sbuf_tensor("cb_s", [128, 2, TBS], BF16))
        da_t = ph2.enter_context(nc.sbuf_tensor("da_t", [128, 2, TBS], BF16))
        dbu_t = ph2.enter_context(nc.sbuf_tensor("dbu_t", [128, TBS], BF16))
        h_t = ph2.enter_context(nc.sbuf_tensor("h_t", [128, 2, TBS], BF16))
        prod_t = ph2.enter_context(nc.sbuf_tensor("prod_t", [128, TBS], BF16))
        y_acc = ph2.enter_context(nc.sbuf_tensor("y_acc", [128, CGN, TBS], F32))
        ycp = ph2.enter_context(nc.sbuf_tensor("ycp", [128, 2, TBS], BF16))
        carry = ph2.enter_context(nc.sbuf_tensor("carry", [128, CGN * DSTT], F32))
        bneg_tb = ph2.enter_context(nc.sbuf_tensor("bneg_tb", [DSTT, TBS], BF16))
        c_tb = ph2.enter_context(nc.sbuf_tensor("c_tb", [DSTT, TBS], BF16))
        onehot = ph2.enter_context(nc.sbuf_tensor("onehot", [DSTT, DSTT, 128], BF16))
        dt_bf = ph2.enter_context(nc.sbuf_tensor("dt_bf", [DTR, TBS], BF16))
        xn_q = ph2.enter_context(nc.sbuf_tensor("xn_q", [128, DG, 512], BF16))
        hbuf = ph2.enter_context(nc.sbuf_tensor("hbuf", [128, HG, 512], BF16))
        h2buf = ph2.enter_context(nc.sbuf_tensor("h2buf", [128, HG, 512], BF16))
        wba = ph2.enter_context(nc.sbuf_tensor("wba", [128, 2, HG * 128], BF16))

        p.dma(sy, onehot[:], dr["onehot16"], "dsyM")

        def chan_gen():
            v.memset(up_h[:, :, 0:3], 0.0)
            p.inc(v.memset(carry[:], 0.0), "vxM")
            p.wait(sy, "cc")
            for tb in range(NTB):
                sy.reg_load(sroff, offt[:1, 0:1])
                sy.reg_add(srtmp, sroff, (((tb * TBS) // TLOC) * DIN) * TLOC
                           + (tb * TBS) % TLOC)
                srcu = bass.AP(ag1_out, srtmp,
                               [[TLOC, 128], [128 * TLOC, CGN], [1, TBS]])
                p.wait(sy, "vxM")
                with nc.allow_non_contiguous_dma("AG gather"):
                    p.inc(sy.dma_start(out=up_h[:, :, 3:3 + TBS], in_=srcu,
                                       bounds_check=None), "dsyM", 16)
                p.wait(v, "dsyM")
                p.wait(s, "dsyM"); p.wait(s, "mmM")
                for cg in range(CGN):
                    p.wait(v, "scM")
                    v.tensor_scalar_mul(tmp_a[:, 0:TBS], up_h[:, cg, 0:TBS],
                                        convw_t[:, cg, 0:1])
                    for j in range(1, 4):
                        ic = v.scalar_tensor_tensor(tmp_a[:, 0:TBS],
                                                    up_h[:, cg, j:j + TBS],
                                                    convw_t[:, cg, j:j + 1],
                                                    tmp_a[:, 0:TBS],
                                                    op0=ALU.mult, op1=ALU.add)
                    p.inc(ic, "vxM")
                    p.inc(v.tensor_copy(up_h[:, cg, 0:3], up_h[:, cg, TBS:TBS + 3]),
                          "vxM")
                    p.wait(s, "vxM")
                    p.inc(s.activation(u_ch[:, cg, :], tmp_a[:, 0:TBS], AF.Silu,
                                       bias=convb_t[:, cg:cg + 1]), "scM")
                p.wait(sy, "scM")
                p.dma(sy, u_spill.ap()[:, tb * TBS:(tb + 1) * TBS]
                      .rearrange("(cg q) t -> q cg t", q=128), u_ch[:], "dsyM")
                p.wait(t, "scM"); p.wait(t, "vxM")
                for nb in range(TBS // 512):
                    for cg in range(CGN):
                        i1 = t.matmul(pb[4 + nb][:96, :], wxp_t[:, cg, :],
                                      u_ch[:, cg, nb * 512:(nb + 1) * 512],
                                      start=(cg == 0), stop=(cg == CGN - 1))
                    p.inc(i1, "mmM")
                p.wait(sy, "mmM")
                for nb in range(TBS // 512):
                    pass
                p.wait(v, "mmM"); p.wait(v, "scM")
                for nb in range(TBS // 512):
                    p.wait(v, "dsyM", at=None if nb == 0 else None)
                    p.inc(v.tensor_copy(tmp_a[:96, nb * 512:(nb + 1) * 512],
                                        pb[4 + nb][:96, :]), "vxM")
                p.wait(sy, "vxM")
                p.dma(sy, ar_in.ap()[:, tb * TBS:(tb + 1) * TBS],
                      tmp_a[:96, 0:TBS], "dsyM")
                yield
            p.wait(g, "dsyM")
            p.inc(g.collective_compute("AllReduce", ALU.add,
                                       replica_groups=[[0, 1, 2, 3], [4, 5, 6, 7]],
                                       ins=[ar_in.ap()], outs=[ar_out.ap()]), "cc")
            yield
            p.wait(sy, "cc")
            crm_hist = {}
            scan_hist = {}
            hpar_hist = {}
            for tb in range(NTB):
                t0 = tb * TBS
                p.wait(sy, "vxM"); p.wait(sy, "scM"); p.wait(sy, "mmM")
                p.dma(sy, u_ch[:], u_spill.ap()[:, t0:t0 + TBS]
                      .rearrange("(cg q) t -> q cg t", q=128), "dsyM")
                p.dma(sy, xdbl_tb[:], ar_out.ap()[:, t0:t0 + TBS], "dsyM")
                p.wait(g, "vxM")
                p.dma(g, bneg_tb[:], ar_out.ap()[64:80, t0:t0 + TBS], "dgpM")
                p.dma(g, c_tb[:], ar_out.ap()[80:96, t0:t0 + TBS], "dgpM")
                p.wait(v, "dsyM"); p.wait(v, "dgpM"); p.wait(v, "mmM")
                p.inc(v.tensor_scalar_mul(bneg_tb[:], bneg_tb[:], -1.0), "vxM")
                p.inc(v.tensor_copy(dt_bf[:], xdbl_tb[0:DTR, :]), "vxM")
                p.wait(t, "scM"); p.wait(t, "vxM")
                p.wait(s, "vxM")
                for cg in range(CGN):
                    for nb in range(TBS // 512):
                        i1 = t.matmul(pb[4 + nb][:, :],
                                      wdt_t[:, cg * 128:(cg + 1) * 128],
                                      dt_bf[:, nb * 512:(nb + 1) * 512],
                                      start=True, stop=True)
                        p.inc(i1, "mmM")
                        p.wait(s, "mmM")
                        p.inc(s.activation(dl_ch[:, cg, nb * 512:(nb + 1) * 512],
                                           pb[4 + nb][:, :], AF.Sigmoid, scale=-1.0,
                                           bias=negbdt_t[:, cg:cg + 1]), "scM")
                        p.wait(t, "scM")
                for cg in range(CGN):
                    p.inc(s.activation(dl_ch[:, cg, :], dl_ch[:, cg, :], AF.Ln), "scM")
                p.wait(v, "scM")
                for cg in range(CGN):
                    p.inc(v.tensor_mul(du_ch[:, cg, :], dl_ch[:, cg, :],
                                       u_ch[:, cg, :]), "vxM")
                p.inc(v.memset(y_acc[:], 0.0), "vxM")
                yield
                for si in range(DSTT):
                    sp = si % 2
                    prev = scan_hist.get(("bc_evac", 1 - sp))
                    if prev is not None:
                        p.wait(t, "scM", at=prev)
                    i1 = None
                    for nb in range(TBS // 512):
                        t.matmul(pb[4 + nb][:, :], onehot[:, si, :],
                                 bneg_tb[:, nb * 512:(nb + 1) * 512],
                                 start=True, stop=True)
                        i1 = t.matmul(pb[6 + nb][:, :], onehot[:, si, :],
                                      c_tb[:, nb * 512:(nb + 1) * 512],
                                      start=True, stop=True)
                    p.inc(i1, "mmM")
                    p.wait(s, "mmM")
                    rd = scan_hist.get(("bc_read", sp))
                    if rd is not None:
                        p.wait(s, "vxM", at=rd)
                    for nb in range(TBS // 512):
                        s.copy(bb_s[:, sp, nb * 512:(nb + 1) * 512], pb[4 + nb][:, :])
                        cnt = p.inc(s.copy(cb_s[:, sp, nb * 512:(nb + 1) * 512],
                                           pb[6 + nb][:, :]), "scM")
                    scan_hist[("bc_evac", sp)] = cnt
                    for cg in range(CGN):
                        k = si * CGN + cg
                        pp = k % 2
                        if pp in scan_hist:
                            p.wait(s, "vxM", at=scan_hist[pp])
                        p.inc(s.activation(da_t[:, pp, :], dl_ch[:, cg, :], AF.Exp,
                                           scale=e_a[:, cg, si:si + 1]), "scM")
                        p.wait(v, "scM")
                        p.inc(v.tensor_mul(dbu_t[:], du_ch[:, cg, :],
                                           bb_s[:, sp, :]), "vxM")
                        cr = crm_hist.get((cg, si))
                        if cr is not None:
                            p.wait(v, "crM", at=cr)
                        hp = hpar_hist.get(pp)
                        if hp is not None:
                            p.wait(v, "crM", at=hp)
                        scan_hist[pp] = p.inc(
                            v.tensor_tensor_scan(h_t[:, pp, :], da_t[:, pp, :],
                                                 dbu_t[:], carry[:, k:k + 1],
                                                 ALU.mult, ALU.add), "vxM")
                        p.wait(s, "vxM")
                        cnt2 = p.inc(s.copy(carry[:, k:k + 1],
                                            h_t[:, pp, TBS - 1:TBS]), "crM")
                        crm_hist[(cg, si)] = cnt2
                        hpar_hist[pp] = cnt2
                        v.tensor_mul(prod_t[:], h_t[:, pp, :], cb_s[:, sp, :])
                        scan_hist[("bc_read", sp)] = p.inc(
                            v.tensor_add(y_acc[:, cg, :], y_acc[:, cg, :],
                                         prod_t[:]), "vxM")
                    yield
                for cg in range(CGN):
                    p.inc(v.scalar_tensor_tensor(y_acc[:, cg, :], u_ch[:, cg, :],
                                                 dvec_t[:, cg:cg + 1], y_acc[:, cg, :],
                                                 op0=ALU.mult, op1=ALU.add), "vxM")
                    p.wait(v, "dsyM")
                    p.inc(v.tensor_copy(ycp[:, cg % 2, :], y_acc[:, cg, :]), "vxM")
                    p.wait(sy, "vxM")
                    p.dma(sy, ag2_in.ap()[cg * 128:(cg + 1) * 128, t0:t0 + TBS],
                          ycp[:, cg % 2, :], "dsyM")
                yield
            p.wait(g, "dsyM")
            p.inc(g.collective_compute("AllGather", ALU.bypass,
                                       replica_groups=[[0, 1, 2, 3], [4, 5, 6, 7]],
                                       ins=[ag2_in.ap()], outs=[ag2_out.ap()]), "cc")

        def ffn_gen():
            for q in range(TLOC // 512):
                q0 = q * 512
                p.wait(sy, "mmF")
                p.dma(sy, xn_q[:], xn_sp.ap()[:, :, q0:q0 + 512], "dsyF")

                def evac_g1(oc, tqi, bank):
                    p.wait(s, "mmF")
                    cnt = p.inc(s.activation(hbuf[:, oc, :], bank[:, :], AF.Silu,
                                             scale=w1s_t[:, oc:oc + 1]), "scF")
                    return ("scF", cnt)

                def evac_g2(oc, tqi, bank):
                    p.wait(v, "mmF"); p.wait(v, "scF")
                    cnt = p.inc(v.scalar_tensor_tensor(hbuf[:, oc, :], bank[:, :],
                                                       w2s_t[:, oc:oc + 1],
                                                       hbuf[:, oc, :],
                                                       op0=ALU.mult, op1=ALU.mult),
                                "vxF")
                    return ("vxF", cnt)

                yield from gemm_gen(lambda oc: dr["w1T"][oc], HG, DG, wba,
                                    lambda kg, tqi: xn_q[:, kg, :], evac_g1,
                                    1, lambda oc, tqi: oc % 2,
                                    pre_wait=("dsyF",), mmtag="mmF")
                yield from gemm_gen(lambda oc: dr["w2T"][oc], HG, DG, wba,
                                    lambda kg, tqi: xn_q[:, kg, :], evac_g2,
                                    1, lambda oc, tqi: 2 + oc % 2,
                                    pre_wait=("scF",), mmtag="mmF")

                def evac_h1(oc, tqi, bank):
                    p.wait(s, "mmF")
                    cnt = p.inc(s.activation(h2buf[:, oc, :], bank[:, :], AF.Gelu,
                                             scale=h1s_t[:, oc:oc + 1]), "scF")
                    return ("scF", cnt)

                yield from gemm_gen(lambda oc: dr["h1T"][oc], HG, HG, wba,
                                    lambda kg, tqi: hbuf[:, kg, :], evac_h1, 1,
                                    lambda oc, tqi: oc % 2, pre_wait=("vxF",),
                                    mmtag="mmF")

                def evac_h2(oc, tqi, bank):
                    p.wait(s, "mmF")
                    cnt = p.inc(s.activation(hbuf[:, oc, :], bank[:, :], AF.Gelu,
                                             scale=h2s_t[:, oc:oc + 1]), "scF")
                    return ("scF", cnt)

                yield from gemm_gen(lambda oc: dr["h2T"][oc], HG, HG, wba,
                                    lambda kg, tqi: h2buf[:, kg, :], evac_h2, 1,
                                    lambda oc, tqi: 2 + oc % 2, pre_wait=("scF",),
                                    mmtag="mmF")

                def evac_w3(oc, tqi, bank):
                    p.wait(s, "mmF"); p.wait(s, "dsyF")
                    cnt = p.inc(s.activation(stg_y[:], bank[:, :], AF.Copy,
                                             scale=w3s_t[:, oc:oc + 1]), "scF")
                    p.wait(sy, "scF")
                    p.dma(sy, ffn_spill.ap()[:, oc, q0:q0 + 512], stg_y[:], "dsyF")
                    return ("scF", cnt)

                yield from gemm_gen(lambda oc: dr["w3T"][oc], DG, HG, wba,
                                    lambda kg, tqi: hbuf[:, kg, :], evac_w3, 1,
                                    lambda oc, tqi: oc % 2, pre_wait=("scF",),
                                    mmtag="mmF")

        chang, ffng = chan_gen(), ffn_gen()
        for _ in range(136):
            next(ffng)
        run_interleaved([(chang, 1), (ffng, 3)])

    ph4 = contextlib.ExitStack()
    with ph4:
        phase_barrier()
        ph4b = ph4.enter_context(contextlib.ExitStack())
        x1b = ph4b.enter_context(nc.sbuf_tensor("x1b", [128, DG, TLOC], F32, side="right"))
        ph4a = ph4.enter_context(contextlib.ExitStack())
        xn2b = ph4a.enter_context(nc.sbuf_tensor("xn2b", [128, DG, TLOC], BF16))
        szq = ph4a.enter_context(nc.sbuf_tensor("szq", [128, DIN // 128, 512], BF16))
        yzq = ph4a.enter_context(nc.sbuf_tensor("yzq", [128, DIN // 128, 512], BF16))
        wbo = ph4a.enter_context(nc.sbuf_tensor("wbo", [128, 2, DIN], BF16))

        p.dma(sy, x1b[:], x1_spill.ap(), "dsy")
        p.dma(sy, xn2b[:], xn2_spill.ap(), "dsy")
        p.wait(t, "dsy"); p.wait(v, "dsy")
        for q in range(TLOC // 512):
            q0 = q * 512

            def evac_z(oc, tqi, bank):
                p.wait(s, "mm"); p.wait(s, "vx")
                cnt = p.inc(s.activation(szq[:, oc, :], bank[:, :], AF.Silu), "sc")
                return ("sc", cnt)

            gemm(lambda oc: dr["w_in_z"][oc], DIN // 128, DG, wbo,
                 lambda kg, tqi: xn2b[:, kg, q0:q0 + 512], evac_z, 1,
                 lambda oc, tqi: 4 + oc % 2)
            g.reg_load(roff, offt[:1, 1:2])
            g.reg_add(rtmp, roff, q0)
            srcy = bass.AP(ag2_out, rtmp, [[T, 128], [128 * T, DIN // 128], [1, 512]])
            p.wait(g, "cc")
            p.wait(g, "vx"); p.wait(g, "mm")
            with nc.allow_non_contiguous_dma("strided gather from AG output"):
                p.inc(g.dma_start(out=yzq[:, :, :], in_=srcy, bounds_check=None),
                      "dgp", 16)
            p.wait(v, "dgp"); p.wait(v, "sc")
            for oc in range(DIN // 128):
                p.inc(v.tensor_mul(yzq[:, oc, :], yzq[:, oc, :], szq[:, oc, :]), "vx")

            def evac_wo(oc, tqi, bank):
                p.wait(v, "mm")
                cnt = p.inc(v.tensor_add(x1b[:, oc, q0:q0 + 512],
                                         x1b[:, oc, q0:q0 + 512], bank[:, :]), "vx")
                return ("vx", cnt)

            gemm(lambda oc: dr["w_out"][oc], DG, DIN // 128, wbo,
                 lambda kg, tqi: yzq[:, kg, :], evac_wo, 1,
                 lambda oc, tqi: oc % 2, pre_wait=("vx",))
        for gi in range(DG):
            p.wait(sy, "vx")
            p.dma(sy, tmp_c[:, 0:TLOC], ffn_spill.ap()[:, gi, :], "dsy")
            p.wait(v, "dsy")
            p.inc(v.tensor_add(x1b[:, gi, :], x1b[:, gi, :], tmp_c[:, 0:TLOC]), "vx")

        if upto == "l0dbg":
            dbg_ffn = nc.dram_tensor("dbg_ffn", [128, DG, TLOC], BF16,
                                     kind="ExternalOutput").ap()
            dbg_y = nc.dram_tensor("dbg_y", [DIN, T], BF16, kind="ExternalOutput").ap()
            dbg_xdbl = nc.dram_tensor("dbg_xdbl", [96, T], F32, kind="ExternalOutput").ap()
            dbg_xn = nc.dram_tensor("dbg_xn", [128, DG, TLOC], BF16,
                                    kind="ExternalOutput").ap()
            dbg_u = nc.dram_tensor("dbg_u", [CH, T], BF16, kind="ExternalOutput").ap()
            dbg_up = nc.dram_tensor("dbg_up", [DIN, TLOC], BF16, kind="ExternalOutput").ap()
            p.dma(g, dbg_up, ag1_in.ap(), "dgp")
            dbg_xn2 = nc.dram_tensor("dbg_xn2", [128, DG, TLOC], BF16, kind="ExternalOutput").ap()
            p.dma(g, dbg_xn2, xn2_spill.ap(), "dgp")
            p.dma(g, dbg_ffn, ffn_spill.ap(), "dgp")
            p.dma(g, dbg_y, ag2_out.ap(), "dgp")
            p.dma(g, dbg_u, u_spill.ap(), "dgp")
            p.dma(g, dbg_xdbl, ar_out.ap(), "dgp")
            p.wait(sy, "vx")
            p.dma(sy, dbg_xn, xn[:], "dsy")
            p.dma(sy, out_ap, x1b[:], "dst")
            p.wait(sy, "dst"); p.wait(sy, "dsy"); p.wait(g, "dgp")
            return p, dr, out_ap
        if upto == "l0":
            p.wait(sy, "vx")
            p.dma(sy, out_ap, x1b[:], "dst")
            p.wait(sy, "dst")
            return p, dr, out_ap

        ph4a.close()
        phase_barrier()
        l1 = ph4.enter_context(contextlib.ExitStack())
        xn1 = l1.enter_context(nc.sbuf_tensor("xn1", [128, DG, TLOC], BF16))

        e2g = l1.enter_context(nc.sbuf_tensor("e2g", [128, 8, DG], BF16))
        e3g = l1.enter_context(nc.sbuf_tensor("e3g", [128, 8, DG], F32))
        e4g = l1.enter_context(nc.sbuf_tensor("e4g", [128, 8, 2, DG], F32))
        bndc = l1.enter_context(nc.sbuf_tensor("bndc", [128, DG], F32))
        kmx = l1.enter_context(nc.sbuf_tensor("kmx", [128, DG], F32))
        aend = l1.enter_context(nc.sbuf_tensor("aend", [128, 2, DG], F32))
        ainb = l1.enter_context(nc.sbuf_tensor("ainb", [128, 2, DG], F32))
        accf = l1.enter_context(nc.sbuf_tensor("accf", [128, 2, DG], F32))
        gq_t = l1.enter_context(nc.sbuf_tensor("gq_t", [128, 24], F32))
        thrc = l1.enter_context(nc.sbuf_tensor("thrc", [128, 2], F32))
        ones_col_f = l1.enter_context(nc.sbuf_tensor("ones_col_f", [128, 1], F32))
        tmp_b = l1.enter_context(nc.sbuf_tensor("tmp_b", [128, TLOC], F32))
        tmp_e = l1.enter_context(nc.sbuf_tensor("tmp_e", [128, TLOC], BF16))

        ln_emit(x1b, DG, xn1, LN_EPS)
        p.wait(sy, "vx")
        with nc.allow_non_contiguous_dma("tiny boundary column"):
            p.dma(sy, e2_in.ap(), xn1[:, :, TLOC - 1:TLOC], "dsy")
        p.wait(g, "dsy")
        p.inc(g.collective_compute("AllGather", ALU.bypass,
                                   replica_groups=[list(range(8))],
                                   ins=[e2_in.ap()], outs=[e2_out.ap()]), "cc")
        p.dma(sy, x1_spill.ap(), x1b[:], "dsy")
        p.wait(v, "dsy"); p.wait(s, "dsy"); p.wait(t, "dsy"); p.wait(g, "dsy")
        ph4b.close()

        p.wait(v, "dsy")
        v.memset(gq_t[:], 0.0)
        p.inc(v.memset(ones_col_f[:], 1.0), "vx")
        with nc.sbuf_tensor("qbuf", [128, HF], F32) as qbuf:
            for mi, mat in enumerate((dr["wfcT"], dr["wprT"])):
                for kg in range(DG):
                    p.wait(g, "vx")
                    p.dma(g, qbuf[:], mat[kg], "dgp")
                    p.wait(v, "dgp")
                    p.inc(v.tensor_reduce(gq_t[:, mi * 8 + kg:mi * 8 + kg + 1],
                                          qbuf[:], axis=AX.X, op=ALU.add,
                                          apply_absolute_value=True), "vx")
            p.wait(t, "vx")
            p.inc(t.matmul(pb[4][:1, 0:16], ones_col_f[:], gq_t[:, 0:16], start=True,
                           stop=True), "mm")
            p.wait(v, "mm")
            v.tensor_reduce(gq_t[:1, 16:17], pb[4][:1, 0:8], axis=AX.X, op=ALU.add)
            p.inc(v.tensor_reduce(gq_t[:1, 17:18], pb[4][:1, 8:16], axis=AX.X,
                                  op=ALU.add), "vx")
            p.wait(v, "vx")
            p.inc(v.tensor_scalar(gq_t[:1, 20:22], gq_t[:1, 16:18], 0.5 / (HF * D),
                                  0.5 * ABS_EPS, op0=ALU.mult, op1=ALU.add), "vx")
            p.wait(t, "vx")
            p.inc(t.matmul(pb[5][:, 0:2], ones_row_f[:], gq_t[:1, 20:22], start=True,
                           stop=True), "mm")
            p.wait(v, "mm")
            p.inc(v.tensor_copy(thrc[:], pb[5][:, 0:2]), "vx")
            for mat, col in ((dr["wfcT"], 0), (dr["wprT"], 1)):
                for kg in range(DG):
                    p.wait(g, "vx"); p.wait(g, "sc"); p.wait(g, "dsy")
                    p.dma(g, qbuf[:], mat[kg], "dgp")
                    p.wait(s, "dgp"); p.wait(s, "vx"); p.wait(s, "dsy")
                    p.inc(s.sign(tmp_c[:, 0:HF] if False else tmp_c[:, 0:TLOC], qbuf[:, 0:TLOC]), "sc")
                    p.inc(s.sign(tmp_e[:, 0:TLOC], qbuf[:, TLOC:HF]), "sc")
                    p.inc(s.activation(tmp_a[:, 0:TLOC], qbuf[:, 0:TLOC], AF.Abs), "sc")
                    p.inc(s.activation(tmp_b[:, 0:TLOC], qbuf[:, TLOC:HF], AF.Abs), "sc")
                    p.wait(v, "sc"); p.wait(v, "dsy")
                    p.inc(v.scalar_tensor_tensor(tmp_c[:, 0:TLOC], tmp_a[:, 0:TLOC],
                                                 thrc[:, col:col + 1],
                                                 tmp_c[:, 0:TLOC],
                                                 op0=ALU.is_ge, op1=ALU.mult), "vx")
                    p.inc(v.scalar_tensor_tensor(tmp_e[:, 0:TLOC], tmp_b[:, 0:TLOC],
                                                 thrc[:, col:col + 1],
                                                 tmp_e[:, 0:TLOC],
                                                 op0=ALU.is_ge, op1=ALU.mult), "vx")
                    p.wait(sy, "vx")
                    if col == 0:
                        dst0 = wq_spill.ap()[4 * kg:4 * kg + 2].rearrange("b p k m -> p b k m")
                        dst1 = wq_spill.ap()[4 * kg + 2:4 * kg + 4].rearrange("b p k m -> p b k m")
                    else:
                        dst0 = wqp_spill.ap()[kg][:, 0:HG // 2, :]
                        dst1 = wqp_spill.ap()[kg][:, HG // 2:, :]
                    with nc.allow_non_contiguous_dma("band-order spill"):
                        p.dma(sy, dst0, tmp_c[:, 0:TLOC], "dsy")
                        p.dma(sy, dst1, tmp_e[:, 0:TLOC], "dsy")
        with contextlib.ExitStack() as lx:
            hq = lx.enter_context(nc.sbuf_tensor("hq", [128, HG, 512], BF16))
            wqa = lx.enter_context(nc.sbuf_tensor("wqa", [128, 2, HG * 128], BF16))
            lnstg = lx.enter_context(nc.sbuf_tensor("lnstg", [128, 512], F32))
            wrk = lx.enter_context(nc.sbuf_tensor("wrk", [128, 2, D], BF16))
            kvstg = lx.enter_context(nc.sbuf_tensor("kvstg", [128, 512], BF16))
            lxm = lx.enter_context(contextlib.ExitStack())
            xmix = lxm.enter_context(nc.sbuf_tensor("xmix", [128, DG, TLOC], BF16))
            rwkv_tiles = {}

            def ffn1_gen():
                for q in range(TLOC // 512):
                    q0 = q * 512

                    def evac_fc(oc, tqi, bank):
                        p.wait(s, "mmF")
                        cnt = p.inc(s.activation(hq[:, oc, :], bank[:, :], AF.Gelu),
                                    "scF")
                        return ("scF", cnt)

                    yield from gemm_gen(lambda oc: wq_spill.ap()[oc], HG, DG, wqa,
                                        lambda kg, tqi: xn1[:, kg, q0:q0 + 512],
                                        evac_fc, 1, lambda oc, tqi: 4 + oc % 2,
                                        pre_wait=("vx", "scF"), mmtag="mmF")
                    if upto == "fulldbg" and q == 0:
                        d_hq0 = nc.dram_tensor("d_hq0", [128, HG, 512], BF16,
                                               kind="ExternalOutput").ap()
                        p.wait(sy, "scF")
                        p.dma(sy, d_hq0[:], hq[:], "dsyF")
                        p.wait(v, "dsyF"); p.wait(s, "dsyF"); p.wait(t, "dsyF")
                    p.wait(v, "scF")
                    ln_emit(hq, HG, hq, LN_EPS, gamma_ap=lnw_t, beta_ap=lnb_t,
                            tcols=512, vxt="vxF", sct="scF", mmt="mmF", ps0=6,
                            lnstg=lnstg)
                    if upto == "fulldbg" and q == 0:
                        d_hq1 = nc.dram_tensor("d_hq1", [128, HG, 512], BF16,
                                               kind="ExternalOutput").ap()
                        p.wait(sy, "vxF")
                        p.dma(sy, d_hq1[:], hq[:], "dsyF")
                        p.wait(t, "dsyF")
                    yield

                    def evac_pr(oc, tqi, bank):
                        p.wait(s, "mmF"); p.wait(s, "dsyF")
                        cnt = p.inc(s.copy(stg_y[:], bank[:, :]), "scF")
                        p.wait(sy, "scF")
                        p.dma(sy, ffn_spill.ap()[:, oc, q0:q0 + 512], stg_y[:], "dsyF")
                        return ("scF", cnt)

                    yield from gemm_gen(lambda oc: wqp_spill.ap()[oc], DG, HG, wqa,
                                        lambda kg, tqi: hq[:, kg, :], evac_pr, 1,
                                        lambda oc, tqi: 4 + oc % 2,
                                        pre_wait=("vxF",), mmtag="mmF")

            def rwkv_gen():
                p.wait(g, "cc")
                p.dma(g, e2g[:], e2_out.ap().rearrange("j p g -> p j g"), "dgpM")
                p.wait(v, "dgpM")
                v.memset(bndc[:], 0.0)
                for j in range(8):
                    ia = v.scalar_tensor_tensor(bndc[:], e2g[:, j, :], selp[:, j:j + 1],
                                                bndc[:], op0=ALU.mult, op1=ALU.add)
                p.inc(ia, "vxM")
                p.wait(v, "dsy")
                yield
                for mx, wname, mixcol in ((0, "wkT", 2), (1, "wvT", 3), (2, "wrT", 4)):
                    p.wait(v, "mmM")
                    for gi in range(DG):
                        v.tensor_copy(tmp_c[:, 1:TLOC], xn1[:, gi, 0:TLOC - 1])
                        v.tensor_copy(tmp_c[:, 0:1], bndc[:, gi:gi + 1])
                        v.tensor_scalar_mul(tmp_a[:, 0:TLOC], tmp_c[:, 0:TLOC],
                                            omix[:, gi:gi + 1, mx]
                                            .rearrange("p g -> p g"))
                        ia = v.scalar_tensor_tensor(xmix[:, gi, :], xn1[:, gi, :],
                                                    tvec_t[:, gi:gi + 1, mixcol]
                                                    .rearrange("p g -> p g"),
                                                    tmp_a[:, 0:TLOC],
                                                    op0=ALU.mult, op1=ALU.add)
                        p.inc(ia, "vxM")
                        if gi % 4 == 3:
                            yield

                    def evac_kv(oc, tqi, bank, mx=mx):
                        p.wait(s, "mmM"); p.wait(s, "dsyM")
                        if mx == 2:
                            cnt = p.inc(s.activation(kvstg[:], bank[:, :],
                                                     AF.Sigmoid), "scM")
                        else:
                            cnt = p.inc(s.copy(kvstg[:], bank[:, :]), "scM")
                        p.wait(sy, "scM")
                        p.dma(sy, rwkv_spill.ap()[mx, oc, :, tqi * 512:(tqi + 1) * 512],
                              kvstg[:], "dsyM")
                        return ("scM", cnt)

                    yield from gemm_gen(lambda oc: dr[wname][oc], DG, DG, wrk,
                                        lambda kg, tqi: xmix[:, kg,
                                                             tqi * 512:(tqi + 1) * 512],
                                        evac_kv, 4,
                                        lambda oc, tqi: (oc * 4 + tqi) % 4,
                                        pre_wait=("vxM",), mmtag="mmM")
                lxm.close()
                kgt = lx.enter_context(nc.sbuf_tensor("kgt", [128, TLOC], BF16))
                vgt = lx.enter_context(nc.sbuf_tensor("vgt", [128, TLOC], BF16))
                sgt = lx.enter_context(nc.sbuf_tensor("sgt", [128, TLOC], BF16))
                ekv_t = lx.enter_context(nc.sbuf_tensor("ekv_t", [128, TLOC], F32))
                nmt = lx.enter_context(nc.sbuf_tensor("nmt", [128, 1 + TLOC], F32))
                dnt = lx.enter_context(nc.sbuf_tensor("dnt", [128, 1 + TLOC], F32))
                tmp_d = lx.enter_context(nc.sbuf_tensor("tmp_d", [128, TLOC], F32))
                rwkv_tiles["tmp_d"] = tmp_d
                ones2k = lx.enter_context(nc.sbuf_tensor("ones2k", [128, TLOC], BF16))
                iota_t = lx.enter_context(nc.sbuf_tensor("iota_t", [128, TLOC], F32))
                rwkv_tiles["ywo_tq"] = lx.enter_context(
                    nc.sbuf_tensor("ywo_tq", [128, DG, 512], BF16))
                rwkv_tiles["f1t"] = lx.enter_context(
                    nc.sbuf_tensor("f1t", [128, 512], BF16))
                ywo_tq, f1t = rwkv_tiles["ywo_tq"], rwkv_tiles["f1t"]
                p.wait(g, "vxM"); p.wait(g, "mmM"); p.wait(g, "scM")
                p.inc(g.iota(iota_t[:], pattern=[[1, TLOC]], base=1,
                             channel_multiplier=0,
                             allow_small_or_imprecise_dtypes=True), "gpM")
                p.wait(v, "mmM"); p.wait(v, "scM")
                p.inc(v.memset(ones2k[:], 1.0), "vxM")
                for gi in range(DG):
                    p.wait(g, "vxM")
                    p.dma(g, kgt[:], rwkv_spill.ap()[0, gi], "dgpM")
                    p.wait(v, "dgpM")
                    p.inc(v.reduce_max(kmx[:, gi:gi + 1], kgt[:], axis=AX.X), "vxM")
                    yield
                p.wait(sy, "vxM")
                p.dma(sy, e3_in.ap(), kmx[:], "dsyM")
                p.wait(g, "dsyM")
                p.inc(g.collective_compute("AllGather", ALU.bypass,
                                           replica_groups=[list(range(8))],
                                           ins=[e3_in.ap()], outs=[e3_out.ap()]), "cc")
                yield
                p.wait(g, "cc")
                p.dma(g, e3g[:], e3_out.ap().rearrange("j p g -> p j g"), "dgpM")
                p.wait(v, "dgpM")
                v.memset(kmx[:], -1e38)
                for j in range(8):
                    ia = v.scalar_tensor_tensor(kmx[:], e3g[:, j, :],
                                                selc_pen[:, j:j + 1], kmx[:],
                                                op0=ALU.add, op1=ALU.max)
                p.inc(ia, "vxM")
                p.inc(v.tensor_scalar_mul(kmx[:], kmx[:], -1.0), "vxM")
                yield
                for gi in range(DG):
                    p.wait(g, "vxM"); p.wait(g, "scM")
                    p.dma(g, kgt[:], rwkv_spill.ap()[0, gi], "dgpM")
                    p.dma(g, vgt[:], rwkv_spill.ap()[1, gi], "dgpM")
                    p.wait(s, "dgpM"); p.wait(s, "vxM")
                    p.inc(s.activation(tmp_b[:, 0:TLOC], kgt[:], AF.Exp,
                                       bias=kmx[:, gi:gi + 1]), "scM")
                    p.wait(v, "scM"); p.wait(v, "dgpM")
                    p.inc(v.tensor_mul(ekv_t[:], tmp_b[:, 0:TLOC], vgt[:]), "vxM")
                    v.tensor_scalar_mul(tmp_a[:, 0:TLOC], ones2k[:],
                                        wlane[:, gi:gi + 1, 1].rearrange("p g -> p g"))
                    p.inc(v.tensor_tensor_scan(nmt[:, 1:], tmp_a[:, 0:TLOC], ekv_t[:],
                                               0.0, ALU.mult, ALU.add), "vxM")
                    p.inc(v.tensor_tensor_scan(dnt[:, 1:], tmp_a[:, 0:TLOC],
                                               tmp_b[:, 0:TLOC], 0.0, ALU.mult,
                                               ALU.add), "vxM")
                    v.tensor_copy(aend[:, 0, gi:gi + 1], nmt[:, TLOC:TLOC + 1])
                    p.inc(v.tensor_copy(aend[:, 1, gi:gi + 1], dnt[:, TLOC:TLOC + 1]),
                          "vxM")
                    yield
                p.wait(sy, "vxM")
                p.dma(sy, e4_in.ap(), aend[:], "dsyM")
                p.wait(g, "dsyM")
                p.inc(g.collective_compute("AllGather", ALU.bypass,
                                           replica_groups=[list(range(8))],
                                           ins=[e4_in.ap()], outs=[e4_out.ap()]), "cc")
                yield
                p.wait(g, "cc")
                p.dma(g, e4g[:], e4_out.ap().rearrange("j p c g -> p j c g"), "dgpM")
                p.wait(v, "dgpM")
                v.memset(ainb[:], 0.0)
                for j in range(8):
                    if j in (0, 4):
                        v.tensor_copy(accf[:], e4g[:, j, :, :])
                    else:
                        for comp in range(2):
                            v.tensor_mul(accf[:, comp, :], accf[:, comp, :],
                                         wlane[:, :, 2])
                            v.tensor_add(accf[:, comp, :], accf[:, comp, :],
                                         e4g[:, j, comp, :])
                    ia = None
                    for comp in range(2):
                        ia = v.scalar_tensor_tensor(ainb[:, comp, :], accf[:, comp, :],
                                                    selp[:, j:j + 1], ainb[:, comp, :],
                                                    op0=ALU.mult, op1=ALU.add)
                p.inc(ia, "vxM")
                yield
                for gi in range(DG):
                    p.wait(g, "vxM"); p.wait(g, "scM")
                    p.dma(g, kgt[:], rwkv_spill.ap()[0, gi], "dgpM")
                    p.dma(g, vgt[:], rwkv_spill.ap()[1, gi], "dgpM")
                    p.dma(g, sgt[:], rwkv_spill.ap()[2, gi], "dgpM")
                    p.wait(s, "dgpM"); p.wait(s, "vxM")
                    p.inc(s.activation(tmp_b[:, 0:TLOC], kgt[:], AF.Exp,
                                       bias=kmx[:, gi:gi + 1]), "scM")
                    p.inc(s.activation(tmp_d[:], iota_t[:], AF.Exp,
                                       scale=wlane[:, gi:gi + 1, 0]
                                       .rearrange("p g -> p g")), "scM")
                    if upto == "fulldbg" and gi == 2:
                        d_p3 = nc.dram_tensor("d_p3", [128, 2, TLOC], F32,
                                              kind="ExternalOutput").ap()
                        p.wait(sy, "scM")
                        p.dma(sy, d_p3[:, 0, :], tmp_d[:], "dsyM")
                        p.dma(sy, d_p3[:, 1, :], tmp_b[:, 0:TLOC], "dsyM")
                        p.wait(v, "dsyM"); p.wait(s, "dsyM")
                    p.wait(v, "scM"); p.wait(v, "dgpM")
                    p.inc(v.tensor_mul(ekv_t[:], tmp_b[:, 0:TLOC], vgt[:]), "vxM")
                    v.tensor_scalar_mul(tmp_a[:, 0:TLOC], ones2k[:],
                                        wlane[:, gi:gi + 1, 1].rearrange("p g -> p g"))
                    p.inc(v.tensor_tensor_scan(nmt[:, 1:], tmp_a[:, 0:TLOC], ekv_t[:],
                                               0.0, ALU.mult, ALU.add), "vxM")
                    p.inc(v.tensor_tensor_scan(dnt[:, 1:], tmp_a[:, 0:TLOC],
                                               tmp_b[:, 0:TLOC], 0.0, ALU.mult,
                                               ALU.add), "vxM")
                    p.wait(v, "vxM")
                    p.inc(v.scalar_tensor_tensor(nmt[:, 1:], tmp_d[:],
                                                 ainb[:, 0, gi:gi + 1], nmt[:, 1:],
                                                 op0=ALU.mult, op1=ALU.add), "vxM")
                    p.inc(v.scalar_tensor_tensor(dnt[:, 1:], tmp_d[:],
                                                 ainb[:, 1, gi:gi + 1], dnt[:, 1:],
                                                 op0=ALU.mult, op1=ALU.add), "vxM")
                    v.tensor_copy(nmt[:, 0:1], ainb[:, 0, gi:gi + 1])
                    p.inc(v.tensor_copy(dnt[:, 0:1], ainb[:, 1, gi:gi + 1]), "vxM")
                    p.inc(v.tensor_scalar_mul(ekv_t[:], ekv_t[:],
                                              wlane[:, gi:gi + 1, 3]
                                              .rearrange("p g -> p g")), "vxM")
                    p.inc(v.tensor_scalar_mul(tmp_b[:, 0:TLOC], tmp_b[:, 0:TLOC],
                                              wlane[:, gi:gi + 1, 3]
                                              .rearrange("p g -> p g")), "vxM")
                    p.inc(v.tensor_add(ekv_t[:], ekv_t[:], nmt[:, 0:TLOC]), "vxM")
                    p.inc(v.tensor_add(tmp_b[:, 0:TLOC], tmp_b[:, 0:TLOC],
                                       dnt[:, 0:TLOC]), "vxM")
                    p.wait(s, "vxM")
                    s.activation(tmp_d[:], tmp_b[:, 0:TLOC], AF.Ln)
                    p.inc(s.activation(tmp_d[:], tmp_d[:], AF.Exp, scale=-1.0), "scM")
                    if upto == "fulldbg" and gi == 2:
                        d_p2 = nc.dram_tensor("d_p2", [128, 4, TLOC], F32,
                                              kind="ExternalOutput").ap()
                        p.wait(sy, "scM"); p.wait(sy, "vxM")
                        p.dma(sy, d_p2[:, 0, :], tmp_b[:, 0:TLOC], "dsyM")
                        p.dma(sy, d_p2[:, 1, :], tmp_d[:], "dsyM")
                        p.dma(sy, d_p2[:, 2, :], ekv_t[:], "dsyM")
                        p.dma(sy, d_p2[:, 3, :], dnt[:, 0:TLOC], "dsyM")
                        p.wait(v, "dsyM")
                    p.wait(v, "scM")
                    p.inc(v.tensor_mul(ekv_t[:], ekv_t[:], tmp_d[:]), "vxM")
                    p.wait(v, "dsyM")
                    p.inc(v.tensor_mul(tmp_c[:, 0:TLOC], ekv_t[:], sgt[:]), "vxM")
                    p.wait(sy, "vxM")
                    p.dma(sy, rwkv_spill.ap()[3, gi], tmp_c[:, 0:TLOC], "dsyM")
                    yield
            def merge_gen():
                ywo_tq, f1t = rwkv_tiles["ywo_tq"], rwkv_tiles["f1t"]
                tmp_d = rwkv_tiles["tmp_d"]
                p.wait(g, "mmM"); p.wait(g, "scM"); p.wait(g, "vxM")
                wo_hist = {}
                for tqi in range(TLOC // 512):
                    q0 = tqi * 512
                    p.wait(sy, "mmM"); p.wait(sy, "dsyM")
                    p.dma(sy, ywo_tq[:],
                          rwkv_spill.ap()[3, :, :, q0:q0 + 512]
                          .rearrange("kg p t -> p kg t"), "dsyM")
                    for oc in range(DG):
                        par = oc % 2
                        key = ("wo", par)
                        if key in wo_hist:
                            p.wait(g, "mmM", at=wo_hist[key])
                        ldc = p.dma(g, wrk[:, par, :], dr["woT"][oc], f"wo{par}")
                        bk = oc % 4
                        if ("bank", bk) in wo_hist:
                            nm, cnt = wo_hist[("bank", bk)]
                            p.wait(t, nm, at=cnt)
                        p.wait(t, f"wo{par}", at=ldc)
                        p.wait(t, "dsyM")
                        for kg in range(DG):
                            i1 = t.matmul(pb[bk][:, :], wrk[:, par, kg * 128:(kg + 1) * 128],
                                          ywo_tq[:, kg, :], start=(kg == 0),
                                          stop=(kg == DG - 1))
                        wo_hist[key] = p.inc(i1, "mmM")
                        p.wait(g, "vxM")
                        p.dma(g, tmp_d[:, 0:512], x1_spill.ap()[:, oc, q0:q0 + 512],
                              "dgpM")
                        p.wait(g, "dsyF")
                        p.dma(g, f1t[:], ffn_spill.ap()[:, oc, q0:q0 + 512], "dgpM")
                        p.wait(v, "mmM"); p.wait(v, "dgpM"); p.wait(v, "dst")
                        v.tensor_add(tmp_a[:, 0:512], pb[bk][:, :], f1t[:])
                        cnt2 = p.inc(v.tensor_add(tmp_a[:, 0:512], tmp_a[:, 0:512],
                                                  tmp_d[:, 0:512]), "vxM")
                        wo_hist[("bank", bk)] = ("vxM", cnt2)
                        p.wait(sy, "vxM")
                        p.dma(sy, out_ap[:, oc, q0:q0 + 512], tmp_a[:, 0:512], "dst")
                    yield

            run_interleaved([(rwkv_gen(), 1), (ffn1_gen(), 2)])
            for _ in merge_gen():
                pass
            if upto == "fulldbg":
                d_kvs = nc.dram_tensor("d_kvs", [4, DG, 128, TLOC], BF16,
                                       kind="ExternalOutput").ap()
                d_f1 = nc.dram_tensor("d_f1", [128, DG, TLOC], BF16,
                                      kind="ExternalOutput").ap()
                d_sm = nc.dram_tensor("d_sm", [128, 64], F32,
                                      kind="ExternalOutput").ap()
                p.dma(g, d_kvs, rwkv_spill.ap(), "dgp")
                p.dma(g, d_f1, ffn_spill.ap(), "dgp")
                p.wait(sy, "vxM")
                p.dma(sy, d_sm[:, 0:8], kmx[:], "dsy")
                p.dma(sy, d_sm[:, 8:24], ainb[:].rearrange("p c g -> p (c g)"), "dsy")
                p.dma(sy, d_sm[:, 24:40], aend[:].rearrange("p c g -> p (c g)"), "dsy")
                p.dma(sy, d_sm[:, 40:42], thrc[:], "dsy")
                d_wqp = nc.dram_tensor("d_wqp", [DG, 128, HG, 128], BF16,
                                       kind="ExternalOutput").ap()
                p.dma(g, d_wqp, wqp_spill.ap(), "dgp")
                p.wait(sy, "dsy"); p.wait(g, "dgp")
            p.wait(sy, "dst")
    return p, dr, out_ap


def _band(w_T, n_oc, n_kg):
    K, M = w_T.shape
    assert K == n_kg * 128 and M == n_oc * 128
    return np.ascontiguousarray(
        w_T.reshape(n_kg, 128, n_oc, 128).transpose(2, 1, 0, 3))


def _prep_core_inputs(inputs, core):
    t0 = (core % 4) * TLOC
    b = core // 4
    r = {}

    def fm(x2d):
        f = x2d.shape[1]
        return np.ascontiguousarray(
            x2d.T.reshape(f // 128, 128, x2d.shape[0]).transpose(1, 0, 2))

    def pcol(vec, ng):
        return np.ascontiguousarray(np.asarray(vec).reshape(ng, 128).T)

    A = lambda k: np.asarray(inputs[k])
    r["x0T"] = fm(A("L0_attn_x")[b, t0:t0 + TLOC, :])
    r["l0_gamma"] = pcol(A("L0_attn_gamma"), DG)
    r["l0_beta"] = pcol(A("L0_attn_beta"), DG)
    w_in = A("L0_attn_W_in")
    r["w_in_u"] = _band(w_in[:, :DIN], DIN // 128, DG)
    r["w_in_z"] = _band(w_in[:, DIN:], DIN // 128, DG)
    chb = (core % 4) * CH
    cw = A("L0_attn_conv_w")[:, 0, :]
    r["conv_w"] = np.ascontiguousarray(
        cw[chb:chb + CH].reshape(CGN, 128, 4).transpose(1, 0, 2))
    r["conv_b"] = pcol(A("L0_attn_conv_b")[chb:chb + CH], CGN)
    r["w_xproj"] = np.ascontiguousarray(
        A("L0_attn_W_xproj")[chb:chb + CH].reshape(CGN, 128, 96).transpose(1, 0, 2))
    r["w_dt"] = np.ascontiguousarray(A("L0_attn_W_dt")[:, chb:chb + CH])
    r["b_dt"] = pcol(A("L0_attn_b_dt")[chb:chb + CH], CGN)
    r["a_log"] = np.ascontiguousarray(
        A("L0_attn_A_log")[chb:chb + CH].reshape(CGN, 128, DSTT).transpose(1, 0, 2))
    r["d_vec"] = pcol(A("L0_attn_D")[chb:chb + CH], CGN)
    r["w_out"] = _band(A("L0_attn_W_out"), DG, DIN // 128)
    r["w1T"] = _band(A("L0_ffn_w1_w").T, HG, DG)
    r["w2T"] = _band(A("L0_ffn_w2_w").T, HG, DG)
    r["w1s"] = pcol(A("L0_ffn_w1_s"), HG)
    r["w2s"] = pcol(A("L0_ffn_w2_s"), HG)
    r["h1T"] = _band(A("L0_ffn_h1_w").T, HG, HG)
    r["h2T"] = _band(A("L0_ffn_h2_w").T, HG, HG)
    r["h1s"] = pcol(A("L0_ffn_h1_s"), HG)
    r["h2s"] = pcol(A("L0_ffn_h2_s"), HG)
    r["w3T"] = _band(A("L0_ffn_w3_w").T, DG, HG)
    r["w3s"] = pcol(A("L0_ffn_w3_s"), DG)
    tv = np.stack([A("L1_attn_time_decay").reshape(D),
                   A("L1_attn_time_first").reshape(D),
                   A("L1_attn_time_mix_k").reshape(D),
                   A("L1_attn_time_mix_v").reshape(D),
                   A("L1_attn_time_mix_r").reshape(D)], axis=1)
    r["tvec"] = np.ascontiguousarray(tv.reshape(DG, 128, 5).transpose(1, 0, 2))
    r["wkT"] = _band(A("L1_attn_Wk").T, DG, DG)
    r["wvT"] = _band(A("L1_attn_Wv").T, DG, DG)
    r["wrT"] = _band(A("L1_attn_Wr").T, DG, DG)
    r["woT"] = _band(A("L1_attn_Wo").T, DG, DG)
    bfc = _band(A("L1_ffn_W_fc").T, HG, DG)
    r["wfcT"] = np.ascontiguousarray(
        bfc.reshape(DG, 4, 128, DG, 128).transpose(0, 2, 1, 3, 4)
        .reshape(DG, 128, HF))
    r["lnw"] = pcol(A("L1_ffn_ln_w"), HG)
    r["lnb"] = pcol(A("L1_ffn_ln_b"), HG)
    bpr = _band(A("L1_ffn_W_proj").T, DG, HG)
    r["wprT"] = np.ascontiguousarray(bpr.reshape(DG, 128, HG * 128))
    sp = np.zeros((1, 8), np.float32)
    if core % 4 != 0:
        sp[0, core - 1] = 1.0
    r["selprev"] = sp
    scv = np.zeros((1, 8), np.float32)
    for j in (range(4) if core < 4 else range(4, 8)):
        scv[0, j] = 1.0
    r["selchain"] = scv
    r["selpen"] = np.where(scv > 0, 0.0, -1e38).astype(np.float32)
    r["dynoff"] = np.array([[chb * TLOC, t0, 0, 0]], np.int32)
    import ml_dtypes
    oh = np.zeros((DSTT, DSTT, 128), np.float32)
    for si_ in range(DSTT):
        oh[si_, si_, :] = 1.0
    r["onehot16"] = oh.astype(ml_dtypes.bfloat16)
    for k in list(r):
        if r[k].dtype == np.float64:
            r[k] = r[k].astype(np.float32)
    return r


def kernel(**inputs):
    prog, dr, out_ap = build_program("full")
    maps = [_prep_core_inputs(inputs, c) for c in range(8)]
    res = run_bass_kernel_spmd(prog.nc, maps, core_ids=list(range(8)))
    out = np.empty((B, T, D), np.float32)
    for c in range(8):
        b, t0 = c // 4, (c % 4) * TLOC
        o = res.results[c]["outT"]
        out[b, t0:t0 + TLOC, :] = o.transpose(2, 1, 0).reshape(TLOC, D)
    return out


# version 15
# speedup vs baseline: 1.3118x; 1.0501x over previous
import contextlib

import numpy as np

import concourse.bass as bass
import concourse.mybir as mybir
from concourse.bass_utils import run_bass_kernel_spmd

F32 = mybir.dt.float32
BF16 = mybir.dt.bfloat16
I32 = mybir.dt.int32
AF = mybir.ActivationFunctionType
ALU = mybir.AluOpType
AX = mybir.AxisListType

B, T, D = 2, 8192, 1024
TLOC = 2048
DG = D // 128
DIN = 2048
DSTT = 16
DTR = 64
CH = DIN // 4
CGN = CH // 128
HF = 4096
HG = HF // 128
LN_EPS = 1e-5
ABS_EPS = 1e-6
TBS = 1024
NTB = T // TBS


class Prog:
    def __init__(self):
        self.nc = bass.Bass("TRN2", target_bir_lowering=False, debug=False)
        self.stack = contextlib.ExitStack()
        self._sems = {}

    def sem(self, name):
        if name not in self._sems:
            h = self.stack.enter_context(self.nc.semaphore(name))
            self._sems[name] = [h, 0]
        return self._sems[name][0]

    def count(self, name):
        self.sem(name)
        return self._sems[name][1]

    def inc(self, instr, name, n=1):
        self.sem(name)
        rec = self._sems[name]
        instr.then_inc(rec[0], n)
        rec[1] += n
        return rec[1]

    def dma(self, eng, out, in_, name):
        return self.inc(eng.dma_start(out=out, in_=in_), name, 16)

    def wait(self, eng, name, at=None):
        self.sem(name)
        rec = self._sems[name]
        val = rec[1] if at is None else at
        if val > 0:
            eng.wait_ge(rec[0], val)


def build_program(upto="full"):
    p = Prog()
    nc = p.nc
    v, s, t, g, sy = nc.vector, nc.scalar, nc.tensor, nc.gpsimd, nc.sync
    sb = p.stack.enter_context

    dr = {}

    def din(name, shape, dtype=F32):
        dr[name] = nc.dram_tensor(name, list(shape), dtype, kind="ExternalInput").ap()
        return dr[name]

    din("x0T", [128, DG, TLOC])
    out_ap = nc.dram_tensor("outT", [128, DG, TLOC], F32, kind="ExternalOutput").ap()

    din("l0_gamma", [128, DG]); din("l0_beta", [128, DG])
    din("w_in_u", [DIN // 128, 128, DG, 128])
    din("w_in_z", [DIN // 128, 128, DG, 128])
    din("conv_w", [128, CGN, 4]); din("conv_b", [128, CGN])
    din("w_xproj", [128, CGN, 96])
    din("w_dt", [DTR, CH])
    din("b_dt", [128, CGN])
    din("a_log", [128, CGN, DSTT])
    din("d_vec", [128, CGN])
    din("w_out", [DG, 128, DIN // 128, 128])
    din("w1T", [HG, 128, DG, 128]); din("w2T", [HG, 128, DG, 128])
    din("w1s", [128, HG]); din("w2s", [128, HG])
    din("h1T", [HG, 128, HG, 128]); din("h2T", [HG, 128, HG, 128])
    din("h1s", [128, HG]); din("h2s", [128, HG])
    din("w3T", [DG, 128, HG, 128]); din("w3s", [128, DG])
    din("tvec", [128, DG, 5])
    din("wkT", [DG, 128, DG, 128]); din("wvT", [DG, 128, DG, 128])
    din("wrT", [DG, 128, DG, 128]); din("woT", [DG, 128, DG, 128])
    din("wfcT", [DG, 128, DG, 128, 4] if False else [DG, 128, HF])
    din("lnw", [128, HG]); din("lnb", [128, HG])
    din("wprT", [DG, 128, HG * 128])
    din("selprev", [1, 8]); din("selchain", [1, 8]); din("selpen", [1, 8])
    din("onehot16", [DSTT, DSTT, 128], BF16)
    din("dynoff", [1, 4], I32)

    ag1_in = nc.dram_tensor("ag1_in", [DIN, TLOC], BF16)
    ag1_out = nc.dram_tensor("ag1_out", [4, DIN, TLOC], BF16)
    ar_in = nc.dram_tensor("ar_in", [96, T], F32)
    ar_out = nc.dram_tensor("ar_out", [96, T], F32)
    ag2_in = nc.dram_tensor("ag2_in", [CH, T], BF16)
    ag2_out = nc.dram_tensor("ag2_out", [DIN, T], BF16)
    u_spill = nc.dram_tensor("u_spill", [CH, T], BF16)
    x1_spill = nc.dram_tensor("x1_spill", [128, DG, TLOC], F32)
    xn2_spill = nc.dram_tensor("xn2_spill", [128, DG, TLOC], BF16)
    xn_sp = nc.dram_tensor("xn_sp", [128, DG, TLOC], BF16)
    ffn_spill = nc.dram_tensor("ffn_spill", [128, DG, TLOC], BF16)
    rwkv_spill = nc.dram_tensor("rwkv_spill", [4, DG, 128, TLOC], BF16)
    wq_spill = nc.dram_tensor("wq_spill", [HG, 128, DG, 128], BF16)
    wqp_spill = nc.dram_tensor("wqp_spill", [DG, 128, HG, 128], BF16)
    e2_out = nc.dram_tensor("e2_out", [8, 128, DG], BF16)
    e2_in = nc.dram_tensor("e2_in", [128, DG], BF16)
    e3_out = nc.dram_tensor("e3_out", [8, 128, DG], F32)
    e3_in = nc.dram_tensor("e3_in", [128, DG], F32)
    e4_out = nc.dram_tensor("e4_out", [8, 128, 2, DG], F32)
    e4_in = nc.dram_tensor("e4_in", [128, 2, DG], F32)

    gamma_t = sb(nc.sbuf_tensor("gamma_t", [128, DG], F32))
    beta_t = sb(nc.sbuf_tensor("beta_t", [128, DG], F32))
    convb_t = sb(nc.sbuf_tensor("convb_t", [128, CGN], F32))
    negbdt_t = sb(nc.sbuf_tensor("negbdt_t", [128, CGN], F32))
    dvec_t = sb(nc.sbuf_tensor("dvec_t", [128, CGN], F32))
    convw_t = sb(nc.sbuf_tensor("convw_t", [128, CGN, 4], F32))
    e_a = sb(nc.sbuf_tensor("e_a", [128, CGN, DSTT], F32))
    wxp_t = sb(nc.sbuf_tensor("wxp_t", [128, CGN, 96], BF16))
    wdt_t = sb(nc.sbuf_tensor("wdt_t", [DTR, CH], BF16))
    tvec_t = sb(nc.sbuf_tensor("tvec_t", [128, DG, 5], F32))
    wlane = sb(nc.sbuf_tensor("wlane", [128, DG, 5], F32))
    omix = sb(nc.sbuf_tensor("omix", [128, DG, 3], F32))
    w1s_t = sb(nc.sbuf_tensor("w1s_t", [128, HG], F32))
    w2s_t = sb(nc.sbuf_tensor("w2s_t", [128, HG], F32))
    h1s_t = sb(nc.sbuf_tensor("h1s_t", [128, HG], F32))
    h2s_t = sb(nc.sbuf_tensor("h2s_t", [128, HG], F32))
    w3s_t = sb(nc.sbuf_tensor("w3s_t", [128, DG], F32))
    lnw_t = sb(nc.sbuf_tensor("lnw_t", [128, HG], F32))
    lnb_t = sb(nc.sbuf_tensor("lnb_t", [128, HG], F32))
    selp = sb(nc.sbuf_tensor("selp", [128, 8], F32))
    selc_pen = sb(nc.sbuf_tensor("selc_pen", [128, 8], F32))
    selrow = sb(nc.sbuf_tensor("selrow", [1, 24], F32))
    ones_col = sb(nc.sbuf_tensor("ones_col", [128, 1], BF16))
    ones_row = sb(nc.sbuf_tensor("ones_row", [1, 128], BF16))
    ones_row_f = sb(nc.sbuf_tensor("ones_row_f", [1, 128], F32))
    offt = sb(nc.sbuf_tensor("offt", [1, 4], I32))
    stat = sb(nc.sbuf_tensor("stat", [128, 32], F32))
    srow = sb(nc.sbuf_tensor("srow", [1, 4, 512], F32))
    stg_x = sb(nc.sbuf_tensor("stg_x", [128, 2, 512], BF16))
    stg_q = sb(nc.sbuf_tensor("stg_q", [128, 2, 512], BF16))
    stg_y = sb(nc.sbuf_tensor("stg_y", [128, 512], BF16))
    epsc = sb(nc.sbuf_tensor("epsc", [1, 2], F32))
    tmp_a = sb(nc.sbuf_tensor("tmp_a", [128, TLOC], F32))
    tmp_c = sb(nc.sbuf_tensor("tmp_c", [128, TLOC], BF16))

    xn_scope = contextlib.ExitStack()
    xn = xn_scope.enter_context(nc.sbuf_tensor("xn", [128, DG, TLOC], BF16))
    pb = [sb(nc.psum_tensor(f"pb{i}", [128, 512], F32)) for i in range(8)]
    roff = sb(g.register("roff"))
    rtmp = sb(g.register("rtmp"))
    sroff = sb(sy.register("sroff"))
    srtmp = sb(sy.register("srtmp"))

    uid = [0]

    def gemm_gen(wdram, n_oc, n_kg, wtile, rhs_fn, evac_fn, n_tq, bank_of,
                 pre_wait=(), mmtag="mm"):
        uid[0] += 1
        ldsem = [f"gw{uid[0]}a", f"gw{uid[0]}b"]
        for nm in pre_wait:
            p.wait(t, nm)
        m0 = p.count(mmtag)
        last_use = [m0, m0]
        bank_free = {}
        for oc in range(n_oc):
            par = oc % 2
            p.wait(g, mmtag, at=last_use[par])
            ldc = p.dma(g, wtile[:, par, 0:n_kg * 128], wdram(oc), ldsem[par])
            for tqi in range(n_tq):
                bk = bank_of(oc, tqi)
                if bk in bank_free:
                    nm, cnt = bank_free[bk]
                    p.wait(t, nm, at=cnt)
                p.wait(t, ldsem[par], at=ldc)
                for kg in range(n_kg):
                    i1 = t.matmul(pb[bk][:, :], wtile[:, par, kg * 128:(kg + 1) * 128],
                                  rhs_fn(kg, tqi), start=(kg == 0), stop=(kg == n_kg - 1))
                mmc = p.inc(i1, mmtag)
                last_use[par] = mmc
                bank_free[bk] = evac_fn(oc, tqi, pb[bk])
            yield

    def gemm(*args, **kw):
        for _ in gemm_gen(*args, **kw):
            pass

    def run_interleaved(gens):
        active = list(gens)
        while active:
            keep = []
            for gen, w in active:
                alive = True
                for _ in range(w):
                    try:
                        next(gen)
                    except StopIteration:
                        alive = False
                        break
                if alive:
                    keep.append((gen, w))
            active = keep

    def phase_barrier(sems=("dsy", "dgp", "vx", "sc", "mm", "gp", "cc",
                            "dsyM", "dgpM", "vxM", "scM", "mmM", "crM",
                            "dsyF", "vxF", "scF", "mmF")):
        for nm in sems:
            for eng in (v, s, t, g, sy):
                p.wait(eng, nm)

    def ln_emit(src, n_g, dst_bf, eps, plain_cb=None, gamma_ap=None, beta_ap=None,
                tcols=TLOC, toff=0, vxt="vx", sct="sc", mmt="mm", ps0=4,
                stgx=None, stgq=None, lnstg=None):
        inv_n = 1.0 / (n_g * 128)
        stgx = stgx if stgx is not None else stg_x
        stgq = stgq if stgq is not None else stg_q
        aply = lnstg if lnstg is not None else tmp_a
        p.wait(s, vxt)
        ps_x, ps_q = pb[ps0], pb[ps0 + 1]
        hist = {}
        for blk in range(tcols // 512):
            t0 = toff + blk * 512
            for gi in range(n_g):
                gp_ = gi % 2
                h_ = hist.get(("stg", gp_))
                if h_ is not None:
                    p.wait(v, mmt, at=h_[0]); p.wait(s, mmt, at=h_[0])
                p.inc(v.tensor_copy(stgx[:, gp_, :], src[:, gi, t0:t0 + 512]), vxt)
                p.inc(s.activation(stgq[:, gp_, :], src[:, gi, t0:t0 + 512],
                                   AF.Square), sct)
                p.wait(t, vxt); p.wait(t, sct)
                last = gi == n_g - 1
                t.matmul(ps_x[:1, :], ones_col[:], stgx[:, gp_, :],
                         start=(gi == 0), stop=last)
                i2 = t.matmul(ps_q[:1, :], ones_col[:], stgq[:, gp_, :],
                              start=(gi == 0), stop=last)
                hist[("stg", gp_)] = (p.inc(i2, mmt), None)
            p.wait(v, mmt)
            p.inc(v.tensor_scalar_mul(srow[:1, 0, :], ps_x[:1, :], inv_n), vxt)
            v.tensor_mul(srow[:1, 1, :], srow[:1, 0, :], srow[:1, 0, :])
            p.inc(v.scalar_tensor_tensor(srow[:1, 1, :], ps_q[:1, :], inv_n,
                                         srow[:1, 1, :], op0=ALU.mult,
                                         op1=ALU.subtract), vxt)
            p.wait(s, vxt)
            s.activation(srow[:1, 2, :], srow[:1, 1, :], AF.Ln, bias=epsc[:1, 0:1])
            p.inc(s.activation(srow[:1, 2, :], srow[:1, 2, :], AF.Exp, scale=-0.5), sct)
            p.wait(v, sct)
            v.tensor_mul(srow[:1, 3, :], srow[:1, 0, :], srow[:1, 2, :])
            p.inc(v.tensor_scalar_mul(srow[:1, 3, :], srow[:1, 3, :], -1.0), vxt)
            p.wait(t, vxt)
            t.matmul(ps_x[:, :], ones_row_f[:], srow[:1, 2, :], start=True, stop=True)
            p.inc(t.matmul(ps_q[:, :], ones_row_f[:], srow[:1, 3, :], start=True,
                           stop=True), mmt)
            p.wait(v, mmt)
            for gi in range(n_g):
                v.tensor_mul(aply[:, 0:512], src[:, gi, t0:t0 + 512], ps_x[:, :])
                ia = v.tensor_add(aply[:, 0:512], aply[:, 0:512], ps_q[:, :])
                if gamma_ap is not None:
                    ia = v.tensor_scalar(dst_bf[:, gi, t0:t0 + 512], aply[:, 0:512],
                                         gamma_ap[:, gi:gi + 1], beta_ap[:, gi:gi + 1],
                                         op0=ALU.mult, op1=ALU.add)
                else:
                    ia = v.tensor_copy(dst_bf[:, gi, t0:t0 + 512], aply[:, 0:512])
                p.inc(ia, vxt)
                if plain_cb is not None:
                    plain_cb(gi, t0)

    def chain_select(out_col_fn, gath_fn, n_items):
        for it in range(n_items):
            v.memset(tmp_a[:, 0:DG], 0.0)
            for j in range(8):
                ia = v.scalar_tensor_tensor(tmp_a[:, 0:DG], gath_fn(it, j),
                                            selp[:, j:j + 1], tmp_a[:, 0:DG],
                                            op0=ALU.mult, op1=ALU.add)
            p.inc(v.tensor_copy(out_col_fn(it), tmp_a[:, 0:DG]), "vx")

    with contextlib.ExitStack() as ph01:
        x1 = ph01.enter_context(nc.sbuf_tensor("x1", [128, DG, TLOC], F32))
        xn2 = ph01.enter_context(nc.sbuf_tensor("xn2", [128, DG, TLOC], BF16))
        wz = ph01.enter_context(nc.sbuf_tensor("wz", [128, 2, DG * 128], BF16))

        p.dma(sy, x1[:], dr["x0T"], "dsy")
        for nm, tl in [("l0_gamma", gamma_t), ("l0_beta", beta_t), ("conv_b", convb_t),
                       ("d_vec", dvec_t), ("conv_w", convw_t), ("a_log", e_a),
                       ("tvec", tvec_t), ("w1s", w1s_t), ("w2s", w2s_t),
                       ("h1s", h1s_t), ("h2s", h2s_t), ("w3s", w3s_t),
                       ("lnw", lnw_t), ("lnb", lnb_t)]:
            p.dma(sy, tl[:], dr[nm], "dsy")
        p.dma(sy, stat[:, 0:CGN], dr["b_dt"], "dsy")
        p.dma(sy, selrow[:, 0:8], dr["selprev"], "dsy")
        p.dma(sy, selrow[:, 8:16], dr["selchain"], "dsy")
        p.dma(sy, selrow[:, 16:24], dr["selpen"], "dsy")
        p.dma(g, wxp_t[:], dr["w_xproj"], "dgp")
        p.dma(g, wdt_t[:], dr["w_dt"], "dgp")
        p.dma(g, offt[:], dr["dynoff"], "dgp")

        p.wait(v, "dsy")
        v.memset(epsc[:], LN_EPS)
        v.memset(ones_col[:], 1.0)
        v.memset(ones_row[:], 1.0)
        v.memset(ones_row_f[:], 1.0)
        p.inc(v.tensor_scalar_mul(negbdt_t[:], stat[:, 0:CGN], -1.0), "vx")
        for j in range(3):
            p.inc(v.tensor_scalar(omix[:, :, j], tvec_t[:, :, 2 + j], -1.0, 1.0,
                                  op0=ALU.mult, op1=ALU.add), "vx")
        p.wait(s, "dsy")
        p.inc(s.activation(e_a[:], e_a[:], AF.Exp), "sc")
        p.inc(s.activation(wlane[:, :, 0], tvec_t[:, :, 0], AF.Exp), "sc")
        p.wait(v, "sc")
        p.inc(v.tensor_scalar_mul(wlane[:, :, 0], wlane[:, :, 0], -1.0), "vx")
        p.wait(s, "vx")
        p.inc(s.activation(wlane[:, :, 1], wlane[:, :, 0], AF.Exp), "sc")
        p.inc(s.activation(wlane[:, :, 2], wlane[:, :, 0], AF.Exp, scale=float(TLOC)), "sc")
        p.inc(s.activation(wlane[:, :, 3], tvec_t[:, :, 1], AF.Exp), "sc")
        p.wait(t, "dsy"); p.wait(t, "vx")
        p.inc(t.matmul(pb[0][:, 0:24], ones_row_f[:], selrow[:1, :], start=True,
                       stop=True), "mm")
        p.wait(v, "mm")
        p.inc(v.tensor_copy(selp[:], pb[0][:, 0:8]), "vx")
        p.inc(v.tensor_copy(selc_pen[:], pb[0][:, 16:24]), "vx")

        def add_xn(gi, t0):
            p.inc(v.tensor_add(x1[:, gi, t0:t0 + 512], x1[:, gi, t0:t0 + 512],
                               tmp_a[:, 0:512]), "vx")

        ln_emit(x1, DG, xn, LN_EPS, plain_cb=add_xn)
        ln_emit(xn, DG, xn2, LN_EPS, gamma_ap=gamma_t, beta_ap=beta_t)

        def evac_u(oc, tqi, bank):
            p.wait(s, "mm")
            p.wait(s, "dsy")
            cnt = p.inc(s.copy(stg_y[:], bank[:, :]), "sc")
            p.wait(sy, "sc")
            p.dma(sy, ag1_in.ap()[oc * 128:(oc + 1) * 128, tqi * 512:(tqi + 1) * 512],
                  stg_y[:], "dsy")
            return ("sc", cnt)

        gemm(lambda oc: dr["w_in_u"][oc], DIN // 128, DG, wz,
             lambda kg, tqi: xn2[:, kg, tqi * 512:(tqi + 1) * 512],
             evac_u, 4, lambda oc, tqi: (oc * 4 + tqi) % 4, pre_wait=("vx",))

        p.wait(g, "dsy")
        p.inc(g.collective_compute("AllGather", ALU.bypass,
                                   replica_groups=[[0, 1, 2, 3], [4, 5, 6, 7]],
                                   ins=[ag1_in.ap()], outs=[ag1_out.ap()]), "cc")
        p.wait(sy, "vx")
        p.dma(sy, x1_spill.ap(), x1[:], "dsy")
        p.dma(sy, xn2_spill.ap(), xn2[:], "dsy")
        p.dma(sy, xn_sp.ap(), xn[:], "dsy")
        p.wait(g, "dsy")

    xn_scope.close()
    with contextlib.ExitStack() as ph2:
        phase_barrier(("dsy", "dgp", "vx", "sc", "mm", "gp"))
        xdbl_tb = ph2.enter_context(nc.sbuf_tensor("xdbl_tb", [96, TBS], F32))
        up_h = ph2.enter_context(nc.sbuf_tensor("up_h", [128, CGN, 3 + TBS], BF16))
        u_ch = ph2.enter_context(nc.sbuf_tensor("u_ch", [128, CGN, TBS], BF16))
        dl_ch = ph2.enter_context(nc.sbuf_tensor("dl_ch", [128, CGN, TBS], BF16))
        du_ch = ph2.enter_context(nc.sbuf_tensor("du_ch", [128, CGN, TBS], BF16))
        bb_s = ph2.enter_context(nc.sbuf_tensor("bb_s", [128, 2, TBS], BF16))
        cb_s = ph2.enter_context(nc.sbuf_tensor("cb_s", [128, 2, TBS], BF16))
        da_t = ph2.enter_context(nc.sbuf_tensor("da_t", [128, 2, TBS], BF16))
        dbu_t = ph2.enter_context(nc.sbuf_tensor("dbu_t", [128, TBS], BF16))
        h_t = ph2.enter_context(nc.sbuf_tensor("h_t", [128, 2, TBS], BF16))
        prod_t = ph2.enter_context(nc.sbuf_tensor("prod_t", [128, TBS], BF16))
        y_acc = ph2.enter_context(nc.sbuf_tensor("y_acc", [128, CGN, TBS], F32))
        ycp = ph2.enter_context(nc.sbuf_tensor("ycp", [128, 2, TBS], BF16))
        carry = ph2.enter_context(nc.sbuf_tensor("carry", [128, CGN * DSTT], F32))
        bneg_tb = ph2.enter_context(nc.sbuf_tensor("bneg_tb", [DSTT, TBS], BF16))
        c_tb = ph2.enter_context(nc.sbuf_tensor("c_tb", [DSTT, TBS], BF16))
        onehot = ph2.enter_context(nc.sbuf_tensor("onehot", [DSTT, DSTT, 128], BF16))
        dt_bf = ph2.enter_context(nc.sbuf_tensor("dt_bf", [DTR, TBS], BF16))
        xn_q = ph2.enter_context(nc.sbuf_tensor("xn_q", [128, DG, 512], BF16))
        hbuf = ph2.enter_context(nc.sbuf_tensor("hbuf", [128, HG, 512], BF16))
        h2buf = ph2.enter_context(nc.sbuf_tensor("h2buf", [128, HG, 512], BF16))
        wba = ph2.enter_context(nc.sbuf_tensor("wba", [128, 2, HG * 128], BF16))

        p.dma(sy, onehot[:], dr["onehot16"], "dsyM")

        def chan_gen():
            v.memset(up_h[:, :, 0:3], 0.0)
            p.inc(v.memset(carry[:], 0.0), "vxM")
            p.wait(sy, "cc")
            for tb in range(NTB):
                sy.reg_load(sroff, offt[:1, 0:1])
                sy.reg_add(srtmp, sroff, (((tb * TBS) // TLOC) * DIN) * TLOC
                           + (tb * TBS) % TLOC)
                srcu = bass.AP(ag1_out, srtmp,
                               [[TLOC, 128], [128 * TLOC, CGN], [1, TBS]])
                p.wait(sy, "vxM")
                with nc.allow_non_contiguous_dma("AG gather"):
                    p.inc(sy.dma_start(out=up_h[:, :, 3:3 + TBS], in_=srcu,
                                       bounds_check=None), "dsyM", 16)
                p.wait(v, "dsyM")
                p.wait(s, "dsyM"); p.wait(s, "mmM")
                for cg in range(CGN):
                    p.wait(v, "scM")
                    v.tensor_scalar_mul(tmp_a[:, 0:TBS], up_h[:, cg, 0:TBS],
                                        convw_t[:, cg, 0:1])
                    for j in range(1, 4):
                        ic = v.scalar_tensor_tensor(tmp_a[:, 0:TBS],
                                                    up_h[:, cg, j:j + TBS],
                                                    convw_t[:, cg, j:j + 1],
                                                    tmp_a[:, 0:TBS],
                                                    op0=ALU.mult, op1=ALU.add)
                    p.inc(ic, "vxM")
                    p.inc(v.tensor_copy(up_h[:, cg, 0:3], up_h[:, cg, TBS:TBS + 3]),
                          "vxM")
                    p.wait(s, "vxM")
                    p.inc(s.activation(u_ch[:, cg, :], tmp_a[:, 0:TBS], AF.Silu,
                                       bias=convb_t[:, cg:cg + 1]), "scM")
                p.wait(sy, "scM")
                p.dma(sy, u_spill.ap()[:, tb * TBS:(tb + 1) * TBS]
                      .rearrange("(cg q) t -> q cg t", q=128), u_ch[:], "dsyM")
                p.wait(t, "scM"); p.wait(t, "vxM")
                for nb in range(TBS // 512):
                    for cg in range(CGN):
                        i1 = t.matmul(pb[4 + nb][:96, :], wxp_t[:, cg, :],
                                      u_ch[:, cg, nb * 512:(nb + 1) * 512],
                                      start=(cg == 0), stop=(cg == CGN - 1))
                    p.inc(i1, "mmM")
                p.wait(sy, "mmM")
                for nb in range(TBS // 512):
                    pass
                p.wait(v, "mmM"); p.wait(v, "scM")
                for nb in range(TBS // 512):
                    p.wait(v, "dsyM", at=None if nb == 0 else None)
                    p.inc(v.tensor_copy(tmp_a[:96, nb * 512:(nb + 1) * 512],
                                        pb[4 + nb][:96, :]), "vxM")
                p.wait(sy, "vxM")
                p.dma(sy, ar_in.ap()[:, tb * TBS:(tb + 1) * TBS],
                      tmp_a[:96, 0:TBS], "dsyM")
                yield
            p.wait(g, "dsyM")
            p.inc(g.collective_compute("AllReduce", ALU.add,
                                       replica_groups=[[0, 1, 2, 3], [4, 5, 6, 7]],
                                       ins=[ar_in.ap()], outs=[ar_out.ap()]), "cc")
            yield
            p.wait(sy, "cc")
            crm_hist = {}
            scan_hist = {}
            hpar_hist = {}
            for tb in range(NTB):
                t0 = tb * TBS
                p.wait(sy, "vxM"); p.wait(sy, "scM"); p.wait(sy, "mmM")
                p.dma(sy, u_ch[:], u_spill.ap()[:, t0:t0 + TBS]
                      .rearrange("(cg q) t -> q cg t", q=128), "dsyM")
                p.dma(sy, xdbl_tb[:], ar_out.ap()[:, t0:t0 + TBS], "dsyM")
                p.wait(g, "vxM")
                p.dma(g, bneg_tb[:], ar_out.ap()[64:80, t0:t0 + TBS], "dgpM")
                p.dma(g, c_tb[:], ar_out.ap()[80:96, t0:t0 + TBS], "dgpM")
                p.wait(v, "dsyM"); p.wait(v, "dgpM"); p.wait(v, "mmM")
                p.inc(v.tensor_scalar_mul(bneg_tb[:], bneg_tb[:], -1.0), "vxM")
                p.inc(v.tensor_copy(dt_bf[:], xdbl_tb[0:DTR, :]), "vxM")
                p.wait(t, "scM"); p.wait(t, "vxM")
                p.wait(s, "vxM")
                for cg in range(CGN):
                    for nb in range(TBS // 512):
                        i1 = t.matmul(pb[4 + nb][:, :],
                                      wdt_t[:, cg * 128:(cg + 1) * 128],
                                      dt_bf[:, nb * 512:(nb + 1) * 512],
                                      start=True, stop=True)
                        p.inc(i1, "mmM")
                        p.wait(s, "mmM")
                        p.inc(s.activation(dl_ch[:, cg, nb * 512:(nb + 1) * 512],
                                           pb[4 + nb][:, :], AF.Sigmoid, scale=-1.0,
                                           bias=negbdt_t[:, cg:cg + 1]), "scM")
                        p.wait(t, "scM")
                for cg in range(CGN):
                    p.inc(s.activation(dl_ch[:, cg, :], dl_ch[:, cg, :], AF.Ln), "scM")
                p.wait(v, "scM")
                for cg in range(CGN):
                    p.inc(v.tensor_mul(du_ch[:, cg, :], dl_ch[:, cg, :],
                                       u_ch[:, cg, :]), "vxM")
                p.inc(v.memset(y_acc[:], 0.0), "vxM")
                yield
                for si in range(DSTT):
                    sp = si % 2
                    prev = scan_hist.get(("bc_evac", 1 - sp))
                    if prev is not None:
                        p.wait(t, "scM", at=prev)
                    i1 = None
                    for nb in range(TBS // 512):
                        t.matmul(pb[4 + nb][:, :], onehot[:, si, :],
                                 bneg_tb[:, nb * 512:(nb + 1) * 512],
                                 start=True, stop=True)
                        i1 = t.matmul(pb[6 + nb][:, :], onehot[:, si, :],
                                      c_tb[:, nb * 512:(nb + 1) * 512],
                                      start=True, stop=True)
                    p.inc(i1, "mmM")
                    p.wait(s, "mmM")
                    rd = scan_hist.get(("bc_read", sp))
                    if rd is not None:
                        p.wait(s, "vxM", at=rd)
                    for nb in range(TBS // 512):
                        s.copy(bb_s[:, sp, nb * 512:(nb + 1) * 512], pb[4 + nb][:, :])
                        cnt = p.inc(s.copy(cb_s[:, sp, nb * 512:(nb + 1) * 512],
                                           pb[6 + nb][:, :]), "scM")
                    scan_hist[("bc_evac", sp)] = cnt
                    for cg in range(CGN):
                        k = si * CGN + cg
                        pp = k % 2
                        if pp in scan_hist:
                            p.wait(s, "vxM", at=scan_hist[pp])
                        p.inc(s.activation(da_t[:, pp, :], dl_ch[:, cg, :], AF.Exp,
                                           scale=e_a[:, cg, si:si + 1]), "scM")
                        p.wait(v, "scM")
                        p.inc(v.tensor_mul(dbu_t[:], du_ch[:, cg, :],
                                           bb_s[:, sp, :]), "vxM")
                        cr = crm_hist.get((cg, si))
                        if cr is not None:
                            p.wait(v, "crM", at=cr)
                        hp = hpar_hist.get(pp)
                        if hp is not None:
                            p.wait(v, "crM", at=hp)
                        scan_hist[pp] = p.inc(
                            v.tensor_tensor_scan(h_t[:, pp, :], da_t[:, pp, :],
                                                 dbu_t[:], carry[:, k:k + 1],
                                                 ALU.mult, ALU.add), "vxM")
                        p.wait(s, "vxM")
                        cnt2 = p.inc(s.copy(carry[:, k:k + 1],
                                            h_t[:, pp, TBS - 1:TBS]), "crM")
                        crm_hist[(cg, si)] = cnt2
                        hpar_hist[pp] = cnt2
                        v.tensor_mul(prod_t[:], h_t[:, pp, :], cb_s[:, sp, :])
                        scan_hist[("bc_read", sp)] = p.inc(
                            v.tensor_add(y_acc[:, cg, :], y_acc[:, cg, :],
                                         prod_t[:]), "vxM")
                    yield
                for cg in range(CGN):
                    p.inc(v.scalar_tensor_tensor(y_acc[:, cg, :], u_ch[:, cg, :],
                                                 dvec_t[:, cg:cg + 1], y_acc[:, cg, :],
                                                 op0=ALU.mult, op1=ALU.add), "vxM")
                    p.wait(v, "dsyM")
                    p.inc(v.tensor_copy(ycp[:, cg % 2, :], y_acc[:, cg, :]), "vxM")
                    p.wait(sy, "vxM")
                    p.dma(sy, ag2_in.ap()[cg * 128:(cg + 1) * 128, t0:t0 + TBS],
                          ycp[:, cg % 2, :], "dsyM")
                yield
            p.wait(g, "dsyM")
            p.inc(g.collective_compute("AllGather", ALU.bypass,
                                       replica_groups=[[0, 1, 2, 3], [4, 5, 6, 7]],
                                       ins=[ag2_in.ap()], outs=[ag2_out.ap()]), "cc")

        def ffn_gen():
            front_mm = None
            for q in range(TLOC // 512):
                q0 = q * 512
                if front_mm is not None:
                    p.wait(sy, "mmF", at=front_mm)
                p.dma(sy, xn_q[:], xn_sp.ap()[:, :, q0:q0 + 512], "dsyF")

                def evac_g1(oc, tqi, bank):
                    p.wait(s, "mmF")
                    cnt = p.inc(s.activation(hbuf[:, oc, :], bank[:, :], AF.Silu,
                                             scale=w1s_t[:, oc:oc + 1]), "scF")
                    return ("scF", cnt)

                def evac_g2(oc, tqi, bank):
                    p.wait(v, "mmF"); p.wait(v, "scF")
                    cnt = p.inc(v.scalar_tensor_tensor(hbuf[:, oc, :], bank[:, :],
                                                       w2s_t[:, oc:oc + 1],
                                                       hbuf[:, oc, :],
                                                       op0=ALU.mult, op1=ALU.mult),
                                "vxF")
                    return ("vxF", cnt)

                yield from gemm_gen(lambda oc: dr["w1T"][oc], HG, DG, wba,
                                    lambda kg, tqi: xn_q[:, kg, :], evac_g1,
                                    1, lambda oc, tqi: oc % 2,
                                    pre_wait=("dsyF",), mmtag="mmF")
                yield from gemm_gen(lambda oc: dr["w2T"][oc], HG, DG, wba,
                                    lambda kg, tqi: xn_q[:, kg, :], evac_g2,
                                    1, lambda oc, tqi: 2 + oc % 2,
                                    pre_wait=("scF",), mmtag="mmF")
                front_mm = p.count("mmF")

                def evac_h1(oc, tqi, bank):
                    p.wait(s, "mmF")
                    cnt = p.inc(s.activation(h2buf[:, oc, :], bank[:, :], AF.Gelu,
                                             scale=h1s_t[:, oc:oc + 1]), "scF")
                    return ("scF", cnt)

                yield from gemm_gen(lambda oc: dr["h1T"][oc], HG, HG, wba,
                                    lambda kg, tqi: hbuf[:, kg, :], evac_h1, 1,
                                    lambda oc, tqi: oc % 2, pre_wait=("vxF",),
                                    mmtag="mmF")

                def evac_h2(oc, tqi, bank):
                    p.wait(s, "mmF")
                    cnt = p.inc(s.activation(hbuf[:, oc, :], bank[:, :], AF.Gelu,
                                             scale=h2s_t[:, oc:oc + 1]), "scF")
                    return ("scF", cnt)

                yield from gemm_gen(lambda oc: dr["h2T"][oc], HG, HG, wba,
                                    lambda kg, tqi: h2buf[:, kg, :], evac_h2, 1,
                                    lambda oc, tqi: 2 + oc % 2, pre_wait=("scF",),
                                    mmtag="mmF")

                def evac_w3(oc, tqi, bank):
                    p.wait(s, "mmF"); p.wait(s, "dsyF")
                    cnt = p.inc(s.activation(stg_y[:], bank[:, :], AF.Copy,
                                             scale=w3s_t[:, oc:oc + 1]), "scF")
                    p.wait(sy, "scF")
                    p.dma(sy, ffn_spill.ap()[:, oc, q0:q0 + 512], stg_y[:], "dsyF")
                    return ("scF", cnt)

                yield from gemm_gen(lambda oc: dr["w3T"][oc], DG, HG, wba,
                                    lambda kg, tqi: hbuf[:, kg, :], evac_w3, 1,
                                    lambda oc, tqi: oc % 2, pre_wait=("scF",),
                                    mmtag="mmF")

        chang, ffng = chan_gen(), ffn_gen()
        for _ in range(64):
            next(ffng)
        run_interleaved([(chang, 1), (ffng, 3)])

    ph4 = contextlib.ExitStack()
    with ph4:
        phase_barrier()
        ph4b = ph4.enter_context(contextlib.ExitStack())
        x1b = ph4b.enter_context(nc.sbuf_tensor("x1b", [128, DG, TLOC], F32, side="right"))
        ph4a = ph4.enter_context(contextlib.ExitStack())
        xn2b = ph4a.enter_context(nc.sbuf_tensor("xn2b", [128, DG, TLOC], BF16))
        szq = ph4a.enter_context(nc.sbuf_tensor("szq", [128, DIN // 128, 512], BF16))
        yzq = ph4a.enter_context(nc.sbuf_tensor("yzq", [128, DIN // 128, 512], BF16))
        wbo = ph4a.enter_context(nc.sbuf_tensor("wbo", [128, 2, DIN], BF16))

        p.dma(sy, x1b[:], x1_spill.ap(), "dsy")
        p.dma(sy, xn2b[:], xn2_spill.ap(), "dsy")
        p.wait(t, "dsy"); p.wait(v, "dsy")
        for q in range(TLOC // 512):
            q0 = q * 512

            def evac_z(oc, tqi, bank):
                p.wait(s, "mm"); p.wait(s, "vx")
                cnt = p.inc(s.activation(szq[:, oc, :], bank[:, :], AF.Silu), "sc")
                return ("sc", cnt)

            gemm(lambda oc: dr["w_in_z"][oc], DIN // 128, DG, wbo,
                 lambda kg, tqi: xn2b[:, kg, q0:q0 + 512], evac_z, 1,
                 lambda oc, tqi: 4 + oc % 2)
            g.reg_load(roff, offt[:1, 1:2])
            g.reg_add(rtmp, roff, q0)
            srcy = bass.AP(ag2_out, rtmp, [[T, 128], [128 * T, DIN // 128], [1, 512]])
            p.wait(g, "cc")
            p.wait(g, "vx"); p.wait(g, "mm")
            with nc.allow_non_contiguous_dma("strided gather from AG output"):
                p.inc(g.dma_start(out=yzq[:, :, :], in_=srcy, bounds_check=None),
                      "dgp", 16)
            p.wait(v, "dgp"); p.wait(v, "sc")
            for oc in range(DIN // 128):
                p.inc(v.tensor_mul(yzq[:, oc, :], yzq[:, oc, :], szq[:, oc, :]), "vx")

            def evac_wo(oc, tqi, bank):
                p.wait(v, "mm")
                cnt = p.inc(v.tensor_add(x1b[:, oc, q0:q0 + 512],
                                         x1b[:, oc, q0:q0 + 512], bank[:, :]), "vx")
                return ("vx", cnt)

            gemm(lambda oc: dr["w_out"][oc], DG, DIN // 128, wbo,
                 lambda kg, tqi: yzq[:, kg, :], evac_wo, 1,
                 lambda oc, tqi: oc % 2, pre_wait=("vx",))
        for gi in range(DG):
            p.wait(sy, "vx")
            p.dma(sy, tmp_c[:, 0:TLOC], ffn_spill.ap()[:, gi, :], "dsy")
            p.wait(v, "dsy")
            p.inc(v.tensor_add(x1b[:, gi, :], x1b[:, gi, :], tmp_c[:, 0:TLOC]), "vx")

        if upto == "l0dbg":
            dbg_ffn = nc.dram_tensor("dbg_ffn", [128, DG, TLOC], BF16,
                                     kind="ExternalOutput").ap()
            dbg_y = nc.dram_tensor("dbg_y", [DIN, T], BF16, kind="ExternalOutput").ap()
            dbg_xdbl = nc.dram_tensor("dbg_xdbl", [96, T], F32, kind="ExternalOutput").ap()
            dbg_xn = nc.dram_tensor("dbg_xn", [128, DG, TLOC], BF16,
                                    kind="ExternalOutput").ap()
            dbg_u = nc.dram_tensor("dbg_u", [CH, T], BF16, kind="ExternalOutput").ap()
            dbg_up = nc.dram_tensor("dbg_up", [DIN, TLOC], BF16, kind="ExternalOutput").ap()
            p.dma(g, dbg_up, ag1_in.ap(), "dgp")
            dbg_xn2 = nc.dram_tensor("dbg_xn2", [128, DG, TLOC], BF16, kind="ExternalOutput").ap()
            p.dma(g, dbg_xn2, xn2_spill.ap(), "dgp")
            p.dma(g, dbg_ffn, ffn_spill.ap(), "dgp")
            p.dma(g, dbg_y, ag2_out.ap(), "dgp")
            p.dma(g, dbg_u, u_spill.ap(), "dgp")
            p.dma(g, dbg_xdbl, ar_out.ap(), "dgp")
            p.wait(sy, "vx")
            p.dma(sy, dbg_xn, xn[:], "dsy")
            p.dma(sy, out_ap, x1b[:], "dst")
            p.wait(sy, "dst"); p.wait(sy, "dsy"); p.wait(g, "dgp")
            return p, dr, out_ap
        if upto == "l0":
            p.wait(sy, "vx")
            p.dma(sy, out_ap, x1b[:], "dst")
            p.wait(sy, "dst")
            return p, dr, out_ap

        ph4a.close()
        phase_barrier()
        l1 = ph4.enter_context(contextlib.ExitStack())
        xn1 = l1.enter_context(nc.sbuf_tensor("xn1", [128, DG, TLOC], BF16))

        e2g = l1.enter_context(nc.sbuf_tensor("e2g", [128, 8, DG], BF16))
        e3g = l1.enter_context(nc.sbuf_tensor("e3g", [128, 8, DG], F32))
        e4g = l1.enter_context(nc.sbuf_tensor("e4g", [128, 8, 2, DG], F32))
        bndc = l1.enter_context(nc.sbuf_tensor("bndc", [128, DG], F32))
        kmx = l1.enter_context(nc.sbuf_tensor("kmx", [128, DG], F32))
        aend = l1.enter_context(nc.sbuf_tensor("aend", [128, 2, DG], F32))
        ainb = l1.enter_context(nc.sbuf_tensor("ainb", [128, 2, DG], F32))
        accf = l1.enter_context(nc.sbuf_tensor("accf", [128, 2, DG], F32))
        gq_t = l1.enter_context(nc.sbuf_tensor("gq_t", [128, 24], F32))
        thrc = l1.enter_context(nc.sbuf_tensor("thrc", [128, 2], F32))
        ones_col_f = l1.enter_context(nc.sbuf_tensor("ones_col_f", [128, 1], F32))
        tmp_b = l1.enter_context(nc.sbuf_tensor("tmp_b", [128, TLOC], F32))
        tmp_e = l1.enter_context(nc.sbuf_tensor("tmp_e", [128, TLOC], BF16))

        ln_emit(x1b, DG, xn1, LN_EPS)
        p.wait(sy, "vx")
        with nc.allow_non_contiguous_dma("tiny boundary column"):
            p.dma(sy, e2_in.ap(), xn1[:, :, TLOC - 1:TLOC], "dsy")
        p.wait(g, "dsy")
        p.inc(g.collective_compute("AllGather", ALU.bypass,
                                   replica_groups=[list(range(8))],
                                   ins=[e2_in.ap()], outs=[e2_out.ap()]), "cc")
        p.dma(sy, x1_spill.ap(), x1b[:], "dsy")
        p.wait(v, "dsy"); p.wait(s, "dsy"); p.wait(t, "dsy"); p.wait(g, "dsy")
        ph4b.close()

        p.wait(v, "dsy")
        v.memset(gq_t[:], 0.0)
        p.inc(v.memset(ones_col_f[:], 1.0), "vx")
        with nc.sbuf_tensor("qbuf", [128, HF], F32) as qbuf:
            for mi, mat in enumerate((dr["wfcT"], dr["wprT"])):
                for kg in range(DG):
                    p.wait(g, "vx")
                    p.dma(g, qbuf[:], mat[kg], "dgp")
                    p.wait(v, "dgp")
                    p.inc(v.tensor_reduce(gq_t[:, mi * 8 + kg:mi * 8 + kg + 1],
                                          qbuf[:], axis=AX.X, op=ALU.add,
                                          apply_absolute_value=True), "vx")
            p.wait(t, "vx")
            p.inc(t.matmul(pb[4][:1, 0:16], ones_col_f[:], gq_t[:, 0:16], start=True,
                           stop=True), "mm")
            p.wait(v, "mm")
            v.tensor_reduce(gq_t[:1, 16:17], pb[4][:1, 0:8], axis=AX.X, op=ALU.add)
            p.inc(v.tensor_reduce(gq_t[:1, 17:18], pb[4][:1, 8:16], axis=AX.X,
                                  op=ALU.add), "vx")
            p.wait(v, "vx")
            p.inc(v.tensor_scalar(gq_t[:1, 20:22], gq_t[:1, 16:18], 0.5 / (HF * D),
                                  0.5 * ABS_EPS, op0=ALU.mult, op1=ALU.add), "vx")
            p.wait(t, "vx")
            p.inc(t.matmul(pb[5][:, 0:2], ones_row_f[:], gq_t[:1, 20:22], start=True,
                           stop=True), "mm")
            p.wait(v, "mm")
            p.inc(v.tensor_copy(thrc[:], pb[5][:, 0:2]), "vx")
            for mat, col in ((dr["wfcT"], 0), (dr["wprT"], 1)):
                for kg in range(DG):
                    p.wait(g, "vx"); p.wait(g, "sc"); p.wait(g, "dsy")
                    p.dma(g, qbuf[:], mat[kg], "dgp")
                    p.wait(s, "dgp"); p.wait(s, "vx"); p.wait(s, "dsy")
                    p.inc(s.sign(tmp_c[:, 0:HF] if False else tmp_c[:, 0:TLOC], qbuf[:, 0:TLOC]), "sc")
                    p.inc(s.sign(tmp_e[:, 0:TLOC], qbuf[:, TLOC:HF]), "sc")
                    p.inc(s.activation(tmp_a[:, 0:TLOC], qbuf[:, 0:TLOC], AF.Abs), "sc")
                    p.inc(s.activation(tmp_b[:, 0:TLOC], qbuf[:, TLOC:HF], AF.Abs), "sc")
                    p.wait(v, "sc"); p.wait(v, "dsy")
                    p.inc(v.scalar_tensor_tensor(tmp_c[:, 0:TLOC], tmp_a[:, 0:TLOC],
                                                 thrc[:, col:col + 1],
                                                 tmp_c[:, 0:TLOC],
                                                 op0=ALU.is_ge, op1=ALU.mult), "vx")
                    p.inc(v.scalar_tensor_tensor(tmp_e[:, 0:TLOC], tmp_b[:, 0:TLOC],
                                                 thrc[:, col:col + 1],
                                                 tmp_e[:, 0:TLOC],
                                                 op0=ALU.is_ge, op1=ALU.mult), "vx")
                    p.wait(sy, "vx")
                    if col == 0:
                        dst0 = wq_spill.ap()[4 * kg:4 * kg + 2].rearrange("b p k m -> p b k m")
                        dst1 = wq_spill.ap()[4 * kg + 2:4 * kg + 4].rearrange("b p k m -> p b k m")
                    else:
                        dst0 = wqp_spill.ap()[kg][:, 0:HG // 2, :]
                        dst1 = wqp_spill.ap()[kg][:, HG // 2:, :]
                    with nc.allow_non_contiguous_dma("band-order spill"):
                        p.dma(sy, dst0, tmp_c[:, 0:TLOC], "dsy")
                        p.dma(sy, dst1, tmp_e[:, 0:TLOC], "dsy")
        with contextlib.ExitStack() as lx:
            hq = lx.enter_context(nc.sbuf_tensor("hq", [128, HG, 512], BF16))
            wqa = lx.enter_context(nc.sbuf_tensor("wqa", [128, 2, HG * 128], BF16))
            lnstg = lx.enter_context(nc.sbuf_tensor("lnstg", [128, 512], F32))
            wrk = lx.enter_context(nc.sbuf_tensor("wrk", [128, 2, D], BF16))
            kvstg = lx.enter_context(nc.sbuf_tensor("kvstg", [128, 512], BF16))
            lxm = lx.enter_context(contextlib.ExitStack())
            xmix = lxm.enter_context(nc.sbuf_tensor("xmix", [128, DG, TLOC], BF16))
            rwkv_tiles = {}

            def ffn1_gen():
                for q in range(TLOC // 512):
                    q0 = q * 512

                    def evac_fc(oc, tqi, bank):
                        p.wait(s, "mmF")
                        cnt = p.inc(s.activation(hq[:, oc, :], bank[:, :], AF.Gelu),
                                    "scF")
                        return ("scF", cnt)

                    yield from gemm_gen(lambda oc: wq_spill.ap()[oc], HG, DG, wqa,
                                        lambda kg, tqi: xn1[:, kg, q0:q0 + 512],
                                        evac_fc, 1, lambda oc, tqi: 4 + oc % 2,
                                        pre_wait=("vx", "scF"), mmtag="mmF")
                    if upto == "fulldbg" and q == 0:
                        d_hq0 = nc.dram_tensor("d_hq0", [128, HG, 512], BF16,
                                               kind="ExternalOutput").ap()
                        p.wait(sy, "scF")
                        p.dma(sy, d_hq0[:], hq[:], "dsyF")
                        p.wait(v, "dsyF"); p.wait(s, "dsyF"); p.wait(t, "dsyF")
                    p.wait(v, "scF")
                    ln_emit(hq, HG, hq, LN_EPS, gamma_ap=lnw_t, beta_ap=lnb_t,
                            tcols=512, vxt="vxF", sct="scF", mmt="mmF", ps0=6,
                            lnstg=lnstg)
                    if upto == "fulldbg" and q == 0:
                        d_hq1 = nc.dram_tensor("d_hq1", [128, HG, 512], BF16,
                                               kind="ExternalOutput").ap()
                        p.wait(sy, "vxF")
                        p.dma(sy, d_hq1[:], hq[:], "dsyF")
                        p.wait(t, "dsyF")
                    yield

                    def evac_pr(oc, tqi, bank):
                        p.wait(s, "mmF"); p.wait(s, "dsyF")
                        cnt = p.inc(s.copy(stg_y[:], bank[:, :]), "scF")
                        p.wait(sy, "scF")
                        p.dma(sy, ffn_spill.ap()[:, oc, q0:q0 + 512], stg_y[:], "dsyF")
                        return ("scF", cnt)

                    yield from gemm_gen(lambda oc: wqp_spill.ap()[oc], DG, HG, wqa,
                                        lambda kg, tqi: hq[:, kg, :], evac_pr, 1,
                                        lambda oc, tqi: 4 + oc % 2,
                                        pre_wait=("vxF",), mmtag="mmF")

            def rwkv_gen():
                p.wait(g, "cc")
                p.dma(g, e2g[:], e2_out.ap().rearrange("j p g -> p j g"), "dgpM")
                p.wait(v, "dgpM")
                v.memset(bndc[:], 0.0)
                for j in range(8):
                    ia = v.scalar_tensor_tensor(bndc[:], e2g[:, j, :], selp[:, j:j + 1],
                                                bndc[:], op0=ALU.mult, op1=ALU.add)
                p.inc(ia, "vxM")
                p.wait(v, "dsy")
                yield
                for mx, wname, mixcol in ((0, "wkT", 2), (1, "wvT", 3), (2, "wrT", 4)):
                    p.wait(v, "mmM")
                    for gi in range(DG):
                        v.tensor_copy(tmp_c[:, 1:TLOC], xn1[:, gi, 0:TLOC - 1])
                        v.tensor_copy(tmp_c[:, 0:1], bndc[:, gi:gi + 1])
                        v.tensor_scalar_mul(tmp_a[:, 0:TLOC], tmp_c[:, 0:TLOC],
                                            omix[:, gi:gi + 1, mx]
                                            .rearrange("p g -> p g"))
                        ia = v.scalar_tensor_tensor(xmix[:, gi, :], xn1[:, gi, :],
                                                    tvec_t[:, gi:gi + 1, mixcol]
                                                    .rearrange("p g -> p g"),
                                                    tmp_a[:, 0:TLOC],
                                                    op0=ALU.mult, op1=ALU.add)
                        p.inc(ia, "vxM")
                        if gi % 4 == 3:
                            yield

                    def evac_kv(oc, tqi, bank, mx=mx):
                        p.wait(s, "mmM"); p.wait(s, "dsyM")
                        if mx == 2:
                            cnt = p.inc(s.activation(kvstg[:], bank[:, :],
                                                     AF.Sigmoid), "scM")
                        else:
                            cnt = p.inc(s.copy(kvstg[:], bank[:, :]), "scM")
                        p.wait(sy, "scM")
                        p.dma(sy, rwkv_spill.ap()[mx, oc, :, tqi * 512:(tqi + 1) * 512],
                              kvstg[:], "dsyM")
                        return ("scM", cnt)

                    yield from gemm_gen(lambda oc: dr[wname][oc], DG, DG, wrk,
                                        lambda kg, tqi: xmix[:, kg,
                                                             tqi * 512:(tqi + 1) * 512],
                                        evac_kv, 4,
                                        lambda oc, tqi: (oc * 4 + tqi) % 4,
                                        pre_wait=("vxM",), mmtag="mmM")
                lxm.close()
                kgt = lx.enter_context(nc.sbuf_tensor("kgt", [128, TLOC], BF16))
                vgt = lx.enter_context(nc.sbuf_tensor("vgt", [128, TLOC], BF16))
                sgt = lx.enter_context(nc.sbuf_tensor("sgt", [128, TLOC], BF16))
                ekv_t = lx.enter_context(nc.sbuf_tensor("ekv_t", [128, TLOC], F32))
                nmt = lx.enter_context(nc.sbuf_tensor("nmt", [128, 1 + TLOC], F32))
                dnt = lx.enter_context(nc.sbuf_tensor("dnt", [128, 1 + TLOC], F32))
                tmp_d = lx.enter_context(nc.sbuf_tensor("tmp_d", [128, TLOC], F32))
                rwkv_tiles["tmp_d"] = tmp_d
                ones2k = lx.enter_context(nc.sbuf_tensor("ones2k", [128, TLOC], BF16))
                iota_t = lx.enter_context(nc.sbuf_tensor("iota_t", [128, TLOC], F32))
                rwkv_tiles["ywo_tq"] = lx.enter_context(
                    nc.sbuf_tensor("ywo_tq", [128, DG, 512], BF16))
                rwkv_tiles["f1t"] = lx.enter_context(
                    nc.sbuf_tensor("f1t", [128, 512], BF16))
                ywo_tq, f1t = rwkv_tiles["ywo_tq"], rwkv_tiles["f1t"]
                p.wait(g, "vxM"); p.wait(g, "mmM"); p.wait(g, "scM")
                p.inc(g.iota(iota_t[:], pattern=[[1, TLOC]], base=1,
                             channel_multiplier=0,
                             allow_small_or_imprecise_dtypes=True), "gpM")
                p.wait(v, "mmM"); p.wait(v, "scM")
                p.inc(v.memset(ones2k[:], 1.0), "vxM")
                for gi in range(DG):
                    p.wait(g, "vxM")
                    p.dma(g, kgt[:], rwkv_spill.ap()[0, gi], "dgpM")
                    p.wait(v, "dgpM")
                    p.inc(v.reduce_max(kmx[:, gi:gi + 1], kgt[:], axis=AX.X), "vxM")
                    yield
                p.wait(sy, "vxM")
                p.dma(sy, e3_in.ap(), kmx[:], "dsyM")
                p.wait(g, "dsyM")
                p.inc(g.collective_compute("AllGather", ALU.bypass,
                                           replica_groups=[list(range(8))],
                                           ins=[e3_in.ap()], outs=[e3_out.ap()]), "cc")
                yield
                p.wait(g, "cc")
                p.dma(g, e3g[:], e3_out.ap().rearrange("j p g -> p j g"), "dgpM")
                p.wait(v, "dgpM")
                v.memset(kmx[:], -1e38)
                for j in range(8):
                    ia = v.scalar_tensor_tensor(kmx[:], e3g[:, j, :],
                                                selc_pen[:, j:j + 1], kmx[:],
                                                op0=ALU.add, op1=ALU.max)
                p.inc(ia, "vxM")
                p.inc(v.tensor_scalar_mul(kmx[:], kmx[:], -1.0), "vxM")
                yield
                for gi in range(DG):
                    p.wait(g, "vxM"); p.wait(g, "scM")
                    p.dma(g, kgt[:], rwkv_spill.ap()[0, gi], "dgpM")
                    p.dma(g, vgt[:], rwkv_spill.ap()[1, gi], "dgpM")
                    p.wait(s, "dgpM"); p.wait(s, "vxM")
                    p.inc(s.activation(tmp_b[:, 0:TLOC], kgt[:], AF.Exp,
                                       bias=kmx[:, gi:gi + 1]), "scM")
                    p.wait(v, "scM"); p.wait(v, "dgpM")
                    p.inc(v.tensor_mul(ekv_t[:], tmp_b[:, 0:TLOC], vgt[:]), "vxM")
                    v.tensor_scalar_mul(tmp_a[:, 0:TLOC], ones2k[:],
                                        wlane[:, gi:gi + 1, 1].rearrange("p g -> p g"))
                    p.inc(v.tensor_tensor_scan(nmt[:, 1:], tmp_a[:, 0:TLOC], ekv_t[:],
                                               0.0, ALU.mult, ALU.add), "vxM")
                    p.inc(v.tensor_tensor_scan(dnt[:, 1:], tmp_a[:, 0:TLOC],
                                               tmp_b[:, 0:TLOC], 0.0, ALU.mult,
                                               ALU.add), "vxM")
                    v.tensor_copy(aend[:, 0, gi:gi + 1], nmt[:, TLOC:TLOC + 1])
                    p.inc(v.tensor_copy(aend[:, 1, gi:gi + 1], dnt[:, TLOC:TLOC + 1]),
                          "vxM")
                    yield
                p.wait(sy, "vxM")
                p.dma(sy, e4_in.ap(), aend[:], "dsyM")
                p.wait(g, "dsyM")
                p.inc(g.collective_compute("AllGather", ALU.bypass,
                                           replica_groups=[list(range(8))],
                                           ins=[e4_in.ap()], outs=[e4_out.ap()]), "cc")
                yield
                p.wait(g, "cc")
                p.dma(g, e4g[:], e4_out.ap().rearrange("j p c g -> p j c g"), "dgpM")
                p.wait(v, "dgpM")
                v.memset(ainb[:], 0.0)
                for j in range(8):
                    if j in (0, 4):
                        v.tensor_copy(accf[:], e4g[:, j, :, :])
                    else:
                        for comp in range(2):
                            v.tensor_mul(accf[:, comp, :], accf[:, comp, :],
                                         wlane[:, :, 2])
                            v.tensor_add(accf[:, comp, :], accf[:, comp, :],
                                         e4g[:, j, comp, :])
                    ia = None
                    for comp in range(2):
                        ia = v.scalar_tensor_tensor(ainb[:, comp, :], accf[:, comp, :],
                                                    selp[:, j:j + 1], ainb[:, comp, :],
                                                    op0=ALU.mult, op1=ALU.add)
                p.inc(ia, "vxM")
                yield
                for gi in range(DG):
                    p.wait(g, "vxM"); p.wait(g, "scM")
                    p.dma(g, kgt[:], rwkv_spill.ap()[0, gi], "dgpM")
                    p.dma(g, vgt[:], rwkv_spill.ap()[1, gi], "dgpM")
                    p.dma(g, sgt[:], rwkv_spill.ap()[2, gi], "dgpM")
                    p.wait(s, "dgpM"); p.wait(s, "vxM")
                    p.inc(s.activation(tmp_b[:, 0:TLOC], kgt[:], AF.Exp,
                                       bias=kmx[:, gi:gi + 1]), "scM")
                    p.inc(s.activation(tmp_d[:], iota_t[:], AF.Exp,
                                       scale=wlane[:, gi:gi + 1, 0]
                                       .rearrange("p g -> p g")), "scM")
                    if upto == "fulldbg" and gi == 2:
                        d_p3 = nc.dram_tensor("d_p3", [128, 2, TLOC], F32,
                                              kind="ExternalOutput").ap()
                        p.wait(sy, "scM")
                        p.dma(sy, d_p3[:, 0, :], tmp_d[:], "dsyM")
                        p.dma(sy, d_p3[:, 1, :], tmp_b[:, 0:TLOC], "dsyM")
                        p.wait(v, "dsyM"); p.wait(s, "dsyM")
                    p.wait(v, "scM"); p.wait(v, "dgpM")
                    p.inc(v.tensor_mul(ekv_t[:], tmp_b[:, 0:TLOC], vgt[:]), "vxM")
                    v.tensor_scalar_mul(tmp_a[:, 0:TLOC], ones2k[:],
                                        wlane[:, gi:gi + 1, 1].rearrange("p g -> p g"))
                    p.inc(v.tensor_tensor_scan(nmt[:, 1:], tmp_a[:, 0:TLOC], ekv_t[:],
                                               0.0, ALU.mult, ALU.add), "vxM")
                    p.inc(v.tensor_tensor_scan(dnt[:, 1:], tmp_a[:, 0:TLOC],
                                               tmp_b[:, 0:TLOC], 0.0, ALU.mult,
                                               ALU.add), "vxM")
                    p.wait(v, "vxM")
                    p.inc(v.scalar_tensor_tensor(nmt[:, 1:], tmp_d[:],
                                                 ainb[:, 0, gi:gi + 1], nmt[:, 1:],
                                                 op0=ALU.mult, op1=ALU.add), "vxM")
                    p.inc(v.scalar_tensor_tensor(dnt[:, 1:], tmp_d[:],
                                                 ainb[:, 1, gi:gi + 1], dnt[:, 1:],
                                                 op0=ALU.mult, op1=ALU.add), "vxM")
                    v.tensor_copy(nmt[:, 0:1], ainb[:, 0, gi:gi + 1])
                    p.inc(v.tensor_copy(dnt[:, 0:1], ainb[:, 1, gi:gi + 1]), "vxM")
                    p.inc(v.tensor_scalar_mul(ekv_t[:], ekv_t[:],
                                              wlane[:, gi:gi + 1, 3]
                                              .rearrange("p g -> p g")), "vxM")
                    p.inc(v.tensor_scalar_mul(tmp_b[:, 0:TLOC], tmp_b[:, 0:TLOC],
                                              wlane[:, gi:gi + 1, 3]
                                              .rearrange("p g -> p g")), "vxM")
                    p.inc(v.tensor_add(ekv_t[:], ekv_t[:], nmt[:, 0:TLOC]), "vxM")
                    p.inc(v.tensor_add(tmp_b[:, 0:TLOC], tmp_b[:, 0:TLOC],
                                       dnt[:, 0:TLOC]), "vxM")
                    p.wait(s, "vxM")
                    s.activation(tmp_d[:], tmp_b[:, 0:TLOC], AF.Ln)
                    p.inc(s.activation(tmp_d[:], tmp_d[:], AF.Exp, scale=-1.0), "scM")
                    if upto == "fulldbg" and gi == 2:
                        d_p2 = nc.dram_tensor("d_p2", [128, 4, TLOC], F32,
                                              kind="ExternalOutput").ap()
                        p.wait(sy, "scM"); p.wait(sy, "vxM")
                        p.dma(sy, d_p2[:, 0, :], tmp_b[:, 0:TLOC], "dsyM")
                        p.dma(sy, d_p2[:, 1, :], tmp_d[:], "dsyM")
                        p.dma(sy, d_p2[:, 2, :], ekv_t[:], "dsyM")
                        p.dma(sy, d_p2[:, 3, :], dnt[:, 0:TLOC], "dsyM")
                        p.wait(v, "dsyM")
                    p.wait(v, "scM")
                    p.inc(v.tensor_mul(ekv_t[:], ekv_t[:], tmp_d[:]), "vxM")
                    p.wait(v, "dsyM")
                    p.inc(v.tensor_mul(tmp_c[:, 0:TLOC], ekv_t[:], sgt[:]), "vxM")
                    p.wait(sy, "vxM")
                    p.dma(sy, rwkv_spill.ap()[3, gi], tmp_c[:, 0:TLOC], "dsyM")
                    yield
            def merge_gen():
                ywo_tq, f1t = rwkv_tiles["ywo_tq"], rwkv_tiles["f1t"]
                tmp_d = rwkv_tiles["tmp_d"]
                p.wait(g, "mmM"); p.wait(g, "scM"); p.wait(g, "vxM")
                wo_hist = {}
                for tqi in range(TLOC // 512):
                    q0 = tqi * 512
                    p.wait(sy, "mmM"); p.wait(sy, "dsyM")
                    p.dma(sy, ywo_tq[:],
                          rwkv_spill.ap()[3, :, :, q0:q0 + 512]
                          .rearrange("kg p t -> p kg t"), "dsyM")
                    for oc in range(DG):
                        par = oc % 2
                        key = ("wo", par)
                        if key in wo_hist:
                            p.wait(g, "mmM", at=wo_hist[key])
                        ldc = p.dma(g, wrk[:, par, :], dr["woT"][oc], f"wo{par}")
                        bk = oc % 4
                        if ("bank", bk) in wo_hist:
                            nm, cnt = wo_hist[("bank", bk)]
                            p.wait(t, nm, at=cnt)
                        p.wait(t, f"wo{par}", at=ldc)
                        p.wait(t, "dsyM")
                        for kg in range(DG):
                            i1 = t.matmul(pb[bk][:, :], wrk[:, par, kg * 128:(kg + 1) * 128],
                                          ywo_tq[:, kg, :], start=(kg == 0),
                                          stop=(kg == DG - 1))
                        wo_hist[key] = p.inc(i1, "mmM")
                        p.wait(g, "vxM")
                        p.dma(g, tmp_d[:, 0:512], x1_spill.ap()[:, oc, q0:q0 + 512],
                              "dgpM")
                        p.wait(g, "dsyF")
                        p.dma(g, f1t[:], ffn_spill.ap()[:, oc, q0:q0 + 512], "dgpM")
                        p.wait(v, "mmM"); p.wait(v, "dgpM"); p.wait(v, "dst")
                        v.tensor_add(tmp_a[:, 0:512], pb[bk][:, :], f1t[:])
                        cnt2 = p.inc(v.tensor_add(tmp_a[:, 0:512], tmp_a[:, 0:512],
                                                  tmp_d[:, 0:512]), "vxM")
                        wo_hist[("bank", bk)] = ("vxM", cnt2)
                        p.wait(sy, "vxM")
                        p.dma(sy, out_ap[:, oc, q0:q0 + 512], tmp_a[:, 0:512], "dst")
                    yield

            run_interleaved([(rwkv_gen(), 1), (ffn1_gen(), 2)])
            for _ in merge_gen():
                pass
            if upto == "fulldbg":
                d_kvs = nc.dram_tensor("d_kvs", [4, DG, 128, TLOC], BF16,
                                       kind="ExternalOutput").ap()
                d_f1 = nc.dram_tensor("d_f1", [128, DG, TLOC], BF16,
                                      kind="ExternalOutput").ap()
                d_sm = nc.dram_tensor("d_sm", [128, 64], F32,
                                      kind="ExternalOutput").ap()
                p.dma(g, d_kvs, rwkv_spill.ap(), "dgp")
                p.dma(g, d_f1, ffn_spill.ap(), "dgp")
                p.wait(sy, "vxM")
                p.dma(sy, d_sm[:, 0:8], kmx[:], "dsy")
                p.dma(sy, d_sm[:, 8:24], ainb[:].rearrange("p c g -> p (c g)"), "dsy")
                p.dma(sy, d_sm[:, 24:40], aend[:].rearrange("p c g -> p (c g)"), "dsy")
                p.dma(sy, d_sm[:, 40:42], thrc[:], "dsy")
                d_wqp = nc.dram_tensor("d_wqp", [DG, 128, HG, 128], BF16,
                                       kind="ExternalOutput").ap()
                p.dma(g, d_wqp, wqp_spill.ap(), "dgp")
                p.wait(sy, "dsy"); p.wait(g, "dgp")
            p.wait(sy, "dst")
    return p, dr, out_ap


def _band(w_T, n_oc, n_kg):
    K, M = w_T.shape
    assert K == n_kg * 128 and M == n_oc * 128
    return np.ascontiguousarray(
        w_T.reshape(n_kg, 128, n_oc, 128).transpose(2, 1, 0, 3))


def _prep_core_inputs(inputs, core):
    t0 = (core % 4) * TLOC
    b = core // 4
    r = {}

    def fm(x2d):
        f = x2d.shape[1]
        return np.ascontiguousarray(
            x2d.T.reshape(f // 128, 128, x2d.shape[0]).transpose(1, 0, 2))

    def pcol(vec, ng):
        return np.ascontiguousarray(np.asarray(vec).reshape(ng, 128).T)

    A = lambda k: np.asarray(inputs[k])
    r["x0T"] = fm(A("L0_attn_x")[b, t0:t0 + TLOC, :])
    r["l0_gamma"] = pcol(A("L0_attn_gamma"), DG)
    r["l0_beta"] = pcol(A("L0_attn_beta"), DG)
    w_in = A("L0_attn_W_in")
    r["w_in_u"] = _band(w_in[:, :DIN], DIN // 128, DG)
    r["w_in_z"] = _band(w_in[:, DIN:], DIN // 128, DG)
    chb = (core % 4) * CH
    cw = A("L0_attn_conv_w")[:, 0, :]
    r["conv_w"] = np.ascontiguousarray(
        cw[chb:chb + CH].reshape(CGN, 128, 4).transpose(1, 0, 2))
    r["conv_b"] = pcol(A("L0_attn_conv_b")[chb:chb + CH], CGN)
    r["w_xproj"] = np.ascontiguousarray(
        A("L0_attn_W_xproj")[chb:chb + CH].reshape(CGN, 128, 96).transpose(1, 0, 2))
    r["w_dt"] = np.ascontiguousarray(A("L0_attn_W_dt")[:, chb:chb + CH])
    r["b_dt"] = pcol(A("L0_attn_b_dt")[chb:chb + CH], CGN)
    r["a_log"] = np.ascontiguousarray(
        A("L0_attn_A_log")[chb:chb + CH].reshape(CGN, 128, DSTT).transpose(1, 0, 2))
    r["d_vec"] = pcol(A("L0_attn_D")[chb:chb + CH], CGN)
    r["w_out"] = _band(A("L0_attn_W_out"), DG, DIN // 128)
    r["w1T"] = _band(A("L0_ffn_w1_w").T, HG, DG)
    r["w2T"] = _band(A("L0_ffn_w2_w").T, HG, DG)
    r["w1s"] = pcol(A("L0_ffn_w1_s"), HG)
    r["w2s"] = pcol(A("L0_ffn_w2_s"), HG)
    r["h1T"] = _band(A("L0_ffn_h1_w").T, HG, HG)
    r["h2T"] = _band(A("L0_ffn_h2_w").T, HG, HG)
    r["h1s"] = pcol(A("L0_ffn_h1_s"), HG)
    r["h2s"] = pcol(A("L0_ffn_h2_s"), HG)
    r["w3T"] = _band(A("L0_ffn_w3_w").T, DG, HG)
    r["w3s"] = pcol(A("L0_ffn_w3_s"), DG)
    tv = np.stack([A("L1_attn_time_decay").reshape(D),
                   A("L1_attn_time_first").reshape(D),
                   A("L1_attn_time_mix_k").reshape(D),
                   A("L1_attn_time_mix_v").reshape(D),
                   A("L1_attn_time_mix_r").reshape(D)], axis=1)
    r["tvec"] = np.ascontiguousarray(tv.reshape(DG, 128, 5).transpose(1, 0, 2))
    r["wkT"] = _band(A("L1_attn_Wk").T, DG, DG)
    r["wvT"] = _band(A("L1_attn_Wv").T, DG, DG)
    r["wrT"] = _band(A("L1_attn_Wr").T, DG, DG)
    r["woT"] = _band(A("L1_attn_Wo").T, DG, DG)
    bfc = _band(A("L1_ffn_W_fc").T, HG, DG)
    r["wfcT"] = np.ascontiguousarray(
        bfc.reshape(DG, 4, 128, DG, 128).transpose(0, 2, 1, 3, 4)
        .reshape(DG, 128, HF))
    r["lnw"] = pcol(A("L1_ffn_ln_w"), HG)
    r["lnb"] = pcol(A("L1_ffn_ln_b"), HG)
    bpr = _band(A("L1_ffn_W_proj").T, DG, HG)
    r["wprT"] = np.ascontiguousarray(bpr.reshape(DG, 128, HG * 128))
    sp = np.zeros((1, 8), np.float32)
    if core % 4 != 0:
        sp[0, core - 1] = 1.0
    r["selprev"] = sp
    scv = np.zeros((1, 8), np.float32)
    for j in (range(4) if core < 4 else range(4, 8)):
        scv[0, j] = 1.0
    r["selchain"] = scv
    r["selpen"] = np.where(scv > 0, 0.0, -1e38).astype(np.float32)
    r["dynoff"] = np.array([[chb * TLOC, t0, 0, 0]], np.int32)
    import ml_dtypes
    oh = np.zeros((DSTT, DSTT, 128), np.float32)
    for si_ in range(DSTT):
        oh[si_, si_, :] = 1.0
    r["onehot16"] = oh.astype(ml_dtypes.bfloat16)
    for k in list(r):
        if r[k].dtype == np.float64:
            r[k] = r[k].astype(np.float32)
    return r


def kernel(**inputs):
    prog, dr, out_ap = build_program("full")
    maps = [_prep_core_inputs(inputs, c) for c in range(8)]
    res = run_bass_kernel_spmd(prog.nc, maps, core_ids=list(range(8)))
    out = np.empty((B, T, D), np.float32)
    for c in range(8):
        b, t0 = c // 4, (c % 4) * TLOC
        o = res.results[c]["outT"]
        out[b, t0:t0 + TLOC, :] = o.transpose(2, 1, 0).reshape(TLOC, D)
    return out
